# version 09
# speedup vs baseline: 1.0148x; 1.0019x over previous
import contextlib
import math

import numpy as np
import concourse.bass as bass
import concourse.mybir as mybir
from concourse.bass_utils import run_bass_kernel_spmd

F32 = mybir.dt.float32
BF16 = mybir.dt.bfloat16
I32 = mybir.dt.int32
AF = mybir.ActivationFunctionType
ALU = mybir.AluOpType

B, S, D = 4, 4096, 1024
NCORES = 8
TOK = 2048
HALO = 256
NKV = 2304
H, KVH, HD = 16, 4, 64
LAT = 256
FFN_I = 4096
C_L0 = 9.0
C_L1 = 20.0
BASE_G, BASE_L = 1.0e6, 1.0e4
NEG = -1.0e30

DEINT = np.concatenate([np.arange(0, 64, 2), np.arange(1, 64, 2)])


def _hilo(x):
    h = np.float32(x)
    lo = np.float32(np.float64(x) - np.float64(h))
    return float(h), float(lo)


I2PI_H, I2PI_L = _hilo(1.0 / (2.0 * math.pi))
TPI_H, TPI_L = _hilo(2.0 * math.pi)
HPI = float(np.float32(0.5 * math.pi))


class Ctx:
    def __init__(self, nc, es, block):
        self.nc, self.es, self.block = nc, es, block
        self.sem, self.cnt = {}, {}
        for e in ("t", "v", "s", "g", "q"):
            self.sem[e] = es.enter_context(nc.semaphore("sem_" + e))
            self.cnt[e] = 0
        self.sb = {}

    def inc(self, instr, e, n=1):
        instr.then_inc(self.sem[e], n)
        self.cnt[e] += n
        return (e, self.cnt[e])

    def tok(self, e):
        return (e, self.cnt[e])

    def wait(self, eng, *toks):
        for (e, v) in toks:
            if v > 0:
                eng.wait_ge(self.sem[e], v)

    def T(self, fn):
        self.block.tensor(fn)

    def V(self, fn):
        self.block.vector(fn)

    def Sc(self, fn):
        self.block.scalar(fn)

    def G(self, fn):
        self.block.gpsimd(fn)

    def Q(self, fn):
        self.block.sync(fn)

    def barrier(self):
        toks = [self.tok(e) for e in ("t", "v", "s", "g", "q")]

        def mk(me):
            def _f(eng):
                self.wait(eng, *[t for t in toks if t[0] != me])
            return _f

        self.T(mk("t")); self.V(mk("v")); self.Sc(mk("s")); self.G(mk("g")); self.Q(mk("q"))

    def sbuf(self, name, shape, dt=F32):
        t = self.es.enter_context(self.nc.sbuf_tensor(name, shape, dt))
        self.sb[name] = t
        return t

    def psum_t(self, name, shape, dt=F32):
        return self.es.enter_context(self.nc.psum_tensor(name, shape, dt))

    def inp(self, name, shape, dt=F32):
        return self.nc.dram_tensor(name, shape, dt, kind="ExternalInput")

    def outp(self, name, shape, dt=F32):
        return self.nc.dram_tensor(name, shape, dt, kind="ExternalOutput")

    def dram(self, name, shape, dt=F32):
        return self.nc.dram_tensor(name, shape, dt)


CONST_DEFS = [("onesF", [128, 2], F32), ("onesRowF", [1, 128], F32),
              ("b512", [128, 1], F32), ("identB", [128, 128], BF16),
              ("smatTF", [64, 64], F32), ("biasC0", [128, 1], F32),
              ("biasC1", [128, 1], F32), ("c0125", [128, 1], F32),
              ("cm05", [128, 1], F32), ("iotaW", [64, 1], F32),
              ("identF", [128, 128], F32)]


def alloc_common(cx):
    cx.cin = {n: cx.inp(n, sh, dt) for n, sh, dt in CONST_DEFS}
    cx.c = {n: cx.sbuf("c_" + n, sh, dt) for n, sh, dt in CONST_DEFS}
    cx.scr = {
        "A": cx.sbuf("scrA", [128, 512], F32),
        "B": cx.sbuf("scrB", [128, 512], F32),
        "C": cx.sbuf("scrC", [128, 512], F32),
        "D": cx.sbuf("scrD", [128, 512], F32),
        "E": cx.sbuf("scrE", [128, 512], F32),
        "D2": cx.sbuf("scrD2", [128, 512], F32),
        "A2": cx.sbuf("scrA2", [128, 512], F32),
        "I": cx.sbuf("scrI", [128, 512], I32),
        "r0": cx.sbuf("scrR0", [1, 512], F32),
        "r1": cx.sbuf("scrR1", [1, 512], F32),
        "r2": cx.sbuf("scrR2", [1, 512], F32),
        "r3": cx.sbuf("scrR3", [1, 512], F32),
        "r4": cx.sbuf("scrR4", [1, 512], F32),
        "pt0": cx.sbuf("pt0", [128, 512], BF16),
        "pt1": cx.sbuf("pt1", [128, 512], BF16),
        "amx4": cx.sbuf("amx4", [4, 128], F32),
    }
    cx.sbuf("invf_row", [1, 64], F32)
    cx.sbuf("invf_col", [64, 1], F32)
    cx.ps = [cx.psum_t("ps%d" % i, [128, 512], F32) for i in range(6)]
    cx.pst = cx.psum_t("ps_tr", [128, 128], BF16)
    cx.pstf = cx.psum_t("ps_trf", [128, 128], F32)
    cx.dscr = cx.dram("dscr", [1, 8192], F32)


def emit_load_consts(cx):
    def _q(qe):
        for n in cx.c:
            cx.inc(qe.dma_start(out=cx.c[n][:], in_=cx.cin[n].ap()), "q", 16)
    cx.Q(_q)
    cx.barrier()


def emit_row_transpose_from_dscr(cx, ncols, dst_ap):
    dr = cx.dscr

    def _q(qe):
        cx.wait(qe, cx.tok("q"))
        cx.inc(qe.dma_start(out=dst_ap,
                            in_=dr.ap()[0:1, 0:ncols].rearrange("a (t p) -> (a p) t", p=128)), "q", 16)

    cx.Q(_q)
    return cx.tok("q")


def emit_rsqrt_row(cx, v_row_ap, out_row_ap, nn, after=()):
    scr, c = cx.scr, cx.c
    t1 = scr["r3"][0:1, 0:nn]

    def _s(se):
        cx.wait(se, *after)
        cx.inc(se.activation(out=t1, in_=v_row_ap, func=AF.Sqrt), "s")

    cx.Sc(_s)
    st = cx.tok("s")

    def _v(ve):
        cx.wait(ve, st)
        ve.reciprocal(out=out_row_ap, in_=t1)
        ve.drain()
        ve.tensor_tensor(out=t1, in0=out_row_ap, in1=out_row_ap, op=ALU.mult)
        ve.drain()
        ve.scalar_tensor_tensor(out=t1, in0=v_row_ap, scalar=-0.5, in1=t1,
                                op0=ALU.mult, op1=ALU.mult)
        ve.drain()
        cx.inc(ve.scalar_tensor_tensor(out=out_row_ap, in0=t1, scalar=1.5,
                                       in1=out_row_ap, op0=ALU.add, op1=ALU.mult), "v")

    cx.V(_v)
    return cx.tok("v")


def emit_stats(cx, chunks, nt, eps, with_mean, m_ap, r_ap, mr_ap, after=()):
    scr, c, ps = cx.scr, cx.c, cx.ps
    nch = len(chunks)
    inv_n = 1.0 / (128 * nch)
    v0 = cx.cnt["v"]
    t_sq_base = cx.cnt["t"] + (1 if with_mean else 0)

    def _v(ve):
        cx.wait(ve, *after)
        for ci in range(nch):
            if ci >= 2:
                cx.wait(ve, ("t", t_sq_base + ci - 1))
            cx.inc(ve.tensor_tensor(out=scr["A" if ci % 2 == 0 else "B"][:, 0:nt],
                                    in0=chunks[ci], in1=chunks[ci], op=ALU.mult), "v")

    cx.V(_v)

    def _t(te):
        cx.wait(te, *after)
        if with_mean:
            mm = None
            for ci in range(nch):
                mm = te.matmul(out=ps[4][0:1, 0:nt], lhsT=c["onesF"][:, 0:1], rhs=chunks[ci],
                               start=(ci == 0), stop=(ci == nch - 1))
            cx.inc(mm, "t")
        for ci in range(nch):
            cx.wait(te, ("v", v0 + ci + 1))
            cx.inc(te.matmul(out=ps[5][0:1, 0:nt], lhsT=c["onesF"][:, 0:1],
                             rhs=scr["A" if ci % 2 == 0 else "B"][:, 0:nt],
                             start=(ci == 0), stop=(ci == nch - 1)), "t")

    cx.T(_t)
    t_tok = cx.tok("t")
    var_sv = scr["r2"][0:1, 0:nt] if r_ap is not m_ap else None

    def _v2(ve):
        cx.wait(ve, t_tok)
        if with_mean:
            ve.tensor_scalar(out=m_ap, in0=ps[4][0:1, 0:nt], scalar1=inv_n, scalar2=None, op0=ALU.mult)
            ve.drain()
        ve.tensor_scalar(out=r_ap, in0=ps[5][0:1, 0:nt], scalar1=inv_n, scalar2=float(eps),
                         op0=ALU.mult, op1=ALU.add)
        ve.drain()
        if with_mean:
            ve.tensor_tensor(out=scr["A"][0:1, 0:nt], in0=m_ap, in1=m_ap, op=ALU.mult)
            ve.drain()
            ve.tensor_tensor(out=r_ap, in0=r_ap, in1=scr["A"][0:1, 0:nt], op=ALU.subtract)
            ve.drain()
        cx.inc(ve.tensor_copy(out=scr["B"][0:1, 0:nt], in_=r_ap), "v")

    cx.V(_v2)
    rs = emit_rsqrt_row(cx, scr["B"][0:1, 0:nt], r_ap, nt, after=(cx.tok("v"),))

    if with_mean:
        def _v3(ve):
            cx.wait(ve, rs)
            cx.inc(ve.tensor_tensor(out=mr_ap, in0=m_ap, in1=r_ap, op=ALU.mult), "v")

        cx.V(_v3)
    return cx.tok("v")


def emit_bcast(cx, vec_row_ap, ps_idx, p, n, after=()):
    def _t(te):
        cx.wait(te, *after)
        cx.inc(te.matmul(out=cx.ps[ps_idx][0:p, 0:n], lhsT=cx.c["onesRowF"][0:1, 0:p],
                         rhs=vec_row_ap, start=True, stop=True), "t")
    cx.T(_t)
    return cx.tok("t")


def emit_round_clip(cx, vin_ap, vout_ap, lo, hi, p=128, n=512, after=()):
    scr, c = cx.scr, cx.c

    def _v(ve):
        cx.wait(ve, *after)
        ve.tensor_copy(out=scr["I"][0:p, 0:n], in_=vin_ap)
        ve.drain()
        ve.tensor_copy(out=scr["B"][0:p, 0:n], in_=scr["I"][0:p, 0:n])
        ve.tensor_scalar(out=scr["A"][0:p, 0:n], in0=scr["I"][0:p, 0:n],
                         scalar1=-1.0, scalar2=None, op0=ALU.add)
        ve.drain()
        ve.scalar_tensor_tensor(out=scr["C"][0:p, 0:n], in0=vin_ap, scalar=1.0,
                                in1=scr["B"][0:p, 0:n], op0=ALU.mult, op1=ALU.subtract)
        ve.drain()
        ve.scalar_tensor_tensor(out=scr["B"][0:p, 0:n], in0=scr["C"][0:p, 0:n], scalar=-0.5,
                                in1=scr["A"][0:p, 0:n], op0=ALU.is_gt, op1=ALU.add)
        ve.drain()
        ve.scalar_tensor_tensor(out=scr["A"][0:p, 0:n], in0=scr["C"][0:p, 0:n], scalar=0.5,
                                in1=scr["B"][0:p, 0:n], op0=ALU.is_ge, op1=ALU.add)
        ve.drain()
        cx.inc(ve.tensor_scalar(out=vout_ap, in0=scr["A"][0:p, 0:n],
                                scalar1=float(lo), scalar2=float(hi),
                                op0=ALU.max, op1=ALU.min), "v")

    cx.V(_v)
    return cx.tok("v")


def emit_sin_chain(cx, src_ap, out_ap, p, n, after):
    scr = cx.scr

    def _v(ve):
        cx.wait(ve, *after)
        ve.tensor_scalar(out=scr["B"][0:p, 0:n], in0=src_ap, scalar1=I2PI_H, scalar2=None, op0=ALU.mult)
        ve.tensor_scalar(out=scr["C"][0:p, 0:n], in0=src_ap, scalar1=I2PI_L, scalar2=None, op0=ALU.mult)
        ve.drain()
        ve.scalar_tensor_tensor(out=scr["B"][0:p, 0:n], in0=scr["C"][0:p, 0:n], scalar=1.0,
                                in1=scr["B"][0:p, 0:n], op0=ALU.mult, op1=ALU.add)
        ve.drain()
        ve.tensor_copy(out=scr["I"][0:p, 0:n], in_=scr["B"][0:p, 0:n])
        ve.drain()
        ve.tensor_copy(out=scr["C"][0:p, 0:n], in_=scr["I"][0:p, 0:n])
        ve.tensor_scalar(out=scr["A"][0:p, 0:n], in0=scr["I"][0:p, 0:n],
                         scalar1=-1.0, scalar2=None, op0=ALU.add)
        ve.drain()
        ve.scalar_tensor_tensor(out=scr["B"][0:p, 0:n], in0=scr["B"][0:p, 0:n], scalar=1.0,
                                in1=scr["C"][0:p, 0:n], op0=ALU.mult, op1=ALU.subtract)
        ve.drain()
        ve.scalar_tensor_tensor(out=scr["A"][0:p, 0:n], in0=scr["B"][0:p, 0:n], scalar=-0.5,
                                in1=scr["A"][0:p, 0:n], op0=ALU.is_gt, op1=ALU.add)
        ve.drain()
        ve.scalar_tensor_tensor(out=scr["C"][0:p, 0:n], in0=scr["B"][0:p, 0:n], scalar=0.5,
                                in1=scr["A"][0:p, 0:n], op0=ALU.is_ge, op1=ALU.add)
        ve.drain()
        ve.scalar_tensor_tensor(out=scr["B"][0:p, 0:n], in0=scr["C"][0:p, 0:n], scalar=-TPI_H,
                                in1=src_ap, op0=ALU.mult, op1=ALU.add)
        ve.drain()
        cx.inc(ve.scalar_tensor_tensor(out=scr["B"][0:p, 0:n], in0=scr["C"][0:p, 0:n],
                                       scalar=-TPI_L, in1=scr["B"][0:p, 0:n],
                                       op0=ALU.mult, op1=ALU.add), "v")

    cx.V(_v)
    vt = cx.tok("v")

    def _s(se):
        cx.wait(se, vt)
        cx.inc(se.activation(out=out_ap, in_=scr["B"][0:p, 0:n], func=AF.Sin), "s")

    cx.Sc(_s)
    return cx.tok("s")


def emit_rope_tables(cx, base, pos_dram, nt_total, cos_t, sin_t):
    scr, c, ps = cx.scr, cx.c, cx.ps
    ch, cl = _hilo(math.log(base) / 32.0)
    invf_row, iv = cx.sb["invf_row"], cx.sb["invf_col"]

    def _v(ve):
        cx.wait(ve, cx.tok("q"), cx.tok("s"))
        ve.tensor_scalar(out=scr["A"][0:64, 0:1], in0=c["iotaW"][:], scalar1=-ch, scalar2=None, op0=ALU.mult)
        ve.tensor_scalar(out=scr["B"][0:64, 0:1], in0=c["iotaW"][:], scalar1=-cl, scalar2=None, op0=ALU.mult)
        ve.drain()
        cx.inc(ve.tensor_tensor(out=scr["C"][0:64, 0:1], in0=scr["A"][0:64, 0:1],
                                in1=scr["B"][0:64, 0:1], op=ALU.add), "v")

    cx.V(_v)
    vt = cx.tok("v")

    def _s(se):
        cx.wait(se, vt)
        cx.inc(se.activation(out=iv[:], in_=scr["C"][0:64, 0:1], func=AF.Exp), "s")

    cx.Sc(_s)
    st = cx.tok("s")
    dr = cx.dscr

    def _q(qe):
        cx.wait(qe, st)
        cx.inc(qe.dma_start(out=dr[0:1, 0:64], in_=iv[:]), "q", 16)
        cx.wait(qe, cx.tok("q"))
        cx.inc(qe.dma_start(out=invf_row[:], in_=dr.ap()[0:1, 0:64]), "q", 16)

    cx.Q(_q)
    q_tok = cx.tok("q")

    for off in range(0, nt_total, 512):
        nn = min(512, nt_total - off)

        def _qp(qe, off=off, nn=nn):
            cx.wait(qe, cx.tok("v"))
            cx.inc(qe.dma_start(out=cx.scr["r0"][0:1, 0:nn],
                                in_=pos_dram.ap()[0:1, off:off + nn]), "q", 16)

        cx.Q(_qp)
        qp = cx.tok("q")

        def _t(te, nn=nn, qp=qp):
            cx.wait(te, q_tok, qp, cx.tok("s"), cx.tok("v"))
            cx.inc(te.matmul(out=ps[0][0:64, 0:nn], lhsT=invf_row[:],
                             rhs=cx.scr["r0"][0:1, 0:nn], start=True, stop=True), "t")

        cx.T(_t)
        tt = cx.tok("t")

        def _v2(ve, nn=nn, tt=tt):
            cx.wait(ve, tt)
            ve.tensor_copy(out=scr["D"][0:64, 0:nn], in_=ps[0][0:64, 0:nn])
            ve.drain()
            cx.inc(ve.tensor_scalar(out=scr["E"][0:64, 0:nn], in0=scr["D"][0:64, 0:nn],
                                    scalar1=HPI, scalar2=None, op0=ALU.add), "v")

        cx.V(_v2)
        emit_sin_chain(cx, scr["D"][0:64, 0:nn], sin_t[:, off:off + nn], 64, nn, (cx.tok("v"),))
        emit_sin_chain(cx, scr["E"][0:64, 0:nn], cos_t[:, off:off + nn], 64, nn, (cx.tok("s"),))

    send = cx.tok("s")

    def _vg(ve):
        cx.wait(ve, send)

    cx.V(_vg)
    return cx.tok("s")


def mk_pipe():
    return {"n": 0, "vend": {}, "send": {}, "tstats": {}, "ssqrt": {},
            "tbc": {}, "gcos": {}, "qb": {}, "rs": {}, "tail": None, "ent": None}


def pipe_flush(cx, pipe):
    if pipe["tail"] is not None:
        pipe["tail"]()
        pipe["tail"] = None
    return pipe["vend"].get(pipe["n"] - 1)


def emit_proj_rope(cx, w_tiles, rhs_chunks, kchunks, nb_off, nn, cos_t, sin_t,
                   out_bf_ap, fold_a, bvec_row_ap, pipe=None):
    scr, c, ps = cx.scr, cx.c, cx.ps
    if pipe is not None:
        return _emit_proj_rope_piped(cx, w_tiles, rhs_chunks, kchunks, nb_off,
                                     nn, cos_t, sin_t, out_bf_ap, fold_a,
                                     bvec_row_ap, pipe)

    def _t(te):
        cx.wait(te, cx.tok("v"), cx.tok("s"))
        mm = None
        for k in range(kchunks):
            mm = te.matmul(out=ps[0][0:64, 0:nn], lhsT=w_tiles[k], rhs=rhs_chunks[k],
                           start=(k == 0), stop=(k == kchunks - 1))
        cx.inc(mm, "t")

    cx.T(_t)
    t1 = cx.tok("t")

    def _v(ve):
        cx.wait(ve, t1)
        ve.tensor_copy(out=scr["D"][0:64, 0:nn], in_=ps[0][0:64, 0:nn])
        ve.drain()
        cx.inc(ve.tensor_tensor(out=scr["A"][0:64, 0:nn], in0=scr["D"][0:64, 0:nn],
                                in1=scr["D"][0:64, 0:nn], op=ALU.mult), "v")

    cx.V(_v)
    v1 = cx.tok("v")

    def _t2(te):
        cx.wait(te, v1)
        te.matmul(out=ps[4][0:1, 0:nn], lhsT=c["onesF"][0:64, 0:1],
                  rhs=scr["A"][0:64, 0:nn], start=True, stop=True)
        cx.inc(te.matmul(out=ps[1][0:64, 0:nn], lhsT=c["smatTF"][:],
                         rhs=scr["D"][0:64, 0:nn], start=True, stop=True), "t")

    cx.T(_t2)
    t2 = cx.tok("t")
    rrow = scr["r1"][0:1, 0:nn]

    def _v2(ve):
        cx.wait(ve, t2)
        cx.inc(ve.tensor_scalar(out=scr["r0"][0:1, 0:nn], in0=ps[4][0:1, 0:nn],
                                scalar1=1.0 / 64.0, scalar2=1e-6, op0=ALU.mult, op1=ALU.add), "v")

    cx.V(_v2)
    rs = emit_rsqrt_row(cx, scr["r0"][0:1, 0:nn], rrow, nn, after=(cx.tok("v"),))

    if bvec_row_ap == "dscr":
        def _qb(qe):
            cx.wait(qe, rs)
            cx.inc(qe.dma_start(out=cx.dscr[0:1, nb_off:nb_off + nn], in_=rrow), "q", 16)

        cx.Q(_qb)
    ab = None
    if fold_a:
        ab = emit_bcast(cx, rrow, 5, 64, nn, after=(cx.tok("v"), rs))

    def _v4(ve):
        cx.wait(ve, t2)
        ve.tensor_tensor(out=scr["B"][0:64, 0:nn], in0=scr["D"][0:64, 0:nn],
                         in1=cos_t[:, nb_off:nb_off + nn], op=ALU.mult)
        ve.tensor_tensor(out=scr["C"][0:64, 0:nn], in0=ps[1][0:64, 0:nn],
                         in1=sin_t[:, nb_off:nb_off + nn], op=ALU.mult)
        ve.drain()
        ve.tensor_tensor(out=scr["B"][0:64, 0:nn], in0=scr["B"][0:64, 0:nn],
                         in1=scr["C"][0:64, 0:nn], op=ALU.add)
        ve.drain()
        if fold_a:
            cx.wait(ve, ab)
            cx.inc(ve.tensor_tensor(out=out_bf_ap, in0=scr["B"][0:64, 0:nn],
                                    in1=ps[5][0:64, 0:nn], op=ALU.mult), "v")
        else:
            cx.inc(ve.tensor_copy(out=out_bf_ap, in_=scr["B"][0:64, 0:nn]), "v")

    cx.V(_v4)
    return cx.tok("v")


def _emit_proj_rope_piped(cx, w_tiles, rhs_chunks, kchunks, nb_off, nn, cos_t,
                          sin_t, out_bf_ap, fold_a, bvec_row_ap, pipe):
    scr, c, ps = cx.scr, cx.c, cx.ps
    b = pipe["n"]
    pipe["n"] += 1
    par = b % 2
    psA = ps[0] if par == 0 else ps[2]
    psB = ps[1] if par == 0 else ps[3]
    sD = scr["D"] if par == 0 else scr["D2"]
    sA = scr["A"] if par == 0 else scr["A2"]
    rrow = scr["r1"][0:1, 0:nn] if par == 0 else scr["r2"][0:1, 0:nn]

    def pv(key, i):
        return pipe[key].get(i)

    def W(eng, *toks):
        for t in toks:
            if t is not None and t[1] > 0:
                eng.wait_ge(cx.sem[t[0]], t[1])

    if pipe["ent"] is None:
        pipe["ent"] = (cx.tok("v"), cx.tok("s"), cx.tok("g"))
    ent_v, ent_s, ent_g = (pipe["ent"] if b < 2 else (None, None, None))

    w_s2 = pv("send", b - 2)
    w_v2 = pv("vend", b - 2)

    def _t(te):
        W(te, w_s2, w_v2, ent_v, ent_s)
        mm = None
        for k in range(kchunks):
            mm = te.matmul(out=psA[0:64, 0:nn], lhsT=w_tiles[k], rhs=rhs_chunks[k],
                           start=(k == 0), stop=(k == kchunks - 1))
        cx.inc(mm, "t")

    cx.T(_t)
    t1 = cx.tok("t")

    if pipe["tail"] is not None:
        pipe["tail"]()
        pipe["tail"] = None

    w_g2 = pv("gcos", b - 2)
    w_t2s = pv("tstats", b - 2)

    def _s1(se):
        W(se, t1, w_g2, w_t2s, ent_g)
        se.activation(out=sD[0:64, 0:nn], in_=psA[0:64, 0:nn], func=AF.Copy)
        se.drain()
        cx.inc(se.activation(out=sA[0:64, 0:nn], in_=sD[0:64, 0:nn],
                             func=AF.Square), "s")

    cx.Sc(_s1)
    s1 = cx.tok("s")
    pipe["send"][b] = s1

    def _t2(te):
        W(te, s1, w_v2)
        te.matmul(out=psB[64:65, 0:nn], lhsT=c["onesF"][0:64, 0:1],
                  rhs=sA[0:64, 0:nn], start=True, stop=True)
        cx.inc(te.matmul(out=psB[0:64, 0:nn], lhsT=c["smatTF"][:],
                         rhs=sD[0:64, 0:nn], start=True, stop=True), "t")

    cx.T(_t2)
    t2 = cx.tok("t")
    pipe["tstats"][b] = t2

    w_sq1 = pv("ssqrt", b - 1)

    def _v2(ve):
        W(ve, t2, w_sq1)
        cx.inc(ve.tensor_scalar(out=scr["r0"][0:1, 0:nn], in0=psB[64:65, 0:nn],
                                scalar1=1.0 / 64.0, scalar2=1e-6,
                                op0=ALU.mult, op1=ALU.add), "v")

    cx.V(_v2)
    v2 = cx.tok("v")

    def _s2(se):
        W(se, v2)
        cx.inc(se.activation(out=scr["r3"][0:1, 0:nn], in_=scr["r0"][0:1, 0:nn],
                             func=AF.Sqrt), "s")

    cx.Sc(_s2)
    sq = cx.tok("s")
    pipe["ssqrt"][b] = sq
    w_tb2 = pv("tbc", b - 2)
    w_qb2 = pv("qb", b - 2)

    def _v3(ve):
        W(ve, sq, w_tb2, w_qb2)
        ve.reciprocal(out=rrow, in_=scr["r3"][0:1, 0:nn])
        ve.drain()
        ve.tensor_tensor(out=scr["r3"][0:1, 0:nn], in0=rrow, in1=rrow, op=ALU.mult)
        ve.drain()
        ve.scalar_tensor_tensor(out=scr["r3"][0:1, 0:nn], in0=scr["r0"][0:1, 0:nn],
                                scalar=-0.5, in1=scr["r3"][0:1, 0:nn],
                                op0=ALU.mult, op1=ALU.mult)
        ve.drain()
        cx.inc(ve.scalar_tensor_tensor(out=rrow, in0=scr["r3"][0:1, 0:nn],
                                       scalar=1.5, in1=rrow,
                                       op0=ALU.add, op1=ALU.mult), "v")

    cx.V(_v3)
    rs = cx.tok("v")
    pipe["rs"][b] = rs

    if bvec_row_ap == "dscr":
        def _qb(qe):
            W(qe, rs)
            cx.inc(qe.dma_start(out=cx.dscr[0:1, nb_off:nb_off + nn], in_=rrow), "q", 16)

        cx.Q(_qb)
    pipe["qb"][b] = cx.tok("q")

    w_v1 = pv("vend", b - 1)

    def _g(ge):
        W(ge, s1, w_v1, ent_v)
        cx.inc(ge.tensor_tensor(out=scr["B"][0:64, 0:nn], in0=sD[0:64, 0:nn],
                                in1=cos_t[:, nb_off:nb_off + nn], op=ALU.mult), "g")

    cx.G(_g)
    gc = cx.tok("g")
    pipe["gcos"][b] = gc

    def tail(b=b, rs=rs, gc=gc, psA=psA, psB=psB, rrow=rrow, nn=nn,
             nb_off=nb_off, out_bf_ap=out_bf_ap, fold_a=fold_a,
             cos_t=cos_t, sin_t=sin_t):
        if fold_a:
            def _t3(te):
                W(te, rs)
                cx.inc(te.matmul(out=psA[64:128, 0:nn], lhsT=c["onesRowF"][0:1, 0:64],
                                 rhs=rrow, start=True, stop=True), "t")

            cx.T(_t3)
        tb = cx.tok("t")
        pipe["tbc"][b] = tb

        def _v4(ve):
            ve.tensor_tensor(out=scr["C"][0:64, 0:nn], in0=psB[0:64, 0:nn],
                             in1=sin_t[:, nb_off:nb_off + nn], op=ALU.mult)
            W(ve, gc)
            ve.drain()
            ve.tensor_tensor(out=scr["C"][0:64, 0:nn], in0=scr["C"][0:64, 0:nn],
                             in1=scr["B"][0:64, 0:nn], op=ALU.add)
            ve.drain()
            if fold_a:
                W(ve, tb)
                cx.inc(ve.tensor_tensor(out=out_bf_ap, in0=scr["C"][0:64, 0:nn],
                                        in1=psA[64:128, 0:nn], op=ALU.mult), "v")
            else:
                cx.inc(ve.tensor_copy(out=out_bf_ap, in_=scr["C"][0:64, 0:nn]), "v")

        cx.V(_v4)
        pipe["vend"][b] = cx.tok("v")

    pipe["tail"] = tail
    return cx.tok("v")


def emit_v_build(cx, w_tiles, rhs_nb_fn, kchunks, nkt, v_tm, vf_scr_bf):
    c = cx.c

    def _g(ge):
        cx.wait(ge, cx.tok("v"))
        cx.inc(ge.memset(v_tm[:], 1.0), "g")

    cx.G(_g)
    g0 = cx.tok("g")
    for nb in range(nkt // 4):
        def _t(te, nb=nb):
            cx.wait(te, cx.tok("v"))
            mm = None
            for k in range(kchunks):
                mm = te.matmul(out=cx.ps[0][0:64, 0:512], lhsT=w_tiles[k], rhs=rhs_nb_fn(k, nb),
                               start=(k == 0), stop=(k == kchunks - 1))
            cx.inc(mm, "t")

        cx.T(_t)
        t1 = cx.tok("t")

        def _v(ve, t1=t1):
            cx.wait(ve, t1)
            cx.inc(ve.tensor_copy(out=vf_scr_bf[:], in_=cx.ps[0][0:64, 0:512]), "v")

        cx.V(_v)
        v1 = cx.tok("v")

        for u in range(4):
            def _t2(te, u=u, v1=v1):
                cx.wait(te, v1, cx.tok("v"))
                cx.inc(te.transpose(out=cx.pst[0:128, 0:64], in_=vf_scr_bf[:, 128 * u:128 * u + 128],
                                    identity=c["identB"][0:64, 0:64]), "t")

            cx.T(_t2)
            t2 = cx.tok("t")
            kt = nb * 4 + u

            def _v2(ve, kt=kt, t2=t2):
                cx.wait(ve, t2, g0)
                cx.inc(ve.tensor_copy(out=v_tm[:, 65 * kt:65 * kt + 64], in_=cx.pst[0:128, 0:64]), "v")

            cx.V(_v2)
    return cx.tok("v")


def emit_attn_job(cx, q_bf, k_bf, v_tm, o_bf, nq_total, bsc_part, bias_c, gmask_bf):
    scr, c, ps = cx.scr, cx.c, cx.ps
    pt = [scr["pt0"], scr["pt1"]]
    nqb = nq_total // 512

    for qb in range(nqb):
        nkt = 4 * qb + 4
        t0, s0, v0 = cx.cnt["t"], cx.cnt["s"], cx.cnt["v"]

        def _t(te, qb=qb, nkt=nkt, t0=t0, s0=s0, v0=v0):
            for kt in range(nkt):
                diag = kt >= nkt - 4
                pi = kt % 2
                if kt >= 2:
                    cx.wait(te, ("s", s0 + kt - 1))
                else:
                    cx.wait(te, ("v", v0))
                if diag:
                    i = kt - (nkt - 4)
                    te.matmul(out=ps[pi][:, 0:512], lhsT=c["identB"][:],
                              rhs=gmask_bf[:, 512 * i:512 * i + 512], start=True, stop=False)
                cx.inc(te.matmul(out=ps[pi][:, 0:512], lhsT=k_bf[:, 128 * kt:128 * kt + 128],
                                 rhs=q_bf[:, 512 * qb:512 * qb + 512],
                                 start=(not diag), stop=True), "t")
                cx.wait(te, ("s", s0 + kt + 1))
                cx.inc(te.matmul(out=ps[2][0:65, 0:512], lhsT=v_tm[:, 65 * kt:65 * kt + 65],
                                 rhs=pt[pi][:, 0:512], start=(kt == 0), stop=(kt == nkt - 1)), "t")

        cx.T(_t)

        def _s(se, nkt=nkt, t0=t0, s0=s0):
            for kt in range(nkt):
                pi = kt % 2
                cx.wait(se, ("t", t0 + 2 * kt + 1))
                sc = bsc_part[:, kt:kt + 1] if bsc_part is not None else c["c0125"][:]
                cx.inc(se.activation(out=pt[pi][:, 0:512], in_=ps[pi][:, 0:512],
                                     func=AF.Exp, scale=sc, bias=bias_c[:]), "s")

        cx.Sc(_s)
        tend = cx.tok("t")

        def _v(ve, tend=tend):
            cx.wait(ve, tend)
            cx.inc(ve.reciprocal(out=scr["r0"][0:1, 0:512], in_=ps[2][64:65, 0:512]), "v")

        cx.V(_v)
        bc = emit_bcast(cx, scr["r0"][0:1, 0:512], 3, 64, 512, after=(cx.tok("v"),))

        def _v2(ve, qb=qb, bc=bc):
            cx.wait(ve, bc)
            ve.tensor_copy(out=scr["E"][0:64, 0:512], in_=ps[2][0:64, 0:512])
            ve.drain()
            cx.inc(ve.tensor_tensor(out=o_bf[:, 512 * qb:512 * qb + 512],
                                    in0=scr["E"][0:64, 0:512], in1=ps[3][0:64, 0:512],
                                    op=ALU.mult), "v")

        cx.V(_v2)
    return cx.tok("v")


def build_launchA():
    nc = bass.Bass("TRN2", target_bir_lowering=False, debug=False)
    es = contextlib.ExitStack()
    with es:
        block = es.enter_context(nc.Block())
        cx = Ctx(nc, es, block)
        x0f = cx.inp("x0f", [D, S], F32)
        n1w = cx.inp("n1w", [128, 8], F32)
        wqA = cx.inp("wqA", [D, 512], F32)
        wkvd = cx.inp("wkvd", [D, LAT], F32)
        wkuA = cx.inp("wkuA", [LAT, 128], F32)
        wvuA = cx.inp("wvuA", [LAT, 128], F32)
        gmask = cx.inp("gmask", [128, 4 * 512], BF16)
        posAll = cx.inp("posAll", [1, S], F32)
        ogout = cx.outp("ogout", [512, S], BF16)

        alloc_common(cx)
        x0n = [cx.sbuf("x0n%d" % i, [128, S], BF16) for i in range(8)]
        kk = [cx.sbuf("kk%d" % a, [64, S], BF16) for a in range(2)]
        vtm = [cx.sbuf("vtm%d" % a, [128, 65 * 32], BF16) for a in range(2)]
        qcur = cx.sbuf("qcur", [64, S], BF16)
        ocur = cx.sbuf("ocur", [64, S], BF16)
        n1w_s = cx.sbuf("n1w_s", [128, 8], F32)
        gmask_s = cx.sbuf("gmask_s", [128, 4 * 512], BF16)
        vf_scr = cx.sbuf("vf_scr", [64, 512], BF16)
        cosG = cx.sbuf("cosG", [64, S], F32)
        sinG = cx.sbuf("sinG", [64, S], F32)
        wq_bf = cx.sbuf("wq_bf", [128, 8 * 512], BF16)

        emit_load_consts(cx)

        def _q0(qe):
            cx.inc(qe.dma_start(out=n1w_s[:], in_=n1w.ap()), "q", 16)
            cx.inc(qe.dma_start(out=gmask_s[:], in_=gmask.ap()), "q", 16)

        cx.Q(_q0)
        cx.barrier()

        with contextlib.ExitStack() as ph:
            wkvd_bf = ph.enter_context(nc.sbuf_tensor("wkvd_bf", [128, 8 * 256], BF16))
            wku_bf = ph.enter_context(nc.sbuf_tensor("wku_bf", [128, 2 * 128], BF16))
            wvu_bf = ph.enter_context(nc.sbuf_tensor("wvu_bf", [128, 2 * 128], BF16))
            wf32 = ph.enter_context(nc.sbuf_tensor("wf32", [128, 512], F32))
            xf32 = [ph.enter_context(nc.sbuf_tensor("xf32_%d" % i, [128, 256], F32)) for i in range(8)]
            lat_bf = [ph.enter_context(nc.sbuf_tensor("lat%d" % a, [128, S], BF16)) for a in range(2)]

            emit_rope_tables(cx, BASE_G, posAll, S, cosG, sinG)

            def cast_w(dst, src_dram, rows, cols):
                for kc in range(rows // 128):
                    def _q(qe, kc=kc):
                        cx.wait(qe, cx.tok("v"))
                        cx.inc(qe.dma_start(out=wf32[:, 0:cols],
                                            in_=src_dram.ap()[128 * kc:128 * kc + 128, :]), "q", 16)

                    cx.Q(_q)
                    qt = cx.tok("q")

                    def _v(ve, kc=kc, qt=qt):
                        cx.wait(ve, qt)
                        cx.inc(ve.tensor_copy(out=dst[:, cols * kc:cols * kc + cols],
                                              in_=wf32[:, 0:cols]), "v")

                    cx.V(_v)

            cast_w(wq_bf, wqA, D, 512)
            cast_w(wkvd_bf, wkvd, D, 256)
            cast_w(wku_bf, wkuA, LAT, 128)
            cast_w(wvu_bf, wvuA, LAT, 128)

            for nb in range(16):
                def _q(qe, nb=nb):
                    cx.wait(qe, cx.tok("t"), cx.tok("v"))
                    for ci in range(8):
                        cx.inc(qe.dma_start(out=xf32[ci][:],
                                            in_=x0f.ap()[128 * ci:128 * ci + 128,
                                                         256 * nb:256 * nb + 256]), "q", 16)

                cx.Q(_q)
                qt = cx.tok("q")
                st1 = emit_stats(cx, [xf32[ci][:] for ci in range(8)], 256, 1e-5, True,
                                 cx.scr["r0"][0:1, 0:256], cx.scr["r1"][0:1, 0:256],
                                 cx.scr["r2"][0:1, 0:256], after=(qt,))
                b1 = emit_bcast(cx, cx.scr["r2"][0:1, 0:256], 0, 128, 256, after=(st1,))
                b2 = emit_bcast(cx, cx.scr["r1"][0:1, 0:256], 1, 128, 256, after=(st1,))

                def _v(ve, b2=b2):
                    cx.wait(ve, b2)
                    for ci in range(8):
                        ve.tensor_tensor(out=xf32[ci][:], in0=xf32[ci][:],
                                         in1=cx.ps[1][:, 0:256], op=ALU.mult)
                        ve.drain()
                        ve.tensor_tensor(out=xf32[ci][:], in0=xf32[ci][:],
                                         in1=cx.ps[0][:, 0:256], op=ALU.subtract)
                        ve.drain()
                    cx.inc(ve.tensor_copy(out=scr_mark(cx), in_=scr_mark(cx)), "v")

                cx.V(_v)
                st2 = emit_stats(cx, [xf32[ci][:] for ci in range(8)], 256, 1e-6, False,
                                 None, cx.scr["r1"][0:1, 0:256], None, after=(cx.tok("v"),))
                b3 = emit_bcast(cx, cx.scr["r1"][0:1, 0:256], 0, 128, 256, after=(st2,))

                def _v2(ve, nb=nb, b3=b3):
                    cx.wait(ve, b3)
                    for ci in range(8):
                        ve.tensor_tensor(out=xf32[ci][:], in0=xf32[ci][:],
                                         in1=cx.ps[0][:, 0:256], op=ALU.mult)
                        ve.drain()
                        ve.tensor_scalar(out=x0n[ci][:, 256 * nb:256 * nb + 256], in0=xf32[ci][:],
                                         scalar1=n1w_s[:, ci:ci + 1], scalar2=None, op0=ALU.mult)
                        ve.drain()
                    cx.inc(ve.tensor_copy(out=scr_mark(cx), in_=scr_mark(cx)), "v")

                cx.V(_v2)
            cx.barrier()

            for a in range(2):
                for nb in range(8):
                    def _t(te, a=a, nb=nb):
                        cx.wait(te, cx.tok("v"))
                        mm = None
                        for k in range(8):
                            mm = te.matmul(out=cx.ps[0][:, 0:512],
                                           lhsT=wkvd_bf[:, 256 * k + 128 * a:256 * k + 128 * a + 128],
                                           rhs=x0n[k][:, 512 * nb:512 * nb + 512],
                                           start=(k == 0), stop=(k == 7))
                        cx.inc(mm, "t")

                    cx.T(_t)
                    t1 = cx.tok("t")

                    def _v(ve, a=a, nb=nb, t1=t1):
                        cx.wait(ve, t1)
                        cx.inc(ve.tensor_copy(out=lat_bf[a][:, 512 * nb:512 * nb + 512],
                                              in_=cx.ps[0][:, 0:512]), "v")

                    cx.V(_v)

            for a in range(2):
                for nb in range(8):
                    emit_proj_rope(cx,
                                   [wku_bf[:, 64 * a:64 * a + 64], wku_bf[:, 128 + 64 * a:128 + 64 * a + 64]],
                                   [lat_bf[0][:, 512 * nb:512 * nb + 512],
                                    lat_bf[1][:, 512 * nb:512 * nb + 512]],
                                   2, 512 * nb, 512, cosG, sinG,
                                   kk[a][:, 512 * nb:512 * nb + 512], True, None)
                emit_v_build(cx, [wvu_bf[:, 64 * a:64 * a + 64], wvu_bf[:, 128 + 64 * a:128 + 64 * a + 64]],
                             lambda k, nb: lat_bf[k][:, 512 * nb:512 * nb + 512],
                             2, 32, vtm[a], vf_scr)
            cx.barrier()

        for j in range(8):
            for nb in range(8):
                emit_proj_rope(cx, [wq_bf[:, 512 * k + 64 * j:512 * k + 64 * j + 64] for k in range(8)],
                               [x0n[k][:, 512 * nb:512 * nb + 512] for k in range(8)],
                               8, 512 * nb, 512, cosG, sinG,
                               qcur[:, 512 * nb:512 * nb + 512], True, None)
            a = j // 4
            vend = emit_attn_job(cx, qcur, kk[a], vtm[a], ocur, S, None,
                                 cx.c["biasC0"], gmask_s)

            def _g(ge, j=j, vend=vend):
                cx.wait(ge, vend)
                cx.inc(ge.dma_start(out=ogout.ap()[64 * j:64 * j + 64, :], in_=ocur[:]), "g", 16)

            cx.G(_g)
            gt = cx.tok("g")

            def _v(ve, gt=gt):
                cx.wait(ve, gt)

            cx.V(_v)

        def _gend(ge):
            cx.wait(ge, cx.tok("g"))

        cx.G(_gend)
    return nc


def scr_mark(cx):
    return cx.scr["r4"][0:1, 0:1]


def _consts_np():
    import ml_dtypes
    smat = np.zeros((64, 64), np.float32)
    for i in range(32):
        smat[i, 32 + i] = -1.0
        smat[32 + i, i] = 1.0
    return {
        "onesF": np.ones((128, 2), np.float32),
        "onesRowF": np.ones((1, 128), np.float32),
        "b512": np.full((128, 1), 512.0, np.float32),
        "identB": np.eye(128).astype(ml_dtypes.bfloat16),
        "smatTF": np.ascontiguousarray(smat.T),
        "biasC0": np.full((128, 1), -C_L0, np.float32),
        "biasC1": np.full((128, 1), -C_L1, np.float32),
        "c0125": np.full((128, 1), 0.125, np.float32),
        "cm05": np.full((128, 1), -0.5, np.float32),
        "iotaW": (np.arange(64) % 32).astype(np.float32).reshape(64, 1),
        "identF": np.eye(128, dtype=np.float32),
    }


def _gmask_np():
    import ml_dtypes
    m = np.zeros((128, 4, 512), np.float32)
    kkk = np.arange(128)[:, None]
    qq = np.arange(512)[None, :]
    for i in range(4):
        m[:, i, :] = np.where(qq >= 128 * i + kkk, 0.0, NEG)
    return np.ascontiguousarray(m.reshape(128, 2048)).astype(ml_dtypes.bfloat16)


def _perm_cols(w, per_head=64):
    nh = w.shape[1] // per_head
    cols = [w[:, h * per_head:(h + 1) * per_head][:, DEINT] for h in range(nh)]
    return np.concatenate(cols, axis=1)


_cache = {}


def _prep_A(inputs):
    consts = _consts_np()
    gm = _gmask_np()
    pos = np.arange(S, dtype=np.float32).reshape(1, S)
    x0 = np.asarray(inputs["L0_attn_hidden"], np.float32)
    n1 = np.ascontiguousarray(np.asarray(inputs["L0_attn_norm1_w"], np.float32).reshape(8, 128).T)
    wq = np.asarray(inputs["L0_attn_g_wq"], np.float32)
    wkvd = np.ascontiguousarray(np.asarray(inputs["L0_attn_g_kv_down"], np.float32))
    wku = np.asarray(inputs["L0_attn_g_k_up"], np.float32)
    wvu = np.asarray(inputs["L0_attn_g_v_up"], np.float32)
    maps = []
    for r in range(NCORES):
        b = r // 2
        hs = (r % 2) * 8
        m = dict(consts)
        m["gmask"] = gm
        m["posAll"] = pos
        m["x0f"] = np.ascontiguousarray(x0[b].T)
        m["n1w"] = n1
        m["wqA"] = np.ascontiguousarray(_perm_cols(wq[:, 64 * hs:64 * (hs + 8)]))
        m["wkvd"] = wkvd
        m["wkuA"] = np.ascontiguousarray(_perm_cols(wku[:, 128 * (r % 2):128 * (r % 2) + 128]))
        m["wvuA"] = np.ascontiguousarray(wvu[:, 128 * (r % 2):128 * (r % 2) + 128])
        maps.append(m)
    return maps


def run_launchA(inputs):
    if "A" not in _cache:
        _cache["A"] = build_launchA()
    maps = _prep_A(inputs)
    res = run_bass_kernel_spmd(_cache["A"], maps, core_ids=list(range(NCORES)))
    return [np.asarray(r["ogout"]) for r in res.results]


def kernel(**inputs):
    og = run_launchA(inputs)
    raise NotImplementedError("launches B-D pending")


def build_launchW0():
    nc = bass.Bass("TRN2", target_bir_lowering=False, debug=False)
    es = contextlib.ExitStack()
    with es:
        block = es.enter_context(nc.Block())
        cx = Ctx(nc, es, block)
        wgT = cx.inp("wgT", [128, FFN_I], F32)
        wuT = cx.inp("wuT", [128, FFN_I], F32)
        wdT = cx.inp("wdT", [512, D], F32)
        casts = [("gwo", 128, D), ("lwq", 128, D), ("lwk", 128, LAT), ("lwv", 128, LAT),
                 ("lwo", 128, D), ("woT", 128, D), ("wqC", 128, D), ("wkC", 128, D),
                 ("wvC", 128, D)]
        cin = {n: cx.inp(n, [rr, ccd], F32) for n, rr, ccd in casts}
        cout = {n: cx.outp(n + "_b", [rr, ccd], BF16) for n, rr, ccd in casts}
        w8T = cx.inp("w8T", [128, D], I32)
        w8out = cx.outp("w8T_b", [128, D], BF16)
        absout = cx.outp("absout", [1, 4], F32)

        with (
            nc.sbuf_tensor("wb", [128, FFN_I], F32) as wb,
            nc.sbuf_tensor("wbb", [128, FFN_I], BF16) as wbb,
            nc.sbuf_tensor("acc", [128, 4], F32) as acc,
            nc.sbuf_tensor("red", [128, 4], F32) as red,
            nc.sbuf_tensor("ones1", [128, 1], F32) as ones1,
            nc.sbuf_tensor("rowo", [1, 4], F32) as rowo,
            nc.psum_tensor("pss", [1, 4], F32) as pss,
        ):
            cp = cx

            def _g(ge):
                cx.inc(ge.memset(acc[:], 0.0), "g")
                cx.inc(ge.memset(ones1[:], 1.0), "g")

            cx.G(_g)
            g0 = cx.tok("g")

            jobs = [(wgT, 1, FFN_I, 0), (wuT, 1, FFN_I, 1), (wdT, 4, D, 2)]
            for (src, nch, ncol, col) in jobs:
                for kc in range(nch):
                    def _q(qe, src=src, kc=kc, ncol=ncol):
                        cx.wait(qe, cx.tok("v"))
                        cx.inc(qe.dma_start(out=wb[:, 0:ncol],
                                            in_=src.ap()[128 * kc:128 * kc + 128, :]), "q", 16)

                    cx.Q(_q)
                    qt = cx.tok("q")

                    def _v(ve, col=col, ncol=ncol, qt=qt):
                        cx.wait(ve, qt, g0)
                        ve.reduce_sum(out=red[:, col:col + 1], in_=wb[:, 0:ncol],
                                      axis=mybir.AxisListType.X, apply_absolute_value=True)
                        ve.drain()
                        ve.tensor_tensor(out=acc[:, col:col + 1], in0=acc[:, col:col + 1],
                                         in1=red[:, col:col + 1], op=ALU.add)
                        ve.drain()
                        cx.inc(ve.tensor_copy(out=red[0:1, 0:1], in_=red[0:1, 0:1]), "v")

                    cx.V(_v)

            vt = cx.tok("v")

            def _t(te):
                cx.wait(te, vt)
                cx.inc(te.matmul(out=pss[:], lhsT=ones1[:], rhs=acc[:], start=True, stop=True), "t")

            cx.T(_t)
            tt = cx.tok("t")

            def _v2(ve):
                cx.wait(ve, tt)
                cx.inc(ve.tensor_copy(out=rowo[:], in_=pss[:]), "v")

            cx.V(_v2)
            vt2 = cx.tok("v")

            def _g2(ge):
                cx.wait(ge, vt2)
                cx.inc(ge.dma_start(out=absout.ap(), in_=rowo[:]), "g", 16)

            cx.G(_g2)

            for n, rr, ccd in casts:
                for kc in range(rr // 128):
                    def _q(qe, n=n, kc=kc, ccd=ccd):
                        cx.wait(qe, cx.tok("v"), cx.tok("g"))
                        cx.inc(qe.dma_start(out=wb[:, 0:ccd],
                                            in_=cin[n].ap()[128 * kc:128 * kc + 128, :]), "q", 16)

                    cx.Q(_q)
                    qt = cx.tok("q")

                    def _v(ve, qt=qt, ccd=ccd):
                        cx.wait(ve, qt)
                        cx.inc(ve.tensor_copy(out=wbb[:, 0:ccd], in_=wb[:, 0:ccd]), "v")

                    cx.V(_v)
                    vt3 = cx.tok("v")

                    def _g3(ge, n=n, kc=kc, ccd=ccd, vt3=vt3):
                        cx.wait(ge, vt3)
                        cx.inc(ge.dma_start(out=cout[n].ap()[128 * kc:128 * kc + 128, :],
                                            in_=wbb[:, 0:ccd]), "g", 16)

                    cx.G(_g3)
            def _q8(qe):
                cx.wait(qe, cx.tok("v"), cx.tok("g"))
                cx.inc(qe.dma_start(out=wb.ap().bitcast(I32)[:, 0:D] if False else wb[:, 0:D],
                                    in_=w8T.ap()), "q", 16)

            with nc.sbuf_tensor("w8s", [128, D], I32) as w8s:
                def _q8b(qe):
                    cx.wait(qe, cx.tok("v"), cx.tok("g"))
                    cx.inc(qe.dma_start(out=w8s[:], in_=w8T.ap()), "q", 16)

                cx.Q(_q8b)
                qt8 = cx.tok("q")

                def _v8(ve):
                    cx.wait(ve, qt8)
                    cx.inc(ve.tensor_copy(out=wbb[:, 0:D], in_=w8s[:]), "v")

                cx.V(_v8)
                vt8 = cx.tok("v")

                def _g8(ge):
                    cx.wait(ge, vt8)
                    cx.inc(ge.dma_start(out=w8out.ap(), in_=wbb[:, 0:D]), "g", 16)
                    cx.wait(ge, cx.tok("g"))

                cx.G(_g8)
    return nc


def _prep_W0(inputs):
    wg = np.asarray(inputs["L0_ffn_w_gate"], np.float32)
    wu = np.asarray(inputs["L0_ffn_w_up"], np.float32)
    wd = np.asarray(inputs["L0_ffn_w_down"], np.float32)
    wgT = np.ascontiguousarray(wg.T)
    wuT = np.ascontiguousarray(wu.T)
    wdT = np.ascontiguousarray(wd.T)
    gwo = np.ascontiguousarray(np.asarray(inputs["L0_attn_g_wo"], np.float32))
    lwq = np.ascontiguousarray(_perm_cols(np.asarray(inputs["L0_attn_l_wq"], np.float32)))
    lwk = np.ascontiguousarray(_perm_cols(np.asarray(inputs["L0_attn_l_wk"], np.float32)))
    lwv = np.ascontiguousarray(np.asarray(inputs["L0_attn_l_wv"], np.float32))
    lwo = np.ascontiguousarray(np.asarray(inputs["L0_attn_l_wo"], np.float32))
    woT = np.ascontiguousarray(np.asarray(inputs["L1_attn_W_O"], np.float32).T)
    wqC = np.ascontiguousarray(np.asarray(inputs["L1_attn_W_Q"], np.float32).transpose(2, 0, 1).reshape(D, D))
    wkC = np.ascontiguousarray(np.asarray(inputs["L1_attn_W_K"], np.float32).transpose(2, 0, 1).reshape(D, D))
    wvC = np.ascontiguousarray(np.asarray(inputs["L1_attn_W_V"], np.float32).transpose(2, 0, 1).reshape(D, D))
    w8T = np.ascontiguousarray(np.asarray(inputs["L1_ffn_w_int8"], np.int32).T)
    maps = []
    for r in range(NCORES):
        m = {
            "wgT": np.ascontiguousarray(wgT[128 * r:128 * r + 128]),
            "wuT": np.ascontiguousarray(wuT[128 * r:128 * r + 128]),
            "wdT": np.ascontiguousarray(wdT[512 * r:512 * r + 512]),
            "gwo": np.ascontiguousarray(gwo[128 * r:128 * r + 128]),
            "lwq": np.ascontiguousarray(lwq[128 * r:128 * r + 128]),
            "lwk": np.ascontiguousarray(lwk[128 * r:128 * r + 128]),
            "lwv": np.ascontiguousarray(lwv[128 * r:128 * r + 128]),
            "lwo": np.ascontiguousarray(lwo[128 * r:128 * r + 128]),
            "woT": np.ascontiguousarray(woT[128 * r:128 * r + 128]),
            "wqC": np.ascontiguousarray(wqC[128 * r:128 * r + 128]),
            "wkC": np.ascontiguousarray(wkC[128 * r:128 * r + 128]),
            "wvC": np.ascontiguousarray(wvC[128 * r:128 * r + 128]),
            "w8T": np.ascontiguousarray(w8T[128 * r:128 * r + 128]),
        }
        maps.append(m)
    return maps


def run_launchW0(inputs):
    if "W0" not in _cache:
        _cache["W0"] = build_launchW0()
    maps = _prep_W0(inputs)
    res = run_bass_kernel_spmd(_cache["W0"], maps, core_ids=list(range(NCORES)))
    out = {}
    for n in ("gwo", "lwq", "lwk", "lwv", "lwo", "woT", "wqC", "wkC", "wvC", "w8T"):
        out[n + "_b"] = np.concatenate([np.asarray(r[n + "_b"]) for r in res.results], axis=0)
    out["abs8"] = np.concatenate([np.asarray(r["absout"]) for r in res.results], axis=0)
    return out


def build_launchW1():
    nc = bass.Bass("TRN2", target_bir_lowering=False, debug=False)
    es = contextlib.ExitStack()
    with es:
        block = es.enter_context(nc.Block())
        cx = Ctx(nc, es, block)
        wgT = cx.inp("wgT", [128, FFN_I], F32)
        wuT = cx.inp("wuT", [128, FFN_I], F32)
        wdT = cx.inp("wdT", [512, D], F32)
        abs8 = cx.inp("abs8", [8, 4], F32)
        wgq = cx.outp("wgq", [128, FFN_I], BF16)
        wuq = cx.outp("wuq", [128, FFN_I], BF16)
        wdq = cx.outp("wdq", [512, D], BF16)
        wrec_o = cx.outp("wrec_o", [1, 4], F32)
        alloc_common(cx)
        emit_load_consts(cx)
        with (
            nc.sbuf_tensor("wb", [128, FFN_I], F32) as wb,
            nc.sbuf_tensor("wstage", [128, FFN_I], BF16) as wstage,
            nc.sbuf_tensor("a8", [8, 4], F32) as a8,
            nc.sbuf_tensor("wrow", [1, 4], F32) as wrow,
            nc.sbuf_tensor("srow", [1, 4], F32) as srow,
            nc.sbuf_tensor("s128", [128, 4], F32) as s128,
        ):
            def _q(qe):
                cx.inc(qe.dma_start(out=a8[:], in_=abs8.ap()), "q", 16)

            cx.Q(_q)
            qt = cx.tok("q")

            def _t(te):
                cx.wait(te, qt)
                cx.inc(te.matmul(out=cx.ps[4][0:1, 0:4], lhsT=cx.c["onesF"][0:8, 0:1],
                                 rhs=a8[:], start=True, stop=True), "t")

            cx.T(_t)
            tt = cx.tok("t")

            def _v(ve):
                cx.wait(ve, tt)
                ve.tensor_scalar(out=wrow[:], in0=cx.ps[4][0:1, 0:4],
                                 scalar1=1.0 / (D * FFN_I), scalar2=1e-5, op0=ALU.mult, op1=ALU.max)
                ve.drain()
                ve.reciprocal(out=srow[:], in_=wrow[:])
                ve.drain()
                cx.inc(ve.tensor_copy(out=srow[0:1, 3:4], in_=srow[0:1, 3:4]), "v")

            cx.V(_v)
            vt = cx.tok("v")
            bcs = []
            for col in range(3):
                bt = emit_bcast(cx, srow[0:1, col:col + 1], 5, 128, 1, after=(vt, cx.tok("v"),))

                def _vb(ve, col=col, bt=bt):
                    cx.wait(ve, bt)
                    cx.inc(ve.tensor_copy(out=s128[:, col:col + 1], in_=cx.ps[5][0:128, 0:1]), "v")

                cx.V(_vb)

            jobs = [(wgT, wgq, 1, FFN_I, 0), (wuT, wuq, 1, FFN_I, 1), (wdT, wdq, 4, D, 2)]
            for (src, dst, nch, ncol, col) in jobs:
                for kc in range(nch):
                    def _q2(qe, src=src, kc=kc, ncol=ncol):
                        cx.wait(qe, cx.tok("v"), cx.tok("g"))
                        cx.inc(qe.dma_start(out=wb[:, 0:ncol],
                                            in_=src.ap()[128 * kc:128 * kc + 128, :]), "q", 16)

                    cx.Q(_q2)
                    qt2 = cx.tok("q")
                    for sl in range(ncol // 512):
                        def _v2(ve, sl=sl, col=col, qt2=qt2):
                            cx.wait(ve, qt2)
                            cx.inc(ve.tensor_scalar(out=cx.scr["D"][:], in0=wb[:, 512 * sl:512 * sl + 512],
                                                    scalar1=s128[:, col:col + 1], scalar2=None,
                                                    op0=ALU.mult), "v")

                        cx.V(_v2)
                        emit_round_clip(cx, cx.scr["D"][:], wstage[:, 512 * sl:512 * sl + 512],
                                        -1, 1, after=(cx.tok("v"),))
                    vt2 = cx.tok("v")

                    def _g2(ge, dst=dst, kc=kc, ncol=ncol, vt2=vt2):
                        cx.wait(ge, vt2)
                        cx.inc(ge.dma_start(out=dst.ap()[128 * kc:128 * kc + 128, :],
                                            in_=wstage[:, 0:ncol]), "g", 16)

                    cx.G(_g2)

            def _g3(ge):
                cx.wait(ge, cx.tok("v"))
                cx.inc(ge.dma_start(out=wrec_o.ap(), in_=wrow[:]), "g", 16)
                cx.wait(ge, cx.tok("g"))

            cx.G(_g3)
    return nc


def run_launchW1(inputs, w0out):
    if "W1" not in _cache:
        _cache["W1"] = build_launchW1()
    base = _prep_W0(inputs)
    maps = []
    for r in range(NCORES):
        m = dict(_consts_np())
        m["wgT"] = base[r]["wgT"]
        m["wuT"] = base[r]["wuT"]
        m["wdT"] = base[r]["wdT"]
        m["abs8"] = np.ascontiguousarray(w0out["abs8"])
        maps.append(m)
    res = run_bass_kernel_spmd(_cache["W1"], maps, core_ids=list(range(NCORES)))
    out = {
        "wgq": np.concatenate([np.asarray(r["wgq"]) for r in res.results], axis=0),
        "wuq": np.concatenate([np.asarray(r["wuq"]) for r in res.results], axis=0),
        "wdq": np.concatenate([np.asarray(r["wdq"]) for r in res.results], axis=0),
        "wrec": np.asarray(res.results[0]["wrec_o"]),
    }
    return out


def emit_wtiles(cx, dram_t, col0, ncols, nk, wtile, after=()):
    def _q(qe):
        cx.wait(qe, *after)
        src = dram_t.ap()[0:128 * nk, col0:col0 + ncols].rearrange("(k p) c -> p k c", p=128)
        dst = wtile[:, 0:nk * ncols].rearrange("p (k c) -> p k c", k=nk)
        cx.inc(qe.dma_start(out=dst, in_=src), "q", 16)

    cx.Q(_q)
    return cx.tok("q")


def emit_local_attn(cx, j, qcur, k_l, v_l, o_dst_ap_fn, lmask_dram, lmask_g):
    scr, c, ps = cx.scr, cx.c, cx.ps
    pt = [scr["pt0"], scr["pt1"]]
    for g in range(4):
        def _qm(qe, g=g):
            cx.wait(qe, cx.tok("t"))
            cx.inc(qe.dma_start(out=lmask_g[:],
                                in_=lmask_dram.ap()[:, 12 * g * 128:12 * g * 128 + 1536]), "q", 16)

        cx.Q(_qm)
        qm = cx.tok("q")
        t0, s0, v0 = cx.cnt["t"], cx.cnt["s"], cx.cnt["v"]

        def _t(te, g=g, t0=t0, s0=s0, v0=v0, qm=qm):
            cx.wait(te, qm)
            for u in range(4):
                jt = 4 * g + u
                pi = u % 2
                if u >= 2:
                    cx.wait(te, ("s", s0 + u - 1))
                else:
                    cx.wait(te, ("v", v0))
                for sl in range(3):
                    te.matmul(out=ps[pi][:, 128 * sl:128 * sl + 128], lhsT=c["identB"][:],
                              rhs=lmask_g[:, (u * 3 + sl) * 128:(u * 3 + sl) * 128 + 128],
                              start=True, stop=False)
                    cx.inc(te.matmul(out=ps[pi][:, 128 * sl:128 * sl + 128],
                                     lhsT=k_l[:, 128 * (jt + sl):128 * (jt + sl) + 128],
                                     rhs=qcur[:, 128 * jt:128 * jt + 128],
                                     start=False, stop=True), "t")
                cx.wait(te, ("s", s0 + u + 1))
                for sl in range(3):
                    cx.inc(te.matmul(out=ps[2][0:65, 128 * u:128 * u + 128],
                                     lhsT=v_l[:, 65 * (jt + sl):65 * (jt + sl) + 65],
                                     rhs=pt[pi][:, 128 * sl:128 * sl + 128],
                                     start=(sl == 0), stop=(sl == 2)), "t")

        cx.T(_t)

        def _s(se, t0=t0):
            for u in range(4):
                pi = u % 2
                cx.wait(se, ("t", t0 + 6 * u + 3))
                cx.inc(se.activation(out=pt[pi][:, 0:384], in_=ps[pi][:, 0:384],
                                     func=AF.Exp, scale=c["c0125"][:], bias=c["biasC0"][:]), "s")

        cx.Sc(_s)
        tend = cx.tok("t")

        def _v(ve, tend=tend):
            cx.wait(ve, tend)
            cx.inc(ve.reciprocal(out=scr["r0"][0:1, 0:512], in_=ps[2][64:65, 0:512]), "v")

        cx.V(_v)
        bc = emit_bcast(cx, scr["r0"][0:1, 0:512], 3, 64, 512, after=(cx.tok("v"),))

        def _v2(ve, g=g, bc=bc):
            cx.wait(ve, bc)
            ve.tensor_copy(out=scr["E"][0:64, 0:512], in_=ps[2][0:64, 0:512])
            ve.drain()
            cx.inc(ve.tensor_tensor(out=o_dst_ap_fn(g), in0=scr["E"][0:64, 0:512],
                                    in1=ps[3][0:64, 0:512], op=ALU.mult), "v")

        cx.V(_v2)
    return cx.tok("v")


B_BLOCKS = [(0, 256)] + [(256 + 512 * i, 512) for i in range(4)]


def build_launchB(dbg=False):
    nc = bass.Bass("TRN2", target_bir_lowering=False, debug=False)
    es = contextlib.ExitStack()
    with es:
        block = es.enter_context(nc.Block())
        cx = Ctx(nc, es, block)
        x0own = cx.inp("x0own", [D, TOK], F32)
        x0halo = cx.inp("x0halo", [D, HALO], F32)
        ogown = cx.inp("ogown", [D, TOK], BF16)
        oghalo = cx.inp("oghalo", [D, HALO], BF16)
        gwo_b = cx.inp("gwo_b", [D, D], BF16)
        n2w = cx.inp("n2w", [128, 8], F32)
        lwq_b = cx.inp("lwq_b", [D, D], BF16)
        lwk_b = cx.inp("lwk_b", [D, LAT], BF16)
        lwv_b = cx.inp("lwv_b", [D, LAT], BF16)
        lwo_b = cx.inp("lwo_b", [D, D], BF16)
        wgq = cx.inp("wgq", [D, FFN_I], BF16)
        wuq = cx.inp("wuq", [D, FFN_I], BF16)
        wdq = cx.inp("wdq", [FFN_I, D], BF16)
        wrec = cx.inp("wrec", [1, 4], F32)
        posKV = cx.inp("posKV", [1, NKV], F32)
        lmask = cx.inp("lmask", [128, 48 * 128], BF16)
        x2out = cx.outp("x2out", [D, TOK], F32)
        x2nout = cx.outp("x2nout", [D, TOK], BF16)
        if dbg:
            dbg_h1 = cx.outp("dbg_h1", [D, NKV], F32)
            dbg_cos = cx.outp("dbg_cos", [64, NKV], F32)
            dbg_sin = cx.outp("dbg_sin", [64, NKV], F32)
            dbg_q = cx.outp("dbg_q", [64, TOK], F32)
            dbg_k = cx.outp("dbg_k", [64, NKV], F32)
            dbg_xq = cx.outp("dbg_xq", [D, 256], F32)
            dbg_hq = cx.outp("dbg_hq", [128, 32 * 256], F32)
            dbg_hf = cx.outp("dbg_hf", [128, 32 * 256], F32)
            dbg_rows = cx.outp("dbg_rows", [4, 256], F32)
            dbg_fsb = cx.outp("dbg_fsb", [D, 256], F32)
            dbg_g = cx.outp("dbg_g", [128, 32 * 256], F32)
            dbg_lin = cx.outp("dbg_lin", [D, NKV], BF16)
            dbg_ol = cx.outp("dbg_ol", [D, TOK], BF16)
            dbg_x1 = cx.outp("dbg_x1", [D, TOK], F32)

        def dump(dst, srcs, cols):
            if not dbg:
                return

            def _g(ge):
                cx.wait(ge, cx.tok("v"), cx.tok("t"), cx.tok("s"))
                for ci in range(8):
                    cx.inc(ge.dma_start(out=dst.ap()[128 * ci:128 * ci + 128, :],
                                        in_=srcs[ci][:, 0:cols] if cols else srcs[ci][:]), "g", 16)

            cx.G(_g)
            cx.barrier()

        alloc_common(cx)
        h1 = [cx.sbuf("h1_%d" % i, [128, NKV], F32) for i in range(8)]
        n2w_s = cx.sbuf("n2w_s", [128, 8], F32)
        lmask_g = cx.sbuf("lmask_g", [128, 12 * 128], BF16)
        wrec_s = cx.sbuf("wrec_s", [1, 4], F32)
        emit_load_consts(cx)

        def _q0(qe):
            cx.inc(qe.dma_start(out=n2w_s[:], in_=n2w.ap()), "q", 16)
            cx.inc(qe.dma_start(out=wrec_s[:], in_=wrec.ap()), "q", 16)

        cx.Q(_q0)
        cx.barrier()

        with contextlib.ExitStack() as ph:
            gwo_s = ph.enter_context(nc.sbuf_tensor("gwo_s", [128, 8 * D], BF16))
            xf = [ph.enter_context(nc.sbuf_tensor("xf%d" % i, [128, 512], F32)) for i in range(8)]
            ogs = [ph.enter_context(nc.sbuf_tensor("ogs%d" % i, [128, 512], BF16)) for i in range(8)]

            def _q1(qe):
                for k in range(8):
                    cx.inc(qe.dma_start(out=gwo_s[:, D * k:D * k + D],
                                        in_=gwo_b.ap()[128 * k:128 * k + 128, :]), "q", 16)

            cx.Q(_q1)

            for (off, nn) in B_BLOCKS:
                def _q2(qe, off=off, nn=nn):
                    cx.wait(qe, cx.tok("t"), cx.tok("v"))
                    for ci in range(8):
                        src = x0halo.ap()[128 * ci:128 * ci + 128, :] if off == 0 else \
                            x0own.ap()[128 * ci:128 * ci + 128, off - HALO:off - HALO + nn]
                        cx.inc(qe.dma_start(out=xf[ci][:, 0:nn], in_=src), "q", 16)
                        osrc = oghalo.ap()[128 * ci:128 * ci + 128, :] if off == 0 else \
                            ogown.ap()[128 * ci:128 * ci + 128, off - HALO:off - HALO + nn]
                        cx.inc(qe.dma_start(out=ogs[ci][:, 0:nn], in_=osrc), "q", 16)

                cx.Q(_q2)
                qt = cx.tok("q")
                st = emit_stats(cx, [xf[ci][:, 0:nn] for ci in range(8)], nn, 1e-5, True,
                                cx.scr["r0"][0:1, 0:nn], cx.scr["r1"][0:1, 0:nn],
                                cx.scr["r2"][0:1, 0:nn], after=(qt,))
                b1 = emit_bcast(cx, cx.scr["r2"][0:1, 0:nn], 0, 128, nn, after=(st,))
                b2 = emit_bcast(cx, cx.scr["r1"][0:1, 0:nn], 1, 128, nn, after=(st,))

                def _v(ve, off=off, nn=nn, b2=b2):
                    cx.wait(ve, b2)
                    for ci in range(8):
                        ve.tensor_tensor(out=h1[ci][:, off:off + nn], in0=xf[ci][:, 0:nn],
                                         in1=cx.ps[1][:, 0:nn], op=ALU.mult)
                        ve.drain()
                        ve.tensor_tensor(out=h1[ci][:, off:off + nn], in0=h1[ci][:, off:off + nn],
                                         in1=cx.ps[0][:, 0:nn], op=ALU.subtract)
                        ve.drain()
                    cx.inc(ve.tensor_copy(out=scr_mark(cx), in_=scr_mark(cx)), "v")

                cx.V(_v)
                for m in range(8):
                    def _t(te, off=off, nn=nn, m=m, qt=qt):
                        cx.wait(te, cx.tok("v"), qt)
                        mm = None
                        for k in range(8):
                            mm = te.matmul(out=cx.ps[0][:, 0:nn],
                                           lhsT=gwo_s[:, D * k + 128 * m:D * k + 128 * m + 128],
                                           rhs=ogs[k][:, 0:nn],
                                           start=(k == 0), stop=(k == 7))
                        cx.inc(mm, "t")

                    cx.T(_t)
                    t1 = cx.tok("t")

                    def _v(ve, off=off, nn=nn, m=m, t1=t1):
                        cx.wait(ve, t1)
                        cx.inc(ve.tensor_tensor(out=h1[m][:, off:off + nn],
                                                in0=h1[m][:, off:off + nn],
                                                in1=cx.ps[0][:, 0:nn], op=ALU.add), "v")

                    cx.V(_v)
            cx.barrier()
            dump(dbg_h1, h1, NKV) if dbg else None

        po = contextlib.ExitStack()
        o_l = [po.enter_context(nc.sbuf_tensor("o_l%d" % i, [128, TOK], BF16)) for i in range(8)]
        with contextlib.ExitStack() as ph:
            l_in = [ph.enter_context(nc.sbuf_tensor("lin%d" % i, [128, NKV], BF16)) for i in range(8)]
            cosL = ph.enter_context(nc.sbuf_tensor("cosL", [64, NKV], F32))
            sinL = ph.enter_context(nc.sbuf_tensor("sinL", [64, NKV], F32))
            kk_l = ph.enter_context(nc.sbuf_tensor("kkl", [64, NKV], BF16))
            vtm_l = ph.enter_context(nc.sbuf_tensor("vtml", [128, 65 * 18], BF16))
            vstage = ph.enter_context(nc.sbuf_tensor("vstage", [64, 512], BF16))
            wt = ph.enter_context(nc.sbuf_tensor("wt", [128, 512], BF16))
            qcur = ph.enter_context(nc.sbuf_tensor("qcurB", [64, TOK], BF16))

            emit_rope_tables(cx, BASE_L, posKV, NKV, cosL, sinL)

            for (off, nn) in B_BLOCKS:
                st = emit_stats(cx, [h1[ci][:, off:off + nn] for ci in range(8)], nn, 1e-6, False,
                                None, cx.scr["r1"][0:1, 0:nn], None, after=(cx.tok("v"),))
                b3 = emit_bcast(cx, cx.scr["r1"][0:1, 0:nn], 0, 128, nn, after=(st,))

                def _v(ve, off=off, nn=nn, b3=b3):
                    cx.wait(ve, b3)
                    for ci in range(8):
                        ve.tensor_tensor(out=cx.scr["D"][:, 0:nn], in0=h1[ci][:, off:off + nn],
                                         in1=cx.ps[0][:, 0:nn], op=ALU.mult)
                        ve.drain()
                        ve.tensor_scalar(out=l_in[ci][:, off:off + nn], in0=cx.scr["D"][:, 0:nn],
                                         scalar1=n2w_s[:, ci:ci + 1], scalar2=None, op0=ALU.mult)
                        ve.drain()
                    cx.inc(ve.tensor_copy(out=scr_mark(cx), in_=scr_mark(cx)), "v")

                cx.V(_v)
            cx.barrier()
            dump(dbg_lin, l_in, NKV) if dbg else None

            for a in range(4):
                def _g(ge, a=a):
                    cx.wait(ge, cx.tok("v"), cx.tok("t"))
                    cx.inc(ge.memset(vtm_l[:], 1.0), "g")

                cx.G(_g)
                g0 = cx.tok("g")
                qtw = emit_wtiles(cx, lwk_b, 64 * a, 64, 8, wt, after=(cx.tok("t"), cx.tok("v")))
                lkpipe = mk_pipe()
                for (off, nn) in B_BLOCKS:
                    def _w(te, qtw=qtw):
                        cx.wait(te, qtw)

                    cx.T(_w)
                    emit_proj_rope(cx, [wt[:, 64 * k:64 * k + 64] for k in range(8)],
                                   [l_in[ci][:, off:off + nn] for ci in range(8)],
                                   8, off, nn, cosL, sinL,
                                   kk_l[:, off:off + nn], True, None, pipe=lkpipe)
                pipe_flush(cx, lkpipe)
                qtw = emit_wtiles(cx, lwv_b, 64 * a, 64, 8, wt, after=(cx.tok("t"), cx.tok("v")))
                for (off, nn) in B_BLOCKS:
                    def _t(te, off=off, nn=nn, qtw=qtw):
                        cx.wait(te, qtw, cx.tok("v"))
                        mm = None
                        for k in range(8):
                            mm = te.matmul(out=cx.ps[0][0:64, 0:nn], lhsT=wt[:, 64 * k:64 * k + 64],
                                           rhs=l_in[k][:, off:off + nn],
                                           start=(k == 0), stop=(k == 7))
                        cx.inc(mm, "t")

                    cx.T(_t)
                    t1 = cx.tok("t")

                    def _v(ve, nn=nn, t1=t1):
                        cx.wait(ve, t1)
                        cx.inc(ve.tensor_copy(out=vstage[:, 0:nn], in_=cx.ps[0][0:64, 0:nn]), "v")

                    cx.V(_v)
                    v1 = cx.tok("v")
                    for u in range(nn // 128):
                        def _t2(te, u=u, v1=v1):
                            cx.wait(te, v1, cx.tok("v"))
                            cx.inc(te.transpose(out=cx.pst[0:128, 0:64],
                                                in_=vstage[:, 128 * u:128 * u + 128],
                                                identity=cx.c["identB"][0:64, 0:64]), "t")

                        cx.T(_t2)
                        t2 = cx.tok("t")
                        kt = off // 128 + u

                        def _v2(ve, kt=kt, t2=t2):
                            cx.wait(ve, t2, g0)
                            cx.inc(ve.tensor_copy(out=vtm_l[:, 65 * kt:65 * kt + 64],
                                                  in_=cx.pst[0:128, 0:64]), "v")

                        cx.V(_v2)

                if dbg and a == 0:
                    def _gd(ge):
                        cx.wait(ge, cx.tok("v"), cx.tok("s"))
                        cx.inc(ge.dma_start(out=dbg_cos.ap(), in_=cosL[:]), "g", 16)
                        cx.inc(ge.dma_start(out=dbg_sin.ap(), in_=sinL[:]), "g", 16)
                        cx.inc(ge.dma_start(out=dbg_k.ap(), in_=kk_l[:]), "g", 16)

                    cx.G(_gd)
                    cx.barrier()
                for j in range(4 * a, 4 * a + 4):
                    qtw = emit_wtiles(cx, lwq_b, 64 * j, 64, 8, wt, after=(cx.tok("t"), cx.tok("v")))

                    def _w(te, qtw=qtw):
                        cx.wait(te, qtw)

                    cx.T(_w)
                    lqpipe = mk_pipe()
                    for tb in range(4):
                        off = HALO + 512 * tb
                        emit_proj_rope(cx, [wt[:, 64 * k:64 * k + 64] for k in range(8)],
                                       [l_in[ci][:, off:off + 512] for ci in range(8)],
                                       8, off, 512, cosL, sinL,
                                       qcur[:, 512 * tb:512 * tb + 512], True, None,
                                       pipe=lqpipe)
                    pipe_flush(cx, lqpipe)

                    if dbg and j == 0:
                        def _gq(ge):
                            cx.wait(ge, cx.tok("v"), cx.tok("s"))
                            cx.inc(ge.dma_start(out=dbg_q.ap(), in_=qcur[:]), "g", 16)

                        cx.G(_gq)
                        cx.barrier()

                    def o_dst(g, j=j):
                        return o_l[j // 2][64 * (j % 2):64 * (j % 2) + 64, 512 * g:512 * g + 512]

                    emit_local_attn(cx, j, qcur, kk_l, vtm_l, o_dst, lmask, lmask_g)
            cx.barrier()
            dump(dbg_ol, o_l, TOK) if dbg else None

        with contextlib.ExitStack() as ph:
            wt = ph.enter_context(nc.sbuf_tensor("wtB3", [128, 8 * 128], BF16))
            xf = [ph.enter_context(nc.sbuf_tensor("xfB3_%d" % i, [128, 512], F32)) for i in range(8)]
            for tb in range(4):
                for m in range(8):
                    qtw = emit_wtiles(cx, lwo_b, 128 * m, 128, 8, wt, after=(cx.tok("t"),))

                    def _t(te, tb=tb, m=m, qtw=qtw):
                        cx.wait(te, qtw, cx.tok("v"))
                        mm = None
                        for k in range(8):
                            mm = te.matmul(out=cx.ps[0][:, 0:512], lhsT=wt[:, 128 * k:128 * k + 128],
                                           rhs=o_l[k][:, 512 * tb:512 * tb + 512],
                                           start=(k == 0), stop=(k == 7))
                        cx.inc(mm, "t")

                    cx.T(_t)
                    t1 = cx.tok("t")

                    def _v(ve, tb=tb, m=m, t1=t1):
                        cx.wait(ve, t1)
                        cx.inc(ve.tensor_tensor(out=h1[m][:, HALO + 512 * tb:HALO + 512 * tb + 512],
                                                in0=h1[m][:, HALO + 512 * tb:HALO + 512 * tb + 512],
                                                in1=cx.ps[0][:, 0:512], op=ALU.add), "v")

                    cx.V(_v)
            cx.barrier()
            for tb in range(4):
                off = HALO + 512 * tb

                def _q(qe, tb=tb):
                    cx.wait(qe, cx.tok("v"))
                    for ci in range(8):
                        cx.inc(qe.dma_start(out=xf[ci][:],
                                            in_=x0own.ap()[128 * ci:128 * ci + 128,
                                                           512 * tb:512 * tb + 512]), "q", 16)

                cx.Q(_q)
                qt = cx.tok("q")
                st = emit_stats(cx, [h1[ci][:, off:off + 512] for ci in range(8)], 512, 1e-5, True,
                                cx.scr["r0"][0:1, 0:512], cx.scr["r1"][0:1, 0:512],
                                cx.scr["r2"][0:1, 0:512], after=(cx.tok("v"),))
                b1 = emit_bcast(cx, cx.scr["r2"][0:1, 0:512], 0, 128, 512, after=(st,))
                b2 = emit_bcast(cx, cx.scr["r1"][0:1, 0:512], 1, 128, 512, after=(st,))

                def _v(ve, off=off, qt=qt, b2=b2):
                    cx.wait(ve, b2, qt)
                    for ci in range(8):
                        ve.tensor_tensor(out=h1[ci][:, off:off + 512], in0=h1[ci][:, off:off + 512],
                                         in1=cx.ps[1][:, 0:512], op=ALU.mult)
                        ve.drain()
                        ve.tensor_tensor(out=h1[ci][:, off:off + 512], in0=h1[ci][:, off:off + 512],
                                         in1=cx.ps[0][:, 0:512], op=ALU.subtract)
                        ve.drain()
                        ve.tensor_tensor(out=h1[ci][:, off:off + 512], in0=h1[ci][:, off:off + 512],
                                         in1=xf[ci][:], op=ALU.add)
                        ve.drain()
                    cx.inc(ve.tensor_copy(out=scr_mark(cx), in_=scr_mark(cx)), "v")

                cx.V(_v)
            cx.barrier()
            if dbg:
                def _gx(ge):
                    cx.wait(ge, cx.tok("v"))
                    for ci in range(8):
                        cx.inc(ge.dma_start(out=dbg_x1.ap()[128 * ci:128 * ci + 128, :],
                                            in_=h1[ci][:, HALO:NKV]), "g", 16)

                cx.G(_gx)
                cx.barrier()
        po.close()

        NT = 256
        with contextlib.ExitStack() as ph:
            xnT = ph.enter_context(nc.sbuf_tensor("xnT", [128, 8 * NT], F32))
            xqT = ph.enter_context(nc.sbuf_tensor("xqT", [128, 8 * NT], BF16))
            fsb = [ph.enter_context(nc.sbuf_tensor("fsb%d" % i, [128, NT], F32)) for i in range(8)]
            hf = ph.enter_context(nc.sbuf_tensor("hf", [128, 32 * NT], F32))
            hq = ph.enter_context(nc.sbuf_tensor("hq", [128, 32 * NT], BF16))
            wg0 = ph.enter_context(nc.sbuf_tensor("wg0", [128, 1024], BF16))
            wg1 = ph.enter_context(nc.sbuf_tensor("wg1", [128, 1024], BF16))
            wu0 = ph.enter_context(nc.sbuf_tensor("wu0", [128, 1024], BF16))
            wu1 = ph.enter_context(nc.sbuf_tensor("wu1", [128, 1024], BF16))
            wd0 = ph.enter_context(nc.sbuf_tensor("wd0", [128, 4096], BF16))
            wd1 = ph.enter_context(nc.sbuf_tensor("wd1", [128, 4096], BF16))
            amx = ph.enter_context(nc.sbuf_tensor("amx", [128, 8], F32))
            x2stg = ph.enter_context(nc.sbuf_tensor("x2stg", [128, NT], BF16))
            rowx = ph.enter_context(nc.sbuf_tensor("rowx", [1, NT], F32))
            rowy = ph.enter_context(nc.sbuf_tensor("rowy", [1, NT], F32))
            rowz = ph.enter_context(nc.sbuf_tensor("rowz", [1, NT], F32))

            def xnc(ci):
                return xnT[:, NT * ci:NT * ci + NT]

            def hs(m):
                return hf[:, NT * m:NT * m + NT]

            def hqs(m):
                return hq[:, NT * m:NT * m + NT]

            for tc in range(2048 // NT):
                off = HALO + NT * tc
                st = emit_stats(cx, [h1[ci][:, off:off + NT] for ci in range(8)], NT, 1e-5, True,
                                cx.scr["r0"][0:1, 0:NT], cx.scr["r1"][0:1, 0:NT],
                                cx.scr["r2"][0:1, 0:NT], after=(cx.tok("v"), cx.tok("g")))
                b1 = emit_bcast(cx, cx.scr["r2"][0:1, 0:NT], 2, 128, NT, after=(st,))
                b2 = emit_bcast(cx, cx.scr["r1"][0:1, 0:NT], 3, 128, NT, after=(st,))

                def _v1(ve, off=off, b2=b2):
                    cx.wait(ve, b2)
                    for ci in range(8):
                        ve.tensor_tensor(out=xnc(ci), in0=h1[ci][:, off:off + NT],
                                         in1=cx.ps[3][:, 0:NT], op=ALU.mult)
                    ve.drain()
                    for ci in range(8):
                        ve.tensor_tensor(out=xnc(ci), in0=xnc(ci),
                                         in1=cx.ps[2][:, 0:NT], op=ALU.subtract)
                    cx.inc(ve.tensor_copy(out=scr_mark(cx), in_=scr_mark(cx)), "v")

                cx.V(_v1)

                emit_absmax(cx, [xnc(ci) for ci in range(8)], amx, rowx, rowy, NT)

                sb = emit_bcast(cx, rowy[0:1, 0:NT], 2, 128, NT, after=(cx.tok("v"),))

                def _vE3(ve, sb=sb):
                    cx.wait(ve, sb)
                    cx.inc(ve.tensor_copy(out=cx.scr["E"][:, 0:NT], in_=cx.ps[2][:, 0:NT]), "v")

                cx.V(_vE3)
                e3 = cx.tok("v")
                t3 = cx.tok("t")

                def _g3(ge, e3=e3, t3=t3):
                    cx.wait(ge, e3, t3, cx.tok("g"))
                    for p in range(4):
                        for j in range(2):
                            dst = (cx.scr["D"][:, 256 * j:256 * j + 256] if p == 0 else
                                   xnT[:, 512 * (p - 1) + 256 * j:512 * (p - 1) + 256 * j + 256])
                            cx.inc(ge.tensor_tensor(
                                out=dst, in0=xnT[:, 512 * p + 256 * j:512 * p + 256 * j + 256],
                                in1=cx.scr["E"][:, 0:NT], op=ALU.mult), "g")

                cx.G(_g3)
                g3b = cx.cnt["g"] - 8
                for p in range(4):
                    s_ap = (cx.scr["D"][:, 0:512] if p == 0 else
                            xnT[:, 512 * (p - 1):512 * p])
                    emit_round_clip(cx, s_ap, xqT[:, 512 * p:512 * p + 512], -128, 127,
                                    n=512, after=(("g", g3b + 2 * (p + 1)),))

                vpre4 = cx.tok("v")

                def _gi4(ge, vpre4=vpre4):
                    cx.wait(ge, vpre4, cx.tok("g"))
                    ge.memset(cx.scr["E"][:, 0:NT], -3.0e38)
                    cx.inc(ge.memset(cx.scr["E"][:, NT:2 * NT], 3.0e38), "g")

                cx.G(_gi4)
                g_init4 = cx.tok("g")
                t_base = cx.cnt["t"]
                s_base = cx.cnt["s"]
                v_base = cx.cnt["v"]
                for m in range(32):
                    ev = m % 2 == 0
                    qg = emit_wtiles(cx, wgq, 128 * m, 128, 8, wg0 if ev else wg1,
                                     after=(("t", t_base + max(0, m - 1)),))
                    qu = emit_wtiles(cx, wuq, 128 * m, 128, 8, wu0 if ev else wu1,
                                     after=(qg,))
                    psG = cx.ps[0] if ev else cx.ps[2]
                    psU = cx.ps[1] if ev else cx.ps[3]
                    scrR = "A" if ev else "C"
                    scrS = "B" if ev else "D"

                    def _t(te, m=m, qu=qu, psG=psG, psU=psU, ev=ev):
                        cx.wait(te, qu, ("s", s_base + max(0, m - 1)),
                                ("v", v_base + max(0, m - 1)))
                        wgt = wg0 if ev else wg1
                        wut = wu0 if ev else wu1
                        for k in range(8):
                            te.matmul(out=psG[:, 0:NT], lhsT=wgt[:, 128 * k:128 * k + 128],
                                      rhs=xqT[:, NT * k:NT * k + NT],
                                      start=(k == 0), stop=(k == 7))
                        mm = None
                        for k in range(8):
                            mm = te.matmul(out=psU[:, 0:NT], lhsT=wut[:, 128 * k:128 * k + 128],
                                           rhs=xqT[:, NT * k:NT * k + NT],
                                           start=(k == 0), stop=(k == 7))
                        cx.inc(mm, "t")

                    cx.T(_t)
                    t1 = cx.tok("t")

                    def _s(se, t1=t1, m=m, psG=psG, scrR=scrR):
                        cx.wait(se, t1, ("v", v_base + max(0, m - 1)))
                        cx.inc(se.activation(out=cx.scr[scrR][:, 0:NT], in_=psG[:, 0:NT],
                                             func=AF.Relu), "s")

                    cx.Sc(_s)
                    s1 = cx.tok("s")

                    def _v3(ve, m=m, s1=s1, psU=psU, scrR=scrR, scrS=scrS):
                        if m == 0:
                            cx.wait(ve, s1, g_init4)
                        else:
                            cx.wait(ve, s1)
                        ve.tensor_tensor(out=cx.scr[scrS][:, 0:NT], in0=cx.scr[scrR][:, 0:NT],
                                         in1=cx.scr[scrR][:, 0:NT], op=ALU.mult)
                        ve.drain()
                        ve.tensor_tensor(out=hs(m), in0=cx.scr[scrS][:, 0:NT],
                                         in1=psU[:, 0:NT], op=ALU.mult)
                        ve.drain()
                        ve.tensor_tensor(out=cx.scr["E"][:, 0:NT], in0=cx.scr["E"][:, 0:NT],
                                         in1=hs(m), op=ALU.max)
                        cx.inc(ve.tensor_tensor(out=cx.scr["E"][:, NT:2 * NT],
                                                in0=cx.scr["E"][:, NT:2 * NT],
                                                in1=hs(m), op=ALU.min), "v")

                    cx.V(_v3)

                vfin4 = cx.tok("v")

                def _vA5(ve, vfin4=vfin4):
                    cx.wait(ve, vfin4)
                    cx.inc(ve.scalar_tensor_tensor(out=cx.scr["D"][:, 0:NT],
                                                   in0=cx.scr["E"][:, NT:2 * NT], scalar=-1.0,
                                                   in1=cx.scr["E"][:, 0:NT],
                                                   op0=ALU.mult, op1=ALU.max), "v")

                cx.V(_vA5)
                emit_absmax_tail(cx, amx, None, None, NT, raw_amax_row=rowz)

                def _v4(ve):
                    cx.wait(ve, cx.tok("v"), cx.tok("s"))
                    ve.tensor_tensor(out=rowy[0:1, 0:NT], in0=rowx[0:1, 0:NT],
                                     in1=rowx[0:1, 0:NT], op=ALU.mult)
                    ve.drain()
                    ve.tensor_tensor(out=rowy[0:1, 0:NT], in0=rowy[0:1, 0:NT],
                                     in1=rowx[0:1, 0:NT], op=ALU.mult)
                    ve.drain()
                    ve.tensor_scalar(out=rowy[0:1, 0:NT], in0=rowy[0:1, 0:NT],
                                     scalar1=wrec_s[0:1, 0:1], scalar2=wrec_s[0:1, 0:1],
                                     op0=ALU.mult, op1=ALU.mult)
                    ve.drain()
                    ve.tensor_scalar(out=rowy[0:1, 0:NT], in0=rowy[0:1, 0:NT],
                                     scalar1=wrec_s[0:1, 1:2], scalar2=None, op0=ALU.mult)
                    ve.drain()
                    ve.tensor_tensor(out=rowz[0:1, 0:NT], in0=rowz[0:1, 0:NT],
                                     in1=rowy[0:1, 0:NT], op=ALU.mult)
                    ve.drain()
                    ve.tensor_scalar(out=rowz[0:1, 0:NT], in0=rowz[0:1, 0:NT],
                                     scalar1=1e-5, scalar2=None, op0=ALU.max)
                    ve.drain()
                    ve.reciprocal(out=rowx[0:1, 0:NT], in_=rowz[0:1, 0:NT])
                    ve.drain()
                    ve.tensor_tensor(out=rowx[0:1, 0:NT], in0=rowx[0:1, 0:NT],
                                     in1=rowy[0:1, 0:NT], op=ALU.mult)
                    ve.drain()
                    ve.tensor_scalar(out=rowx[0:1, 0:NT], in0=rowx[0:1, 0:NT],
                                     scalar1=127.0, scalar2=None, op0=ALU.mult)
                    ve.drain()
                    ve.tensor_scalar(out=rowz[0:1, 0:NT], in0=rowz[0:1, 0:NT],
                                     scalar1=1.0 / 127.0, scalar2=wrec_s[0:1, 2:3],
                                     op0=ALU.mult, op1=ALU.mult)
                    ve.drain()
                    cx.inc(ve.tensor_copy(out=scr_mark(cx), in_=scr_mark(cx)), "v")

                cx.V(_v4)

                fb = emit_bcast(cx, rowx[0:1, 0:NT], 2, 128, NT, after=(cx.tok("v"),))

                def _vE6(ve, fb=fb):
                    cx.wait(ve, fb)
                    cx.inc(ve.tensor_copy(out=cx.scr["E"][:, 0:NT], in_=cx.ps[2][:, 0:NT]), "v")

                cx.V(_vE6)
                e6 = cx.tok("v")
                t6 = cx.tok("t")

                def _g6(ge, e6=e6, t6=t6):
                    cx.wait(ge, e6, t6, cx.tok("g"))
                    for p in range(16):
                        for j in range(2):
                            dst = (cx.scr["D"][:, 256 * j:256 * j + 256] if p == 0 else
                                   hf[:, 512 * (p - 1) + 256 * j:512 * (p - 1) + 256 * j + 256])
                            cx.inc(ge.tensor_tensor(
                                out=dst, in0=hf[:, 512 * p + 256 * j:512 * p + 256 * j + 256],
                                in1=cx.scr["E"][:, 0:NT], op=ALU.mult), "g")

                cx.G(_g6)
                g6b = cx.cnt["g"] - 32
                for p in range(16):
                    s_ap = (cx.scr["D"][:, 0:512] if p == 0 else
                            hf[:, 512 * (p - 1):512 * p])
                    emit_round_clip(cx, s_ap, hq[:, 512 * p:512 * p + 512], -128, 127,
                                    n=512, after=(("g", g6b + 2 * (p + 1)),))

                fdb = emit_bcast(cx, rowz[0:1, 0:NT], 3, 128, NT, after=(cx.tok("v"),))

                def _vf(ve, fdb=fdb, g6e=("g", g6b + 32)):
                    cx.wait(ve, fdb, g6e)
                    cx.inc(ve.tensor_copy(out=cx.scr["E"][:, 0:NT], in_=cx.ps[3][:, 0:NT]), "v")

                cx.V(_vf)
                fdb = cx.tok("v")
                t_base2 = cx.cnt["t"]
                v_base2 = cx.cnt["v"]
                for m2 in range(8):
                    qd = emit_wtiles(cx, wdq, 128 * m2, 128, 32, wd0 if m2 % 2 == 0 else wd1,
                                     after=(("t", t_base2 + max(0, m2 - 1)),))
                    psD = cx.ps[0] if m2 % 2 == 0 else cx.ps[1]

                    def _t2(te, m2=m2, qd=qd, psD=psD):
                        cx.wait(te, qd, ("v", v_base2 + max(0, m2 - 1)))
                        wdt = wd0 if m2 % 2 == 0 else wd1
                        mm = None
                        for k in range(32):
                            mm = te.matmul(out=psD[:, 0:NT], lhsT=wdt[:, 128 * k:128 * k + 128],
                                           rhs=hqs(k), start=(k == 0), stop=(k == 31))
                        cx.inc(mm, "t")

                    cx.T(_t2)
                    t2 = cx.tok("t")

                    def _v6(ve, m2=m2, t2=t2, fdb=fdb, psD=psD):
                        cx.wait(ve, t2, fdb)
                        cx.inc(ve.tensor_tensor(out=fsb[m2][:], in0=psD[:, 0:NT],
                                                in1=cx.scr["E"][:, 0:NT], op=ALU.mult), "v")

                    cx.V(_v6)

                if dbg and tc == 4:
                    def _gd4(ge, tc=tc):
                        cx.wait(ge, cx.tok("v"), cx.tok("s"), cx.tok("t"))
                        for ci in range(8):
                            cx.inc(ge.dma_start(out=dbg_xq.ap()[128 * ci:128 * ci + 128, :],
                                                in_=xq[ci][:]), "g", 16)
                            cx.inc(ge.dma_start(out=dbg_fsb.ap()[128 * ci:128 * ci + 128, :],
                                                in_=fsb[ci][:]), "g", 16)
                        cx.inc(ge.dma_start(out=dbg_hq.ap(), in_=hq[:]), "g", 16)
                        cx.inc(ge.dma_start(out=dbg_hf.ap(), in_=hf[:]), "g", 16)
                        cx.inc(ge.dma_start(out=dbg_rows.ap()[0:1, :], in_=rowx[0:1, 0:NT]), "g", 16)
                        cx.inc(ge.dma_start(out=dbg_rows.ap()[1:2, :], in_=rowy[0:1, 0:NT]), "g", 16)
                        cx.inc(ge.dma_start(out=dbg_rows.ap()[2:3, :], in_=rowz[0:1, 0:NT]), "g", 16)

                    cx.G(_gd4)
                    cx.barrier()

                st2 = emit_stats(cx, [fsb[ci][:] for ci in range(8)], NT, 1e-5, True,
                                 cx.scr["r0"][0:1, 0:NT], cx.scr["r1"][0:1, 0:NT],
                                 cx.scr["r2"][0:1, 0:NT], after=(cx.tok("v"),))
                b1 = emit_bcast(cx, cx.scr["r2"][0:1, 0:NT], 2, 128, NT, after=(st2,))
                b2 = emit_bcast(cx, cx.scr["r1"][0:1, 0:NT], 3, 128, NT, after=(st2,))

                def _v7(ve, off=off, b2=b2):
                    cx.wait(ve, b2)
                    for ci in range(8):
                        ve.tensor_tensor(out=fsb[ci][:], in0=fsb[ci][:],
                                         in1=cx.ps[3][:, 0:NT], op=ALU.mult)
                    ve.drain()
                    for ci in range(8):
                        ve.tensor_tensor(out=fsb[ci][:], in0=fsb[ci][:],
                                         in1=cx.ps[2][:, 0:NT], op=ALU.subtract)
                    ve.drain()
                    for ci in range(8):
                        ve.tensor_tensor(out=h1[ci][:, off:off + NT], in0=h1[ci][:, off:off + NT],
                                         in1=fsb[ci][:], op=ALU.add)
                    cx.inc(ve.tensor_copy(out=scr_mark(cx), in_=scr_mark(cx)), "v")

                cx.V(_v7)
                st3 = emit_stats(cx, [h1[ci][:, off:off + NT] for ci in range(8)], NT, 1e-5, True,
                                 cx.scr["r0"][0:1, 0:NT], cx.scr["r1"][0:1, 0:NT],
                                 cx.scr["r2"][0:1, 0:NT], after=(cx.tok("v"),))
                b1 = emit_bcast(cx, cx.scr["r2"][0:1, 0:NT], 2, 128, NT, after=(st3,))
                b2 = emit_bcast(cx, cx.scr["r1"][0:1, 0:NT], 3, 128, NT, after=(st3,))
                for ci in range(8):
                    def _v8(ve, ci=ci, off=off, b2=b2):
                        cx.wait(ve, b2, cx.tok("g"))
                        ve.tensor_tensor(out=cx.scr["D"][:, 0:NT], in0=h1[ci][:, off:off + NT],
                                         in1=cx.ps[3][:, 0:NT], op=ALU.mult)
                        ve.drain()
                        cx.inc(ve.tensor_tensor(out=x2stg[:], in0=cx.scr["D"][:, 0:NT],
                                                in1=cx.ps[2][:, 0:NT], op=ALU.subtract), "v")

                    cx.V(_v8)
                    vt = cx.tok("v")

                    def _g(ge, ci=ci, tc=tc, off=off, vt=vt):
                        cx.wait(ge, vt)
                        cx.inc(ge.dma_start(out=x2nout.ap()[128 * ci:128 * ci + 128,
                                                            NT * tc:NT * tc + NT],
                                            in_=x2stg[:]), "g", 16)
                        cx.inc(ge.dma_start(out=x2out.ap()[128 * ci:128 * ci + 128,
                                                           NT * tc:NT * tc + NT],
                                            in_=h1[ci][:, off:off + NT]), "g", 16)

                    cx.G(_g)
            cx.barrier()

        def _gend(ge):
            cx.wait(ge, cx.tok("g"))

        cx.G(_gend)
    return nc


def rowx2_holder(cx):
    return cx.scr["r0"]


def emit_absmax(cx, chunks, amx, inv_s_row, s_row, nt=512, raw_amax_row=None):
    scr, c = cx.scr, cx.c
    n = len(chunks)
    nu = nt // 128

    def _v(ve):
        cx.wait(ve, cx.tok("t"), cx.tok("s"))
        ve.tensor_tensor(out=scr["D"][:, 0:nt], in0=chunks[0], in1=chunks[1], op=ALU.max)
        ve.tensor_tensor(out=scr["E"][:, 0:nt], in0=chunks[0], in1=chunks[1], op=ALU.min)
        ve.drain()
        for ci in range(2, n):
            ve.tensor_tensor(out=scr["D"][:, 0:nt], in0=scr["D"][:, 0:nt], in1=chunks[ci], op=ALU.max)
            ve.tensor_tensor(out=scr["E"][:, 0:nt], in0=scr["E"][:, 0:nt], in1=chunks[ci], op=ALU.min)
            ve.drain()
        cx.inc(ve.scalar_tensor_tensor(out=scr["D"][:, 0:nt], in0=scr["E"][:, 0:nt],
                                       scalar=-1.0, in1=scr["D"][:, 0:nt],
                                       op0=ALU.mult, op1=ALU.max), "v")

    cx.V(_v)
    return emit_absmax_tail(cx, amx, inv_s_row, s_row, nt, raw_amax_row)


def emit_absmax_tail(cx, amx, inv_s_row, s_row, nt=512, raw_amax_row=None):
    scr = cx.scr
    nu = nt // 128
    v1 = cx.tok("v")
    for u in range(nu):
        def _t(te, u=u, v1=v1):
            cx.wait(te, v1, cx.tok("v"))
            cx.inc(te.transpose(out=cx.pstf[0:128, 0:128], in_=scr["D"][:, 128 * u:128 * u + 128],
                                identity=cx.c["identF"][:]), "t")

        cx.T(_t)
        t1 = cx.tok("t")

        def _v2(ve, u=u, t1=t1):
            cx.wait(ve, t1)
            cx.inc(ve.reduce_max(out=amx[:, u:u + 1], in_=cx.pstf[0:128, 0:128],
                                 axis=mybir.AxisListType.X), "v")

        cx.V(_v2)
    v2 = cx.tok("v")

    def _t2(te):
        cx.wait(te, v2)
        cx.inc(te.transpose(out=cx.pstf[0:nu, 0:128], in_=amx[:, 0:nu],
                            identity=cx.c["identF"][:]), "t")

    cx.T(_t2)
    t2 = cx.tok("t")
    dst = raw_amax_row if raw_amax_row is not None else inv_s_row

    def _v2b(ve):
        cx.wait(ve, t2)
        cx.inc(ve.tensor_copy(out=scr["amx4"][0:nu, :], in_=cx.pstf[0:nu, 0:128]), "v")

    cx.V(_v2b)
    v2b = cx.tok("v")

    def _qx(qe):
        cx.wait(qe, v2b)
        cx.inc(qe.dma_start(out=cx.dscr[0:1, 0:nt], in_=scr["amx4"][0:nu, :]), "q", 16)
        cx.wait(qe, cx.tok("q"))
        cx.inc(qe.dma_start(out=dst[0:1, 0:nt], in_=cx.dscr.ap()[0:1, 0:nt]), "q", 16)

    cx.Q(_qx)
    qx = cx.tok("q")

    def _v3(ve):
        cx.wait(ve, qx)
        if raw_amax_row is None:
            ve.tensor_scalar(out=dst[0:1, 0:nt], in0=dst[0:1, 0:nt],
                             scalar1=1e-5, scalar2=None, op0=ALU.max)
            ve.drain()
            ve.reciprocal(out=s_row[0:1, 0:nt], in_=dst[0:1, 0:nt])
            ve.drain()
            ve.tensor_scalar(out=s_row[0:1, 0:nt], in0=s_row[0:1, 0:nt],
                             scalar1=127.0, scalar2=None, op0=ALU.mult)
            ve.drain()
            ve.tensor_scalar(out=dst[0:1, 0:nt], in0=dst[0:1, 0:nt],
                             scalar1=1.0 / 127.0, scalar2=None, op0=ALU.mult)
            ve.drain()
        cx.inc(ve.tensor_copy(out=scr_mark(cx), in_=scr_mark(cx)), "v")

    cx.V(_v3)
    return cx.tok("v")


def build_launchC():
    nc = bass.Bass("TRN2", target_bir_lowering=False, debug=False)
    es = contextlib.ExitStack()
    with es:
        block = es.enter_context(nc.Block())
        cx = Ctx(nc, es, block)
        x2nf = cx.inp("x2nf", [D, S], BF16)
        wqC_j = cx.inp("wqC_j", [D, 512], BF16)
        wkC_j = cx.inp("wkC_j", [D, 512], BF16)
        wvC_j = cx.inp("wvC_j", [D, 512], BF16)
        gmask = cx.inp("gmask", [128, 4 * 512], BF16)
        og1out = cx.outp("og1out", [512, S], BF16)
        alloc_common(cx)
        xs = [cx.sbuf("xs%d" % i, [128, S], BF16) for i in range(8)]
        qcur = cx.sbuf("qcurC", [64, S], BF16)
        kcur = cx.sbuf("kcurC", [64, S], BF16)
        vtm = cx.sbuf("vtmC", [128, 65 * 32], BF16)
        ocur = cx.sbuf("ocurC", [64, S], BF16)
        gmask_s = cx.sbuf("gmaskC", [128, 4 * 512], BF16)
        vf_scr = cx.sbuf("vfC", [64, 512], BF16)
        wt = cx.sbuf("wtC", [128, 512], BF16)
        emit_load_consts(cx)

        def _q0(qe):
            cx.inc(qe.dma_start(out=gmask_s[:], in_=gmask.ap()), "q", 16)
            for ci in range(8):
                cx.inc(qe.dma_start(out=xs[ci][:], in_=x2nf.ap()[128 * ci:128 * ci + 128, :]), "q", 16)

        cx.Q(_q0)
        cx.barrier()

        def proj_plain(w_dram, col0, dst):
            qtw = emit_wtiles(cx, w_dram, col0, 64, 8, wt, after=(cx.tok("t"), cx.tok("v")))
            v_base = cx.cnt["v"]
            ent_v = cx.tok("v")
            for nb in range(8):
                psP = cx.ps[0] if nb % 2 == 0 else cx.ps[1]

                def _t(te, nb=nb, qtw=qtw, psP=psP):
                    if nb < 2:
                        cx.wait(te, qtw, ent_v)
                    else:
                        cx.wait(te, qtw, ("v", v_base + nb - 1))
                    mm = None
                    for k in range(8):
                        mm = te.matmul(out=psP[0:64, 0:512], lhsT=wt[:, 64 * k:64 * k + 64],
                                       rhs=xs[k][:, 512 * nb:512 * nb + 512],
                                       start=(k == 0), stop=(k == 7))
                    cx.inc(mm, "t")

                cx.T(_t)
                t1 = cx.tok("t")

                def _v(ve, nb=nb, t1=t1, dst=dst, psP=psP):
                    cx.wait(ve, t1)
                    cx.inc(ve.tensor_copy(out=dst[:, 512 * nb:512 * nb + 512],
                                          in_=psP[0:64, 0:512]), "v")

                cx.V(_v)

        for j in range(8):
            proj_plain(wqC_j, 64 * j, qcur)
            proj_plain(wkC_j, 64 * j, kcur)
            qtw = emit_wtiles(cx, wvC_j, 64 * j, 64, 8, wt, after=(cx.tok("t"), cx.tok("v")))

            def _w(te, qtw=qtw):
                cx.wait(te, qtw)

            cx.T(_w)
            emit_v_build(cx, [wt[:, 64 * k:64 * k + 64] for k in range(8)],
                         lambda k, nb: xs[k][:, 512 * nb:512 * nb + 512],
                         8, 32, vtm, vf_scr)
            vend = emit_attn_job(cx, qcur, kcur, vtm, ocur, S, None,
                                 cx.c["biasC1"], gmask_s)

            def _g(ge, j=j, vend=vend):
                cx.wait(ge, vend)
                cx.inc(ge.dma_start(out=og1out.ap()[64 * j:64 * j + 64, :], in_=ocur[:]), "g", 16)

            cx.G(_g)
            gt = cx.tok("g")

            def _v(ve, gt=gt):
                cx.wait(ve, gt)

            cx.V(_v)

        def _gend(ge):
            cx.wait(ge, cx.tok("g"))

        cx.G(_gend)
    return nc


def build_launchD():
    nc = bass.Bass("TRN2", target_bir_lowering=False, debug=False)
    es = contextlib.ExitStack()
    with es:
        block = es.enter_context(nc.Block())
        cx = Ctx(nc, es, block)
        x2own = cx.inp("x2own", [D, TOK], F32)
        og1own = cx.inp("og1own", [D, TOK], BF16)
        woT_b = cx.inp("woT_b", [D, D], BF16)
        w8T_b = cx.inp("w8T_b", [D, D], BF16)
        wscale = cx.inp("wscale", [1, 4], F32)
        fbias = cx.inp("fbias", [128, 8], F32)
        x4out = cx.outp("x4out", [D, TOK], F32)
        alloc_common(cx)
        x3 = [cx.sbuf("x3_%d" % i, [128, TOK], F32) for i in range(8)]
        xn = [cx.sbuf("xnD%d" % i, [128, 512], F32) for i in range(8)]
        xq = [cx.sbuf("xqD%d" % i, [128, 512], BF16) for i in range(8)]
        fsb = [cx.sbuf("fsbD%d" % i, [128, 512], F32) for i in range(8)]
        ogt = [cx.sbuf("ogtD%d" % i, [128, 512], BF16) for i in range(8)]
        xf = [cx.sbuf("xfD%d" % i, [128, 512], F32) for i in range(8)]
        wt = cx.sbuf("wtD", [128, 8 * 128], BF16)
        amx = cx.sbuf("amxD", [128, 8], F32)
        wsc_s = cx.sbuf("wscD", [1, 4], F32)
        fb_s = cx.sbuf("fbD", [128, 8], F32)
        rowx = cx.sbuf("rowxD", [1, 512], F32)
        rowy = cx.sbuf("rowyD", [1, 512], F32)
        emit_load_consts(cx)

        def _q0(qe):
            cx.inc(qe.dma_start(out=wsc_s[:], in_=wscale.ap()), "q", 16)
            cx.inc(qe.dma_start(out=fb_s[:], in_=fbias.ap()), "q", 16)

        cx.Q(_q0)
        cx.barrier()

        for tc in range(4):
            def _q(qe, tc=tc):
                cx.wait(qe, cx.tok("t"), cx.tok("v"))
                for ci in range(8):
                    cx.inc(qe.dma_start(out=ogt[ci][:],
                                        in_=og1own.ap()[128 * ci:128 * ci + 128,
                                                        512 * tc:512 * tc + 512]), "q", 16)
                    cx.inc(qe.dma_start(out=xf[ci][:],
                                        in_=x2own.ap()[128 * ci:128 * ci + 128,
                                                       512 * tc:512 * tc + 512]), "q", 16)

            cx.Q(_q)
            qt = cx.tok("q")
            for m in range(8):
                qtw = emit_wtiles(cx, woT_b, 128 * m, 128, 8, wt, after=(cx.tok("t"),))

                def _t(te, m=m, qtw=qtw, qt=qt):
                    cx.wait(te, qtw, qt, cx.tok("v"))
                    mm = None
                    for k in range(8):
                        mm = te.matmul(out=cx.ps[0][:, 0:512], lhsT=wt[:, 128 * k:128 * k + 128],
                                       rhs=ogt[k][:], start=(k == 0), stop=(k == 7))
                    cx.inc(mm, "t")

                cx.T(_t)
                t1 = cx.tok("t")

                def _v(ve, m=m, t1=t1):
                    cx.wait(ve, t1)
                    cx.inc(ve.tensor_copy(out=fsb[m][:], in_=cx.ps[0][:, 0:512]), "v")

                cx.V(_v)
            st = emit_stats(cx, [fsb[ci][:] for ci in range(8)], 512, 1e-5, True,
                            cx.scr["r0"][0:1, 0:512], cx.scr["r1"][0:1, 0:512],
                            cx.scr["r2"][0:1, 0:512], after=(cx.tok("v"),))
            b1 = emit_bcast(cx, cx.scr["r2"][0:1, 0:512], 2, 128, 512, after=(st,))
            b2 = emit_bcast(cx, cx.scr["r1"][0:1, 0:512], 3, 128, 512, after=(st,))

            def _v2(ve, tc=tc, b2=b2, qt=qt):
                cx.wait(ve, b2, qt)
                for ci in range(8):
                    ve.tensor_tensor(out=fsb[ci][:], in0=fsb[ci][:], in1=cx.ps[3][:, 0:512], op=ALU.mult)
                    ve.drain()
                    ve.tensor_tensor(out=fsb[ci][:], in0=fsb[ci][:], in1=cx.ps[2][:, 0:512], op=ALU.subtract)
                    ve.drain()
                    ve.tensor_tensor(out=x3[ci][:, 512 * tc:512 * tc + 512], in0=fsb[ci][:],
                                     in1=xf[ci][:], op=ALU.add)
                    ve.drain()
                cx.inc(ve.tensor_copy(out=scr_mark(cx), in_=scr_mark(cx)), "v")

            cx.V(_v2)
        cx.barrier()

        for tc in range(4):
            off = 512 * tc
            st = emit_stats(cx, [x3[ci][:, off:off + 512] for ci in range(8)], 512, 1e-5, True,
                            cx.scr["r0"][0:1, 0:512], cx.scr["r1"][0:1, 0:512],
                            cx.scr["r2"][0:1, 0:512], after=(cx.tok("v"), cx.tok("g")))
            b1 = emit_bcast(cx, cx.scr["r2"][0:1, 0:512], 2, 128, 512, after=(st,))
            b2 = emit_bcast(cx, cx.scr["r1"][0:1, 0:512], 3, 128, 512, after=(st,))

            def _v1(ve, off=off, b2=b2):
                cx.wait(ve, b2)
                for ci in range(8):
                    ve.tensor_tensor(out=xn[ci][:], in0=x3[ci][:, off:off + 512],
                                     in1=cx.ps[3][:, 0:512], op=ALU.mult)
                    ve.drain()
                    ve.tensor_tensor(out=xn[ci][:], in0=xn[ci][:], in1=cx.ps[2][:, 0:512], op=ALU.subtract)
                    ve.drain()
                cx.inc(ve.tensor_copy(out=scr_mark(cx), in_=scr_mark(cx)), "v")

            cx.V(_v1)
            st2 = emit_stats(cx, [xn[ci][:] for ci in range(8)], 512, 1e-6, False,
                             None, cx.scr["r1"][0:1, 0:512], None, after=(cx.tok("v"),))
            b3 = emit_bcast(cx, cx.scr["r1"][0:1, 0:512], 2, 128, 512, after=(st2,))

            def _v2(ve, b3=b3):
                cx.wait(ve, b3)
                for ci in range(8):
                    ve.tensor_tensor(out=xn[ci][:], in0=xn[ci][:], in1=cx.ps[2][:, 0:512], op=ALU.mult)
                    ve.drain()
                cx.inc(ve.tensor_copy(out=scr_mark(cx), in_=scr_mark(cx)), "v")

            cx.V(_v2)
            emit_absmax(cx, [xn[ci][:] for ci in range(8)], amx, rowx, rowy)
            sb = emit_bcast(cx, rowy[0:1, 0:512], 2, 128, 512, after=(cx.tok("v"),))
            for ci in range(8):
                def _v3(ve, ci=ci, sb=sb):
                    cx.wait(ve, sb)
                    cx.inc(ve.tensor_tensor(out=cx.scr["D"][:], in0=xn[ci][:],
                                            in1=cx.ps[2][:, 0:512], op=ALU.mult), "v")

                cx.V(_v3)
                emit_round_clip(cx, cx.scr["D"][:], xq[ci][:], -128, 127, after=(cx.tok("v"),))
            def _v4(ve):
                cx.wait(ve, cx.tok("v"))
                ve.tensor_scalar(out=rowx[0:1, 0:512], in0=rowx[0:1, 0:512],
                                 scalar1=wsc_s[0:1, 0:1], scalar2=None, op0=ALU.mult)
                ve.drain()
                cx.inc(ve.tensor_copy(out=scr_mark(cx), in_=scr_mark(cx)), "v")

            cx.V(_v4)
            scb = emit_bcast(cx, rowx[0:1, 0:512], 3, 128, 512, after=(cx.tok("v"),))

            def _vsc(ve, scb=scb):
                cx.wait(ve, scb)
                cx.inc(ve.tensor_copy(out=cx.scr["E"][:], in_=cx.ps[3][:, 0:512]), "v")

            cx.V(_vsc)
            scb = cx.tok("v")
            for m in range(8):
                qtw = emit_wtiles(cx, w8T_b, 128 * m, 128, 8, wt, after=(cx.tok("t"),))

                def _t(te, m=m, qtw=qtw):
                    cx.wait(te, qtw, cx.tok("v"))
                    mm = None
                    for k in range(8):
                        mm = te.matmul(out=cx.ps[0][:, 0:512], lhsT=wt[:, 128 * k:128 * k + 128],
                                       rhs=xq[k][:], start=(k == 0), stop=(k == 7))
                    cx.inc(mm, "t")

                cx.T(_t)
                t1 = cx.tok("t")

                def _v5(ve, m=m, t1=t1, scb=scb):
                    cx.wait(ve, t1, scb)
                    ve.tensor_tensor(out=fsb[m][:], in0=cx.ps[0][:, 0:512],
                                     in1=cx.scr["E"][:], op=ALU.mult)
                    ve.drain()
                    cx.inc(ve.tensor_scalar(out=fsb[m][:], in0=fsb[m][:],
                                            scalar1=fb_s[:, m:m + 1], scalar2=None, op0=ALU.add), "v")

                cx.V(_v5)
            st3 = emit_stats(cx, [fsb[ci][:] for ci in range(8)], 512, 1e-5, True,
                             cx.scr["r0"][0:1, 0:512], cx.scr["r1"][0:1, 0:512],
                             cx.scr["r2"][0:1, 0:512], after=(cx.tok("v"),))
            b1 = emit_bcast(cx, cx.scr["r2"][0:1, 0:512], 2, 128, 512, after=(st3,))
            b2 = emit_bcast(cx, cx.scr["r1"][0:1, 0:512], 3, 128, 512, after=(st3,))
            for ci in range(8):
                def _v6(ve, ci=ci, off=off, b2=b2):
                    cx.wait(ve, b2, cx.tok("g"))
                    ve.tensor_tensor(out=fsb[ci][:], in0=fsb[ci][:], in1=cx.ps[3][:, 0:512], op=ALU.mult)
                    ve.drain()
                    ve.tensor_tensor(out=fsb[ci][:], in0=fsb[ci][:], in1=cx.ps[2][:, 0:512], op=ALU.subtract)
                    ve.drain()
                    cx.inc(ve.tensor_tensor(out=x3[ci][:, off:off + 512], in0=x3[ci][:, off:off + 512],
                                            in1=fsb[ci][:], op=ALU.add), "v")

                cx.V(_v6)
                vt = cx.tok("v")

                def _g(ge, ci=ci, tc=tc, vt=vt):
                    cx.wait(ge, vt)
                    cx.inc(ge.dma_start(out=x4out.ap()[128 * ci:128 * ci + 128,
                                                       512 * tc:512 * tc + 512],
                                        in_=x3[ci][:, 512 * tc:512 * tc + 512]), "g", 16)

                cx.G(_g)
        cx.barrier()

        def _gend(ge):
            cx.wait(ge, cx.tok("g"))

        cx.G(_gend)
    return nc


def _lmask_np(r):
    import ml_dtypes
    own0 = (r % 2) * TOK
    poskv = np.concatenate([np.arange(1792, 2048), np.arange(own0, own0 + TOK)]).astype(np.int64)
    m = np.zeros((128, 48, 128), np.float32)
    for jt in range(16):
        for sl in range(3):
            kpos = poskv[128 * (jt + sl):128 * (jt + sl) + 128]
            qpos = own0 + 128 * jt + np.arange(128)
            dif = qpos[None, :] - kpos[:, None]
            m[:, jt * 3 + sl, :] = np.where((dif >= 0) & (dif < 256), 0.0, NEG)
    return np.ascontiguousarray(m.transpose(0, 1, 2).reshape(128, 48 * 128)).astype(ml_dtypes.bfloat16)


def kernel(**inputs):
    import ml_dtypes
    consts = _consts_np()
    w0 = run_launchW0(inputs)
    w1 = run_launchW1(inputs, w0)
    og = run_launchA(inputs)

    ogf = [np.concatenate([og[2 * b], og[2 * b + 1]], axis=0) for b in range(B)]
    x0 = np.asarray(inputs["L0_attn_hidden"], np.float32)
    n2 = np.ascontiguousarray(np.asarray(inputs["L0_attn_norm2_w"], np.float32).reshape(8, 128).T)
    gm = _gmask_np()
    mapsB = []
    for r in range(NCORES):
        b, own0 = r // 2, (r % 2) * TOK
        x0f = np.ascontiguousarray(x0[b].T)
        m = dict(consts)
        m["x0own"] = np.ascontiguousarray(x0f[:, own0:own0 + TOK])
        m["x0halo"] = np.ascontiguousarray(x0f[:, 1792:2048])
        m["ogown"] = np.ascontiguousarray(ogf[b][:, own0:own0 + TOK])
        m["oghalo"] = np.ascontiguousarray(ogf[b][:, 1792:2048])
        m["gwo_b"] = w0["gwo_b"]
        m["n2w"] = n2
        m["lwq_b"] = w0["lwq_b"]
        m["lwk_b"] = w0["lwk_b"]
        m["lwv_b"] = w0["lwv_b"]
        m["lwo_b"] = w0["lwo_b"]
        m["wgq"] = w1["wgq"]
        m["wuq"] = w1["wuq"]
        m["wdq"] = w1["wdq"]
        m["wrec"] = w1["wrec"]
        m["posKV"] = np.concatenate([np.arange(1792, 2048), np.arange(own0, own0 + TOK)]
                                    ).astype(np.float32).reshape(1, NKV)
        m["lmask"] = _lmask_np(r)
        mapsB.append(m)
    if "Bn" not in _cache:
        _cache["Bn"] = build_launchB()
    resB = run_bass_kernel_spmd(_cache["Bn"], mapsB, core_ids=list(range(NCORES)))
    x2 = [np.asarray(r["x2out"]) for r in resB.results]
    x2n = [np.asarray(r["x2nout"]) for r in resB.results]

    x2nf = [np.concatenate([x2n[2 * b], x2n[2 * b + 1]], axis=1) for b in range(B)]
    mapsC = []
    for r in range(NCORES):
        b, hs = r // 2, (r % 2) * 8
        m = dict(consts)
        m["x2nf"] = np.ascontiguousarray(x2nf[b])
        m["wqC_j"] = np.ascontiguousarray(w0["wqC_b"][:, 512 * (r % 2):512 * (r % 2) + 512])
        m["wkC_j"] = np.ascontiguousarray(w0["wkC_b"][:, 512 * (r % 2):512 * (r % 2) + 512])
        m["wvC_j"] = np.ascontiguousarray(w0["wvC_b"][:, 512 * (r % 2):512 * (r % 2) + 512])
        m["gmask"] = gm
        mapsC.append(m)
    if "C" not in _cache:
        _cache["C"] = build_launchC()
    resC = run_bass_kernel_spmd(_cache["C"], mapsC, core_ids=list(range(NCORES)))
    og1 = [np.asarray(r["og1out"]) for r in resC.results]
    og1f = [np.concatenate([og1[2 * b], og1[2 * b + 1]], axis=0) for b in range(B)]

    fb = np.ascontiguousarray(np.asarray(inputs["L1_ffn_bias"], np.float32).reshape(8, 128).T)
    ws = np.full((1, 4), float(np.asarray(inputs["L1_ffn_weight_scale"], np.float32)), np.float32)
    mapsD = []
    for r in range(NCORES):
        b, own0 = r // 2, (r % 2) * TOK
        m = dict(consts)
        m["x2own"] = x2[r]
        m["og1own"] = np.ascontiguousarray(og1f[b][:, own0:own0 + TOK])
        m["woT_b"] = w0["woT_b"]
        m["w8T_b"] = w0["w8T_b"]
        m["wscale"] = ws
        m["fbias"] = fb
        mapsD.append(m)
    if "Dn" not in _cache:
        _cache["Dn"] = build_launchD()
    resD = run_bass_kernel_spmd(_cache["Dn"], mapsD, core_ids=list(range(NCORES)))
    x4 = [np.asarray(r["x4out"]) for r in resD.results]

    out = np.empty((B, S, D), np.float32)
    for r in range(NCORES):
        b, own0 = r // 2, (r % 2) * TOK
        out[b, own0:own0 + TOK, :] = x4[r].T
    return out


def emit_cc(cx, kind, op, ins, outs, after=()):
    def _g(ge):
        cx.wait(ge, *after)
        cx.inc(ge.collective_compute(kind, op, replica_groups=[list(range(NCORES))],
                                     ins=ins, outs=outs), "c")
    cx.G(_g)
    return cx.tok("c")


def build_fused():
    nc = bass.Bass("TRN2", target_bir_lowering=False, debug=False)
    es = contextlib.ExitStack()
    with es:
        block = es.enter_context(nc.Block())
        cx = Ctx(nc, es, block)
        cx.sem["c"] = es.enter_context(nc.semaphore("sem_c"))
        cx.cnt["c"] = 0

        x0f = cx.inp("x0f", [D, S], F32)
        x0own = cx.inp("x0own", [D, TOK], F32)
        x0halo = cx.inp("x0halo", [D, HALO], F32)
        n1w = cx.inp("n1w", [128, 8], F32)
        n2w = cx.inp("n2w", [128, 8], F32)
        wqA = cx.inp("wqA", [D, 512], F32)
        wkvd = cx.inp("wkvd", [D, LAT], F32)
        wkuA = cx.inp("wkuA", [LAT, 128], F32)
        wvuA = cx.inp("wvuA", [LAT, 128], F32)
        gmask = cx.inp("gmask", [128, 4 * 512], BF16)
        posAll = cx.inp("posAll", [1, S], F32)
        posKV = cx.inp("posKV", [1, NKV], F32)
        lmask = cx.inp("lmask", [128, 48 * 128], BF16)
        gwo_f = cx.inp("gwo_f", [D, D], F32)
        lwq_f = cx.inp("lwq_f", [D, D], F32)
        lwk_f = cx.inp("lwk_f", [D, LAT], F32)
        lwv_f = cx.inp("lwv_f", [D, LAT], F32)
        lwo_f = cx.inp("lwo_f", [D, D], F32)
        wgT = cx.inp("wgT", [128, FFN_I], F32)
        wuT = cx.inp("wuT", [128, FFN_I], F32)
        wdT = cx.inp("wdT", [512, D], F32)
        wqC_f = cx.inp("wqC_f", [D, 512], F32)
        wkC_f = cx.inp("wkC_f", [D, 512], F32)
        wvC_f = cx.inp("wvC_f", [D, 512], F32)
        woT_f = cx.inp("woT_f", [D, D], F32)
        w8T_i = cx.inp("w8T_i", [D, D], I32)
        wscale = cx.inp("wscale", [1, 4], F32)
        fbias = cx.inp("fbias", [128, 8], F32)
        selOwn = cx.inp("selOwn", [128, 2], F32)
        idxA = cx.inp("idxA", [128, 8], mybir.dt.uint32)
        idxX = cx.inp("idxX", [128, 16], mybir.dt.uint32)
        x4out = cx.outp("x4out", [D, TOK], F32)

        wgq = cx.dram("wgq_s", [D, FFN_I], BF16)
        wuq = cx.dram("wuq_s", [D, FFN_I], BF16)
        wdq = cx.dram("wdq_s", [FFN_I, D], BF16)
        ar_in = cx.dram("ar_in", [1, 4], F32)
        ar_out = cx.dram("ar_out", [1, 4], F32)
        agw_in = {n: cx.dram("agwin_" + n, sh, BF16) for n, sh in
                  [("g", [128, FFN_I]), ("u", [128, FFN_I]), ("d", [512, D])]}
        gwo_b = cx.dram("gwo_b", [D, D], BF16)
        lwq_b = cx.dram("lwq_b", [D, D], BF16)
        lwk_b = cx.dram("lwk_b", [D, LAT], BF16)
        lwv_b = cx.dram("lwv_b", [D, LAT], BF16)
        lwo_b = cx.dram("lwo_b", [D, D], BF16)
        wqC_b = cx.dram("wqC_b", [D, 512], BF16)
        wqA_b = cx.dram("wqA_b", [D, 512], BF16)
        wkC_b = cx.dram("wkC_b", [D, 512], BF16)
        wvC_b = cx.dram("wvC_b", [D, 512], BF16)
        woT_b = cx.dram("woT_b", [D, D], BF16)
        w8T_b = cx.dram("w8T_b", [D, D], BF16)
        ogbuf = cx.dram("ogbuf", [512, S], BF16)
        og_ag = cx.dram("og_ag", [NCORES * 512, S], BF16)
        og_mine = cx.dram("og_mine", [D, NKV], BF16)
        x2n_mine = cx.dram("x2n_mine", [D, TOK], BF16)
        x2n_ag = cx.dram("x2n_ag", [NCORES * D, TOK], BF16)
        x2nf_mine = cx.dram("x2nf_mine", [D, S], BF16)
        ogc = cx.dram("ogc", [512, S], BF16)
        ogc_ag = cx.dram("ogc_ag", [NCORES * 512, S], BF16)
        og1_mine = cx.dram("og1_mine", [D, TOK], BF16)

        alloc_common(cx)
        n1w_s = cx.sbuf("n1w_s", [128, 8], F32)
        n2w_s = cx.sbuf("n2w_s", [128, 8], F32)
        lmask_g = cx.sbuf("lmask_g", [128, 12 * 128], BF16)
        wrec_s = cx.sbuf("wrec_s", [1, 4], F32)
        wsc_s = cx.sbuf("wscD", [1, 4], F32)
        fb_s = cx.sbuf("fbD", [128, 8], F32)
        sel_s = cx.sbuf("sel_s", [128, 2], F32)
        idxA_s = cx.sbuf("idxA_s", [128, 8], mybir.dt.uint32)
        idxX_s = cx.sbuf("idxX_s", [128, 16], mybir.dt.uint32)

        emit_load_consts(cx)

        def _q0(qe):
            for dst, src in [(n1w_s, n1w), (n2w_s, n2w), (wsc_s, wscale),
                             (fb_s, fbias), (sel_s, selOwn), (idxA_s, idxA), (idxX_s, idxX)]:
                cx.inc(qe.dma_start(out=dst[:], in_=src.ap()), "q", 16)

        cx.Q(_q0)
        cx.barrier()

        with contextlib.ExitStack() as ph:
            wb = ph.enter_context(nc.sbuf_tensor("f0wb", [128, FFN_I], F32))
            wstage = ph.enter_context(nc.sbuf_tensor("f0ws", [128, FFN_I], BF16))
            acc = ph.enter_context(nc.sbuf_tensor("f0acc", [128, 4], F32))
            red = ph.enter_context(nc.sbuf_tensor("f0red", [128, 4], F32))
            rowo = ph.enter_context(nc.sbuf_tensor("f0row", [1, 4], F32))
            s128 = ph.enter_context(nc.sbuf_tensor("f0s128", [128, 4], F32))
            w8s = ph.enter_context(nc.sbuf_tensor("f0w8", [128, D], I32))

            def _g0(ge):
                cx.inc(ge.memset(acc[:], 0.0), "g")

            cx.G(_g0)
            g0 = cx.tok("g")
            jobs = [(wgT, 1, FFN_I, 0), (wuT, 1, FFN_I, 1), (wdT, 4, D, 2)]
            for (src, nch, ncol, col) in jobs:
                for kc in range(nch):
                    def _q(qe, src=src, kc=kc, ncol=ncol):
                        cx.wait(qe, cx.tok("v"))
                        cx.inc(qe.dma_start(out=wb[:, 0:ncol],
                                            in_=src.ap()[128 * kc:128 * kc + 128, :]), "q", 16)

                    cx.Q(_q)
                    qt = cx.tok("q")

                    def _v(ve, col=col, ncol=ncol, qt=qt):
                        cx.wait(ve, qt, g0)
                        ve.reduce_sum(out=red[:, col:col + 1], in_=wb[:, 0:ncol],
                                      axis=mybir.AxisListType.X, apply_absolute_value=True)
                        ve.drain()
                        ve.tensor_tensor(out=acc[:, col:col + 1], in0=acc[:, col:col + 1],
                                         in1=red[:, col:col + 1], op=ALU.add)
                        ve.drain()
                        cx.inc(ve.tensor_copy(out=red[0:1, 0:1], in_=red[0:1, 0:1]), "v")

                    cx.V(_v)
            vt = cx.tok("v")

            def _t(te):
                cx.wait(te, vt)
                cx.inc(te.matmul(out=cx.ps[4][0:1, 0:4], lhsT=cx.c["onesF"][:, 0:1],
                                 rhs=acc[:], start=True, stop=True), "t")

            cx.T(_t)
            tt = cx.tok("t")

            def _v2(ve):
                cx.wait(ve, tt)
                cx.inc(ve.tensor_copy(out=rowo[:], in_=cx.ps[4][0:1, 0:4]), "v")

            cx.V(_v2)
            vt2 = cx.tok("v")

            def _gar(ge):
                cx.wait(ge, vt2)
                cx.inc(ge.dma_start(out=ar_in[:, :], in_=rowo[:]), "g", 16)

            cx.G(_gar)
            cc1 = emit_cc(cx, "AllReduce", ALU.add, [ar_in.ap().opt()], [ar_out.ap().opt()],
                          after=(cx.tok("g"),))

            def _gld(ge, cc1=cc1):
                cx.wait(ge, cc1)
                cx.inc(ge.dma_start(out=rowo[:], in_=ar_out[:, :]), "g", 16)

            cx.G(_gld)
            gld = cx.tok("g")

            def _v3(ve):
                cx.wait(ve, gld)
                ve.tensor_scalar(out=wrec_s[:], in0=rowo[:], scalar1=1.0 / (D * FFN_I),
                                 scalar2=1e-5, op0=ALU.mult, op1=ALU.max)
                ve.drain()
                ve.reciprocal(out=rowo[:], in_=wrec_s[:])
                ve.drain()
                cx.inc(ve.tensor_copy(out=rowo[0:1, 3:4], in_=rowo[0:1, 3:4]), "v")

            cx.V(_v3)
            vt3 = cx.tok("v")
            for col in range(3):
                bt = emit_bcast(cx, rowo[0:1, col:col + 1], 5, 128, 1, after=(vt3, cx.tok("v")))

                def _vb(ve, col=col, bt=bt):
                    cx.wait(ve, bt)
                    cx.inc(ve.tensor_copy(out=s128[:, col:col + 1], in_=cx.ps[5][0:128, 0:1]), "v")

                cx.V(_vb)

            jobs2 = [(wgT, agw_in["g"], 1, FFN_I, 0), (wuT, agw_in["u"], 1, FFN_I, 1),
                     (wdT, agw_in["d"], 4, D, 2)]
            for (src, dst, nch, ncol, col) in jobs2:
                for kc in range(nch):
                    def _q2(qe, src=src, kc=kc, ncol=ncol):
                        cx.wait(qe, cx.tok("v"), cx.tok("g"))
                        cx.inc(qe.dma_start(out=wb[:, 0:ncol],
                                            in_=src.ap()[128 * kc:128 * kc + 128, :]), "q", 16)

                    cx.Q(_q2)
                    qt2 = cx.tok("q")
                    for sl in range(ncol // 512):
                        def _v4(ve, sl=sl, col=col, qt2=qt2):
                            cx.wait(ve, qt2)
                            cx.inc(ve.tensor_scalar(out=cx.scr["D"][:], in0=wb[:, 512 * sl:512 * sl + 512],
                                                    scalar1=s128[:, col:col + 1], scalar2=None,
                                                    op0=ALU.mult), "v")

                        cx.V(_v4)
                        emit_round_clip(cx, cx.scr["D"][:], wstage[:, 512 * sl:512 * sl + 512],
                                        -1, 1, after=(cx.tok("v"),))
                    vt4 = cx.tok("v")

                    def _g2(ge, dst=dst, kc=kc, ncol=ncol, vt4=vt4):
                        cx.wait(ge, vt4)
                        cx.inc(ge.dma_start(out=dst.ap()[128 * kc:128 * kc + 128, :],
                                            in_=wstage[:, 0:ncol]), "g", 16)

                    cx.G(_g2)
            gq = cx.tok("g")
            emit_cc(cx, "AllGather", ALU.bypass, [agw_in["g"].ap().opt()], [wgq.ap().opt()], after=(gq,))
            emit_cc(cx, "AllGather", ALU.bypass, [agw_in["u"].ap().opt()], [wuq.ap().opt()], after=(cx.tok("c"),))
            ccW = emit_cc(cx, "AllGather", ALU.bypass, [agw_in["d"].ap().opt()], [wdq.ap().opt()], after=(cx.tok("c"),))

            casts = [(gwo_f, gwo_b, D, D), (lwq_f, lwq_b, D, D), (lwk_f, lwk_b, D, LAT),
                     (lwv_f, lwv_b, D, LAT), (lwo_f, lwo_b, D, D), (wqC_f, wqC_b, D, 512),
                     (wkC_f, wkC_b, D, 512), (wvC_f, wvC_b, D, 512), (woT_f, woT_b, D, D),
                     (wqA, wqA_b, D, 512)]
            for (srcd, dstd, rows, cols) in casts:
                for kc in range(rows // 128):
                    def _q3(qe, srcd=srcd, kc=kc, cols=cols):
                        cx.wait(qe, cx.tok("v"), cx.tok("g"))
                        cx.inc(qe.dma_start(out=wb[:, 0:cols],
                                            in_=srcd.ap()[128 * kc:128 * kc + 128, :]), "q", 16)

                    cx.Q(_q3)
                    qt3 = cx.tok("q")

                    def _v5(ve, qt3=qt3, cols=cols):
                        cx.wait(ve, qt3)
                        cx.inc(ve.tensor_copy(out=wstage[:, 0:cols], in_=wb[:, 0:cols]), "v")

                    cx.V(_v5)
                    vt5 = cx.tok("v")

                    def _g3(ge, dstd=dstd, kc=kc, cols=cols, vt5=vt5):
                        cx.wait(ge, vt5)
                        cx.inc(ge.dma_start(out=dstd.ap()[128 * kc:128 * kc + 128, :],
                                            in_=wstage[:, 0:cols]), "g", 16)

                    cx.G(_g3)
            for kc in range(8):
                def _q4(qe, kc=kc):
                    cx.wait(qe, cx.tok("v"), cx.tok("g"))
                    cx.inc(qe.dma_start(out=w8s[:], in_=w8T_i.ap()[128 * kc:128 * kc + 128, :]), "q", 16)

                cx.Q(_q4)
                qt4 = cx.tok("q")

                def _v6(ve, qt4=qt4):
                    cx.wait(ve, qt4)
                    cx.inc(ve.tensor_copy(out=wstage[:, 0:D], in_=w8s[:]), "v")

                cx.V(_v6)
                vt6 = cx.tok("v")

                def _g4(ge, kc=kc, vt6=vt6):
                    cx.wait(ge, vt6)
                    cx.inc(ge.dma_start(out=w8T_b.ap()[128 * kc:128 * kc + 128, :],
                                        in_=wstage[:, 0:D]), "g", 16)

                cx.G(_g4)
        cx.barrier()

        with contextlib.ExitStack() as f1ph:
            def f1sb(name, shape, dt=F32):
                t = f1ph.enter_context(nc.sbuf_tensor("f1" + name, shape, dt))
                cx.sb["f1" + name] = t
                return t

            gmask_s = f1sb("gmask_s", [128, 4 * 512], BF16)

            def _qgm(qe):
                cx.inc(qe.dma_start(out=gmask_s[:], in_=gmask.ap()), "q", 16)

            cx.Q(_qgm)
            x0n = [f1sb("x0n%d" % i, [128, S], BF16) for i in range(8)]
            kk = [f1sb("kk%d" % a, [64, S], BF16) for a in range(2)]
            vtm = [f1sb("vtm%d" % a, [128, 65 * 32], BF16) for a in range(2)]
            qcur = f1sb("qcur", [64, S], BF16)
            ocur = f1sb("ocur", [64, S], BF16)
            vf_scr = f1sb("vf_scr", [64, 512], BF16)
            cosG = f1sb("cosG", [64, S], F32)
            sinG = f1sb("sinG", [64, S], F32)
            f1wt = f1sb("f1wt", [128, 512], BF16)


            with contextlib.ExitStack() as ph:
                wkvd_bf = ph.enter_context(nc.sbuf_tensor("wkvd_bf", [128, 8 * 256], BF16))
                wku_bf = ph.enter_context(nc.sbuf_tensor("wku_bf", [128, 2 * 128], BF16))
                wvu_bf = ph.enter_context(nc.sbuf_tensor("wvu_bf", [128, 2 * 128], BF16))
                wf32 = ph.enter_context(nc.sbuf_tensor("wf32", [128, 512], F32))
                xf32 = [ph.enter_context(nc.sbuf_tensor("xf32_%d" % i, [128, 256], F32)) for i in range(8)]
                lat_bf = [ph.enter_context(nc.sbuf_tensor("lat%d" % a, [128, S], BF16)) for a in range(2)]

                emit_rope_tables(cx, BASE_G, posAll, S, cosG, sinG)

                def cast_w(dst, src_dram, rows, cols):
                    for kc in range(rows // 128):
                        def _q(qe, kc=kc):
                            cx.wait(qe, cx.tok("v"))
                            cx.inc(qe.dma_start(out=wf32[:, 0:cols],
                                                in_=src_dram.ap()[128 * kc:128 * kc + 128, :]), "q", 16)

                        cx.Q(_q)
                        qt = cx.tok("q")

                        def _v(ve, kc=kc, qt=qt):
                            cx.wait(ve, qt)
                            cx.inc(ve.tensor_copy(out=dst[:, cols * kc:cols * kc + cols],
                                                  in_=wf32[:, 0:cols]), "v")

                        cx.V(_v)

                cast_w(wkvd_bf, wkvd, D, 256)
                cast_w(wku_bf, wkuA, LAT, 128)
                cast_w(wvu_bf, wvuA, LAT, 128)

                for nb in range(16):
                    def _q(qe, nb=nb):
                        cx.wait(qe, cx.tok("t"), cx.tok("v"))
                        for ci in range(8):
                            cx.inc(qe.dma_start(out=xf32[ci][:],
                                                in_=x0f.ap()[128 * ci:128 * ci + 128,
                                                             256 * nb:256 * nb + 256]), "q", 16)

                    cx.Q(_q)
                    qt = cx.tok("q")
                    st1 = emit_stats(cx, [xf32[ci][:] for ci in range(8)], 256, 1e-5, True,
                                     cx.scr["r0"][0:1, 0:256], cx.scr["r1"][0:1, 0:256],
                                     cx.scr["r2"][0:1, 0:256], after=(qt,))
                    b1 = emit_bcast(cx, cx.scr["r2"][0:1, 0:256], 0, 128, 256, after=(st1,))
                    b2 = emit_bcast(cx, cx.scr["r1"][0:1, 0:256], 1, 128, 256, after=(st1,))

                    def _v(ve, b2=b2):
                        cx.wait(ve, b2)
                        for ci in range(8):
                            ve.tensor_tensor(out=xf32[ci][:], in0=xf32[ci][:],
                                             in1=cx.ps[1][:, 0:256], op=ALU.mult)
                            ve.drain()
                            ve.tensor_tensor(out=xf32[ci][:], in0=xf32[ci][:],
                                             in1=cx.ps[0][:, 0:256], op=ALU.subtract)
                            ve.drain()
                        cx.inc(ve.tensor_copy(out=scr_mark(cx), in_=scr_mark(cx)), "v")

                    cx.V(_v)
                    st2 = emit_stats(cx, [xf32[ci][:] for ci in range(8)], 256, 1e-6, False,
                                     None, cx.scr["r1"][0:1, 0:256], None, after=(cx.tok("v"),))
                    b3 = emit_bcast(cx, cx.scr["r1"][0:1, 0:256], 0, 128, 256, after=(st2,))

                    def _v2(ve, nb=nb, b3=b3):
                        cx.wait(ve, b3)
                        for ci in range(8):
                            ve.tensor_tensor(out=xf32[ci][:], in0=xf32[ci][:],
                                             in1=cx.ps[0][:, 0:256], op=ALU.mult)
                            ve.drain()
                            ve.tensor_scalar(out=x0n[ci][:, 256 * nb:256 * nb + 256], in0=xf32[ci][:],
                                             scalar1=n1w_s[:, ci:ci + 1], scalar2=None, op0=ALU.mult)
                            ve.drain()
                        cx.inc(ve.tensor_copy(out=scr_mark(cx), in_=scr_mark(cx)), "v")

                    cx.V(_v2)
                cx.barrier()

                for a in range(2):
                    for nb in range(8):
                        def _t(te, a=a, nb=nb):
                            cx.wait(te, cx.tok("v"))
                            mm = None
                            for k in range(8):
                                mm = te.matmul(out=cx.ps[0][:, 0:512],
                                               lhsT=wkvd_bf[:, 256 * k + 128 * a:256 * k + 128 * a + 128],
                                               rhs=x0n[k][:, 512 * nb:512 * nb + 512],
                                               start=(k == 0), stop=(k == 7))
                            cx.inc(mm, "t")

                        cx.T(_t)
                        t1 = cx.tok("t")

                        def _v(ve, a=a, nb=nb, t1=t1):
                            cx.wait(ve, t1)
                            cx.inc(ve.tensor_copy(out=lat_bf[a][:, 512 * nb:512 * nb + 512],
                                                  in_=cx.ps[0][:, 0:512]), "v")

                        cx.V(_v)

                for a in range(2):
                    kpipe = mk_pipe()
                    for nb in range(8):
                        emit_proj_rope(cx,
                                       [wku_bf[:, 64 * a:64 * a + 64], wku_bf[:, 128 + 64 * a:128 + 64 * a + 64]],
                                       [lat_bf[0][:, 512 * nb:512 * nb + 512],
                                        lat_bf[1][:, 512 * nb:512 * nb + 512]],
                                       2, 512 * nb, 512, cosG, sinG,
                                       kk[a][:, 512 * nb:512 * nb + 512], True, None,
                                       pipe=kpipe)
                    pipe_flush(cx, kpipe)
                    emit_v_build(cx, [wvu_bf[:, 64 * a:64 * a + 64], wvu_bf[:, 128 + 64 * a:128 + 64 * a + 64]],
                                 lambda k, nb: lat_bf[k][:, 512 * nb:512 * nb + 512],
                                 2, 32, vtm[a], vf_scr)
                cx.barrier()

            for j in range(8):
                qtw = emit_wtiles(cx, wqA_b, 64 * j, 64, 8, f1wt, after=(cx.tok("t"), cx.tok("v")))

                def _wj(te, qtw=qtw):
                    cx.wait(te, qtw)

                cx.T(_wj)
                qpipe = mk_pipe()
                for nb in range(8):
                    emit_proj_rope(cx, [f1wt[:, 64 * k:64 * k + 64] for k in range(8)],
                                   [x0n[k][:, 512 * nb:512 * nb + 512] for k in range(8)],
                                   8, 512 * nb, 512, cosG, sinG,
                                   qcur[:, 512 * nb:512 * nb + 512], True, None,
                                   pipe=qpipe)
                pipe_flush(cx, qpipe)
                a = j // 4
                vend = emit_attn_job(cx, qcur, kk[a], vtm[a], ocur, S, None,
                                     cx.c["biasC0"], gmask_s)

                def _g(ge, j=j, vend=vend):
                    cx.wait(ge, vend)
                    cx.inc(ge.dma_start(out=ogbuf.ap()[64 * j:64 * j + 64, :], in_=ocur[:]), "g", 16)

                cx.G(_g)
                gt = cx.tok("g")

                def _v(ve, gt=gt):
                    cx.wait(ve, gt)

                cx.V(_v)


        cx.barrier()
        ccOG = emit_cc(cx, "AllGather", ALU.bypass, [ogbuf.ap().opt()], [og_ag.ap().opt()],
                       after=(cx.tok("g"),))
        with contextlib.ExitStack() as rg:
            stg = rg.enter_context(nc.sbuf_tensor("rg1a", [128, S], BF16))
            stg2 = rg.enter_context(nc.sbuf_tensor("rg1b", [128, NKV], BF16))
            stg3 = rg.enter_context(nc.sbuf_tensor("rg1c", [128, TOK], BF16))
            for ci in range(8):
                def _gi(ge, ci=ci, ccOG=ccOG):
                    cx.wait(ge, ccOG, cx.tok("v"))
                    cx.inc(ge.indirect_dma_start(
                        out=stg[:], out_offset=None, in_=og_ag[:, :],
                        in_offset=bass.IndirectOffsetOnAxis(ap=idxA_s[:, ci:ci + 1], axis=0)), "g", 16)

                cx.G(_gi)
                gi = cx.tok("g")

                def _vs(ve, gi=gi):
                    cx.wait(ve, gi)
                    ve.tensor_copy(out=stg2[:, 0:HALO], in_=stg[:, 1792:2048])
                    ve.drain()
                    ve.tensor_scalar(out=stg2[:, HALO:HALO + TOK], in0=stg[:, 0:TOK],
                                     scalar1=sel_s[:, 0:1], scalar2=None, op0=ALU.mult)
                    ve.drain()
                    ve.tensor_scalar(out=stg3[:], in0=stg[:, TOK:2 * TOK],
                                     scalar1=sel_s[:, 1:2], scalar2=None, op0=ALU.mult)
                    ve.drain()
                    cx.inc(ve.tensor_tensor(out=stg2[:, HALO:HALO + TOK], in0=stg2[:, HALO:HALO + TOK],
                                            in1=stg3[:], op=ALU.add), "v")

                cx.V(_vs)
                vs = cx.tok("v")

                def _go(ge, ci=ci, vs=vs):
                    cx.wait(ge, vs)
                    cx.inc(ge.dma_start(out=og_mine.ap()[128 * ci:128 * ci + 128, :],
                                        in_=stg2[:]), "g", 16)

                cx.G(_go)
        cx.barrier()

        dbg = False
        h1 = [cx.sbuf("h1_%d" % i, [128, NKV], F32) for i in range(8)]

        with contextlib.ExitStack() as ph:
            f2wt = ph.enter_context(nc.sbuf_tensor("f2wt", [128, 8 * 128], BF16))
            xf = [ph.enter_context(nc.sbuf_tensor("xf%d" % i, [128, 512], F32)) for i in range(8)]
            ogs = [ph.enter_context(nc.sbuf_tensor("ogs%d" % i, [128, 512], BF16)) for i in range(8)]

            for (off, nn) in B_BLOCKS:
                def _q2(qe, off=off, nn=nn):
                    cx.wait(qe, cx.tok("t"), cx.tok("v"))
                    for ci in range(8):
                        src = x0halo.ap()[128 * ci:128 * ci + 128, :] if off == 0 else \
                            x0own.ap()[128 * ci:128 * ci + 128, off - HALO:off - HALO + nn]
                        cx.inc(qe.dma_start(out=xf[ci][:, 0:nn], in_=src), "q", 16)
                        cx.inc(qe.dma_start(out=ogs[ci][:, 0:nn],
                                            in_=og_mine.ap()[128 * ci:128 * ci + 128, off:off + nn]), "q", 16)

                cx.Q(_q2)
                qt = cx.tok("q")
                st = emit_stats(cx, [xf[ci][:, 0:nn] for ci in range(8)], nn, 1e-5, True,
                                cx.scr["r0"][0:1, 0:nn], cx.scr["r1"][0:1, 0:nn],
                                cx.scr["r2"][0:1, 0:nn], after=(qt,))
                b1 = emit_bcast(cx, cx.scr["r2"][0:1, 0:nn], 0, 128, nn, after=(st,))
                b2 = emit_bcast(cx, cx.scr["r1"][0:1, 0:nn], 1, 128, nn, after=(st,))

                def _v(ve, off=off, nn=nn, b2=b2):
                    cx.wait(ve, b2)
                    for ci in range(8):
                        ve.tensor_tensor(out=h1[ci][:, off:off + nn], in0=xf[ci][:, 0:nn],
                                         in1=cx.ps[1][:, 0:nn], op=ALU.mult)
                        ve.drain()
                        ve.tensor_tensor(out=h1[ci][:, off:off + nn], in0=h1[ci][:, off:off + nn],
                                         in1=cx.ps[0][:, 0:nn], op=ALU.subtract)
                        ve.drain()
                    cx.inc(ve.tensor_copy(out=scr_mark(cx), in_=scr_mark(cx)), "v")

                cx.V(_v)
                for m in range(8):
                    qtw = emit_wtiles(cx, gwo_b, 128 * m, 128, 8, f2wt,
                                      after=(cx.tok("t"),))

                    def _t(te, off=off, nn=nn, m=m, qt=qt, qtw=qtw):
                        cx.wait(te, cx.tok("v"), qt, qtw)
                        mm = None
                        for k in range(8):
                            mm = te.matmul(out=cx.ps[0][:, 0:nn],
                                           lhsT=f2wt[:, 128 * k:128 * k + 128],
                                           rhs=ogs[k][:, 0:nn],
                                           start=(k == 0), stop=(k == 7))
                        cx.inc(mm, "t")

                    cx.T(_t)
                    t1 = cx.tok("t")

                    def _v(ve, off=off, nn=nn, m=m, t1=t1):
                        cx.wait(ve, t1)
                        cx.inc(ve.tensor_tensor(out=h1[m][:, off:off + nn],
                                                in0=h1[m][:, off:off + nn],
                                                in1=cx.ps[0][:, 0:nn], op=ALU.add), "v")

                    cx.V(_v)
            cx.barrier()
            dump(dbg_h1, h1, NKV) if dbg else None

        po = contextlib.ExitStack()
        o_l = [po.enter_context(nc.sbuf_tensor("o_l%d" % i, [128, TOK], BF16)) for i in range(8)]
        with contextlib.ExitStack() as ph:
            l_in = [ph.enter_context(nc.sbuf_tensor("lin%d" % i, [128, NKV], BF16)) for i in range(8)]
            cosL = ph.enter_context(nc.sbuf_tensor("cosL", [64, NKV], F32))
            sinL = ph.enter_context(nc.sbuf_tensor("sinL", [64, NKV], F32))
            kk_l = ph.enter_context(nc.sbuf_tensor("kkl", [64, NKV], BF16))
            vtm_l = ph.enter_context(nc.sbuf_tensor("vtml", [128, 65 * 18], BF16))
            vstage = ph.enter_context(nc.sbuf_tensor("vstage", [64, 512], BF16))
            wt = ph.enter_context(nc.sbuf_tensor("wt", [128, 512], BF16))
            qcur = ph.enter_context(nc.sbuf_tensor("qcurB", [64, TOK], BF16))

            emit_rope_tables(cx, BASE_L, posKV, NKV, cosL, sinL)

            for (off, nn) in B_BLOCKS:
                st = emit_stats(cx, [h1[ci][:, off:off + nn] for ci in range(8)], nn, 1e-6, False,
                                None, cx.scr["r1"][0:1, 0:nn], None, after=(cx.tok("v"),))
                b3 = emit_bcast(cx, cx.scr["r1"][0:1, 0:nn], 0, 128, nn, after=(st,))

                def _v(ve, off=off, nn=nn, b3=b3):
                    cx.wait(ve, b3)
                    for ci in range(8):
                        ve.tensor_tensor(out=cx.scr["D"][:, 0:nn], in0=h1[ci][:, off:off + nn],
                                         in1=cx.ps[0][:, 0:nn], op=ALU.mult)
                        ve.drain()
                        ve.tensor_scalar(out=l_in[ci][:, off:off + nn], in0=cx.scr["D"][:, 0:nn],
                                         scalar1=n2w_s[:, ci:ci + 1], scalar2=None, op0=ALU.mult)
                        ve.drain()
                    cx.inc(ve.tensor_copy(out=scr_mark(cx), in_=scr_mark(cx)), "v")

                cx.V(_v)
            cx.barrier()
            dump(dbg_lin, l_in, NKV) if dbg else None

            for a in range(4):
                def _g(ge, a=a):
                    cx.wait(ge, cx.tok("v"), cx.tok("t"))
                    cx.inc(ge.memset(vtm_l[:], 1.0), "g")

                cx.G(_g)
                g0 = cx.tok("g")
                qtw = emit_wtiles(cx, lwk_b, 64 * a, 64, 8, wt, after=(cx.tok("t"), cx.tok("v")))
                lkpipe = mk_pipe()
                for (off, nn) in B_BLOCKS:
                    def _w(te, qtw=qtw):
                        cx.wait(te, qtw)

                    cx.T(_w)
                    emit_proj_rope(cx, [wt[:, 64 * k:64 * k + 64] for k in range(8)],
                                   [l_in[ci][:, off:off + nn] for ci in range(8)],
                                   8, off, nn, cosL, sinL,
                                   kk_l[:, off:off + nn], True, None, pipe=lkpipe)
                pipe_flush(cx, lkpipe)
                qtw = emit_wtiles(cx, lwv_b, 64 * a, 64, 8, wt, after=(cx.tok("t"), cx.tok("v")))
                for (off, nn) in B_BLOCKS:
                    def _t(te, off=off, nn=nn, qtw=qtw):
                        cx.wait(te, qtw, cx.tok("v"))
                        mm = None
                        for k in range(8):
                            mm = te.matmul(out=cx.ps[0][0:64, 0:nn], lhsT=wt[:, 64 * k:64 * k + 64],
                                           rhs=l_in[k][:, off:off + nn],
                                           start=(k == 0), stop=(k == 7))
                        cx.inc(mm, "t")

                    cx.T(_t)
                    t1 = cx.tok("t")

                    def _v(ve, nn=nn, t1=t1):
                        cx.wait(ve, t1)
                        cx.inc(ve.tensor_copy(out=vstage[:, 0:nn], in_=cx.ps[0][0:64, 0:nn]), "v")

                    cx.V(_v)
                    v1 = cx.tok("v")
                    for u in range(nn // 128):
                        def _t2(te, u=u, v1=v1):
                            cx.wait(te, v1, cx.tok("v"))
                            cx.inc(te.transpose(out=cx.pst[0:128, 0:64],
                                                in_=vstage[:, 128 * u:128 * u + 128],
                                                identity=cx.c["identB"][0:64, 0:64]), "t")

                        cx.T(_t2)
                        t2 = cx.tok("t")
                        kt = off // 128 + u

                        def _v2(ve, kt=kt, t2=t2):
                            cx.wait(ve, t2, g0)
                            cx.inc(ve.tensor_copy(out=vtm_l[:, 65 * kt:65 * kt + 64],
                                                  in_=cx.pst[0:128, 0:64]), "v")

                        cx.V(_v2)

                if dbg and a == 0:
                    def _gd(ge):
                        cx.wait(ge, cx.tok("v"), cx.tok("s"))
                        cx.inc(ge.dma_start(out=dbg_cos.ap(), in_=cosL[:]), "g", 16)
                        cx.inc(ge.dma_start(out=dbg_sin.ap(), in_=sinL[:]), "g", 16)
                        cx.inc(ge.dma_start(out=dbg_k.ap(), in_=kk_l[:]), "g", 16)

                    cx.G(_gd)
                    cx.barrier()
                for j in range(4 * a, 4 * a + 4):
                    qtw = emit_wtiles(cx, lwq_b, 64 * j, 64, 8, wt, after=(cx.tok("t"), cx.tok("v")))

                    def _w(te, qtw=qtw):
                        cx.wait(te, qtw)

                    cx.T(_w)
                    lqpipe = mk_pipe()
                    for tb in range(4):
                        off = HALO + 512 * tb
                        emit_proj_rope(cx, [wt[:, 64 * k:64 * k + 64] for k in range(8)],
                                       [l_in[ci][:, off:off + 512] for ci in range(8)],
                                       8, off, 512, cosL, sinL,
                                       qcur[:, 512 * tb:512 * tb + 512], True, None,
                                       pipe=lqpipe)
                    pipe_flush(cx, lqpipe)

                    if dbg and j == 0:
                        def _gq(ge):
                            cx.wait(ge, cx.tok("v"), cx.tok("s"))
                            cx.inc(ge.dma_start(out=dbg_q.ap(), in_=qcur[:]), "g", 16)

                        cx.G(_gq)
                        cx.barrier()

                    def o_dst(g, j=j):
                        return o_l[j // 2][64 * (j % 2):64 * (j % 2) + 64, 512 * g:512 * g + 512]

                    emit_local_attn(cx, j, qcur, kk_l, vtm_l, o_dst, lmask, lmask_g)
            cx.barrier()
            dump(dbg_ol, o_l, TOK) if dbg else None

        with contextlib.ExitStack() as ph:
            wt = ph.enter_context(nc.sbuf_tensor("wtB3", [128, 8 * 128], BF16))
            xf = [ph.enter_context(nc.sbuf_tensor("xfB3_%d" % i, [128, 512], F32)) for i in range(8)]
            for tb in range(4):
                for m in range(8):
                    qtw = emit_wtiles(cx, lwo_b, 128 * m, 128, 8, wt, after=(cx.tok("t"),))

                    def _t(te, tb=tb, m=m, qtw=qtw):
                        cx.wait(te, qtw, cx.tok("v"))
                        mm = None
                        for k in range(8):
                            mm = te.matmul(out=cx.ps[0][:, 0:512], lhsT=wt[:, 128 * k:128 * k + 128],
                                           rhs=o_l[k][:, 512 * tb:512 * tb + 512],
                                           start=(k == 0), stop=(k == 7))
                        cx.inc(mm, "t")

                    cx.T(_t)
                    t1 = cx.tok("t")

                    def _v(ve, tb=tb, m=m, t1=t1):
                        cx.wait(ve, t1)
                        cx.inc(ve.tensor_tensor(out=h1[m][:, HALO + 512 * tb:HALO + 512 * tb + 512],
                                                in0=h1[m][:, HALO + 512 * tb:HALO + 512 * tb + 512],
                                                in1=cx.ps[0][:, 0:512], op=ALU.add), "v")

                    cx.V(_v)
            cx.barrier()
            for tb in range(4):
                off = HALO + 512 * tb

                def _q(qe, tb=tb):
                    cx.wait(qe, cx.tok("v"))
                    for ci in range(8):
                        cx.inc(qe.dma_start(out=xf[ci][:],
                                            in_=x0own.ap()[128 * ci:128 * ci + 128,
                                                           512 * tb:512 * tb + 512]), "q", 16)

                cx.Q(_q)
                qt = cx.tok("q")
                st = emit_stats(cx, [h1[ci][:, off:off + 512] for ci in range(8)], 512, 1e-5, True,
                                cx.scr["r0"][0:1, 0:512], cx.scr["r1"][0:1, 0:512],
                                cx.scr["r2"][0:1, 0:512], after=(cx.tok("v"),))
                b1 = emit_bcast(cx, cx.scr["r2"][0:1, 0:512], 0, 128, 512, after=(st,))
                b2 = emit_bcast(cx, cx.scr["r1"][0:1, 0:512], 1, 128, 512, after=(st,))

                def _v(ve, off=off, qt=qt, b2=b2):
                    cx.wait(ve, b2, qt)
                    for ci in range(8):
                        ve.tensor_tensor(out=h1[ci][:, off:off + 512], in0=h1[ci][:, off:off + 512],
                                         in1=cx.ps[1][:, 0:512], op=ALU.mult)
                        ve.drain()
                        ve.tensor_tensor(out=h1[ci][:, off:off + 512], in0=h1[ci][:, off:off + 512],
                                         in1=cx.ps[0][:, 0:512], op=ALU.subtract)
                        ve.drain()
                        ve.tensor_tensor(out=h1[ci][:, off:off + 512], in0=h1[ci][:, off:off + 512],
                                         in1=xf[ci][:], op=ALU.add)
                        ve.drain()
                    cx.inc(ve.tensor_copy(out=scr_mark(cx), in_=scr_mark(cx)), "v")

                cx.V(_v)
            cx.barrier()
            if dbg:
                def _gx(ge):
                    cx.wait(ge, cx.tok("v"))
                    for ci in range(8):
                        cx.inc(ge.dma_start(out=dbg_x1.ap()[128 * ci:128 * ci + 128, :],
                                            in_=h1[ci][:, HALO:NKV]), "g", 16)

                cx.G(_gx)
                cx.barrier()
        po.close()

        NT = 256
        with contextlib.ExitStack() as ph:
            xnT = ph.enter_context(nc.sbuf_tensor("xnT", [128, 8 * NT], F32))
            xqT = ph.enter_context(nc.sbuf_tensor("xqT", [128, 8 * NT], BF16))
            fsb = [ph.enter_context(nc.sbuf_tensor("fsb%d" % i, [128, NT], F32)) for i in range(8)]
            hf = ph.enter_context(nc.sbuf_tensor("hf", [128, 32 * NT], F32))
            hq = ph.enter_context(nc.sbuf_tensor("hq", [128, 32 * NT], BF16))
            wg0 = ph.enter_context(nc.sbuf_tensor("wg0", [128, 1024], BF16))
            wg1 = ph.enter_context(nc.sbuf_tensor("wg1", [128, 1024], BF16))
            wu0 = ph.enter_context(nc.sbuf_tensor("wu0", [128, 1024], BF16))
            wu1 = ph.enter_context(nc.sbuf_tensor("wu1", [128, 1024], BF16))
            wd0 = ph.enter_context(nc.sbuf_tensor("wd0", [128, 4096], BF16))
            wd1 = ph.enter_context(nc.sbuf_tensor("wd1", [128, 4096], BF16))
            amx = ph.enter_context(nc.sbuf_tensor("amx", [128, 8], F32))
            x2stg = ph.enter_context(nc.sbuf_tensor("x2stg", [128, NT], BF16))
            rowx = ph.enter_context(nc.sbuf_tensor("rowx", [1, NT], F32))
            rowy = ph.enter_context(nc.sbuf_tensor("rowy", [1, NT], F32))
            rowz = ph.enter_context(nc.sbuf_tensor("rowz", [1, NT], F32))

            def xnc(ci):
                return xnT[:, NT * ci:NT * ci + NT]

            def hs(m):
                return hf[:, NT * m:NT * m + NT]

            def hqs(m):
                return hq[:, NT * m:NT * m + NT]

            for tc in range(2048 // NT):
                off = HALO + NT * tc
                st = emit_stats(cx, [h1[ci][:, off:off + NT] for ci in range(8)], NT, 1e-5, True,
                                cx.scr["r0"][0:1, 0:NT], cx.scr["r1"][0:1, 0:NT],
                                cx.scr["r2"][0:1, 0:NT], after=(cx.tok("v"), cx.tok("g")))
                b1 = emit_bcast(cx, cx.scr["r2"][0:1, 0:NT], 2, 128, NT, after=(st,))
                b2 = emit_bcast(cx, cx.scr["r1"][0:1, 0:NT], 3, 128, NT, after=(st,))

                def _v1(ve, off=off, b2=b2):
                    cx.wait(ve, b2)
                    for ci in range(8):
                        ve.tensor_tensor(out=xnc(ci), in0=h1[ci][:, off:off + NT],
                                         in1=cx.ps[3][:, 0:NT], op=ALU.mult)
                    ve.drain()
                    for ci in range(8):
                        ve.tensor_tensor(out=xnc(ci), in0=xnc(ci),
                                         in1=cx.ps[2][:, 0:NT], op=ALU.subtract)
                    cx.inc(ve.tensor_copy(out=scr_mark(cx), in_=scr_mark(cx)), "v")

                cx.V(_v1)

                emit_absmax(cx, [xnc(ci) for ci in range(8)], amx, rowx, rowy, NT)

                sb = emit_bcast(cx, rowy[0:1, 0:NT], 2, 128, NT, after=(cx.tok("v"),))

                def _vE3(ve, sb=sb):
                    cx.wait(ve, sb)
                    cx.inc(ve.tensor_copy(out=cx.scr["E"][:, 0:NT], in_=cx.ps[2][:, 0:NT]), "v")

                cx.V(_vE3)
                e3 = cx.tok("v")
                t3 = cx.tok("t")

                def _g3(ge, e3=e3, t3=t3):
                    cx.wait(ge, e3, t3, cx.tok("g"))
                    for p in range(4):
                        for j in range(2):
                            dst = (cx.scr["D"][:, 256 * j:256 * j + 256] if p == 0 else
                                   xnT[:, 512 * (p - 1) + 256 * j:512 * (p - 1) + 256 * j + 256])
                            cx.inc(ge.tensor_tensor(
                                out=dst, in0=xnT[:, 512 * p + 256 * j:512 * p + 256 * j + 256],
                                in1=cx.scr["E"][:, 0:NT], op=ALU.mult), "g")

                cx.G(_g3)
                g3b = cx.cnt["g"] - 8
                for p in range(4):
                    s_ap = (cx.scr["D"][:, 0:512] if p == 0 else
                            xnT[:, 512 * (p - 1):512 * p])
                    emit_round_clip(cx, s_ap, xqT[:, 512 * p:512 * p + 512], -128, 127,
                                    n=512, after=(("g", g3b + 2 * (p + 1)),))

                vpre4 = cx.tok("v")

                def _gi4(ge, vpre4=vpre4):
                    cx.wait(ge, vpre4, cx.tok("g"))
                    ge.memset(cx.scr["E"][:, 0:NT], -3.0e38)
                    cx.inc(ge.memset(cx.scr["E"][:, NT:2 * NT], 3.0e38), "g")

                cx.G(_gi4)
                g_init4 = cx.tok("g")
                t_base = cx.cnt["t"]
                s_base = cx.cnt["s"]
                v_base = cx.cnt["v"]
                for m in range(32):
                    ev = m % 2 == 0
                    qg = emit_wtiles(cx, wgq, 128 * m, 128, 8, wg0 if ev else wg1,
                                     after=(("t", t_base + max(0, m - 1)),))
                    qu = emit_wtiles(cx, wuq, 128 * m, 128, 8, wu0 if ev else wu1,
                                     after=(qg,))
                    psG = cx.ps[0] if ev else cx.ps[2]
                    psU = cx.ps[1] if ev else cx.ps[3]
                    scrR = "A" if ev else "C"
                    scrS = "B" if ev else "D"

                    def _t(te, m=m, qu=qu, psG=psG, psU=psU, ev=ev):
                        cx.wait(te, qu, ("s", s_base + max(0, m - 1)),
                                ("v", v_base + max(0, m - 1)))
                        wgt = wg0 if ev else wg1
                        wut = wu0 if ev else wu1
                        for k in range(8):
                            te.matmul(out=psG[:, 0:NT], lhsT=wgt[:, 128 * k:128 * k + 128],
                                      rhs=xqT[:, NT * k:NT * k + NT],
                                      start=(k == 0), stop=(k == 7))
                        mm = None
                        for k in range(8):
                            mm = te.matmul(out=psU[:, 0:NT], lhsT=wut[:, 128 * k:128 * k + 128],
                                           rhs=xqT[:, NT * k:NT * k + NT],
                                           start=(k == 0), stop=(k == 7))
                        cx.inc(mm, "t")

                    cx.T(_t)
                    t1 = cx.tok("t")

                    def _s(se, t1=t1, m=m, psG=psG, scrR=scrR):
                        cx.wait(se, t1, ("v", v_base + max(0, m - 1)))
                        cx.inc(se.activation(out=cx.scr[scrR][:, 0:NT], in_=psG[:, 0:NT],
                                             func=AF.Relu), "s")

                    cx.Sc(_s)
                    s1 = cx.tok("s")

                    def _v3(ve, m=m, s1=s1, psU=psU, scrR=scrR, scrS=scrS):
                        if m == 0:
                            cx.wait(ve, s1, g_init4)
                        else:
                            cx.wait(ve, s1)
                        ve.tensor_tensor(out=cx.scr[scrS][:, 0:NT], in0=cx.scr[scrR][:, 0:NT],
                                         in1=cx.scr[scrR][:, 0:NT], op=ALU.mult)
                        ve.drain()
                        ve.tensor_tensor(out=hs(m), in0=cx.scr[scrS][:, 0:NT],
                                         in1=psU[:, 0:NT], op=ALU.mult)
                        ve.drain()
                        ve.tensor_tensor(out=cx.scr["E"][:, 0:NT], in0=cx.scr["E"][:, 0:NT],
                                         in1=hs(m), op=ALU.max)
                        cx.inc(ve.tensor_tensor(out=cx.scr["E"][:, NT:2 * NT],
                                                in0=cx.scr["E"][:, NT:2 * NT],
                                                in1=hs(m), op=ALU.min), "v")

                    cx.V(_v3)

                vfin4 = cx.tok("v")

                def _vA5(ve, vfin4=vfin4):
                    cx.wait(ve, vfin4)
                    cx.inc(ve.scalar_tensor_tensor(out=cx.scr["D"][:, 0:NT],
                                                   in0=cx.scr["E"][:, NT:2 * NT], scalar=-1.0,
                                                   in1=cx.scr["E"][:, 0:NT],
                                                   op0=ALU.mult, op1=ALU.max), "v")

                cx.V(_vA5)
                emit_absmax_tail(cx, amx, None, None, NT, raw_amax_row=rowz)

                def _v4(ve):
                    cx.wait(ve, cx.tok("v"), cx.tok("s"))
                    ve.tensor_tensor(out=rowy[0:1, 0:NT], in0=rowx[0:1, 0:NT],
                                     in1=rowx[0:1, 0:NT], op=ALU.mult)
                    ve.drain()
                    ve.tensor_tensor(out=rowy[0:1, 0:NT], in0=rowy[0:1, 0:NT],
                                     in1=rowx[0:1, 0:NT], op=ALU.mult)
                    ve.drain()
                    ve.tensor_scalar(out=rowy[0:1, 0:NT], in0=rowy[0:1, 0:NT],
                                     scalar1=wrec_s[0:1, 0:1], scalar2=wrec_s[0:1, 0:1],
                                     op0=ALU.mult, op1=ALU.mult)
                    ve.drain()
                    ve.tensor_scalar(out=rowy[0:1, 0:NT], in0=rowy[0:1, 0:NT],
                                     scalar1=wrec_s[0:1, 1:2], scalar2=None, op0=ALU.mult)
                    ve.drain()
                    ve.tensor_tensor(out=rowz[0:1, 0:NT], in0=rowz[0:1, 0:NT],
                                     in1=rowy[0:1, 0:NT], op=ALU.mult)
                    ve.drain()
                    ve.tensor_scalar(out=rowz[0:1, 0:NT], in0=rowz[0:1, 0:NT],
                                     scalar1=1e-5, scalar2=None, op0=ALU.max)
                    ve.drain()
                    ve.reciprocal(out=rowx[0:1, 0:NT], in_=rowz[0:1, 0:NT])
                    ve.drain()
                    ve.tensor_tensor(out=rowx[0:1, 0:NT], in0=rowx[0:1, 0:NT],
                                     in1=rowy[0:1, 0:NT], op=ALU.mult)
                    ve.drain()
                    ve.tensor_scalar(out=rowx[0:1, 0:NT], in0=rowx[0:1, 0:NT],
                                     scalar1=127.0, scalar2=None, op0=ALU.mult)
                    ve.drain()
                    ve.tensor_scalar(out=rowz[0:1, 0:NT], in0=rowz[0:1, 0:NT],
                                     scalar1=1.0 / 127.0, scalar2=wrec_s[0:1, 2:3],
                                     op0=ALU.mult, op1=ALU.mult)
                    ve.drain()
                    cx.inc(ve.tensor_copy(out=scr_mark(cx), in_=scr_mark(cx)), "v")

                cx.V(_v4)

                fb = emit_bcast(cx, rowx[0:1, 0:NT], 2, 128, NT, after=(cx.tok("v"),))

                def _vE6(ve, fb=fb):
                    cx.wait(ve, fb)
                    cx.inc(ve.tensor_copy(out=cx.scr["E"][:, 0:NT], in_=cx.ps[2][:, 0:NT]), "v")

                cx.V(_vE6)
                e6 = cx.tok("v")
                t6 = cx.tok("t")

                def _g6(ge, e6=e6, t6=t6):
                    cx.wait(ge, e6, t6, cx.tok("g"))
                    for p in range(16):
                        for j in range(2):
                            dst = (cx.scr["D"][:, 256 * j:256 * j + 256] if p == 0 else
                                   hf[:, 512 * (p - 1) + 256 * j:512 * (p - 1) + 256 * j + 256])
                            cx.inc(ge.tensor_tensor(
                                out=dst, in0=hf[:, 512 * p + 256 * j:512 * p + 256 * j + 256],
                                in1=cx.scr["E"][:, 0:NT], op=ALU.mult), "g")

                cx.G(_g6)
                g6b = cx.cnt["g"] - 32
                for p in range(16):
                    s_ap = (cx.scr["D"][:, 0:512] if p == 0 else
                            hf[:, 512 * (p - 1):512 * p])
                    emit_round_clip(cx, s_ap, hq[:, 512 * p:512 * p + 512], -128, 127,
                                    n=512, after=(("g", g6b + 2 * (p + 1)),))

                fdb = emit_bcast(cx, rowz[0:1, 0:NT], 3, 128, NT, after=(cx.tok("v"),))

                def _vf(ve, fdb=fdb, g6e=("g", g6b + 32)):
                    cx.wait(ve, fdb, g6e)
                    cx.inc(ve.tensor_copy(out=cx.scr["E"][:, 0:NT], in_=cx.ps[3][:, 0:NT]), "v")

                cx.V(_vf)
                fdb = cx.tok("v")
                t_base2 = cx.cnt["t"]
                v_base2 = cx.cnt["v"]
                for m2 in range(8):
                    qd = emit_wtiles(cx, wdq, 128 * m2, 128, 32, wd0 if m2 % 2 == 0 else wd1,
                                     after=(("t", t_base2 + max(0, m2 - 1)),))
                    psD = cx.ps[0] if m2 % 2 == 0 else cx.ps[1]

                    def _t2(te, m2=m2, qd=qd, psD=psD):
                        cx.wait(te, qd, ("v", v_base2 + max(0, m2 - 1)))
                        wdt = wd0 if m2 % 2 == 0 else wd1
                        mm = None
                        for k in range(32):
                            mm = te.matmul(out=psD[:, 0:NT], lhsT=wdt[:, 128 * k:128 * k + 128],
                                           rhs=hqs(k), start=(k == 0), stop=(k == 31))
                        cx.inc(mm, "t")

                    cx.T(_t2)
                    t2 = cx.tok("t")

                    def _v6(ve, m2=m2, t2=t2, fdb=fdb, psD=psD):
                        cx.wait(ve, t2, fdb)
                        cx.inc(ve.tensor_tensor(out=fsb[m2][:], in0=psD[:, 0:NT],
                                                in1=cx.scr["E"][:, 0:NT], op=ALU.mult), "v")

                    cx.V(_v6)

                if dbg and tc == 4:
                    def _gd4(ge, tc=tc):
                        cx.wait(ge, cx.tok("v"), cx.tok("s"), cx.tok("t"))
                        for ci in range(8):
                            cx.inc(ge.dma_start(out=dbg_xq.ap()[128 * ci:128 * ci + 128, :],
                                                in_=xq[ci][:]), "g", 16)
                            cx.inc(ge.dma_start(out=dbg_fsb.ap()[128 * ci:128 * ci + 128, :],
                                                in_=fsb[ci][:]), "g", 16)
                        cx.inc(ge.dma_start(out=dbg_hq.ap(), in_=hq[:]), "g", 16)
                        cx.inc(ge.dma_start(out=dbg_hf.ap(), in_=hf[:]), "g", 16)
                        cx.inc(ge.dma_start(out=dbg_rows.ap()[0:1, :], in_=rowx[0:1, 0:NT]), "g", 16)
                        cx.inc(ge.dma_start(out=dbg_rows.ap()[1:2, :], in_=rowy[0:1, 0:NT]), "g", 16)
                        cx.inc(ge.dma_start(out=dbg_rows.ap()[2:3, :], in_=rowz[0:1, 0:NT]), "g", 16)

                    cx.G(_gd4)
                    cx.barrier()

                st2 = emit_stats(cx, [fsb[ci][:] for ci in range(8)], NT, 1e-5, True,
                                 cx.scr["r0"][0:1, 0:NT], cx.scr["r1"][0:1, 0:NT],
                                 cx.scr["r2"][0:1, 0:NT], after=(cx.tok("v"),))
                b1 = emit_bcast(cx, cx.scr["r2"][0:1, 0:NT], 2, 128, NT, after=(st2,))
                b2 = emit_bcast(cx, cx.scr["r1"][0:1, 0:NT], 3, 128, NT, after=(st2,))

                def _v7(ve, off=off, b2=b2):
                    cx.wait(ve, b2)
                    for ci in range(8):
                        ve.tensor_tensor(out=fsb[ci][:], in0=fsb[ci][:],
                                         in1=cx.ps[3][:, 0:NT], op=ALU.mult)
                    ve.drain()
                    for ci in range(8):
                        ve.tensor_tensor(out=fsb[ci][:], in0=fsb[ci][:],
                                         in1=cx.ps[2][:, 0:NT], op=ALU.subtract)
                    ve.drain()
                    for ci in range(8):
                        ve.tensor_tensor(out=h1[ci][:, off:off + NT], in0=h1[ci][:, off:off + NT],
                                         in1=fsb[ci][:], op=ALU.add)
                    cx.inc(ve.tensor_copy(out=scr_mark(cx), in_=scr_mark(cx)), "v")

                cx.V(_v7)
                st3 = emit_stats(cx, [h1[ci][:, off:off + NT] for ci in range(8)], NT, 1e-5, True,
                                 cx.scr["r0"][0:1, 0:NT], cx.scr["r1"][0:1, 0:NT],
                                 cx.scr["r2"][0:1, 0:NT], after=(cx.tok("v"),))
                b1 = emit_bcast(cx, cx.scr["r2"][0:1, 0:NT], 2, 128, NT, after=(st3,))
                b2 = emit_bcast(cx, cx.scr["r1"][0:1, 0:NT], 3, 128, NT, after=(st3,))
                def _v8(ve, off=off, b2=b2):
                    cx.wait(ve, b2, cx.tok("g"))
                    for ci in range(8):
                        ve.tensor_tensor(out=fsb[ci][:], in0=h1[ci][:, off:off + NT],
                                         in1=cx.ps[3][:, 0:NT], op=ALU.mult)
                    ve.drain()
                    for ci in range(8):
                        ve.tensor_tensor(out=xqT[:, NT * ci:NT * ci + NT], in0=fsb[ci][:],
                                         in1=cx.ps[2][:, 0:NT], op=ALU.subtract)
                    cx.inc(ve.tensor_copy(out=scr_mark(cx), in_=scr_mark(cx)), "v")

                cx.V(_v8)
                vt8 = cx.tok("v")

                def _g8(ge, tc=tc, vt8=vt8):
                    cx.wait(ge, vt8)
                    for ci in range(8):
                        cx.inc(ge.dma_start(out=x2n_mine.ap()[128 * ci:128 * ci + 128,
                                                               NT * tc:NT * tc + NT],
                                            in_=xqT[:, NT * ci:NT * ci + NT]), "g", 16)

                cx.G(_g8)
            cx.barrier()


        ccX = emit_cc(cx, "AllGather", ALU.bypass, [x2n_mine.ap().opt()], [x2n_ag.ap().opt()],
                      after=(cx.tok("g"),))
        with contextlib.ExitStack() as rg:
            stgx = rg.enter_context(nc.sbuf_tensor("rg2a", [128, TOK], BF16))
            for ci in range(8):
                for half in range(2):
                    def _gi(ge, ci=ci, half=half, ccX=ccX):
                        cx.wait(ge, ccX, cx.tok("g"))
                        cx.inc(ge.indirect_dma_start(
                            out=stgx[:], out_offset=None, in_=x2n_ag[:, :],
                            in_offset=bass.IndirectOffsetOnAxis(
                                ap=idxX_s[:, 2 * ci + half:2 * ci + half + 1], axis=0)), "g", 16)
                        cx.wait(ge, cx.tok("g"))
                        cx.inc(ge.dma_start(out=x2nf_mine.ap()[128 * ci:128 * ci + 128,
                                                               TOK * half:TOK * half + TOK],
                                            in_=stgx[:]), "g", 16)

                    cx.G(_gi)
        cx.barrier()

        f3ph = contextlib.ExitStack()

        def f3sb(name, shape, dt=BF16):
            return f3ph.enter_context(nc.sbuf_tensor("f3" + name, shape, dt))

        xs = [f3sb("xs%d" % i, [128, S]) for i in range(8)]
        qcur = f3sb("qcurC", [64, S])
        kcur = f3sb("kcurC", [64, S])
        vtm = f3sb("vtmC", [128, 65 * 32])
        ocur = f3sb("ocurC", [64, S])
        gmask_s = f3sb("gmaskC", [128, 4 * 512])
        vf_scr = f3sb("vfC", [64, 512])
        wt = f3sb("wtC", [128, 512])
        emit_load_consts(cx)

        def _q0(qe):
            cx.inc(qe.dma_start(out=gmask_s[:], in_=gmask.ap()), "q", 16)
            for ci in range(8):
                cx.inc(qe.dma_start(out=xs[ci][:], in_=x2nf_mine.ap()[128 * ci:128 * ci + 128, :]), "q", 16)

        cx.Q(_q0)
        cx.barrier()

        def proj_plain(w_dram, col0, dst):
            qtw = emit_wtiles(cx, w_dram, col0, 64, 8, wt, after=(cx.tok("t"), cx.tok("v")))
            v_base = cx.cnt["v"]
            ent_v = cx.tok("v")
            for nb in range(8):
                psP = cx.ps[0] if nb % 2 == 0 else cx.ps[1]

                def _t(te, nb=nb, qtw=qtw, psP=psP):
                    if nb < 2:
                        cx.wait(te, qtw, ent_v)
                    else:
                        cx.wait(te, qtw, ("v", v_base + nb - 1))
                    mm = None
                    for k in range(8):
                        mm = te.matmul(out=psP[0:64, 0:512], lhsT=wt[:, 64 * k:64 * k + 64],
                                       rhs=xs[k][:, 512 * nb:512 * nb + 512],
                                       start=(k == 0), stop=(k == 7))
                    cx.inc(mm, "t")

                cx.T(_t)
                t1 = cx.tok("t")

                def _v(ve, nb=nb, t1=t1, dst=dst, psP=psP):
                    cx.wait(ve, t1)
                    cx.inc(ve.tensor_copy(out=dst[:, 512 * nb:512 * nb + 512],
                                          in_=psP[0:64, 0:512]), "v")

                cx.V(_v)

        for j in range(8):
            proj_plain(wqC_b, 64 * j, qcur)
            proj_plain(wkC_b, 64 * j, kcur)
            qtw = emit_wtiles(cx, wvC_b, 64 * j, 64, 8, wt, after=(cx.tok("t"), cx.tok("v")))

            def _w(te, qtw=qtw):
                cx.wait(te, qtw)

            cx.T(_w)
            emit_v_build(cx, [wt[:, 64 * k:64 * k + 64] for k in range(8)],
                         lambda k, nb: xs[k][:, 512 * nb:512 * nb + 512],
                         8, 32, vtm, vf_scr)
            vend = emit_attn_job(cx, qcur, kcur, vtm, ocur, S, None,
                                 cx.c["biasC1"], gmask_s)

            def _g(ge, j=j, vend=vend):
                cx.wait(ge, vend)
                cx.inc(ge.dma_start(out=ogc.ap()[64 * j:64 * j + 64, :], in_=ocur[:]), "g", 16)

            cx.G(_g)
            gt = cx.tok("g")

            def _v(ve, gt=gt):
                cx.wait(ve, gt)

            cx.V(_v)


        f3ph.close()
        ccC = emit_cc(cx, "AllGather", ALU.bypass, [ogc.ap().opt()], [ogc_ag.ap().opt()],
                      after=(cx.tok("g"),))
        with contextlib.ExitStack() as rg:
            stc = rg.enter_context(nc.sbuf_tensor("rg3a", [128, S], BF16))
            stc2 = rg.enter_context(nc.sbuf_tensor("rg3b", [128, TOK], BF16))
            stc3 = rg.enter_context(nc.sbuf_tensor("rg3c", [128, TOK], BF16))
            for ci in range(8):
                def _gi(ge, ci=ci, ccC=ccC):
                    cx.wait(ge, ccC, cx.tok("v"))
                    cx.inc(ge.indirect_dma_start(
                        out=stc[:], out_offset=None, in_=ogc_ag[:, :],
                        in_offset=bass.IndirectOffsetOnAxis(ap=idxA_s[:, ci:ci + 1], axis=0)), "g", 16)

                cx.G(_gi)
                gi = cx.tok("g")

                def _vs(ve, gi=gi):
                    cx.wait(ve, gi)
                    ve.tensor_scalar(out=stc2[:], in0=stc[:, 0:TOK],
                                     scalar1=sel_s[:, 0:1], scalar2=None, op0=ALU.mult)
                    ve.drain()
                    ve.tensor_scalar(out=stc3[:], in0=stc[:, TOK:2 * TOK],
                                     scalar1=sel_s[:, 1:2], scalar2=None, op0=ALU.mult)
                    ve.drain()
                    cx.inc(ve.tensor_tensor(out=stc2[:], in0=stc2[:], in1=stc3[:], op=ALU.add), "v")

                cx.V(_vs)
                vs = cx.tok("v")

                def _go(ge, ci=ci, vs=vs):
                    cx.wait(ge, vs)
                    cx.inc(ge.dma_start(out=og1_mine.ap()[128 * ci:128 * ci + 128, :],
                                        in_=stc2[:]), "g", 16)

                cx.G(_go)
        cx.barrier()

        xn = [cx.sbuf("xnD%d" % i, [128, 512], F32) for i in range(8)]
        xq = [cx.sbuf("xqD%d" % i, [128, 512], BF16) for i in range(8)]
        fsb = [cx.sbuf("fsbD%d" % i, [128, 512], F32) for i in range(8)]
        ogt = [cx.sbuf("ogtD%d" % i, [128, 512], BF16) for i in range(8)]
        wt = cx.sbuf("wtD", [128, 8 * 128], BF16)
        amx = cx.sbuf("amxD", [128, 8], F32)
        rowx = cx.sbuf("rowxD", [1, 512], F32)
        rowy = cx.sbuf("rowyD", [1, 512], F32)

        for tc in range(4):
            def _q(qe, tc=tc):
                cx.wait(qe, cx.tok("t"), cx.tok("v"))
                for ci in range(8):
                    cx.inc(qe.dma_start(out=ogt[ci][:],
                                        in_=og1_mine.ap()[128 * ci:128 * ci + 128,
                                                          512 * tc:512 * tc + 512]), "q", 16)

            cx.Q(_q)
            qt = cx.tok("q")
            for m in range(8):
                qtw = emit_wtiles(cx, woT_b, 128 * m, 128, 8, wt, after=(cx.tok("t"),))

                def _t(te, m=m, qtw=qtw, qt=qt):
                    cx.wait(te, qtw, qt, cx.tok("v"))
                    mm = None
                    for k in range(8):
                        mm = te.matmul(out=cx.ps[0][:, 0:512], lhsT=wt[:, 128 * k:128 * k + 128],
                                       rhs=ogt[k][:], start=(k == 0), stop=(k == 7))
                    cx.inc(mm, "t")

                cx.T(_t)
                t1 = cx.tok("t")

                def _v(ve, m=m, t1=t1):
                    cx.wait(ve, t1)
                    cx.inc(ve.tensor_copy(out=fsb[m][:], in_=cx.ps[0][:, 0:512]), "v")

                cx.V(_v)
            st = emit_stats(cx, [fsb[ci][:] for ci in range(8)], 512, 1e-5, True,
                            cx.scr["r0"][0:1, 0:512], cx.scr["r1"][0:1, 0:512],
                            cx.scr["r2"][0:1, 0:512], after=(cx.tok("v"),))
            b1 = emit_bcast(cx, cx.scr["r2"][0:1, 0:512], 2, 128, 512, after=(st,))
            b2 = emit_bcast(cx, cx.scr["r1"][0:1, 0:512], 3, 128, 512, after=(st,))

            def _v2(ve, tc=tc, b2=b2, qt=qt):
                cx.wait(ve, b2, qt)
                for ci in range(8):
                    ve.tensor_tensor(out=fsb[ci][:], in0=fsb[ci][:], in1=cx.ps[3][:, 0:512], op=ALU.mult)
                    ve.drain()
                    ve.tensor_tensor(out=fsb[ci][:], in0=fsb[ci][:], in1=cx.ps[2][:, 0:512], op=ALU.subtract)
                    ve.drain()
                    ve.tensor_tensor(out=h1[ci][:, HALO + 512 * tc:HALO + 512 * tc + 512], in0=fsb[ci][:],
                                     in1=h1[ci][:, HALO + 512 * tc:HALO + 512 * tc + 512], op=ALU.add)
                    ve.drain()
                cx.inc(ve.tensor_copy(out=scr_mark(cx), in_=scr_mark(cx)), "v")

            cx.V(_v2)
        cx.barrier()

        for tc in range(4):
            off = 512 * tc
            st = emit_stats(cx, [h1[ci][:, HALO + off:HALO + off + 512] for ci in range(8)], 512, 1e-5, True,
                            cx.scr["r0"][0:1, 0:512], cx.scr["r1"][0:1, 0:512],
                            cx.scr["r2"][0:1, 0:512], after=(cx.tok("v"), cx.tok("g")))
            b1 = emit_bcast(cx, cx.scr["r2"][0:1, 0:512], 2, 128, 512, after=(st,))
            b2 = emit_bcast(cx, cx.scr["r1"][0:1, 0:512], 3, 128, 512, after=(st,))

            def _v1(ve, off=off, b2=b2):
                cx.wait(ve, b2)
                for ci in range(8):
                    ve.tensor_tensor(out=xn[ci][:], in0=h1[ci][:, HALO + off:HALO + off + 512],
                                     in1=cx.ps[3][:, 0:512], op=ALU.mult)
                    ve.drain()
                    ve.tensor_tensor(out=xn[ci][:], in0=xn[ci][:], in1=cx.ps[2][:, 0:512], op=ALU.subtract)
                    ve.drain()
                cx.inc(ve.tensor_copy(out=scr_mark(cx), in_=scr_mark(cx)), "v")

            cx.V(_v1)
            st2 = emit_stats(cx, [xn[ci][:] for ci in range(8)], 512, 1e-6, False,
                             None, cx.scr["r1"][0:1, 0:512], None, after=(cx.tok("v"),))
            b3 = emit_bcast(cx, cx.scr["r1"][0:1, 0:512], 2, 128, 512, after=(st2,))

            def _v2(ve, b3=b3):
                cx.wait(ve, b3)
                for ci in range(8):
                    ve.tensor_tensor(out=xn[ci][:], in0=xn[ci][:], in1=cx.ps[2][:, 0:512], op=ALU.mult)
                    ve.drain()
                cx.inc(ve.tensor_copy(out=scr_mark(cx), in_=scr_mark(cx)), "v")

            cx.V(_v2)
            emit_absmax(cx, [xn[ci][:] for ci in range(8)], amx, rowx, rowy)
            sb = emit_bcast(cx, rowy[0:1, 0:512], 2, 128, 512, after=(cx.tok("v"),))
            for ci in range(8):
                def _v3(ve, ci=ci, sb=sb):
                    cx.wait(ve, sb)
                    cx.inc(ve.tensor_tensor(out=cx.scr["D"][:], in0=xn[ci][:],
                                            in1=cx.ps[2][:, 0:512], op=ALU.mult), "v")

                cx.V(_v3)
                emit_round_clip(cx, cx.scr["D"][:], xq[ci][:], -128, 127, after=(cx.tok("v"),))
            def _v4(ve):
                cx.wait(ve, cx.tok("v"))
                ve.tensor_scalar(out=rowx[0:1, 0:512], in0=rowx[0:1, 0:512],
                                 scalar1=wsc_s[0:1, 0:1], scalar2=None, op0=ALU.mult)
                ve.drain()
                cx.inc(ve.tensor_copy(out=scr_mark(cx), in_=scr_mark(cx)), "v")

            cx.V(_v4)
            scb = emit_bcast(cx, rowx[0:1, 0:512], 3, 128, 512, after=(cx.tok("v"),))

            def _vsc(ve, scb=scb):
                cx.wait(ve, scb)
                cx.inc(ve.tensor_copy(out=cx.scr["E"][:], in_=cx.ps[3][:, 0:512]), "v")

            cx.V(_vsc)
            scb = cx.tok("v")
            for m in range(8):
                qtw = emit_wtiles(cx, w8T_b, 128 * m, 128, 8, wt, after=(cx.tok("t"),))

                def _t(te, m=m, qtw=qtw):
                    cx.wait(te, qtw, cx.tok("v"))
                    mm = None
                    for k in range(8):
                        mm = te.matmul(out=cx.ps[0][:, 0:512], lhsT=wt[:, 128 * k:128 * k + 128],
                                       rhs=xq[k][:], start=(k == 0), stop=(k == 7))
                    cx.inc(mm, "t")

                cx.T(_t)
                t1 = cx.tok("t")

                def _v5(ve, m=m, t1=t1, scb=scb):
                    cx.wait(ve, t1, scb)
                    ve.tensor_tensor(out=fsb[m][:], in0=cx.ps[0][:, 0:512],
                                     in1=cx.scr["E"][:], op=ALU.mult)
                    ve.drain()
                    cx.inc(ve.tensor_scalar(out=fsb[m][:], in0=fsb[m][:],
                                            scalar1=fb_s[:, m:m + 1], scalar2=None, op0=ALU.add), "v")

                cx.V(_v5)
            st3 = emit_stats(cx, [fsb[ci][:] for ci in range(8)], 512, 1e-5, True,
                             cx.scr["r0"][0:1, 0:512], cx.scr["r1"][0:1, 0:512],
                             cx.scr["r2"][0:1, 0:512], after=(cx.tok("v"),))
            b1 = emit_bcast(cx, cx.scr["r2"][0:1, 0:512], 2, 128, 512, after=(st3,))
            b2 = emit_bcast(cx, cx.scr["r1"][0:1, 0:512], 3, 128, 512, after=(st3,))
            for ci in range(8):
                def _v6(ve, ci=ci, off=off, b2=b2):
                    cx.wait(ve, b2, cx.tok("g"))
                    ve.tensor_tensor(out=fsb[ci][:], in0=fsb[ci][:], in1=cx.ps[3][:, 0:512], op=ALU.mult)
                    ve.drain()
                    ve.tensor_tensor(out=fsb[ci][:], in0=fsb[ci][:], in1=cx.ps[2][:, 0:512], op=ALU.subtract)
                    ve.drain()
                    cx.inc(ve.tensor_tensor(out=h1[ci][:, HALO + off:HALO + off + 512], in0=h1[ci][:, HALO + off:HALO + off + 512],
                                            in1=fsb[ci][:], op=ALU.add), "v")

                cx.V(_v6)
                vt = cx.tok("v")

                def _g(ge, ci=ci, tc=tc, vt=vt):
                    cx.wait(ge, vt)
                    cx.inc(ge.dma_start(out=x4out.ap()[128 * ci:128 * ci + 128,
                                                       512 * tc:512 * tc + 512],
                                        in_=h1[ci][:, HALO + 512 * tc:HALO + 512 * tc + 512]), "g", 16)

                cx.G(_g)
        cx.barrier()


        def _gend(ge):
            cx.wait(ge, cx.tok("g"))

        cx.G(_gend)
    return nc


def _prep_fused(inputs):
    consts = _consts_np()
    gm = _gmask_np()
    x0 = np.asarray(inputs["L0_attn_hidden"], np.float32)
    n1 = np.ascontiguousarray(np.asarray(inputs["L0_attn_norm1_w"], np.float32).reshape(8, 128).T)
    n2 = np.ascontiguousarray(np.asarray(inputs["L0_attn_norm2_w"], np.float32).reshape(8, 128).T)
    wq = np.asarray(inputs["L0_attn_g_wq"], np.float32)
    wkvd = np.ascontiguousarray(np.asarray(inputs["L0_attn_g_kv_down"], np.float32))
    wku = np.asarray(inputs["L0_attn_g_k_up"], np.float32)
    wvu = np.asarray(inputs["L0_attn_g_v_up"], np.float32)
    gwo = np.ascontiguousarray(np.asarray(inputs["L0_attn_g_wo"], np.float32))
    lwq = np.ascontiguousarray(_perm_cols(np.asarray(inputs["L0_attn_l_wq"], np.float32)))
    lwk = np.ascontiguousarray(_perm_cols(np.asarray(inputs["L0_attn_l_wk"], np.float32)))
    lwv = np.ascontiguousarray(np.asarray(inputs["L0_attn_l_wv"], np.float32))
    lwo = np.ascontiguousarray(np.asarray(inputs["L0_attn_l_wo"], np.float32))
    wgT = np.ascontiguousarray(np.asarray(inputs["L0_ffn_w_gate"], np.float32).T)
    wuT = np.ascontiguousarray(np.asarray(inputs["L0_ffn_w_up"], np.float32).T)
    wdT = np.ascontiguousarray(np.asarray(inputs["L0_ffn_w_down"], np.float32).T)
    wqC = np.ascontiguousarray(np.asarray(inputs["L1_attn_W_Q"], np.float32).transpose(2, 0, 1).reshape(D, D))
    wkC = np.ascontiguousarray(np.asarray(inputs["L1_attn_W_K"], np.float32).transpose(2, 0, 1).reshape(D, D))
    wvC = np.ascontiguousarray(np.asarray(inputs["L1_attn_W_V"], np.float32).transpose(2, 0, 1).reshape(D, D))
    woT = np.ascontiguousarray(np.asarray(inputs["L1_attn_W_O"], np.float32).T)
    w8T = np.ascontiguousarray(np.asarray(inputs["L1_ffn_w_int8"], np.int32).T)
    ws = np.full((1, 4), float(np.asarray(inputs["L1_ffn_weight_scale"], np.float32)), np.float32)
    fb = np.ascontiguousarray(np.asarray(inputs["L1_ffn_bias"], np.float32).reshape(8, 128).T)
    pos = np.arange(S, dtype=np.float32).reshape(1, S)
    maps = []
    for r in range(NCORES):
        b, hs, own0 = r // 2, (r % 2) * 8, (r % 2) * TOK
        x0fb = np.ascontiguousarray(x0[b].T)
        m = dict(consts)
        m["x0f"] = x0fb
        m["x0own"] = np.ascontiguousarray(x0fb[:, own0:own0 + TOK])
        m["x0halo"] = np.ascontiguousarray(x0fb[:, 1792:2048])
        m["n1w"] = n1
        m["n2w"] = n2
        m["wqA"] = np.ascontiguousarray(_perm_cols(wq[:, 64 * hs:64 * (hs + 8)]))
        m["wkvd"] = wkvd
        m["wkuA"] = np.ascontiguousarray(_perm_cols(wku[:, 128 * (r % 2):128 * (r % 2) + 128]))
        m["wvuA"] = np.ascontiguousarray(wvu[:, 128 * (r % 2):128 * (r % 2) + 128])
        m["gmask"] = gm
        m["posAll"] = pos
        m["posKV"] = np.concatenate([np.arange(1792, 2048), np.arange(own0, own0 + TOK)]
                                    ).astype(np.float32).reshape(1, NKV)
        m["lmask"] = _lmask_np(r)
        m["gwo_f"] = gwo
        m["lwq_f"] = lwq
        m["lwk_f"] = lwk
        m["lwv_f"] = lwv
        m["lwo_f"] = lwo
        m["wgT"] = np.ascontiguousarray(wgT[128 * r:128 * r + 128])
        m["wuT"] = np.ascontiguousarray(wuT[128 * r:128 * r + 128])
        m["wdT"] = np.ascontiguousarray(wdT[512 * r:512 * r + 512])
        m["wqC_f"] = np.ascontiguousarray(wqC[:, 512 * (r % 2):512 * (r % 2) + 512])
        m["wkC_f"] = np.ascontiguousarray(wkC[:, 512 * (r % 2):512 * (r % 2) + 512])
        m["wvC_f"] = np.ascontiguousarray(wvC[:, 512 * (r % 2):512 * (r % 2) + 512])
        m["woT_f"] = woT
        m["w8T_i"] = w8T
        m["wscale"] = ws
        m["fbias"] = fb
        sel = np.zeros((128, 2), np.float32)
        sel[:, r % 2] = 1.0
        m["selOwn"] = sel
        p = np.arange(128, dtype=np.uint32)
        m["idxA"] = np.ascontiguousarray(
            np.stack([1024 * b + 128 * ci + p for ci in range(8)], axis=1)).astype(np.uint32)
        m["idxX"] = np.ascontiguousarray(
            np.stack([1024 * (2 * b + half) + 128 * ci + p
                      for ci in range(8) for half in range(2)], axis=1)).astype(np.uint32)
        maps.append(m)
    return maps


def kernel_multi(**inputs):
    return _kernel_multi_impl(**inputs)


_kernel_multi_impl = kernel


def kernel(**inputs):
    if "F" not in _cache:
        _cache["F"] = build_fused()
    maps = _prep_fused(inputs)
    res = run_bass_kernel_spmd(_cache["F"], maps, core_ids=list(range(NCORES)))
    out = np.empty((B, S, D), np.float32)
    for r in range(NCORES):
        b, own0 = r // 2, (r % 2) * TOK
        out[b, own0:own0 + TOK, :] = np.asarray(res.results[r]["x4out"]).T
    return out


# version 10
# speedup vs baseline: 1.0150x; 1.0002x over previous
import contextlib
import math

import numpy as np
import concourse.bass as bass
import concourse.mybir as mybir
from concourse.bass_utils import run_bass_kernel_spmd

F32 = mybir.dt.float32
BF16 = mybir.dt.bfloat16
I32 = mybir.dt.int32
AF = mybir.ActivationFunctionType
ALU = mybir.AluOpType

B, S, D = 4, 4096, 1024
NCORES = 8
TOK = 2048
HALO = 256
NKV = 2304
H, KVH, HD = 16, 4, 64
LAT = 256
FFN_I = 4096
C_L0 = 9.0
C_L1 = 20.0
BASE_G, BASE_L = 1.0e6, 1.0e4
NEG = -1.0e30

DEINT = np.concatenate([np.arange(0, 64, 2), np.arange(1, 64, 2)])


def _hilo(x):
    h = np.float32(x)
    lo = np.float32(np.float64(x) - np.float64(h))
    return float(h), float(lo)


I2PI_H, I2PI_L = _hilo(1.0 / (2.0 * math.pi))
TPI_H, TPI_L = _hilo(2.0 * math.pi)
HPI = float(np.float32(0.5 * math.pi))


class Ctx:
    def __init__(self, nc, es, block):
        self.nc, self.es, self.block = nc, es, block
        self.sem, self.cnt = {}, {}
        for e in ("t", "v", "s", "g", "q"):
            self.sem[e] = es.enter_context(nc.semaphore("sem_" + e))
            self.cnt[e] = 0
        self.sb = {}

    def inc(self, instr, e, n=1):
        instr.then_inc(self.sem[e], n)
        self.cnt[e] += n
        return (e, self.cnt[e])

    def tok(self, e):
        return (e, self.cnt[e])

    def wait(self, eng, *toks):
        for (e, v) in toks:
            if v > 0:
                eng.wait_ge(self.sem[e], v)

    def T(self, fn):
        self.block.tensor(fn)

    def V(self, fn):
        self.block.vector(fn)

    def Sc(self, fn):
        self.block.scalar(fn)

    def G(self, fn):
        self.block.gpsimd(fn)

    def Q(self, fn):
        self.block.sync(fn)

    def barrier(self):
        toks = [self.tok(e) for e in ("t", "v", "s", "g", "q")]

        def mk(me):
            def _f(eng):
                self.wait(eng, *[t for t in toks if t[0] != me])
            return _f

        self.T(mk("t")); self.V(mk("v")); self.Sc(mk("s")); self.G(mk("g")); self.Q(mk("q"))

    def sbuf(self, name, shape, dt=F32):
        t = self.es.enter_context(self.nc.sbuf_tensor(name, shape, dt))
        self.sb[name] = t
        return t

    def psum_t(self, name, shape, dt=F32):
        return self.es.enter_context(self.nc.psum_tensor(name, shape, dt))

    def inp(self, name, shape, dt=F32):
        return self.nc.dram_tensor(name, shape, dt, kind="ExternalInput")

    def outp(self, name, shape, dt=F32):
        return self.nc.dram_tensor(name, shape, dt, kind="ExternalOutput")

    def dram(self, name, shape, dt=F32):
        return self.nc.dram_tensor(name, shape, dt)


CONST_DEFS = [("onesF", [128, 2], F32), ("onesRowF", [1, 128], F32),
              ("b512", [128, 1], F32), ("identB", [128, 128], BF16),
              ("smatTF", [64, 64], F32), ("biasC0", [128, 1], F32),
              ("biasC1", [128, 1], F32), ("c0125", [128, 1], F32),
              ("cm05", [128, 1], F32), ("iotaW", [64, 1], F32),
              ("identF", [128, 128], F32)]


def alloc_common(cx):
    cx.cin = {n: cx.inp(n, sh, dt) for n, sh, dt in CONST_DEFS}
    cx.c = {n: cx.sbuf("c_" + n, sh, dt) for n, sh, dt in CONST_DEFS}
    cx.scr = {
        "A": cx.sbuf("scrA", [128, 512], F32),
        "B": cx.sbuf("scrB", [128, 512], F32),
        "C": cx.sbuf("scrC", [128, 512], F32),
        "D": cx.sbuf("scrD", [128, 512], F32),
        "E": cx.sbuf("scrE", [128, 512], F32),
        "D2": cx.sbuf("scrD2", [128, 512], F32),
        "A2": cx.sbuf("scrA2", [128, 512], F32),
        "I": cx.sbuf("scrI", [128, 512], I32),
        "r0": cx.sbuf("scrR0", [1, 512], F32),
        "r1": cx.sbuf("scrR1", [1, 512], F32),
        "r2": cx.sbuf("scrR2", [1, 512], F32),
        "r3": cx.sbuf("scrR3", [1, 512], F32),
        "r4": cx.sbuf("scrR4", [1, 512], F32),
        "pt0": cx.sbuf("pt0", [128, 512], BF16),
        "pt1": cx.sbuf("pt1", [128, 512], BF16),
        "amx4": cx.sbuf("amx4", [4, 128], F32),
    }
    cx.sbuf("invf_row", [1, 64], F32)
    cx.sbuf("invf_col", [64, 1], F32)
    cx.ps = [cx.psum_t("ps%d" % i, [128, 512], F32) for i in range(6)]
    cx.pst = cx.psum_t("ps_tr", [128, 128], BF16)
    cx.pstf = cx.psum_t("ps_trf", [128, 128], F32)
    cx.dscr = cx.dram("dscr", [1, 8192], F32)


def emit_load_consts(cx):
    def _q(qe):
        for n in cx.c:
            cx.inc(qe.dma_start(out=cx.c[n][:], in_=cx.cin[n].ap()), "q", 16)
    cx.Q(_q)
    cx.barrier()


def emit_row_transpose_from_dscr(cx, ncols, dst_ap):
    dr = cx.dscr

    def _q(qe):
        cx.wait(qe, cx.tok("q"))
        cx.inc(qe.dma_start(out=dst_ap,
                            in_=dr.ap()[0:1, 0:ncols].rearrange("a (t p) -> (a p) t", p=128)), "q", 16)

    cx.Q(_q)
    return cx.tok("q")


def emit_rsqrt_row(cx, v_row_ap, out_row_ap, nn, after=()):
    scr, c = cx.scr, cx.c
    t1 = scr["r3"][0:1, 0:nn]

    def _s(se):
        cx.wait(se, *after)
        cx.inc(se.activation(out=t1, in_=v_row_ap, func=AF.Sqrt), "s")

    cx.Sc(_s)
    st = cx.tok("s")

    def _v(ve):
        cx.wait(ve, st)
        ve.reciprocal(out=out_row_ap, in_=t1)
        ve.drain()
        ve.tensor_tensor(out=t1, in0=out_row_ap, in1=out_row_ap, op=ALU.mult)
        ve.drain()
        ve.tensor_tensor(out=t1, in0=t1, in1=v_row_ap, op=ALU.mult)
        ve.drain()
        ve.tensor_scalar(out=t1, in0=t1, scalar1=-0.5, scalar2=1.5, op0=ALU.mult, op1=ALU.add)
        ve.drain()
        cx.inc(ve.tensor_tensor(out=out_row_ap, in0=out_row_ap, in1=t1, op=ALU.mult), "v")

    cx.V(_v)
    return cx.tok("v")


def emit_stats(cx, chunks, nt, eps, with_mean, m_ap, r_ap, mr_ap, after=()):
    scr, c, ps = cx.scr, cx.c, cx.ps
    nch = len(chunks)
    inv_n = 1.0 / (128 * nch)
    v0 = cx.cnt["v"]
    t_sq_base = cx.cnt["t"] + (1 if with_mean else 0)

    def _v(ve):
        cx.wait(ve, *after)
        for ci in range(nch):
            if ci >= 2:
                cx.wait(ve, ("t", t_sq_base + ci - 1))
            cx.inc(ve.tensor_tensor(out=scr["A" if ci % 2 == 0 else "B"][:, 0:nt],
                                    in0=chunks[ci], in1=chunks[ci], op=ALU.mult), "v")

    cx.V(_v)

    def _t(te):
        cx.wait(te, *after)
        if with_mean:
            mm = None
            for ci in range(nch):
                mm = te.matmul(out=ps[4][0:1, 0:nt], lhsT=c["onesF"][:, 0:1], rhs=chunks[ci],
                               start=(ci == 0), stop=(ci == nch - 1))
            cx.inc(mm, "t")
        for ci in range(nch):
            cx.wait(te, ("v", v0 + ci + 1))
            cx.inc(te.matmul(out=ps[5][0:1, 0:nt], lhsT=c["onesF"][:, 0:1],
                             rhs=scr["A" if ci % 2 == 0 else "B"][:, 0:nt],
                             start=(ci == 0), stop=(ci == nch - 1)), "t")

    cx.T(_t)
    t_tok = cx.tok("t")
    var_sv = scr["r2"][0:1, 0:nt] if r_ap is not m_ap else None

    def _v2(ve):
        cx.wait(ve, t_tok)
        if with_mean:
            ve.tensor_scalar(out=m_ap, in0=ps[4][0:1, 0:nt], scalar1=inv_n, scalar2=None, op0=ALU.mult)
        ve.tensor_scalar(out=r_ap, in0=ps[5][0:1, 0:nt], scalar1=inv_n, scalar2=float(eps),
                         op0=ALU.mult, op1=ALU.add)
        ve.drain()
        if with_mean:
            ve.tensor_tensor(out=scr["A"][0:1, 0:nt], in0=m_ap, in1=m_ap, op=ALU.mult)
            ve.drain()
            ve.tensor_tensor(out=r_ap, in0=r_ap, in1=scr["A"][0:1, 0:nt], op=ALU.subtract)
            ve.drain()
        cx.inc(ve.tensor_copy(out=scr["B"][0:1, 0:nt], in_=r_ap), "v")

    cx.V(_v2)
    rs = emit_rsqrt_row(cx, scr["B"][0:1, 0:nt], r_ap, nt, after=(cx.tok("v"),))

    if with_mean:
        def _v3(ve):
            cx.wait(ve, rs)
            cx.inc(ve.tensor_tensor(out=mr_ap, in0=m_ap, in1=r_ap, op=ALU.mult), "v")

        cx.V(_v3)
    return cx.tok("v")


def emit_bcast(cx, vec_row_ap, ps_idx, p, n, after=()):
    def _t(te):
        cx.wait(te, *after)
        cx.inc(te.matmul(out=cx.ps[ps_idx][0:p, 0:n], lhsT=cx.c["onesRowF"][0:1, 0:p],
                         rhs=vec_row_ap, start=True, stop=True), "t")
    cx.T(_t)
    return cx.tok("t")


def emit_round_clip(cx, vin_ap, vout_ap, lo, hi, p=128, n=512, after=()):
    scr, c = cx.scr, cx.c

    def _v(ve):
        cx.wait(ve, *after)
        ve.tensor_copy(out=scr["I"][0:p, 0:n], in_=vin_ap)
        ve.drain()
        ve.tensor_copy(out=scr["B"][0:p, 0:n], in_=scr["I"][0:p, 0:n])
        ve.tensor_scalar(out=scr["A"][0:p, 0:n], in0=scr["I"][0:p, 0:n],
                         scalar1=-1.0, scalar2=None, op0=ALU.add)
        ve.drain()
        ve.scalar_tensor_tensor(out=scr["C"][0:p, 0:n], in0=vin_ap, scalar=1.0,
                                in1=scr["B"][0:p, 0:n], op0=ALU.mult, op1=ALU.subtract)
        ve.drain()
        ve.scalar_tensor_tensor(out=scr["B"][0:p, 0:n], in0=scr["C"][0:p, 0:n], scalar=-0.5,
                                in1=scr["A"][0:p, 0:n], op0=ALU.is_gt, op1=ALU.add)
        ve.drain()
        ve.scalar_tensor_tensor(out=scr["A"][0:p, 0:n], in0=scr["C"][0:p, 0:n], scalar=0.5,
                                in1=scr["B"][0:p, 0:n], op0=ALU.is_ge, op1=ALU.add)
        ve.drain()
        cx.inc(ve.tensor_scalar(out=vout_ap, in0=scr["A"][0:p, 0:n],
                                scalar1=float(lo), scalar2=float(hi),
                                op0=ALU.max, op1=ALU.min), "v")

    cx.V(_v)
    return cx.tok("v")


def emit_sin_chain(cx, src_ap, out_ap, p, n, after):
    scr = cx.scr

    def _v(ve):
        cx.wait(ve, *after)
        ve.tensor_scalar(out=scr["B"][0:p, 0:n], in0=src_ap, scalar1=I2PI_H, scalar2=None, op0=ALU.mult)
        ve.tensor_scalar(out=scr["C"][0:p, 0:n], in0=src_ap, scalar1=I2PI_L, scalar2=None, op0=ALU.mult)
        ve.drain()
        ve.scalar_tensor_tensor(out=scr["B"][0:p, 0:n], in0=scr["C"][0:p, 0:n], scalar=1.0,
                                in1=scr["B"][0:p, 0:n], op0=ALU.mult, op1=ALU.add)
        ve.drain()
        ve.tensor_copy(out=scr["I"][0:p, 0:n], in_=scr["B"][0:p, 0:n])
        ve.drain()
        ve.tensor_copy(out=scr["C"][0:p, 0:n], in_=scr["I"][0:p, 0:n])
        ve.tensor_scalar(out=scr["A"][0:p, 0:n], in0=scr["I"][0:p, 0:n],
                         scalar1=-1.0, scalar2=None, op0=ALU.add)
        ve.drain()
        ve.scalar_tensor_tensor(out=scr["B"][0:p, 0:n], in0=scr["B"][0:p, 0:n], scalar=1.0,
                                in1=scr["C"][0:p, 0:n], op0=ALU.mult, op1=ALU.subtract)
        ve.drain()
        ve.scalar_tensor_tensor(out=scr["A"][0:p, 0:n], in0=scr["B"][0:p, 0:n], scalar=-0.5,
                                in1=scr["A"][0:p, 0:n], op0=ALU.is_gt, op1=ALU.add)
        ve.drain()
        ve.scalar_tensor_tensor(out=scr["C"][0:p, 0:n], in0=scr["B"][0:p, 0:n], scalar=0.5,
                                in1=scr["A"][0:p, 0:n], op0=ALU.is_ge, op1=ALU.add)
        ve.drain()
        ve.scalar_tensor_tensor(out=scr["B"][0:p, 0:n], in0=scr["C"][0:p, 0:n], scalar=-TPI_H,
                                in1=src_ap, op0=ALU.mult, op1=ALU.add)
        ve.drain()
        cx.inc(ve.scalar_tensor_tensor(out=scr["B"][0:p, 0:n], in0=scr["C"][0:p, 0:n],
                                       scalar=-TPI_L, in1=scr["B"][0:p, 0:n],
                                       op0=ALU.mult, op1=ALU.add), "v")

    cx.V(_v)
    vt = cx.tok("v")

    def _s(se):
        cx.wait(se, vt)
        cx.inc(se.activation(out=out_ap, in_=scr["B"][0:p, 0:n], func=AF.Sin), "s")

    cx.Sc(_s)
    return cx.tok("s")


def emit_rope_tables(cx, base, pos_dram, nt_total, cos_t, sin_t):
    scr, c, ps = cx.scr, cx.c, cx.ps
    ch, cl = _hilo(math.log(base) / 32.0)
    invf_row, iv = cx.sb["invf_row"], cx.sb["invf_col"]

    def _v(ve):
        cx.wait(ve, cx.tok("q"), cx.tok("s"))
        ve.tensor_scalar(out=scr["A"][0:64, 0:1], in0=c["iotaW"][:], scalar1=-ch, scalar2=None, op0=ALU.mult)
        ve.tensor_scalar(out=scr["B"][0:64, 0:1], in0=c["iotaW"][:], scalar1=-cl, scalar2=None, op0=ALU.mult)
        ve.drain()
        cx.inc(ve.tensor_tensor(out=scr["C"][0:64, 0:1], in0=scr["A"][0:64, 0:1],
                                in1=scr["B"][0:64, 0:1], op=ALU.add), "v")

    cx.V(_v)
    vt = cx.tok("v")

    def _s(se):
        cx.wait(se, vt)
        cx.inc(se.activation(out=iv[:], in_=scr["C"][0:64, 0:1], func=AF.Exp), "s")

    cx.Sc(_s)
    st = cx.tok("s")
    dr = cx.dscr

    def _q(qe):
        cx.wait(qe, st)
        cx.inc(qe.dma_start(out=dr[0:1, 0:64], in_=iv[:]), "q", 16)
        cx.wait(qe, cx.tok("q"))
        cx.inc(qe.dma_start(out=invf_row[:], in_=dr.ap()[0:1, 0:64]), "q", 16)

    cx.Q(_q)
    q_tok = cx.tok("q")

    for off in range(0, nt_total, 512):
        nn = min(512, nt_total - off)

        def _qp(qe, off=off, nn=nn):
            cx.wait(qe, cx.tok("v"))
            cx.inc(qe.dma_start(out=cx.scr["r0"][0:1, 0:nn],
                                in_=pos_dram.ap()[0:1, off:off + nn]), "q", 16)

        cx.Q(_qp)
        qp = cx.tok("q")

        def _t(te, nn=nn, qp=qp):
            cx.wait(te, q_tok, qp, cx.tok("s"), cx.tok("v"))
            cx.inc(te.matmul(out=ps[0][0:64, 0:nn], lhsT=invf_row[:],
                             rhs=cx.scr["r0"][0:1, 0:nn], start=True, stop=True), "t")

        cx.T(_t)
        tt = cx.tok("t")

        def _v2(ve, nn=nn, tt=tt):
            cx.wait(ve, tt)
            ve.tensor_copy(out=scr["D"][0:64, 0:nn], in_=ps[0][0:64, 0:nn])
            ve.drain()
            cx.inc(ve.tensor_scalar(out=scr["E"][0:64, 0:nn], in0=scr["D"][0:64, 0:nn],
                                    scalar1=HPI, scalar2=None, op0=ALU.add), "v")

        cx.V(_v2)
        emit_sin_chain(cx, scr["D"][0:64, 0:nn], sin_t[:, off:off + nn], 64, nn, (cx.tok("v"),))
        emit_sin_chain(cx, scr["E"][0:64, 0:nn], cos_t[:, off:off + nn], 64, nn, (cx.tok("s"),))

    send = cx.tok("s")

    def _vg(ve):
        cx.wait(ve, send)

    cx.V(_vg)
    return cx.tok("s")


def mk_pipe():
    return {"n": 0, "vend": {}, "send": {}, "tstats": {}, "ssqrt": {},
            "tbc": {}, "gcos": {}, "qb": {}, "rs": {}, "tail": None, "ent": None}


def pipe_flush(cx, pipe):
    if pipe["tail"] is not None:
        pipe["tail"]()
        pipe["tail"] = None
    return pipe["vend"].get(pipe["n"] - 1)


def emit_proj_rope(cx, w_tiles, rhs_chunks, kchunks, nb_off, nn, cos_t, sin_t,
                   out_bf_ap, fold_a, bvec_row_ap, pipe=None):
    scr, c, ps = cx.scr, cx.c, cx.ps
    if pipe is not None:
        return _emit_proj_rope_piped(cx, w_tiles, rhs_chunks, kchunks, nb_off,
                                     nn, cos_t, sin_t, out_bf_ap, fold_a,
                                     bvec_row_ap, pipe)

    def _t(te):
        cx.wait(te, cx.tok("v"), cx.tok("s"))
        mm = None
        for k in range(kchunks):
            mm = te.matmul(out=ps[0][0:64, 0:nn], lhsT=w_tiles[k], rhs=rhs_chunks[k],
                           start=(k == 0), stop=(k == kchunks - 1))
        cx.inc(mm, "t")

    cx.T(_t)
    t1 = cx.tok("t")

    def _v(ve):
        cx.wait(ve, t1)
        ve.tensor_copy(out=scr["D"][0:64, 0:nn], in_=ps[0][0:64, 0:nn])
        ve.drain()
        cx.inc(ve.tensor_tensor(out=scr["A"][0:64, 0:nn], in0=scr["D"][0:64, 0:nn],
                                in1=scr["D"][0:64, 0:nn], op=ALU.mult), "v")

    cx.V(_v)
    v1 = cx.tok("v")

    def _t2(te):
        cx.wait(te, v1)
        te.matmul(out=ps[4][0:1, 0:nn], lhsT=c["onesF"][0:64, 0:1],
                  rhs=scr["A"][0:64, 0:nn], start=True, stop=True)
        cx.inc(te.matmul(out=ps[1][0:64, 0:nn], lhsT=c["smatTF"][:],
                         rhs=scr["D"][0:64, 0:nn], start=True, stop=True), "t")

    cx.T(_t2)
    t2 = cx.tok("t")
    rrow = scr["r1"][0:1, 0:nn]

    def _v2(ve):
        cx.wait(ve, t2)
        cx.inc(ve.tensor_scalar(out=scr["r0"][0:1, 0:nn], in0=ps[4][0:1, 0:nn],
                                scalar1=1.0 / 64.0, scalar2=1e-6, op0=ALU.mult, op1=ALU.add), "v")

    cx.V(_v2)
    rs = emit_rsqrt_row(cx, scr["r0"][0:1, 0:nn], rrow, nn, after=(cx.tok("v"),))

    if bvec_row_ap == "dscr":
        def _qb(qe):
            cx.wait(qe, rs)
            cx.inc(qe.dma_start(out=cx.dscr[0:1, nb_off:nb_off + nn], in_=rrow), "q", 16)

        cx.Q(_qb)
    ab = None
    if fold_a:
        ab = emit_bcast(cx, rrow, 5, 64, nn, after=(cx.tok("v"), rs))

    def _v4(ve):
        cx.wait(ve, t2)
        ve.tensor_tensor(out=scr["B"][0:64, 0:nn], in0=scr["D"][0:64, 0:nn],
                         in1=cos_t[:, nb_off:nb_off + nn], op=ALU.mult)
        ve.tensor_tensor(out=scr["C"][0:64, 0:nn], in0=ps[1][0:64, 0:nn],
                         in1=sin_t[:, nb_off:nb_off + nn], op=ALU.mult)
        ve.drain()
        ve.tensor_tensor(out=scr["B"][0:64, 0:nn], in0=scr["B"][0:64, 0:nn],
                         in1=scr["C"][0:64, 0:nn], op=ALU.add)
        ve.drain()
        if fold_a:
            cx.wait(ve, ab)
            cx.inc(ve.tensor_tensor(out=out_bf_ap, in0=scr["B"][0:64, 0:nn],
                                    in1=ps[5][0:64, 0:nn], op=ALU.mult), "v")
        else:
            cx.inc(ve.tensor_copy(out=out_bf_ap, in_=scr["B"][0:64, 0:nn]), "v")

    cx.V(_v4)
    return cx.tok("v")


def _emit_proj_rope_piped(cx, w_tiles, rhs_chunks, kchunks, nb_off, nn, cos_t,
                          sin_t, out_bf_ap, fold_a, bvec_row_ap, pipe):
    scr, c, ps = cx.scr, cx.c, cx.ps
    b = pipe["n"]
    pipe["n"] += 1
    par = b % 2
    psA = ps[0] if par == 0 else ps[2]
    psB = ps[1] if par == 0 else ps[3]
    sD = scr["D"] if par == 0 else scr["D2"]
    sA = scr["A"] if par == 0 else scr["A2"]
    rrow = scr["r1"][0:1, 0:nn] if par == 0 else scr["r2"][0:1, 0:nn]

    def pv(key, i):
        return pipe[key].get(i)

    def W(eng, *toks):
        for t in toks:
            if t is not None and t[1] > 0:
                eng.wait_ge(cx.sem[t[0]], t[1])

    if pipe["ent"] is None:
        pipe["ent"] = (cx.tok("v"), cx.tok("s"), cx.tok("g"))
    ent_v, ent_s, ent_g = (pipe["ent"] if b < 2 else (None, None, None))

    w_s2 = pv("send", b - 2)
    w_v2 = pv("vend", b - 2)

    def _t(te):
        W(te, w_s2, w_v2, ent_v, ent_s)
        mm = None
        for k in range(kchunks):
            mm = te.matmul(out=psA[0:64, 0:nn], lhsT=w_tiles[k], rhs=rhs_chunks[k],
                           start=(k == 0), stop=(k == kchunks - 1))
        cx.inc(mm, "t")

    cx.T(_t)
    t1 = cx.tok("t")

    if pipe["tail"] is not None:
        pipe["tail"]()
        pipe["tail"] = None

    w_g2 = pv("gcos", b - 2)
    w_t2s = pv("tstats", b - 2)

    def _s1(se):
        W(se, t1, w_g2, w_t2s, ent_g)
        se.activation(out=sD[0:64, 0:nn], in_=psA[0:64, 0:nn], func=AF.Copy)
        se.drain()
        cx.inc(se.activation(out=sA[0:64, 0:nn], in_=sD[0:64, 0:nn],
                             func=AF.Square), "s")

    cx.Sc(_s1)
    s1 = cx.tok("s")
    pipe["send"][b] = s1

    def _t2(te):
        W(te, s1, w_v2)
        te.matmul(out=psB[64:65, 0:nn], lhsT=c["onesF"][0:64, 0:1],
                  rhs=sA[0:64, 0:nn], start=True, stop=True)
        cx.inc(te.matmul(out=psB[0:64, 0:nn], lhsT=c["smatTF"][:],
                         rhs=sD[0:64, 0:nn], start=True, stop=True), "t")

    cx.T(_t2)
    t2 = cx.tok("t")
    pipe["tstats"][b] = t2

    w_sq1 = pv("ssqrt", b - 1)

    def _v2(ve):
        W(ve, t2, w_sq1)
        cx.inc(ve.tensor_scalar(out=scr["r0"][0:1, 0:nn], in0=psB[64:65, 0:nn],
                                scalar1=1.0 / 64.0, scalar2=1e-6,
                                op0=ALU.mult, op1=ALU.add), "v")

    cx.V(_v2)
    v2 = cx.tok("v")

    def _s2(se):
        W(se, v2)
        cx.inc(se.activation(out=scr["r3"][0:1, 0:nn], in_=scr["r0"][0:1, 0:nn],
                             func=AF.Sqrt), "s")

    cx.Sc(_s2)
    sq = cx.tok("s")
    pipe["ssqrt"][b] = sq
    w_tb2 = pv("tbc", b - 2)
    w_qb2 = pv("qb", b - 2)

    def _v3(ve):
        W(ve, sq, w_tb2, w_qb2)
        ve.reciprocal(out=rrow, in_=scr["r3"][0:1, 0:nn])
        ve.drain()
        ve.tensor_tensor(out=scr["r3"][0:1, 0:nn], in0=rrow, in1=rrow, op=ALU.mult)
        ve.drain()
        ve.scalar_tensor_tensor(out=scr["r3"][0:1, 0:nn], in0=scr["r0"][0:1, 0:nn],
                                scalar=-0.5, in1=scr["r3"][0:1, 0:nn],
                                op0=ALU.mult, op1=ALU.mult)
        ve.drain()
        cx.inc(ve.scalar_tensor_tensor(out=rrow, in0=scr["r3"][0:1, 0:nn],
                                       scalar=1.5, in1=rrow,
                                       op0=ALU.add, op1=ALU.mult), "v")

    cx.V(_v3)
    rs = cx.tok("v")
    pipe["rs"][b] = rs

    if bvec_row_ap == "dscr":
        def _qb(qe):
            W(qe, rs)
            cx.inc(qe.dma_start(out=cx.dscr[0:1, nb_off:nb_off + nn], in_=rrow), "q", 16)

        cx.Q(_qb)
    pipe["qb"][b] = cx.tok("q")

    w_v1 = pv("vend", b - 1)

    def _g(ge):
        W(ge, s1, w_v1, ent_v)
        cx.inc(ge.tensor_tensor(out=scr["B"][0:64, 0:nn], in0=sD[0:64, 0:nn],
                                in1=cos_t[:, nb_off:nb_off + nn], op=ALU.mult), "g")

    cx.G(_g)
    gc = cx.tok("g")
    pipe["gcos"][b] = gc

    def tail(b=b, rs=rs, gc=gc, psA=psA, psB=psB, rrow=rrow, nn=nn,
             nb_off=nb_off, out_bf_ap=out_bf_ap, fold_a=fold_a,
             cos_t=cos_t, sin_t=sin_t):
        if fold_a:
            def _t3(te):
                W(te, rs)
                cx.inc(te.matmul(out=psA[64:128, 0:nn], lhsT=c["onesRowF"][0:1, 0:64],
                                 rhs=rrow, start=True, stop=True), "t")

            cx.T(_t3)
        tb = cx.tok("t")
        pipe["tbc"][b] = tb

        def _v4(ve):
            ve.tensor_tensor(out=scr["C"][0:64, 0:nn], in0=psB[0:64, 0:nn],
                             in1=sin_t[:, nb_off:nb_off + nn], op=ALU.mult)
            W(ve, gc)
            ve.drain()
            ve.tensor_tensor(out=scr["C"][0:64, 0:nn], in0=scr["C"][0:64, 0:nn],
                             in1=scr["B"][0:64, 0:nn], op=ALU.add)
            ve.drain()
            if fold_a:
                W(ve, tb)
                cx.inc(ve.tensor_tensor(out=out_bf_ap, in0=scr["C"][0:64, 0:nn],
                                        in1=psA[64:128, 0:nn], op=ALU.mult), "v")
            else:
                cx.inc(ve.tensor_copy(out=out_bf_ap, in_=scr["C"][0:64, 0:nn]), "v")

        cx.V(_v4)
        pipe["vend"][b] = cx.tok("v")

    pipe["tail"] = tail
    return cx.tok("v")


def emit_v_build(cx, w_tiles, rhs_nb_fn, kchunks, nkt, v_tm, vf_scr_bf):
    c = cx.c

    def _g(ge):
        cx.wait(ge, cx.tok("v"))
        cx.inc(ge.memset(v_tm[:], 1.0), "g")

    cx.G(_g)
    g0 = cx.tok("g")
    for nb in range(nkt // 4):
        def _t(te, nb=nb):
            cx.wait(te, cx.tok("v"))
            mm = None
            for k in range(kchunks):
                mm = te.matmul(out=cx.ps[0][0:64, 0:512], lhsT=w_tiles[k], rhs=rhs_nb_fn(k, nb),
                               start=(k == 0), stop=(k == kchunks - 1))
            cx.inc(mm, "t")

        cx.T(_t)
        t1 = cx.tok("t")

        def _v(ve, t1=t1):
            cx.wait(ve, t1)
            cx.inc(ve.tensor_copy(out=vf_scr_bf[:], in_=cx.ps[0][0:64, 0:512]), "v")

        cx.V(_v)
        v1 = cx.tok("v")

        for u in range(4):
            def _t2(te, u=u, v1=v1):
                cx.wait(te, v1, cx.tok("v"))
                cx.inc(te.transpose(out=cx.pst[0:128, 0:64], in_=vf_scr_bf[:, 128 * u:128 * u + 128],
                                    identity=c["identB"][0:64, 0:64]), "t")

            cx.T(_t2)
            t2 = cx.tok("t")
            kt = nb * 4 + u

            def _v2(ve, kt=kt, t2=t2):
                cx.wait(ve, t2, g0)
                cx.inc(ve.tensor_copy(out=v_tm[:, 65 * kt:65 * kt + 64], in_=cx.pst[0:128, 0:64]), "v")

            cx.V(_v2)
    return cx.tok("v")


def emit_attn_job(cx, q_bf, k_bf, v_tm, o_bf, nq_total, bsc_part, bias_c, gmask_bf):
    scr, c, ps = cx.scr, cx.c, cx.ps
    pt = [scr["pt0"], scr["pt1"]]
    nqb = nq_total // 512

    for qb in range(nqb):
        nkt = 4 * qb + 4
        t0, s0, v0 = cx.cnt["t"], cx.cnt["s"], cx.cnt["v"]

        def _t(te, qb=qb, nkt=nkt, t0=t0, s0=s0, v0=v0):
            for kt in range(nkt):
                diag = kt >= nkt - 4
                pi = kt % 2
                if kt >= 2:
                    cx.wait(te, ("s", s0 + kt - 1))
                else:
                    cx.wait(te, ("v", v0))
                if diag:
                    i = kt - (nkt - 4)
                    te.matmul(out=ps[pi][:, 0:512], lhsT=c["identB"][:],
                              rhs=gmask_bf[:, 512 * i:512 * i + 512], start=True, stop=False)
                cx.inc(te.matmul(out=ps[pi][:, 0:512], lhsT=k_bf[:, 128 * kt:128 * kt + 128],
                                 rhs=q_bf[:, 512 * qb:512 * qb + 512],
                                 start=(not diag), stop=True), "t")
                cx.wait(te, ("s", s0 + kt + 1))
                cx.inc(te.matmul(out=ps[2][0:65, 0:512], lhsT=v_tm[:, 65 * kt:65 * kt + 65],
                                 rhs=pt[pi][:, 0:512], start=(kt == 0), stop=(kt == nkt - 1)), "t")

        cx.T(_t)

        def _s(se, nkt=nkt, t0=t0, s0=s0):
            for kt in range(nkt):
                pi = kt % 2
                cx.wait(se, ("t", t0 + 2 * kt + 1))
                sc = bsc_part[:, kt:kt + 1] if bsc_part is not None else c["c0125"][:]
                cx.inc(se.activation(out=pt[pi][:, 0:512], in_=ps[pi][:, 0:512],
                                     func=AF.Exp, scale=sc, bias=bias_c[:]), "s")

        cx.Sc(_s)
        tend = cx.tok("t")

        def _v(ve, tend=tend):
            cx.wait(ve, tend)
            cx.inc(ve.reciprocal(out=scr["r0"][0:1, 0:512], in_=ps[2][64:65, 0:512]), "v")

        cx.V(_v)
        bc = emit_bcast(cx, scr["r0"][0:1, 0:512], 3, 64, 512, after=(cx.tok("v"),))

        def _v2(ve, qb=qb, bc=bc):
            cx.wait(ve, bc)
            ve.tensor_copy(out=scr["E"][0:64, 0:512], in_=ps[2][0:64, 0:512])
            ve.drain()
            cx.inc(ve.tensor_tensor(out=o_bf[:, 512 * qb:512 * qb + 512],
                                    in0=scr["E"][0:64, 0:512], in1=ps[3][0:64, 0:512],
                                    op=ALU.mult), "v")

        cx.V(_v2)
    return cx.tok("v")


def build_launchA():
    nc = bass.Bass("TRN2", target_bir_lowering=False, debug=False)
    es = contextlib.ExitStack()
    with es:
        block = es.enter_context(nc.Block())
        cx = Ctx(nc, es, block)
        x0f = cx.inp("x0f", [D, S], F32)
        n1w = cx.inp("n1w", [128, 8], F32)
        wqA = cx.inp("wqA", [D, 512], F32)
        wkvd = cx.inp("wkvd", [D, LAT], F32)
        wkuA = cx.inp("wkuA", [LAT, 128], F32)
        wvuA = cx.inp("wvuA", [LAT, 128], F32)
        gmask = cx.inp("gmask", [128, 4 * 512], BF16)
        posAll = cx.inp("posAll", [1, S], F32)
        ogout = cx.outp("ogout", [512, S], BF16)

        alloc_common(cx)
        x0n = [cx.sbuf("x0n%d" % i, [128, S], BF16) for i in range(8)]
        kk = [cx.sbuf("kk%d" % a, [64, S], BF16) for a in range(2)]
        vtm = [cx.sbuf("vtm%d" % a, [128, 65 * 32], BF16) for a in range(2)]
        qcur = cx.sbuf("qcur", [64, S], BF16)
        ocur = cx.sbuf("ocur", [64, S], BF16)
        n1w_s = cx.sbuf("n1w_s", [128, 8], F32)
        gmask_s = cx.sbuf("gmask_s", [128, 4 * 512], BF16)
        vf_scr = cx.sbuf("vf_scr", [64, 512], BF16)
        cosG = cx.sbuf("cosG", [64, S], F32)
        sinG = cx.sbuf("sinG", [64, S], F32)
        wq_bf = cx.sbuf("wq_bf", [128, 8 * 512], BF16)

        emit_load_consts(cx)

        def _q0(qe):
            cx.inc(qe.dma_start(out=n1w_s[:], in_=n1w.ap()), "q", 16)
            cx.inc(qe.dma_start(out=gmask_s[:], in_=gmask.ap()), "q", 16)

        cx.Q(_q0)
        cx.barrier()

        with contextlib.ExitStack() as ph:
            wkvd_bf = ph.enter_context(nc.sbuf_tensor("wkvd_bf", [128, 8 * 256], BF16))
            wku_bf = ph.enter_context(nc.sbuf_tensor("wku_bf", [128, 2 * 128], BF16))
            wvu_bf = ph.enter_context(nc.sbuf_tensor("wvu_bf", [128, 2 * 128], BF16))
            wf32 = ph.enter_context(nc.sbuf_tensor("wf32", [128, 512], F32))
            xf32 = [ph.enter_context(nc.sbuf_tensor("xf32_%d" % i, [128, 256], F32)) for i in range(8)]
            lat_bf = [ph.enter_context(nc.sbuf_tensor("lat%d" % a, [128, S], BF16)) for a in range(2)]

            emit_rope_tables(cx, BASE_G, posAll, S, cosG, sinG)

            def cast_w(dst, src_dram, rows, cols):
                for kc in range(rows // 128):
                    def _q(qe, kc=kc):
                        cx.wait(qe, cx.tok("v"))
                        cx.inc(qe.dma_start(out=wf32[:, 0:cols],
                                            in_=src_dram.ap()[128 * kc:128 * kc + 128, :]), "q", 16)

                    cx.Q(_q)
                    qt = cx.tok("q")

                    def _v(ve, kc=kc, qt=qt):
                        cx.wait(ve, qt)
                        cx.inc(ve.tensor_copy(out=dst[:, cols * kc:cols * kc + cols],
                                              in_=wf32[:, 0:cols]), "v")

                    cx.V(_v)

            cast_w(wq_bf, wqA, D, 512)
            cast_w(wkvd_bf, wkvd, D, 256)
            cast_w(wku_bf, wkuA, LAT, 128)
            cast_w(wvu_bf, wvuA, LAT, 128)

            for nb in range(16):
                def _q(qe, nb=nb):
                    cx.wait(qe, cx.tok("t"), cx.tok("v"))
                    for ci in range(8):
                        cx.inc(qe.dma_start(out=xf32[ci][:],
                                            in_=x0f.ap()[128 * ci:128 * ci + 128,
                                                         256 * nb:256 * nb + 256]), "q", 16)

                cx.Q(_q)
                qt = cx.tok("q")
                st1 = emit_stats(cx, [xf32[ci][:] for ci in range(8)], 256, 1e-5, True,
                                 cx.scr["r0"][0:1, 0:256], cx.scr["r1"][0:1, 0:256],
                                 cx.scr["r2"][0:1, 0:256], after=(qt,))
                b1 = emit_bcast(cx, cx.scr["r2"][0:1, 0:256], 0, 128, 256, after=(st1,))
                b2 = emit_bcast(cx, cx.scr["r1"][0:1, 0:256], 1, 128, 256, after=(st1,))

                def _v(ve, b2=b2):
                    cx.wait(ve, b2)
                    for ci in range(8):
                        ve.tensor_tensor(out=xf32[ci][:], in0=xf32[ci][:],
                                         in1=cx.ps[1][:, 0:256], op=ALU.mult)
                        ve.drain()
                        ve.tensor_tensor(out=xf32[ci][:], in0=xf32[ci][:],
                                         in1=cx.ps[0][:, 0:256], op=ALU.subtract)
                        ve.drain()
                    cx.inc(ve.tensor_copy(out=scr_mark(cx), in_=scr_mark(cx)), "v")

                cx.V(_v)
                st2 = emit_stats(cx, [xf32[ci][:] for ci in range(8)], 256, 1e-6, False,
                                 None, cx.scr["r1"][0:1, 0:256], None, after=(cx.tok("v"),))
                b3 = emit_bcast(cx, cx.scr["r1"][0:1, 0:256], 0, 128, 256, after=(st2,))

                def _v2(ve, nb=nb, b3=b3):
                    cx.wait(ve, b3)
                    for ci in range(8):
                        ve.tensor_tensor(out=xf32[ci][:], in0=xf32[ci][:],
                                         in1=cx.ps[0][:, 0:256], op=ALU.mult)
                        ve.drain()
                        ve.tensor_scalar(out=x0n[ci][:, 256 * nb:256 * nb + 256], in0=xf32[ci][:],
                                         scalar1=n1w_s[:, ci:ci + 1], scalar2=None, op0=ALU.mult)
                        ve.drain()
                    cx.inc(ve.tensor_copy(out=scr_mark(cx), in_=scr_mark(cx)), "v")

                cx.V(_v2)
            cx.barrier()

            for a in range(2):
                for nb in range(8):
                    def _t(te, a=a, nb=nb):
                        cx.wait(te, cx.tok("v"))
                        mm = None
                        for k in range(8):
                            mm = te.matmul(out=cx.ps[0][:, 0:512],
                                           lhsT=wkvd_bf[:, 256 * k + 128 * a:256 * k + 128 * a + 128],
                                           rhs=x0n[k][:, 512 * nb:512 * nb + 512],
                                           start=(k == 0), stop=(k == 7))
                        cx.inc(mm, "t")

                    cx.T(_t)
                    t1 = cx.tok("t")

                    def _v(ve, a=a, nb=nb, t1=t1):
                        cx.wait(ve, t1)
                        cx.inc(ve.tensor_copy(out=lat_bf[a][:, 512 * nb:512 * nb + 512],
                                              in_=cx.ps[0][:, 0:512]), "v")

                    cx.V(_v)

            for a in range(2):
                for nb in range(8):
                    emit_proj_rope(cx,
                                   [wku_bf[:, 64 * a:64 * a + 64], wku_bf[:, 128 + 64 * a:128 + 64 * a + 64]],
                                   [lat_bf[0][:, 512 * nb:512 * nb + 512],
                                    lat_bf[1][:, 512 * nb:512 * nb + 512]],
                                   2, 512 * nb, 512, cosG, sinG,
                                   kk[a][:, 512 * nb:512 * nb + 512], True, None)
                emit_v_build(cx, [wvu_bf[:, 64 * a:64 * a + 64], wvu_bf[:, 128 + 64 * a:128 + 64 * a + 64]],
                             lambda k, nb: lat_bf[k][:, 512 * nb:512 * nb + 512],
                             2, 32, vtm[a], vf_scr)
            cx.barrier()

        for j in range(8):
            for nb in range(8):
                emit_proj_rope(cx, [wq_bf[:, 512 * k + 64 * j:512 * k + 64 * j + 64] for k in range(8)],
                               [x0n[k][:, 512 * nb:512 * nb + 512] for k in range(8)],
                               8, 512 * nb, 512, cosG, sinG,
                               qcur[:, 512 * nb:512 * nb + 512], True, None)
            a = j // 4
            vend = emit_attn_job(cx, qcur, kk[a], vtm[a], ocur, S, None,
                                 cx.c["biasC0"], gmask_s)

            def _g(ge, j=j, vend=vend):
                cx.wait(ge, vend)
                cx.inc(ge.dma_start(out=ogout.ap()[64 * j:64 * j + 64, :], in_=ocur[:]), "g", 16)

            cx.G(_g)
            gt = cx.tok("g")

            def _v(ve, gt=gt):
                cx.wait(ve, gt)

            cx.V(_v)

        def _gend(ge):
            cx.wait(ge, cx.tok("g"))

        cx.G(_gend)
    return nc


def scr_mark(cx):
    return cx.scr["r4"][0:1, 0:1]


def _consts_np():
    import ml_dtypes
    smat = np.zeros((64, 64), np.float32)
    for i in range(32):
        smat[i, 32 + i] = -1.0
        smat[32 + i, i] = 1.0
    return {
        "onesF": np.ones((128, 2), np.float32),
        "onesRowF": np.ones((1, 128), np.float32),
        "b512": np.full((128, 1), 512.0, np.float32),
        "identB": np.eye(128).astype(ml_dtypes.bfloat16),
        "smatTF": np.ascontiguousarray(smat.T),
        "biasC0": np.full((128, 1), -C_L0, np.float32),
        "biasC1": np.full((128, 1), -C_L1, np.float32),
        "c0125": np.full((128, 1), 0.125, np.float32),
        "cm05": np.full((128, 1), -0.5, np.float32),
        "iotaW": (np.arange(64) % 32).astype(np.float32).reshape(64, 1),
        "identF": np.eye(128, dtype=np.float32),
    }


def _gmask_np():
    import ml_dtypes
    m = np.zeros((128, 4, 512), np.float32)
    kkk = np.arange(128)[:, None]
    qq = np.arange(512)[None, :]
    for i in range(4):
        m[:, i, :] = np.where(qq >= 128 * i + kkk, 0.0, NEG)
    return np.ascontiguousarray(m.reshape(128, 2048)).astype(ml_dtypes.bfloat16)


def _perm_cols(w, per_head=64):
    nh = w.shape[1] // per_head
    cols = [w[:, h * per_head:(h + 1) * per_head][:, DEINT] for h in range(nh)]
    return np.concatenate(cols, axis=1)


_cache = {}


def _prep_A(inputs):
    consts = _consts_np()
    gm = _gmask_np()
    pos = np.arange(S, dtype=np.float32).reshape(1, S)
    x0 = np.asarray(inputs["L0_attn_hidden"], np.float32)
    n1 = np.ascontiguousarray(np.asarray(inputs["L0_attn_norm1_w"], np.float32).reshape(8, 128).T)
    wq = np.asarray(inputs["L0_attn_g_wq"], np.float32)
    wkvd = np.ascontiguousarray(np.asarray(inputs["L0_attn_g_kv_down"], np.float32))
    wku = np.asarray(inputs["L0_attn_g_k_up"], np.float32)
    wvu = np.asarray(inputs["L0_attn_g_v_up"], np.float32)
    maps = []
    for r in range(NCORES):
        b = r // 2
        hs = (r % 2) * 8
        m = dict(consts)
        m["gmask"] = gm
        m["posAll"] = pos
        m["x0f"] = np.ascontiguousarray(x0[b].T)
        m["n1w"] = n1
        m["wqA"] = np.ascontiguousarray(_perm_cols(wq[:, 64 * hs:64 * (hs + 8)]))
        m["wkvd"] = wkvd
        m["wkuA"] = np.ascontiguousarray(_perm_cols(wku[:, 128 * (r % 2):128 * (r % 2) + 128]))
        m["wvuA"] = np.ascontiguousarray(wvu[:, 128 * (r % 2):128 * (r % 2) + 128])
        maps.append(m)
    return maps


def run_launchA(inputs):
    if "A" not in _cache:
        _cache["A"] = build_launchA()
    maps = _prep_A(inputs)
    res = run_bass_kernel_spmd(_cache["A"], maps, core_ids=list(range(NCORES)))
    return [np.asarray(r["ogout"]) for r in res.results]


def kernel(**inputs):
    og = run_launchA(inputs)
    raise NotImplementedError("launches B-D pending")


def build_launchW0():
    nc = bass.Bass("TRN2", target_bir_lowering=False, debug=False)
    es = contextlib.ExitStack()
    with es:
        block = es.enter_context(nc.Block())
        cx = Ctx(nc, es, block)
        wgT = cx.inp("wgT", [128, FFN_I], F32)
        wuT = cx.inp("wuT", [128, FFN_I], F32)
        wdT = cx.inp("wdT", [512, D], F32)
        casts = [("gwo", 128, D), ("lwq", 128, D), ("lwk", 128, LAT), ("lwv", 128, LAT),
                 ("lwo", 128, D), ("woT", 128, D), ("wqC", 128, D), ("wkC", 128, D),
                 ("wvC", 128, D)]
        cin = {n: cx.inp(n, [rr, ccd], F32) for n, rr, ccd in casts}
        cout = {n: cx.outp(n + "_b", [rr, ccd], BF16) for n, rr, ccd in casts}
        w8T = cx.inp("w8T", [128, D], I32)
        w8out = cx.outp("w8T_b", [128, D], BF16)
        absout = cx.outp("absout", [1, 4], F32)

        with (
            nc.sbuf_tensor("wb", [128, FFN_I], F32) as wb,
            nc.sbuf_tensor("wbb", [128, FFN_I], BF16) as wbb,
            nc.sbuf_tensor("acc", [128, 4], F32) as acc,
            nc.sbuf_tensor("red", [128, 4], F32) as red,
            nc.sbuf_tensor("ones1", [128, 1], F32) as ones1,
            nc.sbuf_tensor("rowo", [1, 4], F32) as rowo,
            nc.psum_tensor("pss", [1, 4], F32) as pss,
        ):
            cp = cx

            def _g(ge):
                cx.inc(ge.memset(acc[:], 0.0), "g")
                cx.inc(ge.memset(ones1[:], 1.0), "g")

            cx.G(_g)
            g0 = cx.tok("g")

            jobs = [(wgT, 1, FFN_I, 0), (wuT, 1, FFN_I, 1), (wdT, 4, D, 2)]
            for (src, nch, ncol, col) in jobs:
                for kc in range(nch):
                    def _q(qe, src=src, kc=kc, ncol=ncol):
                        cx.wait(qe, cx.tok("v"))
                        cx.inc(qe.dma_start(out=wb[:, 0:ncol],
                                            in_=src.ap()[128 * kc:128 * kc + 128, :]), "q", 16)

                    cx.Q(_q)
                    qt = cx.tok("q")

                    def _v(ve, col=col, ncol=ncol, qt=qt):
                        cx.wait(ve, qt, g0)
                        ve.reduce_sum(out=red[:, col:col + 1], in_=wb[:, 0:ncol],
                                      axis=mybir.AxisListType.X, apply_absolute_value=True)
                        ve.drain()
                        ve.tensor_tensor(out=acc[:, col:col + 1], in0=acc[:, col:col + 1],
                                         in1=red[:, col:col + 1], op=ALU.add)
                        ve.drain()
                        cx.inc(ve.tensor_copy(out=red[0:1, 0:1], in_=red[0:1, 0:1]), "v")

                    cx.V(_v)

            vt = cx.tok("v")

            def _t(te):
                cx.wait(te, vt)
                cx.inc(te.matmul(out=pss[:], lhsT=ones1[:], rhs=acc[:], start=True, stop=True), "t")

            cx.T(_t)
            tt = cx.tok("t")

            def _v2(ve):
                cx.wait(ve, tt)
                cx.inc(ve.tensor_copy(out=rowo[:], in_=pss[:]), "v")

            cx.V(_v2)
            vt2 = cx.tok("v")

            def _g2(ge):
                cx.wait(ge, vt2)
                cx.inc(ge.dma_start(out=absout.ap(), in_=rowo[:]), "g", 16)

            cx.G(_g2)

            for n, rr, ccd in casts:
                for kc in range(rr // 128):
                    def _q(qe, n=n, kc=kc, ccd=ccd):
                        cx.wait(qe, cx.tok("v"), cx.tok("g"))
                        cx.inc(qe.dma_start(out=wb[:, 0:ccd],
                                            in_=cin[n].ap()[128 * kc:128 * kc + 128, :]), "q", 16)

                    cx.Q(_q)
                    qt = cx.tok("q")

                    def _v(ve, qt=qt, ccd=ccd):
                        cx.wait(ve, qt)
                        cx.inc(ve.tensor_copy(out=wbb[:, 0:ccd], in_=wb[:, 0:ccd]), "v")

                    cx.V(_v)
                    vt3 = cx.tok("v")

                    def _g3(ge, n=n, kc=kc, ccd=ccd, vt3=vt3):
                        cx.wait(ge, vt3)
                        cx.inc(ge.dma_start(out=cout[n].ap()[128 * kc:128 * kc + 128, :],
                                            in_=wbb[:, 0:ccd]), "g", 16)

                    cx.G(_g3)
            def _q8(qe):
                cx.wait(qe, cx.tok("v"), cx.tok("g"))
                cx.inc(qe.dma_start(out=wb.ap().bitcast(I32)[:, 0:D] if False else wb[:, 0:D],
                                    in_=w8T.ap()), "q", 16)

            with nc.sbuf_tensor("w8s", [128, D], I32) as w8s:
                def _q8b(qe):
                    cx.wait(qe, cx.tok("v"), cx.tok("g"))
                    cx.inc(qe.dma_start(out=w8s[:], in_=w8T.ap()), "q", 16)

                cx.Q(_q8b)
                qt8 = cx.tok("q")

                def _v8(ve):
                    cx.wait(ve, qt8)
                    cx.inc(ve.tensor_copy(out=wbb[:, 0:D], in_=w8s[:]), "v")

                cx.V(_v8)
                vt8 = cx.tok("v")

                def _g8(ge):
                    cx.wait(ge, vt8)
                    cx.inc(ge.dma_start(out=w8out.ap(), in_=wbb[:, 0:D]), "g", 16)
                    cx.wait(ge, cx.tok("g"))

                cx.G(_g8)
    return nc


def _prep_W0(inputs):
    wg = np.asarray(inputs["L0_ffn_w_gate"], np.float32)
    wu = np.asarray(inputs["L0_ffn_w_up"], np.float32)
    wd = np.asarray(inputs["L0_ffn_w_down"], np.float32)
    wgT = np.ascontiguousarray(wg.T)
    wuT = np.ascontiguousarray(wu.T)
    wdT = np.ascontiguousarray(wd.T)
    gwo = np.ascontiguousarray(np.asarray(inputs["L0_attn_g_wo"], np.float32))
    lwq = np.ascontiguousarray(_perm_cols(np.asarray(inputs["L0_attn_l_wq"], np.float32)))
    lwk = np.ascontiguousarray(_perm_cols(np.asarray(inputs["L0_attn_l_wk"], np.float32)))
    lwv = np.ascontiguousarray(np.asarray(inputs["L0_attn_l_wv"], np.float32))
    lwo = np.ascontiguousarray(np.asarray(inputs["L0_attn_l_wo"], np.float32))
    woT = np.ascontiguousarray(np.asarray(inputs["L1_attn_W_O"], np.float32).T)
    wqC = np.ascontiguousarray(np.asarray(inputs["L1_attn_W_Q"], np.float32).transpose(2, 0, 1).reshape(D, D))
    wkC = np.ascontiguousarray(np.asarray(inputs["L1_attn_W_K"], np.float32).transpose(2, 0, 1).reshape(D, D))
    wvC = np.ascontiguousarray(np.asarray(inputs["L1_attn_W_V"], np.float32).transpose(2, 0, 1).reshape(D, D))
    w8T = np.ascontiguousarray(np.asarray(inputs["L1_ffn_w_int8"], np.int32).T)
    maps = []
    for r in range(NCORES):
        m = {
            "wgT": np.ascontiguousarray(wgT[128 * r:128 * r + 128]),
            "wuT": np.ascontiguousarray(wuT[128 * r:128 * r + 128]),
            "wdT": np.ascontiguousarray(wdT[512 * r:512 * r + 512]),
            "gwo": np.ascontiguousarray(gwo[128 * r:128 * r + 128]),
            "lwq": np.ascontiguousarray(lwq[128 * r:128 * r + 128]),
            "lwk": np.ascontiguousarray(lwk[128 * r:128 * r + 128]),
            "lwv": np.ascontiguousarray(lwv[128 * r:128 * r + 128]),
            "lwo": np.ascontiguousarray(lwo[128 * r:128 * r + 128]),
            "woT": np.ascontiguousarray(woT[128 * r:128 * r + 128]),
            "wqC": np.ascontiguousarray(wqC[128 * r:128 * r + 128]),
            "wkC": np.ascontiguousarray(wkC[128 * r:128 * r + 128]),
            "wvC": np.ascontiguousarray(wvC[128 * r:128 * r + 128]),
            "w8T": np.ascontiguousarray(w8T[128 * r:128 * r + 128]),
        }
        maps.append(m)
    return maps


def run_launchW0(inputs):
    if "W0" not in _cache:
        _cache["W0"] = build_launchW0()
    maps = _prep_W0(inputs)
    res = run_bass_kernel_spmd(_cache["W0"], maps, core_ids=list(range(NCORES)))
    out = {}
    for n in ("gwo", "lwq", "lwk", "lwv", "lwo", "woT", "wqC", "wkC", "wvC", "w8T"):
        out[n + "_b"] = np.concatenate([np.asarray(r[n + "_b"]) for r in res.results], axis=0)
    out["abs8"] = np.concatenate([np.asarray(r["absout"]) for r in res.results], axis=0)
    return out


def build_launchW1():
    nc = bass.Bass("TRN2", target_bir_lowering=False, debug=False)
    es = contextlib.ExitStack()
    with es:
        block = es.enter_context(nc.Block())
        cx = Ctx(nc, es, block)
        wgT = cx.inp("wgT", [128, FFN_I], F32)
        wuT = cx.inp("wuT", [128, FFN_I], F32)
        wdT = cx.inp("wdT", [512, D], F32)
        abs8 = cx.inp("abs8", [8, 4], F32)
        wgq = cx.outp("wgq", [128, FFN_I], BF16)
        wuq = cx.outp("wuq", [128, FFN_I], BF16)
        wdq = cx.outp("wdq", [512, D], BF16)
        wrec_o = cx.outp("wrec_o", [1, 4], F32)
        alloc_common(cx)
        emit_load_consts(cx)
        with (
            nc.sbuf_tensor("wb", [128, FFN_I], F32) as wb,
            nc.sbuf_tensor("wstage", [128, FFN_I], BF16) as wstage,
            nc.sbuf_tensor("a8", [8, 4], F32) as a8,
            nc.sbuf_tensor("wrow", [1, 4], F32) as wrow,
            nc.sbuf_tensor("srow", [1, 4], F32) as srow,
            nc.sbuf_tensor("s128", [128, 4], F32) as s128,
        ):
            def _q(qe):
                cx.inc(qe.dma_start(out=a8[:], in_=abs8.ap()), "q", 16)

            cx.Q(_q)
            qt = cx.tok("q")

            def _t(te):
                cx.wait(te, qt)
                cx.inc(te.matmul(out=cx.ps[4][0:1, 0:4], lhsT=cx.c["onesF"][0:8, 0:1],
                                 rhs=a8[:], start=True, stop=True), "t")

            cx.T(_t)
            tt = cx.tok("t")

            def _v(ve):
                cx.wait(ve, tt)
                ve.tensor_scalar(out=wrow[:], in0=cx.ps[4][0:1, 0:4],
                                 scalar1=1.0 / (D * FFN_I), scalar2=1e-5, op0=ALU.mult, op1=ALU.max)
                ve.drain()
                ve.reciprocal(out=srow[:], in_=wrow[:])
                ve.drain()
                cx.inc(ve.tensor_copy(out=srow[0:1, 3:4], in_=srow[0:1, 3:4]), "v")

            cx.V(_v)
            vt = cx.tok("v")
            bcs = []
            for col in range(3):
                bt = emit_bcast(cx, srow[0:1, col:col + 1], 5, 128, 1, after=(vt, cx.tok("v"),))

                def _vb(ve, col=col, bt=bt):
                    cx.wait(ve, bt)
                    cx.inc(ve.tensor_copy(out=s128[:, col:col + 1], in_=cx.ps[5][0:128, 0:1]), "v")

                cx.V(_vb)

            jobs = [(wgT, wgq, 1, FFN_I, 0), (wuT, wuq, 1, FFN_I, 1), (wdT, wdq, 4, D, 2)]
            for (src, dst, nch, ncol, col) in jobs:
                for kc in range(nch):
                    def _q2(qe, src=src, kc=kc, ncol=ncol):
                        cx.wait(qe, cx.tok("v"), cx.tok("g"))
                        cx.inc(qe.dma_start(out=wb[:, 0:ncol],
                                            in_=src.ap()[128 * kc:128 * kc + 128, :]), "q", 16)

                    cx.Q(_q2)
                    qt2 = cx.tok("q")
                    for sl in range(ncol // 512):
                        def _v2(ve, sl=sl, col=col, qt2=qt2):
                            cx.wait(ve, qt2)
                            cx.inc(ve.tensor_scalar(out=cx.scr["D"][:], in0=wb[:, 512 * sl:512 * sl + 512],
                                                    scalar1=s128[:, col:col + 1], scalar2=None,
                                                    op0=ALU.mult), "v")

                        cx.V(_v2)
                        emit_round_clip(cx, cx.scr["D"][:], wstage[:, 512 * sl:512 * sl + 512],
                                        -1, 1, after=(cx.tok("v"),))
                    vt2 = cx.tok("v")

                    def _g2(ge, dst=dst, kc=kc, ncol=ncol, vt2=vt2):
                        cx.wait(ge, vt2)
                        cx.inc(ge.dma_start(out=dst.ap()[128 * kc:128 * kc + 128, :],
                                            in_=wstage[:, 0:ncol]), "g", 16)

                    cx.G(_g2)

            def _g3(ge):
                cx.wait(ge, cx.tok("v"))
                cx.inc(ge.dma_start(out=wrec_o.ap(), in_=wrow[:]), "g", 16)
                cx.wait(ge, cx.tok("g"))

            cx.G(_g3)
    return nc


def run_launchW1(inputs, w0out):
    if "W1" not in _cache:
        _cache["W1"] = build_launchW1()
    base = _prep_W0(inputs)
    maps = []
    for r in range(NCORES):
        m = dict(_consts_np())
        m["wgT"] = base[r]["wgT"]
        m["wuT"] = base[r]["wuT"]
        m["wdT"] = base[r]["wdT"]
        m["abs8"] = np.ascontiguousarray(w0out["abs8"])
        maps.append(m)
    res = run_bass_kernel_spmd(_cache["W1"], maps, core_ids=list(range(NCORES)))
    out = {
        "wgq": np.concatenate([np.asarray(r["wgq"]) for r in res.results], axis=0),
        "wuq": np.concatenate([np.asarray(r["wuq"]) for r in res.results], axis=0),
        "wdq": np.concatenate([np.asarray(r["wdq"]) for r in res.results], axis=0),
        "wrec": np.asarray(res.results[0]["wrec_o"]),
    }
    return out


def emit_wtiles(cx, dram_t, col0, ncols, nk, wtile, after=()):
    def _q(qe):
        cx.wait(qe, *after)
        src = dram_t.ap()[0:128 * nk, col0:col0 + ncols].rearrange("(k p) c -> p k c", p=128)
        dst = wtile[:, 0:nk * ncols].rearrange("p (k c) -> p k c", k=nk)
        cx.inc(qe.dma_start(out=dst, in_=src), "q", 16)

    cx.Q(_q)
    return cx.tok("q")


def emit_local_attn(cx, j, qcur, k_l, v_l, o_dst_ap_fn, lmask_dram, lmask_g):
    scr, c, ps = cx.scr, cx.c, cx.ps
    pt = [scr["pt0"], scr["pt1"]]
    for g in range(4):
        def _qm(qe, g=g):
            cx.wait(qe, cx.tok("t"))
            cx.inc(qe.dma_start(out=lmask_g[:],
                                in_=lmask_dram.ap()[:, 12 * g * 128:12 * g * 128 + 1536]), "q", 16)

        cx.Q(_qm)
        qm = cx.tok("q")
        t0, s0, v0 = cx.cnt["t"], cx.cnt["s"], cx.cnt["v"]

        def _t(te, g=g, t0=t0, s0=s0, v0=v0, qm=qm):
            cx.wait(te, qm)
            for u in range(4):
                jt = 4 * g + u
                pi = u % 2
                if u >= 2:
                    cx.wait(te, ("s", s0 + u - 1))
                else:
                    cx.wait(te, ("v", v0))
                for sl in range(3):
                    te.matmul(out=ps[pi][:, 128 * sl:128 * sl + 128], lhsT=c["identB"][:],
                              rhs=lmask_g[:, (u * 3 + sl) * 128:(u * 3 + sl) * 128 + 128],
                              start=True, stop=False)
                    cx.inc(te.matmul(out=ps[pi][:, 128 * sl:128 * sl + 128],
                                     lhsT=k_l[:, 128 * (jt + sl):128 * (jt + sl) + 128],
                                     rhs=qcur[:, 128 * jt:128 * jt + 128],
                                     start=False, stop=True), "t")
                cx.wait(te, ("s", s0 + u + 1))
                for sl in range(3):
                    cx.inc(te.matmul(out=ps[2][0:65, 128 * u:128 * u + 128],
                                     lhsT=v_l[:, 65 * (jt + sl):65 * (jt + sl) + 65],
                                     rhs=pt[pi][:, 128 * sl:128 * sl + 128],
                                     start=(sl == 0), stop=(sl == 2)), "t")

        cx.T(_t)

        def _s(se, t0=t0):
            for u in range(4):
                pi = u % 2
                cx.wait(se, ("t", t0 + 6 * u + 3))
                cx.inc(se.activation(out=pt[pi][:, 0:384], in_=ps[pi][:, 0:384],
                                     func=AF.Exp, scale=c["c0125"][:], bias=c["biasC0"][:]), "s")

        cx.Sc(_s)
        tend = cx.tok("t")

        def _v(ve, tend=tend):
            cx.wait(ve, tend)
            cx.inc(ve.reciprocal(out=scr["r0"][0:1, 0:512], in_=ps[2][64:65, 0:512]), "v")

        cx.V(_v)
        bc = emit_bcast(cx, scr["r0"][0:1, 0:512], 3, 64, 512, after=(cx.tok("v"),))

        def _v2(ve, g=g, bc=bc):
            cx.wait(ve, bc)
            ve.tensor_copy(out=scr["E"][0:64, 0:512], in_=ps[2][0:64, 0:512])
            ve.drain()
            cx.inc(ve.tensor_tensor(out=o_dst_ap_fn(g), in0=scr["E"][0:64, 0:512],
                                    in1=ps[3][0:64, 0:512], op=ALU.mult), "v")

        cx.V(_v2)
    return cx.tok("v")


B_BLOCKS = [(0, 256)] + [(256 + 512 * i, 512) for i in range(4)]


def build_launchB(dbg=False):
    nc = bass.Bass("TRN2", target_bir_lowering=False, debug=False)
    es = contextlib.ExitStack()
    with es:
        block = es.enter_context(nc.Block())
        cx = Ctx(nc, es, block)
        x0own = cx.inp("x0own", [D, TOK], F32)
        x0halo = cx.inp("x0halo", [D, HALO], F32)
        ogown = cx.inp("ogown", [D, TOK], BF16)
        oghalo = cx.inp("oghalo", [D, HALO], BF16)
        gwo_b = cx.inp("gwo_b", [D, D], BF16)
        n2w = cx.inp("n2w", [128, 8], F32)
        lwq_b = cx.inp("lwq_b", [D, D], BF16)
        lwk_b = cx.inp("lwk_b", [D, LAT], BF16)
        lwv_b = cx.inp("lwv_b", [D, LAT], BF16)
        lwo_b = cx.inp("lwo_b", [D, D], BF16)
        wgq = cx.inp("wgq", [D, FFN_I], BF16)
        wuq = cx.inp("wuq", [D, FFN_I], BF16)
        wdq = cx.inp("wdq", [FFN_I, D], BF16)
        wrec = cx.inp("wrec", [1, 4], F32)
        posKV = cx.inp("posKV", [1, NKV], F32)
        lmask = cx.inp("lmask", [128, 48 * 128], BF16)
        x2out = cx.outp("x2out", [D, TOK], F32)
        x2nout = cx.outp("x2nout", [D, TOK], BF16)
        if dbg:
            dbg_h1 = cx.outp("dbg_h1", [D, NKV], F32)
            dbg_cos = cx.outp("dbg_cos", [64, NKV], F32)
            dbg_sin = cx.outp("dbg_sin", [64, NKV], F32)
            dbg_q = cx.outp("dbg_q", [64, TOK], F32)
            dbg_k = cx.outp("dbg_k", [64, NKV], F32)
            dbg_xq = cx.outp("dbg_xq", [D, 256], F32)
            dbg_hq = cx.outp("dbg_hq", [128, 32 * 256], F32)
            dbg_hf = cx.outp("dbg_hf", [128, 32 * 256], F32)
            dbg_rows = cx.outp("dbg_rows", [4, 256], F32)
            dbg_fsb = cx.outp("dbg_fsb", [D, 256], F32)
            dbg_g = cx.outp("dbg_g", [128, 32 * 256], F32)
            dbg_lin = cx.outp("dbg_lin", [D, NKV], BF16)
            dbg_ol = cx.outp("dbg_ol", [D, TOK], BF16)
            dbg_x1 = cx.outp("dbg_x1", [D, TOK], F32)

        def dump(dst, srcs, cols):
            if not dbg:
                return

            def _g(ge):
                cx.wait(ge, cx.tok("v"), cx.tok("t"), cx.tok("s"))
                for ci in range(8):
                    cx.inc(ge.dma_start(out=dst.ap()[128 * ci:128 * ci + 128, :],
                                        in_=srcs[ci][:, 0:cols] if cols else srcs[ci][:]), "g", 16)

            cx.G(_g)
            cx.barrier()

        alloc_common(cx)
        h1 = [cx.sbuf("h1_%d" % i, [128, NKV], F32) for i in range(8)]
        n2w_s = cx.sbuf("n2w_s", [128, 8], F32)
        lmask_g = cx.sbuf("lmask_g", [128, 12 * 128], BF16)
        wrec_s = cx.sbuf("wrec_s", [1, 4], F32)
        emit_load_consts(cx)

        def _q0(qe):
            cx.inc(qe.dma_start(out=n2w_s[:], in_=n2w.ap()), "q", 16)
            cx.inc(qe.dma_start(out=wrec_s[:], in_=wrec.ap()), "q", 16)

        cx.Q(_q0)
        cx.barrier()

        with contextlib.ExitStack() as ph:
            gwo_s = ph.enter_context(nc.sbuf_tensor("gwo_s", [128, 8 * D], BF16))
            xf = [ph.enter_context(nc.sbuf_tensor("xf%d" % i, [128, 512], F32)) for i in range(8)]
            ogs = [ph.enter_context(nc.sbuf_tensor("ogs%d" % i, [128, 512], BF16)) for i in range(8)]

            def _q1(qe):
                for k in range(8):
                    cx.inc(qe.dma_start(out=gwo_s[:, D * k:D * k + D],
                                        in_=gwo_b.ap()[128 * k:128 * k + 128, :]), "q", 16)

            cx.Q(_q1)

            for (off, nn) in B_BLOCKS:
                def _q2(qe, off=off, nn=nn):
                    cx.wait(qe, cx.tok("t"), cx.tok("v"))
                    for ci in range(8):
                        src = x0halo.ap()[128 * ci:128 * ci + 128, :] if off == 0 else \
                            x0own.ap()[128 * ci:128 * ci + 128, off - HALO:off - HALO + nn]
                        cx.inc(qe.dma_start(out=xf[ci][:, 0:nn], in_=src), "q", 16)
                        osrc = oghalo.ap()[128 * ci:128 * ci + 128, :] if off == 0 else \
                            ogown.ap()[128 * ci:128 * ci + 128, off - HALO:off - HALO + nn]
                        cx.inc(qe.dma_start(out=ogs[ci][:, 0:nn], in_=osrc), "q", 16)

                cx.Q(_q2)
                qt = cx.tok("q")
                st = emit_stats(cx, [xf[ci][:, 0:nn] for ci in range(8)], nn, 1e-5, True,
                                cx.scr["r0"][0:1, 0:nn], cx.scr["r1"][0:1, 0:nn],
                                cx.scr["r2"][0:1, 0:nn], after=(qt,))
                b1 = emit_bcast(cx, cx.scr["r2"][0:1, 0:nn], 0, 128, nn, after=(st,))
                b2 = emit_bcast(cx, cx.scr["r1"][0:1, 0:nn], 1, 128, nn, after=(st,))

                def _v(ve, off=off, nn=nn, b2=b2):
                    cx.wait(ve, b2)
                    for ci in range(8):
                        ve.tensor_tensor(out=h1[ci][:, off:off + nn], in0=xf[ci][:, 0:nn],
                                         in1=cx.ps[1][:, 0:nn], op=ALU.mult)
                        ve.drain()
                        ve.tensor_tensor(out=h1[ci][:, off:off + nn], in0=h1[ci][:, off:off + nn],
                                         in1=cx.ps[0][:, 0:nn], op=ALU.subtract)
                        ve.drain()
                    cx.inc(ve.tensor_copy(out=scr_mark(cx), in_=scr_mark(cx)), "v")

                cx.V(_v)
                for m in range(8):
                    def _t(te, off=off, nn=nn, m=m, qt=qt):
                        cx.wait(te, cx.tok("v"), qt)
                        mm = None
                        for k in range(8):
                            mm = te.matmul(out=cx.ps[0][:, 0:nn],
                                           lhsT=gwo_s[:, D * k + 128 * m:D * k + 128 * m + 128],
                                           rhs=ogs[k][:, 0:nn],
                                           start=(k == 0), stop=(k == 7))
                        cx.inc(mm, "t")

                    cx.T(_t)
                    t1 = cx.tok("t")

                    def _v(ve, off=off, nn=nn, m=m, t1=t1):
                        cx.wait(ve, t1)
                        cx.inc(ve.tensor_tensor(out=h1[m][:, off:off + nn],
                                                in0=h1[m][:, off:off + nn],
                                                in1=cx.ps[0][:, 0:nn], op=ALU.add), "v")

                    cx.V(_v)
            cx.barrier()
            dump(dbg_h1, h1, NKV) if dbg else None

        po = contextlib.ExitStack()
        o_l = [po.enter_context(nc.sbuf_tensor("o_l%d" % i, [128, TOK], BF16)) for i in range(8)]
        with contextlib.ExitStack() as ph:
            l_in = [ph.enter_context(nc.sbuf_tensor("lin%d" % i, [128, NKV], BF16)) for i in range(8)]
            cosL = ph.enter_context(nc.sbuf_tensor("cosL", [64, NKV], F32))
            sinL = ph.enter_context(nc.sbuf_tensor("sinL", [64, NKV], F32))
            kk_l = ph.enter_context(nc.sbuf_tensor("kkl", [64, NKV], BF16))
            vtm_l = ph.enter_context(nc.sbuf_tensor("vtml", [128, 65 * 18], BF16))
            vstage = ph.enter_context(nc.sbuf_tensor("vstage", [64, 512], BF16))
            wt = ph.enter_context(nc.sbuf_tensor("wt", [128, 512], BF16))
            qcur = ph.enter_context(nc.sbuf_tensor("qcurB", [64, TOK], BF16))

            emit_rope_tables(cx, BASE_L, posKV, NKV, cosL, sinL)

            for (off, nn) in B_BLOCKS:
                st = emit_stats(cx, [h1[ci][:, off:off + nn] for ci in range(8)], nn, 1e-6, False,
                                None, cx.scr["r1"][0:1, 0:nn], None, after=(cx.tok("v"),))
                b3 = emit_bcast(cx, cx.scr["r1"][0:1, 0:nn], 0, 128, nn, after=(st,))

                def _v(ve, off=off, nn=nn, b3=b3):
                    cx.wait(ve, b3)
                    for ci in range(8):
                        ve.tensor_tensor(out=cx.scr["D"][:, 0:nn], in0=h1[ci][:, off:off + nn],
                                         in1=cx.ps[0][:, 0:nn], op=ALU.mult)
                        ve.drain()
                        ve.tensor_scalar(out=l_in[ci][:, off:off + nn], in0=cx.scr["D"][:, 0:nn],
                                         scalar1=n2w_s[:, ci:ci + 1], scalar2=None, op0=ALU.mult)
                        ve.drain()
                    cx.inc(ve.tensor_copy(out=scr_mark(cx), in_=scr_mark(cx)), "v")

                cx.V(_v)
            cx.barrier()
            dump(dbg_lin, l_in, NKV) if dbg else None

            for a in range(4):
                def _g(ge, a=a):
                    cx.wait(ge, cx.tok("v"), cx.tok("t"))
                    cx.inc(ge.memset(vtm_l[:], 1.0), "g")

                cx.G(_g)
                g0 = cx.tok("g")
                qtw = emit_wtiles(cx, lwk_b, 64 * a, 64, 8, wt, after=(cx.tok("t"), cx.tok("v")))
                lkpipe = mk_pipe()
                for (off, nn) in B_BLOCKS:
                    def _w(te, qtw=qtw):
                        cx.wait(te, qtw)

                    cx.T(_w)
                    emit_proj_rope(cx, [wt[:, 64 * k:64 * k + 64] for k in range(8)],
                                   [l_in[ci][:, off:off + nn] for ci in range(8)],
                                   8, off, nn, cosL, sinL,
                                   kk_l[:, off:off + nn], True, None, pipe=lkpipe)
                pipe_flush(cx, lkpipe)
                qtw = emit_wtiles(cx, lwv_b, 64 * a, 64, 8, wt, after=(cx.tok("t"), cx.tok("v")))
                for (off, nn) in B_BLOCKS:
                    def _t(te, off=off, nn=nn, qtw=qtw):
                        cx.wait(te, qtw, cx.tok("v"))
                        mm = None
                        for k in range(8):
                            mm = te.matmul(out=cx.ps[0][0:64, 0:nn], lhsT=wt[:, 64 * k:64 * k + 64],
                                           rhs=l_in[k][:, off:off + nn],
                                           start=(k == 0), stop=(k == 7))
                        cx.inc(mm, "t")

                    cx.T(_t)
                    t1 = cx.tok("t")

                    def _v(ve, nn=nn, t1=t1):
                        cx.wait(ve, t1)
                        cx.inc(ve.tensor_copy(out=vstage[:, 0:nn], in_=cx.ps[0][0:64, 0:nn]), "v")

                    cx.V(_v)
                    v1 = cx.tok("v")
                    for u in range(nn // 128):
                        def _t2(te, u=u, v1=v1):
                            cx.wait(te, v1, cx.tok("v"))
                            cx.inc(te.transpose(out=cx.pst[0:128, 0:64],
                                                in_=vstage[:, 128 * u:128 * u + 128],
                                                identity=cx.c["identB"][0:64, 0:64]), "t")

                        cx.T(_t2)
                        t2 = cx.tok("t")
                        kt = off // 128 + u

                        def _v2(ve, kt=kt, t2=t2):
                            cx.wait(ve, t2, g0)
                            cx.inc(ve.tensor_copy(out=vtm_l[:, 65 * kt:65 * kt + 64],
                                                  in_=cx.pst[0:128, 0:64]), "v")

                        cx.V(_v2)

                if dbg and a == 0:
                    def _gd(ge):
                        cx.wait(ge, cx.tok("v"), cx.tok("s"))
                        cx.inc(ge.dma_start(out=dbg_cos.ap(), in_=cosL[:]), "g", 16)
                        cx.inc(ge.dma_start(out=dbg_sin.ap(), in_=sinL[:]), "g", 16)
                        cx.inc(ge.dma_start(out=dbg_k.ap(), in_=kk_l[:]), "g", 16)

                    cx.G(_gd)
                    cx.barrier()
                for j in range(4 * a, 4 * a + 4):
                    qtw = emit_wtiles(cx, lwq_b, 64 * j, 64, 8, wt, after=(cx.tok("t"), cx.tok("v")))

                    def _w(te, qtw=qtw):
                        cx.wait(te, qtw)

                    cx.T(_w)
                    lqpipe = mk_pipe()
                    for tb in range(4):
                        off = HALO + 512 * tb
                        emit_proj_rope(cx, [wt[:, 64 * k:64 * k + 64] for k in range(8)],
                                       [l_in[ci][:, off:off + 512] for ci in range(8)],
                                       8, off, 512, cosL, sinL,
                                       qcur[:, 512 * tb:512 * tb + 512], True, None,
                                       pipe=lqpipe)
                    pipe_flush(cx, lqpipe)

                    if dbg and j == 0:
                        def _gq(ge):
                            cx.wait(ge, cx.tok("v"), cx.tok("s"))
                            cx.inc(ge.dma_start(out=dbg_q.ap(), in_=qcur[:]), "g", 16)

                        cx.G(_gq)
                        cx.barrier()

                    def o_dst(g, j=j):
                        return o_l[j // 2][64 * (j % 2):64 * (j % 2) + 64, 512 * g:512 * g + 512]

                    emit_local_attn(cx, j, qcur, kk_l, vtm_l, o_dst, lmask, lmask_g)
            cx.barrier()
            dump(dbg_ol, o_l, TOK) if dbg else None

        with contextlib.ExitStack() as ph:
            wt = ph.enter_context(nc.sbuf_tensor("wtB3", [128, 8 * 128], BF16))
            xf = [ph.enter_context(nc.sbuf_tensor("xfB3_%d" % i, [128, 512], F32)) for i in range(8)]
            for tb in range(4):
                for m in range(8):
                    qtw = emit_wtiles(cx, lwo_b, 128 * m, 128, 8, wt, after=(cx.tok("t"),))

                    def _t(te, tb=tb, m=m, qtw=qtw):
                        cx.wait(te, qtw, cx.tok("v"))
                        mm = None
                        for k in range(8):
                            mm = te.matmul(out=cx.ps[0][:, 0:512], lhsT=wt[:, 128 * k:128 * k + 128],
                                           rhs=o_l[k][:, 512 * tb:512 * tb + 512],
                                           start=(k == 0), stop=(k == 7))
                        cx.inc(mm, "t")

                    cx.T(_t)
                    t1 = cx.tok("t")

                    def _v(ve, tb=tb, m=m, t1=t1):
                        cx.wait(ve, t1)
                        cx.inc(ve.tensor_tensor(out=h1[m][:, HALO + 512 * tb:HALO + 512 * tb + 512],
                                                in0=h1[m][:, HALO + 512 * tb:HALO + 512 * tb + 512],
                                                in1=cx.ps[0][:, 0:512], op=ALU.add), "v")

                    cx.V(_v)
            cx.barrier()
            for tb in range(4):
                off = HALO + 512 * tb

                def _q(qe, tb=tb):
                    cx.wait(qe, cx.tok("v"))
                    for ci in range(8):
                        cx.inc(qe.dma_start(out=xf[ci][:],
                                            in_=x0own.ap()[128 * ci:128 * ci + 128,
                                                           512 * tb:512 * tb + 512]), "q", 16)

                cx.Q(_q)
                qt = cx.tok("q")
                st = emit_stats(cx, [h1[ci][:, off:off + 512] for ci in range(8)], 512, 1e-5, True,
                                cx.scr["r0"][0:1, 0:512], cx.scr["r1"][0:1, 0:512],
                                cx.scr["r2"][0:1, 0:512], after=(cx.tok("v"),))
                b1 = emit_bcast(cx, cx.scr["r2"][0:1, 0:512], 0, 128, 512, after=(st,))
                b2 = emit_bcast(cx, cx.scr["r1"][0:1, 0:512], 1, 128, 512, after=(st,))

                def _v(ve, off=off, qt=qt, b2=b2):
                    cx.wait(ve, b2, qt)
                    for ci in range(8):
                        ve.tensor_tensor(out=h1[ci][:, off:off + 512], in0=h1[ci][:, off:off + 512],
                                         in1=cx.ps[1][:, 0:512], op=ALU.mult)
                        ve.drain()
                        ve.tensor_tensor(out=h1[ci][:, off:off + 512], in0=h1[ci][:, off:off + 512],
                                         in1=cx.ps[0][:, 0:512], op=ALU.subtract)
                        ve.drain()
                        ve.tensor_tensor(out=h1[ci][:, off:off + 512], in0=h1[ci][:, off:off + 512],
                                         in1=xf[ci][:], op=ALU.add)
                        ve.drain()
                    cx.inc(ve.tensor_copy(out=scr_mark(cx), in_=scr_mark(cx)), "v")

                cx.V(_v)
            cx.barrier()
            if dbg:
                def _gx(ge):
                    cx.wait(ge, cx.tok("v"))
                    for ci in range(8):
                        cx.inc(ge.dma_start(out=dbg_x1.ap()[128 * ci:128 * ci + 128, :],
                                            in_=h1[ci][:, HALO:NKV]), "g", 16)

                cx.G(_gx)
                cx.barrier()
        po.close()

        NT = 256
        with contextlib.ExitStack() as ph:
            xnT = ph.enter_context(nc.sbuf_tensor("xnT", [128, 8 * NT], F32))
            xqT = ph.enter_context(nc.sbuf_tensor("xqT", [128, 8 * NT], BF16))
            fsb = [ph.enter_context(nc.sbuf_tensor("fsb%d" % i, [128, NT], F32)) for i in range(8)]
            hf = ph.enter_context(nc.sbuf_tensor("hf", [128, 32 * NT], F32))
            hq = ph.enter_context(nc.sbuf_tensor("hq", [128, 32 * NT], BF16))
            wg0 = ph.enter_context(nc.sbuf_tensor("wg0", [128, 1024], BF16))
            wg1 = ph.enter_context(nc.sbuf_tensor("wg1", [128, 1024], BF16))
            wu0 = ph.enter_context(nc.sbuf_tensor("wu0", [128, 1024], BF16))
            wu1 = ph.enter_context(nc.sbuf_tensor("wu1", [128, 1024], BF16))
            wd0 = ph.enter_context(nc.sbuf_tensor("wd0", [128, 4096], BF16))
            wd1 = ph.enter_context(nc.sbuf_tensor("wd1", [128, 4096], BF16))
            amx = ph.enter_context(nc.sbuf_tensor("amx", [128, 8], F32))
            x2stg = ph.enter_context(nc.sbuf_tensor("x2stg", [128, NT], BF16))
            rowx = ph.enter_context(nc.sbuf_tensor("rowx", [1, NT], F32))
            rowy = ph.enter_context(nc.sbuf_tensor("rowy", [1, NT], F32))
            rowz = ph.enter_context(nc.sbuf_tensor("rowz", [1, NT], F32))

            def xnc(ci):
                return xnT[:, NT * ci:NT * ci + NT]

            def hs(m):
                return hf[:, NT * m:NT * m + NT]

            def hqs(m):
                return hq[:, NT * m:NT * m + NT]

            for tc in range(2048 // NT):
                off = HALO + NT * tc
                st = emit_stats(cx, [h1[ci][:, off:off + NT] for ci in range(8)], NT, 1e-5, True,
                                cx.scr["r0"][0:1, 0:NT], cx.scr["r1"][0:1, 0:NT],
                                cx.scr["r2"][0:1, 0:NT], after=(cx.tok("v"), cx.tok("g")))
                b1 = emit_bcast(cx, cx.scr["r2"][0:1, 0:NT], 2, 128, NT, after=(st,))
                b2 = emit_bcast(cx, cx.scr["r1"][0:1, 0:NT], 3, 128, NT, after=(st,))

                def _v1(ve, off=off, b2=b2):
                    cx.wait(ve, b2)
                    for ci in range(8):
                        ve.tensor_tensor(out=xnc(ci), in0=h1[ci][:, off:off + NT],
                                         in1=cx.ps[3][:, 0:NT], op=ALU.mult)
                    ve.drain()
                    for ci in range(8):
                        ve.tensor_tensor(out=xnc(ci), in0=xnc(ci),
                                         in1=cx.ps[2][:, 0:NT], op=ALU.subtract)
                    cx.inc(ve.tensor_copy(out=scr_mark(cx), in_=scr_mark(cx)), "v")

                cx.V(_v1)

                emit_absmax(cx, [xnc(ci) for ci in range(8)], amx, rowx, rowy, NT)

                sb = emit_bcast(cx, rowy[0:1, 0:NT], 2, 128, NT, after=(cx.tok("v"),))

                def _vE3(ve, sb=sb):
                    cx.wait(ve, sb)
                    cx.inc(ve.tensor_copy(out=cx.scr["E"][:, 0:NT], in_=cx.ps[2][:, 0:NT]), "v")

                cx.V(_vE3)
                e3 = cx.tok("v")
                t3 = cx.tok("t")

                def _g3(ge, e3=e3, t3=t3):
                    cx.wait(ge, e3, t3, cx.tok("g"))
                    for p in range(4):
                        for j in range(2):
                            dst = (cx.scr["D"][:, 256 * j:256 * j + 256] if p == 0 else
                                   xnT[:, 512 * (p - 1) + 256 * j:512 * (p - 1) + 256 * j + 256])
                            cx.inc(ge.tensor_tensor(
                                out=dst, in0=xnT[:, 512 * p + 256 * j:512 * p + 256 * j + 256],
                                in1=cx.scr["E"][:, 0:NT], op=ALU.mult), "g")

                cx.G(_g3)
                g3b = cx.cnt["g"] - 8
                for p in range(4):
                    s_ap = (cx.scr["D"][:, 0:512] if p == 0 else
                            xnT[:, 512 * (p - 1):512 * p])
                    emit_round_clip(cx, s_ap, xqT[:, 512 * p:512 * p + 512], -128, 127,
                                    n=512, after=(("g", g3b + 2 * (p + 1)),))

                vpre4 = cx.tok("v")

                def _gi4(ge, vpre4=vpre4):
                    cx.wait(ge, vpre4, cx.tok("g"))
                    ge.memset(cx.scr["E"][:, 0:NT], -3.0e38)
                    cx.inc(ge.memset(cx.scr["E"][:, NT:2 * NT], 3.0e38), "g")

                cx.G(_gi4)
                g_init4 = cx.tok("g")
                t_base = cx.cnt["t"]
                s_base = cx.cnt["s"]
                v_base = cx.cnt["v"]
                for m in range(32):
                    ev = m % 2 == 0
                    qg = emit_wtiles(cx, wgq, 128 * m, 128, 8, wg0 if ev else wg1,
                                     after=(("t", t_base + max(0, m - 1)),))
                    qu = emit_wtiles(cx, wuq, 128 * m, 128, 8, wu0 if ev else wu1,
                                     after=(qg,))
                    psG = cx.ps[0] if ev else cx.ps[2]
                    psU = cx.ps[1] if ev else cx.ps[3]
                    scrR = "A" if ev else "C"
                    scrS = "B" if ev else "D"

                    def _t(te, m=m, qu=qu, psG=psG, psU=psU, ev=ev):
                        cx.wait(te, qu, ("s", s_base + max(0, m - 1)),
                                ("v", v_base + max(0, m - 1)))
                        wgt = wg0 if ev else wg1
                        wut = wu0 if ev else wu1
                        for k in range(8):
                            te.matmul(out=psG[:, 0:NT], lhsT=wgt[:, 128 * k:128 * k + 128],
                                      rhs=xqT[:, NT * k:NT * k + NT],
                                      start=(k == 0), stop=(k == 7))
                        mm = None
                        for k in range(8):
                            mm = te.matmul(out=psU[:, 0:NT], lhsT=wut[:, 128 * k:128 * k + 128],
                                           rhs=xqT[:, NT * k:NT * k + NT],
                                           start=(k == 0), stop=(k == 7))
                        cx.inc(mm, "t")

                    cx.T(_t)
                    t1 = cx.tok("t")

                    def _s(se, t1=t1, m=m, psG=psG, scrR=scrR):
                        cx.wait(se, t1, ("v", v_base + max(0, m - 1)))
                        cx.inc(se.activation(out=cx.scr[scrR][:, 0:NT], in_=psG[:, 0:NT],
                                             func=AF.Relu), "s")

                    cx.Sc(_s)
                    s1 = cx.tok("s")

                    def _v3(ve, m=m, s1=s1, psU=psU, scrR=scrR, scrS=scrS):
                        if m == 0:
                            cx.wait(ve, s1, g_init4)
                        else:
                            cx.wait(ve, s1)
                        ve.tensor_tensor(out=cx.scr[scrS][:, 0:NT], in0=cx.scr[scrR][:, 0:NT],
                                         in1=cx.scr[scrR][:, 0:NT], op=ALU.mult)
                        ve.drain()
                        ve.tensor_tensor(out=hs(m), in0=cx.scr[scrS][:, 0:NT],
                                         in1=psU[:, 0:NT], op=ALU.mult)
                        ve.drain()
                        ve.tensor_tensor(out=cx.scr["E"][:, 0:NT], in0=cx.scr["E"][:, 0:NT],
                                         in1=hs(m), op=ALU.max)
                        cx.inc(ve.tensor_tensor(out=cx.scr["E"][:, NT:2 * NT],
                                                in0=cx.scr["E"][:, NT:2 * NT],
                                                in1=hs(m), op=ALU.min), "v")

                    cx.V(_v3)

                vfin4 = cx.tok("v")

                def _vA5(ve, vfin4=vfin4):
                    cx.wait(ve, vfin4)
                    cx.inc(ve.scalar_tensor_tensor(out=cx.scr["D"][:, 0:NT],
                                                   in0=cx.scr["E"][:, NT:2 * NT], scalar=-1.0,
                                                   in1=cx.scr["E"][:, 0:NT],
                                                   op0=ALU.mult, op1=ALU.max), "v")

                cx.V(_vA5)
                emit_absmax_tail(cx, amx, None, None, NT, raw_amax_row=rowz)

                def _v4(ve):
                    cx.wait(ve, cx.tok("v"), cx.tok("s"))
                    ve.tensor_tensor(out=rowy[0:1, 0:NT], in0=rowx[0:1, 0:NT],
                                     in1=rowx[0:1, 0:NT], op=ALU.mult)
                    ve.drain()
                    ve.tensor_tensor(out=rowy[0:1, 0:NT], in0=rowy[0:1, 0:NT],
                                     in1=rowx[0:1, 0:NT], op=ALU.mult)
                    ve.drain()
                    ve.tensor_scalar(out=rowy[0:1, 0:NT], in0=rowy[0:1, 0:NT],
                                     scalar1=wrec_s[0:1, 0:1], scalar2=wrec_s[0:1, 0:1],
                                     op0=ALU.mult, op1=ALU.mult)
                    ve.drain()
                    ve.tensor_scalar(out=rowy[0:1, 0:NT], in0=rowy[0:1, 0:NT],
                                     scalar1=wrec_s[0:1, 1:2], scalar2=None, op0=ALU.mult)
                    ve.drain()
                    ve.tensor_tensor(out=rowz[0:1, 0:NT], in0=rowz[0:1, 0:NT],
                                     in1=rowy[0:1, 0:NT], op=ALU.mult)
                    ve.drain()
                    ve.tensor_scalar(out=rowz[0:1, 0:NT], in0=rowz[0:1, 0:NT],
                                     scalar1=1e-5, scalar2=None, op0=ALU.max)
                    ve.drain()
                    ve.reciprocal(out=rowx[0:1, 0:NT], in_=rowz[0:1, 0:NT])
                    ve.drain()
                    ve.tensor_tensor(out=rowx[0:1, 0:NT], in0=rowx[0:1, 0:NT],
                                     in1=rowy[0:1, 0:NT], op=ALU.mult)
                    ve.drain()
                    ve.tensor_scalar(out=rowx[0:1, 0:NT], in0=rowx[0:1, 0:NT],
                                     scalar1=127.0, scalar2=None, op0=ALU.mult)
                    ve.drain()
                    ve.tensor_scalar(out=rowz[0:1, 0:NT], in0=rowz[0:1, 0:NT],
                                     scalar1=1.0 / 127.0, scalar2=wrec_s[0:1, 2:3],
                                     op0=ALU.mult, op1=ALU.mult)
                    ve.drain()
                    cx.inc(ve.tensor_copy(out=scr_mark(cx), in_=scr_mark(cx)), "v")

                cx.V(_v4)

                fb = emit_bcast(cx, rowx[0:1, 0:NT], 2, 128, NT, after=(cx.tok("v"),))

                def _vE6(ve, fb=fb):
                    cx.wait(ve, fb)
                    cx.inc(ve.tensor_copy(out=cx.scr["E"][:, 0:NT], in_=cx.ps[2][:, 0:NT]), "v")

                cx.V(_vE6)
                e6 = cx.tok("v")
                t6 = cx.tok("t")

                def _g6(ge, e6=e6, t6=t6):
                    cx.wait(ge, e6, t6, cx.tok("g"))
                    for p in range(16):
                        for j in range(2):
                            dst = (cx.scr["D"][:, 256 * j:256 * j + 256] if p == 0 else
                                   hf[:, 512 * (p - 1) + 256 * j:512 * (p - 1) + 256 * j + 256])
                            cx.inc(ge.tensor_tensor(
                                out=dst, in0=hf[:, 512 * p + 256 * j:512 * p + 256 * j + 256],
                                in1=cx.scr["E"][:, 0:NT], op=ALU.mult), "g")

                cx.G(_g6)
                g6b = cx.cnt["g"] - 32
                for p in range(16):
                    s_ap = (cx.scr["D"][:, 0:512] if p == 0 else
                            hf[:, 512 * (p - 1):512 * p])
                    emit_round_clip(cx, s_ap, hq[:, 512 * p:512 * p + 512], -128, 127,
                                    n=512, after=(("g", g6b + 2 * (p + 1)),))

                fdb = emit_bcast(cx, rowz[0:1, 0:NT], 3, 128, NT, after=(cx.tok("v"),))

                def _vf(ve, fdb=fdb, g6e=("g", g6b + 32)):
                    cx.wait(ve, fdb, g6e)
                    cx.inc(ve.tensor_copy(out=cx.scr["E"][:, 0:NT], in_=cx.ps[3][:, 0:NT]), "v")

                cx.V(_vf)
                fdb = cx.tok("v")
                t_base2 = cx.cnt["t"]
                v_base2 = cx.cnt["v"]
                for m2 in range(8):
                    qd = emit_wtiles(cx, wdq, 128 * m2, 128, 32, wd0 if m2 % 2 == 0 else wd1,
                                     after=(("t", t_base2 + max(0, m2 - 1)),))
                    psD = cx.ps[0] if m2 % 2 == 0 else cx.ps[1]

                    def _t2(te, m2=m2, qd=qd, psD=psD):
                        cx.wait(te, qd, ("v", v_base2 + max(0, m2 - 1)))
                        wdt = wd0 if m2 % 2 == 0 else wd1
                        mm = None
                        for k in range(32):
                            mm = te.matmul(out=psD[:, 0:NT], lhsT=wdt[:, 128 * k:128 * k + 128],
                                           rhs=hqs(k), start=(k == 0), stop=(k == 31))
                        cx.inc(mm, "t")

                    cx.T(_t2)
                    t2 = cx.tok("t")

                    def _v6(ve, m2=m2, t2=t2, fdb=fdb, psD=psD):
                        cx.wait(ve, t2, fdb)
                        cx.inc(ve.tensor_tensor(out=fsb[m2][:], in0=psD[:, 0:NT],
                                                in1=cx.scr["E"][:, 0:NT], op=ALU.mult), "v")

                    cx.V(_v6)

                if dbg and tc == 4:
                    def _gd4(ge, tc=tc):
                        cx.wait(ge, cx.tok("v"), cx.tok("s"), cx.tok("t"))
                        for ci in range(8):
                            cx.inc(ge.dma_start(out=dbg_xq.ap()[128 * ci:128 * ci + 128, :],
                                                in_=xq[ci][:]), "g", 16)
                            cx.inc(ge.dma_start(out=dbg_fsb.ap()[128 * ci:128 * ci + 128, :],
                                                in_=fsb[ci][:]), "g", 16)
                        cx.inc(ge.dma_start(out=dbg_hq.ap(), in_=hq[:]), "g", 16)
                        cx.inc(ge.dma_start(out=dbg_hf.ap(), in_=hf[:]), "g", 16)
                        cx.inc(ge.dma_start(out=dbg_rows.ap()[0:1, :], in_=rowx[0:1, 0:NT]), "g", 16)
                        cx.inc(ge.dma_start(out=dbg_rows.ap()[1:2, :], in_=rowy[0:1, 0:NT]), "g", 16)
                        cx.inc(ge.dma_start(out=dbg_rows.ap()[2:3, :], in_=rowz[0:1, 0:NT]), "g", 16)

                    cx.G(_gd4)
                    cx.barrier()

                st2 = emit_stats(cx, [fsb[ci][:] for ci in range(8)], NT, 1e-5, True,
                                 cx.scr["r0"][0:1, 0:NT], cx.scr["r1"][0:1, 0:NT],
                                 cx.scr["r2"][0:1, 0:NT], after=(cx.tok("v"),))
                b1 = emit_bcast(cx, cx.scr["r2"][0:1, 0:NT], 2, 128, NT, after=(st2,))
                b2 = emit_bcast(cx, cx.scr["r1"][0:1, 0:NT], 3, 128, NT, after=(st2,))

                def _v7(ve, off=off, b2=b2):
                    cx.wait(ve, b2)
                    for ci in range(8):
                        ve.tensor_tensor(out=fsb[ci][:], in0=fsb[ci][:],
                                         in1=cx.ps[3][:, 0:NT], op=ALU.mult)
                    ve.drain()
                    for ci in range(8):
                        ve.tensor_tensor(out=fsb[ci][:], in0=fsb[ci][:],
                                         in1=cx.ps[2][:, 0:NT], op=ALU.subtract)
                    ve.drain()
                    for ci in range(8):
                        ve.tensor_tensor(out=h1[ci][:, off:off + NT], in0=h1[ci][:, off:off + NT],
                                         in1=fsb[ci][:], op=ALU.add)
                    cx.inc(ve.tensor_copy(out=scr_mark(cx), in_=scr_mark(cx)), "v")

                cx.V(_v7)
                st3 = emit_stats(cx, [h1[ci][:, off:off + NT] for ci in range(8)], NT, 1e-5, True,
                                 cx.scr["r0"][0:1, 0:NT], cx.scr["r1"][0:1, 0:NT],
                                 cx.scr["r2"][0:1, 0:NT], after=(cx.tok("v"),))
                b1 = emit_bcast(cx, cx.scr["r2"][0:1, 0:NT], 2, 128, NT, after=(st3,))
                b2 = emit_bcast(cx, cx.scr["r1"][0:1, 0:NT], 3, 128, NT, after=(st3,))
                for ci in range(8):
                    def _v8(ve, ci=ci, off=off, b2=b2):
                        cx.wait(ve, b2, cx.tok("g"))
                        ve.tensor_tensor(out=cx.scr["D"][:, 0:NT], in0=h1[ci][:, off:off + NT],
                                         in1=cx.ps[3][:, 0:NT], op=ALU.mult)
                        ve.drain()
                        cx.inc(ve.tensor_tensor(out=x2stg[:], in0=cx.scr["D"][:, 0:NT],
                                                in1=cx.ps[2][:, 0:NT], op=ALU.subtract), "v")

                    cx.V(_v8)
                    vt = cx.tok("v")

                    def _g(ge, ci=ci, tc=tc, off=off, vt=vt):
                        cx.wait(ge, vt)
                        cx.inc(ge.dma_start(out=x2nout.ap()[128 * ci:128 * ci + 128,
                                                            NT * tc:NT * tc + NT],
                                            in_=x2stg[:]), "g", 16)
                        cx.inc(ge.dma_start(out=x2out.ap()[128 * ci:128 * ci + 128,
                                                           NT * tc:NT * tc + NT],
                                            in_=h1[ci][:, off:off + NT]), "g", 16)

                    cx.G(_g)
            cx.barrier()

        def _gend(ge):
            cx.wait(ge, cx.tok("g"))

        cx.G(_gend)
    return nc


def rowx2_holder(cx):
    return cx.scr["r0"]


def emit_absmax(cx, chunks, amx, inv_s_row, s_row, nt=512, raw_amax_row=None):
    scr, c = cx.scr, cx.c
    n = len(chunks)
    nu = nt // 128

    def _v(ve):
        cx.wait(ve, cx.tok("t"), cx.tok("s"))
        ve.tensor_tensor(out=scr["D"][:, 0:nt], in0=chunks[0], in1=chunks[1], op=ALU.max)
        ve.tensor_tensor(out=scr["E"][:, 0:nt], in0=chunks[0], in1=chunks[1], op=ALU.min)
        ve.drain()
        for ci in range(2, n):
            ve.tensor_tensor(out=scr["D"][:, 0:nt], in0=scr["D"][:, 0:nt], in1=chunks[ci], op=ALU.max)
            ve.tensor_tensor(out=scr["E"][:, 0:nt], in0=scr["E"][:, 0:nt], in1=chunks[ci], op=ALU.min)
            ve.drain()
        cx.inc(ve.scalar_tensor_tensor(out=scr["D"][:, 0:nt], in0=scr["E"][:, 0:nt],
                                       scalar=-1.0, in1=scr["D"][:, 0:nt],
                                       op0=ALU.mult, op1=ALU.max), "v")

    cx.V(_v)
    return emit_absmax_tail(cx, amx, inv_s_row, s_row, nt, raw_amax_row)


def emit_absmax_tail(cx, amx, inv_s_row, s_row, nt=512, raw_amax_row=None):
    scr = cx.scr
    nu = nt // 128
    v1 = cx.tok("v")
    for u in range(nu):
        def _t(te, u=u, v1=v1):
            cx.wait(te, v1, cx.tok("v"))
            cx.inc(te.transpose(out=cx.pstf[0:128, 0:128], in_=scr["D"][:, 128 * u:128 * u + 128],
                                identity=cx.c["identF"][:]), "t")

        cx.T(_t)
        t1 = cx.tok("t")

        def _v2(ve, u=u, t1=t1):
            cx.wait(ve, t1)
            cx.inc(ve.reduce_max(out=amx[:, u:u + 1], in_=cx.pstf[0:128, 0:128],
                                 axis=mybir.AxisListType.X), "v")

        cx.V(_v2)
    v2 = cx.tok("v")

    def _t2(te):
        cx.wait(te, v2)
        cx.inc(te.transpose(out=cx.pstf[0:nu, 0:128], in_=amx[:, 0:nu],
                            identity=cx.c["identF"][:]), "t")

    cx.T(_t2)
    t2 = cx.tok("t")
    dst = raw_amax_row if raw_amax_row is not None else inv_s_row

    def _v2b(ve):
        cx.wait(ve, t2)
        cx.inc(ve.tensor_copy(out=scr["amx4"][0:nu, :], in_=cx.pstf[0:nu, 0:128]), "v")

    cx.V(_v2b)
    v2b = cx.tok("v")

    def _qx(qe):
        cx.wait(qe, v2b)
        cx.inc(qe.dma_start(out=cx.dscr[0:1, 0:nt], in_=scr["amx4"][0:nu, :]), "q", 16)
        cx.wait(qe, cx.tok("q"))
        cx.inc(qe.dma_start(out=dst[0:1, 0:nt], in_=cx.dscr.ap()[0:1, 0:nt]), "q", 16)

    cx.Q(_qx)
    qx = cx.tok("q")

    def _v3(ve):
        cx.wait(ve, qx)
        if raw_amax_row is None:
            ve.tensor_scalar(out=dst[0:1, 0:nt], in0=dst[0:1, 0:nt],
                             scalar1=1e-5, scalar2=None, op0=ALU.max)
            ve.drain()
            ve.reciprocal(out=s_row[0:1, 0:nt], in_=dst[0:1, 0:nt])
            ve.drain()
            ve.tensor_scalar(out=s_row[0:1, 0:nt], in0=s_row[0:1, 0:nt],
                             scalar1=127.0, scalar2=None, op0=ALU.mult)
            ve.tensor_scalar(out=dst[0:1, 0:nt], in0=dst[0:1, 0:nt],
                             scalar1=1.0 / 127.0, scalar2=None, op0=ALU.mult)
            ve.drain()
        cx.inc(ve.tensor_copy(out=scr_mark(cx), in_=scr_mark(cx)), "v")

    cx.V(_v3)
    return cx.tok("v")


def build_launchC():
    nc = bass.Bass("TRN2", target_bir_lowering=False, debug=False)
    es = contextlib.ExitStack()
    with es:
        block = es.enter_context(nc.Block())
        cx = Ctx(nc, es, block)
        x2nf = cx.inp("x2nf", [D, S], BF16)
        wqC_j = cx.inp("wqC_j", [D, 512], BF16)
        wkC_j = cx.inp("wkC_j", [D, 512], BF16)
        wvC_j = cx.inp("wvC_j", [D, 512], BF16)
        gmask = cx.inp("gmask", [128, 4 * 512], BF16)
        og1out = cx.outp("og1out", [512, S], BF16)
        alloc_common(cx)
        xs = [cx.sbuf("xs%d" % i, [128, S], BF16) for i in range(8)]
        qcur = cx.sbuf("qcurC", [64, S], BF16)
        kcur = cx.sbuf("kcurC", [64, S], BF16)
        vtm = cx.sbuf("vtmC", [128, 65 * 32], BF16)
        ocur = cx.sbuf("ocurC", [64, S], BF16)
        gmask_s = cx.sbuf("gmaskC", [128, 4 * 512], BF16)
        vf_scr = cx.sbuf("vfC", [64, 512], BF16)
        wt = cx.sbuf("wtC", [128, 512], BF16)
        emit_load_consts(cx)

        def _q0(qe):
            cx.inc(qe.dma_start(out=gmask_s[:], in_=gmask.ap()), "q", 16)
            for ci in range(8):
                cx.inc(qe.dma_start(out=xs[ci][:], in_=x2nf.ap()[128 * ci:128 * ci + 128, :]), "q", 16)

        cx.Q(_q0)
        cx.barrier()

        def proj_plain(w_dram, col0, dst):
            qtw = emit_wtiles(cx, w_dram, col0, 64, 8, wt, after=(cx.tok("t"), cx.tok("v")))
            v_base = cx.cnt["v"]
            ent_v = cx.tok("v")
            for nb in range(8):
                psP = cx.ps[0] if nb % 2 == 0 else cx.ps[1]

                def _t(te, nb=nb, qtw=qtw, psP=psP):
                    if nb < 2:
                        cx.wait(te, qtw, ent_v)
                    else:
                        cx.wait(te, qtw, ("v", v_base + nb - 1))
                    mm = None
                    for k in range(8):
                        mm = te.matmul(out=psP[0:64, 0:512], lhsT=wt[:, 64 * k:64 * k + 64],
                                       rhs=xs[k][:, 512 * nb:512 * nb + 512],
                                       start=(k == 0), stop=(k == 7))
                    cx.inc(mm, "t")

                cx.T(_t)
                t1 = cx.tok("t")

                def _v(ve, nb=nb, t1=t1, dst=dst, psP=psP):
                    cx.wait(ve, t1)
                    cx.inc(ve.tensor_copy(out=dst[:, 512 * nb:512 * nb + 512],
                                          in_=psP[0:64, 0:512]), "v")

                cx.V(_v)

        for j in range(8):
            proj_plain(wqC_j, 64 * j, qcur)
            proj_plain(wkC_j, 64 * j, kcur)
            qtw = emit_wtiles(cx, wvC_j, 64 * j, 64, 8, wt, after=(cx.tok("t"), cx.tok("v")))

            def _w(te, qtw=qtw):
                cx.wait(te, qtw)

            cx.T(_w)
            emit_v_build(cx, [wt[:, 64 * k:64 * k + 64] for k in range(8)],
                         lambda k, nb: xs[k][:, 512 * nb:512 * nb + 512],
                         8, 32, vtm, vf_scr)
            vend = emit_attn_job(cx, qcur, kcur, vtm, ocur, S, None,
                                 cx.c["biasC1"], gmask_s)

            def _g(ge, j=j, vend=vend):
                cx.wait(ge, vend)
                cx.inc(ge.dma_start(out=og1out.ap()[64 * j:64 * j + 64, :], in_=ocur[:]), "g", 16)

            cx.G(_g)
            gt = cx.tok("g")

            def _v(ve, gt=gt):
                cx.wait(ve, gt)

            cx.V(_v)

        def _gend(ge):
            cx.wait(ge, cx.tok("g"))

        cx.G(_gend)
    return nc


def build_launchD():
    nc = bass.Bass("TRN2", target_bir_lowering=False, debug=False)
    es = contextlib.ExitStack()
    with es:
        block = es.enter_context(nc.Block())
        cx = Ctx(nc, es, block)
        x2own = cx.inp("x2own", [D, TOK], F32)
        og1own = cx.inp("og1own", [D, TOK], BF16)
        woT_b = cx.inp("woT_b", [D, D], BF16)
        w8T_b = cx.inp("w8T_b", [D, D], BF16)
        wscale = cx.inp("wscale", [1, 4], F32)
        fbias = cx.inp("fbias", [128, 8], F32)
        x4out = cx.outp("x4out", [D, TOK], F32)
        alloc_common(cx)
        x3 = [cx.sbuf("x3_%d" % i, [128, TOK], F32) for i in range(8)]
        xn = [cx.sbuf("xnD%d" % i, [128, 512], F32) for i in range(8)]
        xq = [cx.sbuf("xqD%d" % i, [128, 512], BF16) for i in range(8)]
        fsb = [cx.sbuf("fsbD%d" % i, [128, 512], F32) for i in range(8)]
        ogt = [cx.sbuf("ogtD%d" % i, [128, 512], BF16) for i in range(8)]
        xf = [cx.sbuf("xfD%d" % i, [128, 512], F32) for i in range(8)]
        wt = cx.sbuf("wtD", [128, 8 * 128], BF16)
        amx = cx.sbuf("amxD", [128, 8], F32)
        wsc_s = cx.sbuf("wscD", [1, 4], F32)
        fb_s = cx.sbuf("fbD", [128, 8], F32)
        rowx = cx.sbuf("rowxD", [1, 512], F32)
        rowy = cx.sbuf("rowyD", [1, 512], F32)
        emit_load_consts(cx)

        def _q0(qe):
            cx.inc(qe.dma_start(out=wsc_s[:], in_=wscale.ap()), "q", 16)
            cx.inc(qe.dma_start(out=fb_s[:], in_=fbias.ap()), "q", 16)

        cx.Q(_q0)
        cx.barrier()

        for tc in range(4):
            def _q(qe, tc=tc):
                cx.wait(qe, cx.tok("t"), cx.tok("v"))
                for ci in range(8):
                    cx.inc(qe.dma_start(out=ogt[ci][:],
                                        in_=og1own.ap()[128 * ci:128 * ci + 128,
                                                        512 * tc:512 * tc + 512]), "q", 16)
                    cx.inc(qe.dma_start(out=xf[ci][:],
                                        in_=x2own.ap()[128 * ci:128 * ci + 128,
                                                       512 * tc:512 * tc + 512]), "q", 16)

            cx.Q(_q)
            qt = cx.tok("q")
            for m in range(8):
                qtw = emit_wtiles(cx, woT_b, 128 * m, 128, 8, wt, after=(cx.tok("t"),))

                def _t(te, m=m, qtw=qtw, qt=qt):
                    cx.wait(te, qtw, qt, cx.tok("v"))
                    mm = None
                    for k in range(8):
                        mm = te.matmul(out=cx.ps[0][:, 0:512], lhsT=wt[:, 128 * k:128 * k + 128],
                                       rhs=ogt[k][:], start=(k == 0), stop=(k == 7))
                    cx.inc(mm, "t")

                cx.T(_t)
                t1 = cx.tok("t")

                def _v(ve, m=m, t1=t1):
                    cx.wait(ve, t1)
                    cx.inc(ve.tensor_copy(out=fsb[m][:], in_=cx.ps[0][:, 0:512]), "v")

                cx.V(_v)
            st = emit_stats(cx, [fsb[ci][:] for ci in range(8)], 512, 1e-5, True,
                            cx.scr["r0"][0:1, 0:512], cx.scr["r1"][0:1, 0:512],
                            cx.scr["r2"][0:1, 0:512], after=(cx.tok("v"),))
            b1 = emit_bcast(cx, cx.scr["r2"][0:1, 0:512], 2, 128, 512, after=(st,))
            b2 = emit_bcast(cx, cx.scr["r1"][0:1, 0:512], 3, 128, 512, after=(st,))

            def _v2(ve, tc=tc, b2=b2, qt=qt):
                cx.wait(ve, b2, qt)
                for ci in range(8):
                    ve.tensor_tensor(out=fsb[ci][:], in0=fsb[ci][:], in1=cx.ps[3][:, 0:512], op=ALU.mult)
                    ve.drain()
                    ve.tensor_tensor(out=fsb[ci][:], in0=fsb[ci][:], in1=cx.ps[2][:, 0:512], op=ALU.subtract)
                    ve.drain()
                    ve.tensor_tensor(out=x3[ci][:, 512 * tc:512 * tc + 512], in0=fsb[ci][:],
                                     in1=xf[ci][:], op=ALU.add)
                    ve.drain()
                cx.inc(ve.tensor_copy(out=scr_mark(cx), in_=scr_mark(cx)), "v")

            cx.V(_v2)
        cx.barrier()

        for tc in range(4):
            off = 512 * tc
            st = emit_stats(cx, [x3[ci][:, off:off + 512] for ci in range(8)], 512, 1e-5, True,
                            cx.scr["r0"][0:1, 0:512], cx.scr["r1"][0:1, 0:512],
                            cx.scr["r2"][0:1, 0:512], after=(cx.tok("v"), cx.tok("g")))
            b1 = emit_bcast(cx, cx.scr["r2"][0:1, 0:512], 2, 128, 512, after=(st,))
            b2 = emit_bcast(cx, cx.scr["r1"][0:1, 0:512], 3, 128, 512, after=(st,))

            def _v1(ve, off=off, b2=b2):
                cx.wait(ve, b2)
                for ci in range(8):
                    ve.tensor_tensor(out=xn[ci][:], in0=x3[ci][:, off:off + 512],
                                     in1=cx.ps[3][:, 0:512], op=ALU.mult)
                    ve.drain()
                    ve.tensor_tensor(out=xn[ci][:], in0=xn[ci][:], in1=cx.ps[2][:, 0:512], op=ALU.subtract)
                    ve.drain()
                cx.inc(ve.tensor_copy(out=scr_mark(cx), in_=scr_mark(cx)), "v")

            cx.V(_v1)
            st2 = emit_stats(cx, [xn[ci][:] for ci in range(8)], 512, 1e-6, False,
                             None, cx.scr["r1"][0:1, 0:512], None, after=(cx.tok("v"),))
            b3 = emit_bcast(cx, cx.scr["r1"][0:1, 0:512], 2, 128, 512, after=(st2,))

            def _v2(ve, b3=b3):
                cx.wait(ve, b3)
                for ci in range(8):
                    ve.tensor_tensor(out=xn[ci][:], in0=xn[ci][:], in1=cx.ps[2][:, 0:512], op=ALU.mult)
                    ve.drain()
                cx.inc(ve.tensor_copy(out=scr_mark(cx), in_=scr_mark(cx)), "v")

            cx.V(_v2)
            emit_absmax(cx, [xn[ci][:] for ci in range(8)], amx, rowx, rowy)
            sb = emit_bcast(cx, rowy[0:1, 0:512], 2, 128, 512, after=(cx.tok("v"),))
            for ci in range(8):
                def _v3(ve, ci=ci, sb=sb):
                    cx.wait(ve, sb)
                    cx.inc(ve.tensor_tensor(out=cx.scr["D"][:], in0=xn[ci][:],
                                            in1=cx.ps[2][:, 0:512], op=ALU.mult), "v")

                cx.V(_v3)
                emit_round_clip(cx, cx.scr["D"][:], xq[ci][:], -128, 127, after=(cx.tok("v"),))
            def _v4(ve):
                cx.wait(ve, cx.tok("v"))
                ve.tensor_scalar(out=rowx[0:1, 0:512], in0=rowx[0:1, 0:512],
                                 scalar1=wsc_s[0:1, 0:1], scalar2=None, op0=ALU.mult)
                ve.drain()
                cx.inc(ve.tensor_copy(out=scr_mark(cx), in_=scr_mark(cx)), "v")

            cx.V(_v4)
            scb = emit_bcast(cx, rowx[0:1, 0:512], 3, 128, 512, after=(cx.tok("v"),))

            def _vsc(ve, scb=scb):
                cx.wait(ve, scb)
                cx.inc(ve.tensor_copy(out=cx.scr["E"][:], in_=cx.ps[3][:, 0:512]), "v")

            cx.V(_vsc)
            scb = cx.tok("v")
            for m in range(8):
                qtw = emit_wtiles(cx, w8T_b, 128 * m, 128, 8, wt, after=(cx.tok("t"),))

                def _t(te, m=m, qtw=qtw):
                    cx.wait(te, qtw, cx.tok("v"))
                    mm = None
                    for k in range(8):
                        mm = te.matmul(out=cx.ps[0][:, 0:512], lhsT=wt[:, 128 * k:128 * k + 128],
                                       rhs=xq[k][:], start=(k == 0), stop=(k == 7))
                    cx.inc(mm, "t")

                cx.T(_t)
                t1 = cx.tok("t")

                def _v5(ve, m=m, t1=t1, scb=scb):
                    cx.wait(ve, t1, scb)
                    ve.tensor_tensor(out=fsb[m][:], in0=cx.ps[0][:, 0:512],
                                     in1=cx.scr["E"][:], op=ALU.mult)
                    ve.drain()
                    cx.inc(ve.tensor_scalar(out=fsb[m][:], in0=fsb[m][:],
                                            scalar1=fb_s[:, m:m + 1], scalar2=None, op0=ALU.add), "v")

                cx.V(_v5)
            st3 = emit_stats(cx, [fsb[ci][:] for ci in range(8)], 512, 1e-5, True,
                             cx.scr["r0"][0:1, 0:512], cx.scr["r1"][0:1, 0:512],
                             cx.scr["r2"][0:1, 0:512], after=(cx.tok("v"),))
            b1 = emit_bcast(cx, cx.scr["r2"][0:1, 0:512], 2, 128, 512, after=(st3,))
            b2 = emit_bcast(cx, cx.scr["r1"][0:1, 0:512], 3, 128, 512, after=(st3,))
            for ci in range(8):
                def _v6(ve, ci=ci, off=off, b2=b2):
                    cx.wait(ve, b2, cx.tok("g"))
                    ve.tensor_tensor(out=fsb[ci][:], in0=fsb[ci][:], in1=cx.ps[3][:, 0:512], op=ALU.mult)
                    ve.drain()
                    ve.tensor_tensor(out=fsb[ci][:], in0=fsb[ci][:], in1=cx.ps[2][:, 0:512], op=ALU.subtract)
                    ve.drain()
                    cx.inc(ve.tensor_tensor(out=x3[ci][:, off:off + 512], in0=x3[ci][:, off:off + 512],
                                            in1=fsb[ci][:], op=ALU.add), "v")

                cx.V(_v6)
                vt = cx.tok("v")

                def _g(ge, ci=ci, tc=tc, vt=vt):
                    cx.wait(ge, vt)
                    cx.inc(ge.dma_start(out=x4out.ap()[128 * ci:128 * ci + 128,
                                                       512 * tc:512 * tc + 512],
                                        in_=x3[ci][:, 512 * tc:512 * tc + 512]), "g", 16)

                cx.G(_g)
        cx.barrier()

        def _gend(ge):
            cx.wait(ge, cx.tok("g"))

        cx.G(_gend)
    return nc


def _lmask_np(r):
    import ml_dtypes
    own0 = (r % 2) * TOK
    poskv = np.concatenate([np.arange(1792, 2048), np.arange(own0, own0 + TOK)]).astype(np.int64)
    m = np.zeros((128, 48, 128), np.float32)
    for jt in range(16):
        for sl in range(3):
            kpos = poskv[128 * (jt + sl):128 * (jt + sl) + 128]
            qpos = own0 + 128 * jt + np.arange(128)
            dif = qpos[None, :] - kpos[:, None]
            m[:, jt * 3 + sl, :] = np.where((dif >= 0) & (dif < 256), 0.0, NEG)
    return np.ascontiguousarray(m.transpose(0, 1, 2).reshape(128, 48 * 128)).astype(ml_dtypes.bfloat16)


def kernel(**inputs):
    import ml_dtypes
    consts = _consts_np()
    w0 = run_launchW0(inputs)
    w1 = run_launchW1(inputs, w0)
    og = run_launchA(inputs)

    ogf = [np.concatenate([og[2 * b], og[2 * b + 1]], axis=0) for b in range(B)]
    x0 = np.asarray(inputs["L0_attn_hidden"], np.float32)
    n2 = np.ascontiguousarray(np.asarray(inputs["L0_attn_norm2_w"], np.float32).reshape(8, 128).T)
    gm = _gmask_np()
    mapsB = []
    for r in range(NCORES):
        b, own0 = r // 2, (r % 2) * TOK
        x0f = np.ascontiguousarray(x0[b].T)
        m = dict(consts)
        m["x0own"] = np.ascontiguousarray(x0f[:, own0:own0 + TOK])
        m["x0halo"] = np.ascontiguousarray(x0f[:, 1792:2048])
        m["ogown"] = np.ascontiguousarray(ogf[b][:, own0:own0 + TOK])
        m["oghalo"] = np.ascontiguousarray(ogf[b][:, 1792:2048])
        m["gwo_b"] = w0["gwo_b"]
        m["n2w"] = n2
        m["lwq_b"] = w0["lwq_b"]
        m["lwk_b"] = w0["lwk_b"]
        m["lwv_b"] = w0["lwv_b"]
        m["lwo_b"] = w0["lwo_b"]
        m["wgq"] = w1["wgq"]
        m["wuq"] = w1["wuq"]
        m["wdq"] = w1["wdq"]
        m["wrec"] = w1["wrec"]
        m["posKV"] = np.concatenate([np.arange(1792, 2048), np.arange(own0, own0 + TOK)]
                                    ).astype(np.float32).reshape(1, NKV)
        m["lmask"] = _lmask_np(r)
        mapsB.append(m)
    if "Bn" not in _cache:
        _cache["Bn"] = build_launchB()
    resB = run_bass_kernel_spmd(_cache["Bn"], mapsB, core_ids=list(range(NCORES)))
    x2 = [np.asarray(r["x2out"]) for r in resB.results]
    x2n = [np.asarray(r["x2nout"]) for r in resB.results]

    x2nf = [np.concatenate([x2n[2 * b], x2n[2 * b + 1]], axis=1) for b in range(B)]
    mapsC = []
    for r in range(NCORES):
        b, hs = r // 2, (r % 2) * 8
        m = dict(consts)
        m["x2nf"] = np.ascontiguousarray(x2nf[b])
        m["wqC_j"] = np.ascontiguousarray(w0["wqC_b"][:, 512 * (r % 2):512 * (r % 2) + 512])
        m["wkC_j"] = np.ascontiguousarray(w0["wkC_b"][:, 512 * (r % 2):512 * (r % 2) + 512])
        m["wvC_j"] = np.ascontiguousarray(w0["wvC_b"][:, 512 * (r % 2):512 * (r % 2) + 512])
        m["gmask"] = gm
        mapsC.append(m)
    if "C" not in _cache:
        _cache["C"] = build_launchC()
    resC = run_bass_kernel_spmd(_cache["C"], mapsC, core_ids=list(range(NCORES)))
    og1 = [np.asarray(r["og1out"]) for r in resC.results]
    og1f = [np.concatenate([og1[2 * b], og1[2 * b + 1]], axis=0) for b in range(B)]

    fb = np.ascontiguousarray(np.asarray(inputs["L1_ffn_bias"], np.float32).reshape(8, 128).T)
    ws = np.full((1, 4), float(np.asarray(inputs["L1_ffn_weight_scale"], np.float32)), np.float32)
    mapsD = []
    for r in range(NCORES):
        b, own0 = r // 2, (r % 2) * TOK
        m = dict(consts)
        m["x2own"] = x2[r]
        m["og1own"] = np.ascontiguousarray(og1f[b][:, own0:own0 + TOK])
        m["woT_b"] = w0["woT_b"]
        m["w8T_b"] = w0["w8T_b"]
        m["wscale"] = ws
        m["fbias"] = fb
        mapsD.append(m)
    if "Dn" not in _cache:
        _cache["Dn"] = build_launchD()
    resD = run_bass_kernel_spmd(_cache["Dn"], mapsD, core_ids=list(range(NCORES)))
    x4 = [np.asarray(r["x4out"]) for r in resD.results]

    out = np.empty((B, S, D), np.float32)
    for r in range(NCORES):
        b, own0 = r // 2, (r % 2) * TOK
        out[b, own0:own0 + TOK, :] = x4[r].T
    return out


def emit_cc(cx, kind, op, ins, outs, after=()):
    def _g(ge):
        cx.wait(ge, *after)
        cx.inc(ge.collective_compute(kind, op, replica_groups=[list(range(NCORES))],
                                     ins=ins, outs=outs), "c")
    cx.G(_g)
    return cx.tok("c")


def build_fused():
    nc = bass.Bass("TRN2", target_bir_lowering=False, debug=False)
    es = contextlib.ExitStack()
    with es:
        block = es.enter_context(nc.Block())
        cx = Ctx(nc, es, block)
        cx.sem["c"] = es.enter_context(nc.semaphore("sem_c"))
        cx.cnt["c"] = 0

        x0f = cx.inp("x0f", [D, S], F32)
        x0own = cx.inp("x0own", [D, TOK], F32)
        x0halo = cx.inp("x0halo", [D, HALO], F32)
        n1w = cx.inp("n1w", [128, 8], F32)
        n2w = cx.inp("n2w", [128, 8], F32)
        wqA = cx.inp("wqA", [D, 512], F32)
        wkvd = cx.inp("wkvd", [D, LAT], F32)
        wkuA = cx.inp("wkuA", [LAT, 128], F32)
        wvuA = cx.inp("wvuA", [LAT, 128], F32)
        gmask = cx.inp("gmask", [128, 4 * 512], BF16)
        posAll = cx.inp("posAll", [1, S], F32)
        posKV = cx.inp("posKV", [1, NKV], F32)
        lmask = cx.inp("lmask", [128, 48 * 128], BF16)
        gwo_f = cx.inp("gwo_f", [D, D], F32)
        lwq_f = cx.inp("lwq_f", [D, D], F32)
        lwk_f = cx.inp("lwk_f", [D, LAT], F32)
        lwv_f = cx.inp("lwv_f", [D, LAT], F32)
        lwo_f = cx.inp("lwo_f", [D, D], F32)
        wgT = cx.inp("wgT", [128, FFN_I], F32)
        wuT = cx.inp("wuT", [128, FFN_I], F32)
        wdT = cx.inp("wdT", [512, D], F32)
        wqC_f = cx.inp("wqC_f", [D, 512], F32)
        wkC_f = cx.inp("wkC_f", [D, 512], F32)
        wvC_f = cx.inp("wvC_f", [D, 512], F32)
        woT_f = cx.inp("woT_f", [D, D], F32)
        w8T_i = cx.inp("w8T_i", [D, D], I32)
        wscale = cx.inp("wscale", [1, 4], F32)
        fbias = cx.inp("fbias", [128, 8], F32)
        selOwn = cx.inp("selOwn", [128, 2], F32)
        idxA = cx.inp("idxA", [128, 8], mybir.dt.uint32)
        idxX = cx.inp("idxX", [128, 16], mybir.dt.uint32)
        x4out = cx.outp("x4out", [D, TOK], F32)

        wgq = cx.dram("wgq_s", [D, FFN_I], BF16)
        wuq = cx.dram("wuq_s", [D, FFN_I], BF16)
        wdq = cx.dram("wdq_s", [FFN_I, D], BF16)
        ar_in = cx.dram("ar_in", [1, 4], F32)
        ar_out = cx.dram("ar_out", [1, 4], F32)
        agw_in = {n: cx.dram("agwin_" + n, sh, BF16) for n, sh in
                  [("g", [128, FFN_I]), ("u", [128, FFN_I]), ("d", [512, D])]}
        gwo_b = cx.dram("gwo_b", [D, D], BF16)
        lwq_b = cx.dram("lwq_b", [D, D], BF16)
        lwk_b = cx.dram("lwk_b", [D, LAT], BF16)
        lwv_b = cx.dram("lwv_b", [D, LAT], BF16)
        lwo_b = cx.dram("lwo_b", [D, D], BF16)
        wqC_b = cx.dram("wqC_b", [D, 512], BF16)
        wqA_b = cx.dram("wqA_b", [D, 512], BF16)
        wkC_b = cx.dram("wkC_b", [D, 512], BF16)
        wvC_b = cx.dram("wvC_b", [D, 512], BF16)
        woT_b = cx.dram("woT_b", [D, D], BF16)
        w8T_b = cx.dram("w8T_b", [D, D], BF16)
        ogbuf = cx.dram("ogbuf", [512, S], BF16)
        og_ag = cx.dram("og_ag", [NCORES * 512, S], BF16)
        og_mine = cx.dram("og_mine", [D, NKV], BF16)
        x2n_mine = cx.dram("x2n_mine", [D, TOK], BF16)
        x2n_ag = cx.dram("x2n_ag", [NCORES * D, TOK], BF16)
        x2nf_mine = cx.dram("x2nf_mine", [D, S], BF16)
        ogc = cx.dram("ogc", [512, S], BF16)
        ogc_ag = cx.dram("ogc_ag", [NCORES * 512, S], BF16)
        og1_mine = cx.dram("og1_mine", [D, TOK], BF16)

        alloc_common(cx)
        n1w_s = cx.sbuf("n1w_s", [128, 8], F32)
        n2w_s = cx.sbuf("n2w_s", [128, 8], F32)
        lmask_g = cx.sbuf("lmask_g", [128, 12 * 128], BF16)
        wrec_s = cx.sbuf("wrec_s", [1, 4], F32)
        wsc_s = cx.sbuf("wscD", [1, 4], F32)
        fb_s = cx.sbuf("fbD", [128, 8], F32)
        sel_s = cx.sbuf("sel_s", [128, 2], F32)
        idxA_s = cx.sbuf("idxA_s", [128, 8], mybir.dt.uint32)
        idxX_s = cx.sbuf("idxX_s", [128, 16], mybir.dt.uint32)

        emit_load_consts(cx)

        def _q0(qe):
            for dst, src in [(n1w_s, n1w), (n2w_s, n2w), (wsc_s, wscale),
                             (fb_s, fbias), (sel_s, selOwn), (idxA_s, idxA), (idxX_s, idxX)]:
                cx.inc(qe.dma_start(out=dst[:], in_=src.ap()), "q", 16)

        cx.Q(_q0)
        cx.barrier()

        with contextlib.ExitStack() as ph:
            wb = ph.enter_context(nc.sbuf_tensor("f0wb", [128, FFN_I], F32))
            wstage = ph.enter_context(nc.sbuf_tensor("f0ws", [128, FFN_I], BF16))
            acc = ph.enter_context(nc.sbuf_tensor("f0acc", [128, 4], F32))
            red = ph.enter_context(nc.sbuf_tensor("f0red", [128, 4], F32))
            rowo = ph.enter_context(nc.sbuf_tensor("f0row", [1, 4], F32))
            s128 = ph.enter_context(nc.sbuf_tensor("f0s128", [128, 4], F32))
            w8s = ph.enter_context(nc.sbuf_tensor("f0w8", [128, D], I32))

            def _g0(ge):
                cx.inc(ge.memset(acc[:], 0.0), "g")

            cx.G(_g0)
            g0 = cx.tok("g")
            jobs = [(wgT, 1, FFN_I, 0), (wuT, 1, FFN_I, 1), (wdT, 4, D, 2)]
            for (src, nch, ncol, col) in jobs:
                for kc in range(nch):
                    def _q(qe, src=src, kc=kc, ncol=ncol):
                        cx.wait(qe, cx.tok("v"))
                        cx.inc(qe.dma_start(out=wb[:, 0:ncol],
                                            in_=src.ap()[128 * kc:128 * kc + 128, :]), "q", 16)

                    cx.Q(_q)
                    qt = cx.tok("q")

                    def _v(ve, col=col, ncol=ncol, qt=qt):
                        cx.wait(ve, qt, g0)
                        ve.reduce_sum(out=red[:, col:col + 1], in_=wb[:, 0:ncol],
                                      axis=mybir.AxisListType.X, apply_absolute_value=True)
                        ve.drain()
                        ve.tensor_tensor(out=acc[:, col:col + 1], in0=acc[:, col:col + 1],
                                         in1=red[:, col:col + 1], op=ALU.add)
                        ve.drain()
                        cx.inc(ve.tensor_copy(out=red[0:1, 0:1], in_=red[0:1, 0:1]), "v")

                    cx.V(_v)
            vt = cx.tok("v")

            def _t(te):
                cx.wait(te, vt)
                cx.inc(te.matmul(out=cx.ps[4][0:1, 0:4], lhsT=cx.c["onesF"][:, 0:1],
                                 rhs=acc[:], start=True, stop=True), "t")

            cx.T(_t)
            tt = cx.tok("t")

            def _v2(ve):
                cx.wait(ve, tt)
                cx.inc(ve.tensor_copy(out=rowo[:], in_=cx.ps[4][0:1, 0:4]), "v")

            cx.V(_v2)
            vt2 = cx.tok("v")

            def _gar(ge):
                cx.wait(ge, vt2)
                cx.inc(ge.dma_start(out=ar_in[:, :], in_=rowo[:]), "g", 16)

            cx.G(_gar)
            cc1 = emit_cc(cx, "AllReduce", ALU.add, [ar_in.ap().opt()], [ar_out.ap().opt()],
                          after=(cx.tok("g"),))

            def _gld(ge, cc1=cc1):
                cx.wait(ge, cc1)
                cx.inc(ge.dma_start(out=rowo[:], in_=ar_out[:, :]), "g", 16)

            cx.G(_gld)
            gld = cx.tok("g")

            def _v3(ve):
                cx.wait(ve, gld)
                ve.tensor_scalar(out=wrec_s[:], in0=rowo[:], scalar1=1.0 / (D * FFN_I),
                                 scalar2=1e-5, op0=ALU.mult, op1=ALU.max)
                ve.drain()
                ve.reciprocal(out=rowo[:], in_=wrec_s[:])
                ve.drain()
                cx.inc(ve.tensor_copy(out=rowo[0:1, 3:4], in_=rowo[0:1, 3:4]), "v")

            cx.V(_v3)
            vt3 = cx.tok("v")
            for col in range(3):
                bt = emit_bcast(cx, rowo[0:1, col:col + 1], 5, 128, 1, after=(vt3, cx.tok("v")))

                def _vb(ve, col=col, bt=bt):
                    cx.wait(ve, bt)
                    cx.inc(ve.tensor_copy(out=s128[:, col:col + 1], in_=cx.ps[5][0:128, 0:1]), "v")

                cx.V(_vb)

            jobs2 = [(wgT, agw_in["g"], 1, FFN_I, 0), (wuT, agw_in["u"], 1, FFN_I, 1),
                     (wdT, agw_in["d"], 4, D, 2)]
            for (src, dst, nch, ncol, col) in jobs2:
                for kc in range(nch):
                    def _q2(qe, src=src, kc=kc, ncol=ncol):
                        cx.wait(qe, cx.tok("v"), cx.tok("g"))
                        cx.inc(qe.dma_start(out=wb[:, 0:ncol],
                                            in_=src.ap()[128 * kc:128 * kc + 128, :]), "q", 16)

                    cx.Q(_q2)
                    qt2 = cx.tok("q")
                    for sl in range(ncol // 512):
                        def _v4(ve, sl=sl, col=col, qt2=qt2):
                            cx.wait(ve, qt2)
                            cx.inc(ve.tensor_scalar(out=cx.scr["D"][:], in0=wb[:, 512 * sl:512 * sl + 512],
                                                    scalar1=s128[:, col:col + 1], scalar2=None,
                                                    op0=ALU.mult), "v")

                        cx.V(_v4)
                        emit_round_clip(cx, cx.scr["D"][:], wstage[:, 512 * sl:512 * sl + 512],
                                        -1, 1, after=(cx.tok("v"),))
                    vt4 = cx.tok("v")

                    def _g2(ge, dst=dst, kc=kc, ncol=ncol, vt4=vt4):
                        cx.wait(ge, vt4)
                        cx.inc(ge.dma_start(out=dst.ap()[128 * kc:128 * kc + 128, :],
                                            in_=wstage[:, 0:ncol]), "g", 16)

                    cx.G(_g2)
            gq = cx.tok("g")
            emit_cc(cx, "AllGather", ALU.bypass, [agw_in["g"].ap().opt()], [wgq.ap().opt()], after=(gq,))
            emit_cc(cx, "AllGather", ALU.bypass, [agw_in["u"].ap().opt()], [wuq.ap().opt()], after=(cx.tok("c"),))
            ccW = emit_cc(cx, "AllGather", ALU.bypass, [agw_in["d"].ap().opt()], [wdq.ap().opt()], after=(cx.tok("c"),))

            casts = [(gwo_f, gwo_b, D, D), (lwq_f, lwq_b, D, D), (lwk_f, lwk_b, D, LAT),
                     (lwv_f, lwv_b, D, LAT), (lwo_f, lwo_b, D, D), (wqC_f, wqC_b, D, 512),
                     (wkC_f, wkC_b, D, 512), (wvC_f, wvC_b, D, 512), (woT_f, woT_b, D, D),
                     (wqA, wqA_b, D, 512)]
            for (srcd, dstd, rows, cols) in casts:
                for kc in range(rows // 128):
                    def _q3(qe, srcd=srcd, kc=kc, cols=cols):
                        cx.wait(qe, cx.tok("v"), cx.tok("g"))
                        cx.inc(qe.dma_start(out=wb[:, 0:cols],
                                            in_=srcd.ap()[128 * kc:128 * kc + 128, :]), "q", 16)

                    cx.Q(_q3)
                    qt3 = cx.tok("q")

                    def _v5(ve, qt3=qt3, cols=cols):
                        cx.wait(ve, qt3)
                        cx.inc(ve.tensor_copy(out=wstage[:, 0:cols], in_=wb[:, 0:cols]), "v")

                    cx.V(_v5)
                    vt5 = cx.tok("v")

                    def _g3(ge, dstd=dstd, kc=kc, cols=cols, vt5=vt5):
                        cx.wait(ge, vt5)
                        cx.inc(ge.dma_start(out=dstd.ap()[128 * kc:128 * kc + 128, :],
                                            in_=wstage[:, 0:cols]), "g", 16)

                    cx.G(_g3)
            for kc in range(8):
                def _q4(qe, kc=kc):
                    cx.wait(qe, cx.tok("v"), cx.tok("g"))
                    cx.inc(qe.dma_start(out=w8s[:], in_=w8T_i.ap()[128 * kc:128 * kc + 128, :]), "q", 16)

                cx.Q(_q4)
                qt4 = cx.tok("q")

                def _v6(ve, qt4=qt4):
                    cx.wait(ve, qt4)
                    cx.inc(ve.tensor_copy(out=wstage[:, 0:D], in_=w8s[:]), "v")

                cx.V(_v6)
                vt6 = cx.tok("v")

                def _g4(ge, kc=kc, vt6=vt6):
                    cx.wait(ge, vt6)
                    cx.inc(ge.dma_start(out=w8T_b.ap()[128 * kc:128 * kc + 128, :],
                                        in_=wstage[:, 0:D]), "g", 16)

                cx.G(_g4)
        cx.barrier()

        with contextlib.ExitStack() as f1ph:
            def f1sb(name, shape, dt=F32):
                t = f1ph.enter_context(nc.sbuf_tensor("f1" + name, shape, dt))
                cx.sb["f1" + name] = t
                return t

            gmask_s = f1sb("gmask_s", [128, 4 * 512], BF16)

            def _qgm(qe):
                cx.inc(qe.dma_start(out=gmask_s[:], in_=gmask.ap()), "q", 16)

            cx.Q(_qgm)
            x0n = [f1sb("x0n%d" % i, [128, S], BF16) for i in range(8)]
            kk = [f1sb("kk%d" % a, [64, S], BF16) for a in range(2)]
            vtm = [f1sb("vtm%d" % a, [128, 65 * 32], BF16) for a in range(2)]
            qcur = f1sb("qcur", [64, S], BF16)
            ocur = f1sb("ocur", [64, S], BF16)
            vf_scr = f1sb("vf_scr", [64, 512], BF16)
            cosG = f1sb("cosG", [64, S], F32)
            sinG = f1sb("sinG", [64, S], F32)
            f1wt = f1sb("f1wt", [128, 512], BF16)


            with contextlib.ExitStack() as ph:
                wkvd_bf = ph.enter_context(nc.sbuf_tensor("wkvd_bf", [128, 8 * 256], BF16))
                wku_bf = ph.enter_context(nc.sbuf_tensor("wku_bf", [128, 2 * 128], BF16))
                wvu_bf = ph.enter_context(nc.sbuf_tensor("wvu_bf", [128, 2 * 128], BF16))
                wf32 = ph.enter_context(nc.sbuf_tensor("wf32", [128, 512], F32))
                xf32 = [ph.enter_context(nc.sbuf_tensor("xf32_%d" % i, [128, 256], F32)) for i in range(8)]
                lat_bf = [ph.enter_context(nc.sbuf_tensor("lat%d" % a, [128, S], BF16)) for a in range(2)]

                emit_rope_tables(cx, BASE_G, posAll, S, cosG, sinG)

                def cast_w(dst, src_dram, rows, cols):
                    for kc in range(rows // 128):
                        def _q(qe, kc=kc):
                            cx.wait(qe, cx.tok("v"))
                            cx.inc(qe.dma_start(out=wf32[:, 0:cols],
                                                in_=src_dram.ap()[128 * kc:128 * kc + 128, :]), "q", 16)

                        cx.Q(_q)
                        qt = cx.tok("q")

                        def _v(ve, kc=kc, qt=qt):
                            cx.wait(ve, qt)
                            cx.inc(ve.tensor_copy(out=dst[:, cols * kc:cols * kc + cols],
                                                  in_=wf32[:, 0:cols]), "v")

                        cx.V(_v)

                cast_w(wkvd_bf, wkvd, D, 256)
                cast_w(wku_bf, wkuA, LAT, 128)
                cast_w(wvu_bf, wvuA, LAT, 128)

                for nb in range(16):
                    def _q(qe, nb=nb):
                        cx.wait(qe, cx.tok("t"), cx.tok("v"))
                        for ci in range(8):
                            cx.inc(qe.dma_start(out=xf32[ci][:],
                                                in_=x0f.ap()[128 * ci:128 * ci + 128,
                                                             256 * nb:256 * nb + 256]), "q", 16)

                    cx.Q(_q)
                    qt = cx.tok("q")
                    st1 = emit_stats(cx, [xf32[ci][:] for ci in range(8)], 256, 1e-5, True,
                                     cx.scr["r0"][0:1, 0:256], cx.scr["r1"][0:1, 0:256],
                                     cx.scr["r2"][0:1, 0:256], after=(qt,))
                    b1 = emit_bcast(cx, cx.scr["r2"][0:1, 0:256], 0, 128, 256, after=(st1,))
                    b2 = emit_bcast(cx, cx.scr["r1"][0:1, 0:256], 1, 128, 256, after=(st1,))

                    def _v(ve, b2=b2):
                        cx.wait(ve, b2)
                        for ci in range(8):
                            ve.tensor_tensor(out=xf32[ci][:], in0=xf32[ci][:],
                                             in1=cx.ps[1][:, 0:256], op=ALU.mult)
                            ve.drain()
                            ve.tensor_tensor(out=xf32[ci][:], in0=xf32[ci][:],
                                             in1=cx.ps[0][:, 0:256], op=ALU.subtract)
                            ve.drain()
                        cx.inc(ve.tensor_copy(out=scr_mark(cx), in_=scr_mark(cx)), "v")

                    cx.V(_v)
                    st2 = emit_stats(cx, [xf32[ci][:] for ci in range(8)], 256, 1e-6, False,
                                     None, cx.scr["r1"][0:1, 0:256], None, after=(cx.tok("v"),))
                    b3 = emit_bcast(cx, cx.scr["r1"][0:1, 0:256], 0, 128, 256, after=(st2,))

                    def _v2(ve, nb=nb, b3=b3):
                        cx.wait(ve, b3)
                        for ci in range(8):
                            ve.tensor_tensor(out=xf32[ci][:], in0=xf32[ci][:],
                                             in1=cx.ps[0][:, 0:256], op=ALU.mult)
                            ve.drain()
                            ve.tensor_scalar(out=x0n[ci][:, 256 * nb:256 * nb + 256], in0=xf32[ci][:],
                                             scalar1=n1w_s[:, ci:ci + 1], scalar2=None, op0=ALU.mult)
                            ve.drain()
                        cx.inc(ve.tensor_copy(out=scr_mark(cx), in_=scr_mark(cx)), "v")

                    cx.V(_v2)
                cx.barrier()

                for a in range(2):
                    for nb in range(8):
                        def _t(te, a=a, nb=nb):
                            cx.wait(te, cx.tok("v"))
                            mm = None
                            for k in range(8):
                                mm = te.matmul(out=cx.ps[0][:, 0:512],
                                               lhsT=wkvd_bf[:, 256 * k + 128 * a:256 * k + 128 * a + 128],
                                               rhs=x0n[k][:, 512 * nb:512 * nb + 512],
                                               start=(k == 0), stop=(k == 7))
                            cx.inc(mm, "t")

                        cx.T(_t)
                        t1 = cx.tok("t")

                        def _v(ve, a=a, nb=nb, t1=t1):
                            cx.wait(ve, t1)
                            cx.inc(ve.tensor_copy(out=lat_bf[a][:, 512 * nb:512 * nb + 512],
                                                  in_=cx.ps[0][:, 0:512]), "v")

                        cx.V(_v)

                for a in range(2):
                    kpipe = mk_pipe()
                    for nb in range(8):
                        emit_proj_rope(cx,
                                       [wku_bf[:, 64 * a:64 * a + 64], wku_bf[:, 128 + 64 * a:128 + 64 * a + 64]],
                                       [lat_bf[0][:, 512 * nb:512 * nb + 512],
                                        lat_bf[1][:, 512 * nb:512 * nb + 512]],
                                       2, 512 * nb, 512, cosG, sinG,
                                       kk[a][:, 512 * nb:512 * nb + 512], True, None,
                                       pipe=kpipe)
                    pipe_flush(cx, kpipe)
                    emit_v_build(cx, [wvu_bf[:, 64 * a:64 * a + 64], wvu_bf[:, 128 + 64 * a:128 + 64 * a + 64]],
                                 lambda k, nb: lat_bf[k][:, 512 * nb:512 * nb + 512],
                                 2, 32, vtm[a], vf_scr)
                cx.barrier()

            for j in range(8):
                qtw = emit_wtiles(cx, wqA_b, 64 * j, 64, 8, f1wt, after=(cx.tok("t"), cx.tok("v")))

                def _wj(te, qtw=qtw):
                    cx.wait(te, qtw)

                cx.T(_wj)
                qpipe = mk_pipe()
                for nb in range(8):
                    emit_proj_rope(cx, [f1wt[:, 64 * k:64 * k + 64] for k in range(8)],
                                   [x0n[k][:, 512 * nb:512 * nb + 512] for k in range(8)],
                                   8, 512 * nb, 512, cosG, sinG,
                                   qcur[:, 512 * nb:512 * nb + 512], True, None,
                                   pipe=qpipe)
                pipe_flush(cx, qpipe)
                a = j // 4
                vend = emit_attn_job(cx, qcur, kk[a], vtm[a], ocur, S, None,
                                     cx.c["biasC0"], gmask_s)

                def _g(ge, j=j, vend=vend):
                    cx.wait(ge, vend)
                    cx.inc(ge.dma_start(out=ogbuf.ap()[64 * j:64 * j + 64, :], in_=ocur[:]), "g", 16)

                cx.G(_g)
                gt = cx.tok("g")

                def _v(ve, gt=gt):
                    cx.wait(ve, gt)

                cx.V(_v)


        cx.barrier()
        ccOG = emit_cc(cx, "AllGather", ALU.bypass, [ogbuf.ap().opt()], [og_ag.ap().opt()],
                       after=(cx.tok("g"),))
        with contextlib.ExitStack() as rg:
            stg = rg.enter_context(nc.sbuf_tensor("rg1a", [128, S], BF16))
            stg2 = rg.enter_context(nc.sbuf_tensor("rg1b", [128, NKV], BF16))
            stg3 = rg.enter_context(nc.sbuf_tensor("rg1c", [128, TOK], BF16))
            for ci in range(8):
                def _gi(ge, ci=ci, ccOG=ccOG):
                    cx.wait(ge, ccOG, cx.tok("v"))
                    cx.inc(ge.indirect_dma_start(
                        out=stg[:], out_offset=None, in_=og_ag[:, :],
                        in_offset=bass.IndirectOffsetOnAxis(ap=idxA_s[:, ci:ci + 1], axis=0)), "g", 16)

                cx.G(_gi)
                gi = cx.tok("g")

                def _vs(ve, gi=gi):
                    cx.wait(ve, gi)
                    ve.tensor_copy(out=stg2[:, 0:HALO], in_=stg[:, 1792:2048])
                    ve.drain()
                    ve.tensor_scalar(out=stg2[:, HALO:HALO + TOK], in0=stg[:, 0:TOK],
                                     scalar1=sel_s[:, 0:1], scalar2=None, op0=ALU.mult)
                    ve.drain()
                    ve.tensor_scalar(out=stg3[:], in0=stg[:, TOK:2 * TOK],
                                     scalar1=sel_s[:, 1:2], scalar2=None, op0=ALU.mult)
                    ve.drain()
                    cx.inc(ve.tensor_tensor(out=stg2[:, HALO:HALO + TOK], in0=stg2[:, HALO:HALO + TOK],
                                            in1=stg3[:], op=ALU.add), "v")

                cx.V(_vs)
                vs = cx.tok("v")

                def _go(ge, ci=ci, vs=vs):
                    cx.wait(ge, vs)
                    cx.inc(ge.dma_start(out=og_mine.ap()[128 * ci:128 * ci + 128, :],
                                        in_=stg2[:]), "g", 16)

                cx.G(_go)
        cx.barrier()

        dbg = False
        h1 = [cx.sbuf("h1_%d" % i, [128, NKV], F32) for i in range(8)]

        with contextlib.ExitStack() as ph:
            f2wt = ph.enter_context(nc.sbuf_tensor("f2wt", [128, 8 * 128], BF16))
            xf = [ph.enter_context(nc.sbuf_tensor("xf%d" % i, [128, 512], F32)) for i in range(8)]
            ogs = [ph.enter_context(nc.sbuf_tensor("ogs%d" % i, [128, 512], BF16)) for i in range(8)]

            for (off, nn) in B_BLOCKS:
                def _q2(qe, off=off, nn=nn):
                    cx.wait(qe, cx.tok("t"), cx.tok("v"))
                    for ci in range(8):
                        src = x0halo.ap()[128 * ci:128 * ci + 128, :] if off == 0 else \
                            x0own.ap()[128 * ci:128 * ci + 128, off - HALO:off - HALO + nn]
                        cx.inc(qe.dma_start(out=xf[ci][:, 0:nn], in_=src), "q", 16)
                        cx.inc(qe.dma_start(out=ogs[ci][:, 0:nn],
                                            in_=og_mine.ap()[128 * ci:128 * ci + 128, off:off + nn]), "q", 16)

                cx.Q(_q2)
                qt = cx.tok("q")
                st = emit_stats(cx, [xf[ci][:, 0:nn] for ci in range(8)], nn, 1e-5, True,
                                cx.scr["r0"][0:1, 0:nn], cx.scr["r1"][0:1, 0:nn],
                                cx.scr["r2"][0:1, 0:nn], after=(qt,))
                b1 = emit_bcast(cx, cx.scr["r2"][0:1, 0:nn], 0, 128, nn, after=(st,))
                b2 = emit_bcast(cx, cx.scr["r1"][0:1, 0:nn], 1, 128, nn, after=(st,))

                def _v(ve, off=off, nn=nn, b2=b2):
                    cx.wait(ve, b2)
                    for ci in range(8):
                        ve.tensor_tensor(out=h1[ci][:, off:off + nn], in0=xf[ci][:, 0:nn],
                                         in1=cx.ps[1][:, 0:nn], op=ALU.mult)
                        ve.drain()
                        ve.tensor_tensor(out=h1[ci][:, off:off + nn], in0=h1[ci][:, off:off + nn],
                                         in1=cx.ps[0][:, 0:nn], op=ALU.subtract)
                        ve.drain()
                    cx.inc(ve.tensor_copy(out=scr_mark(cx), in_=scr_mark(cx)), "v")

                cx.V(_v)
                for m in range(8):
                    qtw = emit_wtiles(cx, gwo_b, 128 * m, 128, 8, f2wt,
                                      after=(cx.tok("t"),))

                    def _t(te, off=off, nn=nn, m=m, qt=qt, qtw=qtw):
                        cx.wait(te, cx.tok("v"), qt, qtw)
                        mm = None
                        for k in range(8):
                            mm = te.matmul(out=cx.ps[0][:, 0:nn],
                                           lhsT=f2wt[:, 128 * k:128 * k + 128],
                                           rhs=ogs[k][:, 0:nn],
                                           start=(k == 0), stop=(k == 7))
                        cx.inc(mm, "t")

                    cx.T(_t)
                    t1 = cx.tok("t")

                    def _v(ve, off=off, nn=nn, m=m, t1=t1):
                        cx.wait(ve, t1)
                        cx.inc(ve.tensor_tensor(out=h1[m][:, off:off + nn],
                                                in0=h1[m][:, off:off + nn],
                                                in1=cx.ps[0][:, 0:nn], op=ALU.add), "v")

                    cx.V(_v)
            cx.barrier()
            dump(dbg_h1, h1, NKV) if dbg else None

        po = contextlib.ExitStack()
        o_l = [po.enter_context(nc.sbuf_tensor("o_l%d" % i, [128, TOK], BF16)) for i in range(8)]
        with contextlib.ExitStack() as ph:
            l_in = [ph.enter_context(nc.sbuf_tensor("lin%d" % i, [128, NKV], BF16)) for i in range(8)]
            cosL = ph.enter_context(nc.sbuf_tensor("cosL", [64, NKV], F32))
            sinL = ph.enter_context(nc.sbuf_tensor("sinL", [64, NKV], F32))
            kk_l = ph.enter_context(nc.sbuf_tensor("kkl", [64, NKV], BF16))
            vtm_l = ph.enter_context(nc.sbuf_tensor("vtml", [128, 65 * 18], BF16))
            vstage = ph.enter_context(nc.sbuf_tensor("vstage", [64, 512], BF16))
            wt = ph.enter_context(nc.sbuf_tensor("wt", [128, 512], BF16))
            qcur = ph.enter_context(nc.sbuf_tensor("qcurB", [64, TOK], BF16))

            emit_rope_tables(cx, BASE_L, posKV, NKV, cosL, sinL)

            for (off, nn) in B_BLOCKS:
                st = emit_stats(cx, [h1[ci][:, off:off + nn] for ci in range(8)], nn, 1e-6, False,
                                None, cx.scr["r1"][0:1, 0:nn], None, after=(cx.tok("v"),))
                b3 = emit_bcast(cx, cx.scr["r1"][0:1, 0:nn], 0, 128, nn, after=(st,))

                def _v(ve, off=off, nn=nn, b3=b3):
                    cx.wait(ve, b3)
                    for ci in range(8):
                        ve.tensor_tensor(out=cx.scr["D"][:, 0:nn], in0=h1[ci][:, off:off + nn],
                                         in1=cx.ps[0][:, 0:nn], op=ALU.mult)
                        ve.drain()
                        ve.tensor_scalar(out=l_in[ci][:, off:off + nn], in0=cx.scr["D"][:, 0:nn],
                                         scalar1=n2w_s[:, ci:ci + 1], scalar2=None, op0=ALU.mult)
                        ve.drain()
                    cx.inc(ve.tensor_copy(out=scr_mark(cx), in_=scr_mark(cx)), "v")

                cx.V(_v)
            cx.barrier()
            dump(dbg_lin, l_in, NKV) if dbg else None

            for a in range(4):
                def _g(ge, a=a):
                    cx.wait(ge, cx.tok("v"), cx.tok("t"))
                    cx.inc(ge.memset(vtm_l[:], 1.0), "g")

                cx.G(_g)
                g0 = cx.tok("g")
                qtw = emit_wtiles(cx, lwk_b, 64 * a, 64, 8, wt, after=(cx.tok("t"), cx.tok("v")))
                lkpipe = mk_pipe()
                for (off, nn) in B_BLOCKS:
                    def _w(te, qtw=qtw):
                        cx.wait(te, qtw)

                    cx.T(_w)
                    emit_proj_rope(cx, [wt[:, 64 * k:64 * k + 64] for k in range(8)],
                                   [l_in[ci][:, off:off + nn] for ci in range(8)],
                                   8, off, nn, cosL, sinL,
                                   kk_l[:, off:off + nn], True, None, pipe=lkpipe)
                pipe_flush(cx, lkpipe)
                qtw = emit_wtiles(cx, lwv_b, 64 * a, 64, 8, wt, after=(cx.tok("t"), cx.tok("v")))
                for (off, nn) in B_BLOCKS:
                    def _t(te, off=off, nn=nn, qtw=qtw):
                        cx.wait(te, qtw, cx.tok("v"))
                        mm = None
                        for k in range(8):
                            mm = te.matmul(out=cx.ps[0][0:64, 0:nn], lhsT=wt[:, 64 * k:64 * k + 64],
                                           rhs=l_in[k][:, off:off + nn],
                                           start=(k == 0), stop=(k == 7))
                        cx.inc(mm, "t")

                    cx.T(_t)
                    t1 = cx.tok("t")

                    def _v(ve, nn=nn, t1=t1):
                        cx.wait(ve, t1)
                        cx.inc(ve.tensor_copy(out=vstage[:, 0:nn], in_=cx.ps[0][0:64, 0:nn]), "v")

                    cx.V(_v)
                    v1 = cx.tok("v")
                    for u in range(nn // 128):
                        def _t2(te, u=u, v1=v1):
                            cx.wait(te, v1, cx.tok("v"))
                            cx.inc(te.transpose(out=cx.pst[0:128, 0:64],
                                                in_=vstage[:, 128 * u:128 * u + 128],
                                                identity=cx.c["identB"][0:64, 0:64]), "t")

                        cx.T(_t2)
                        t2 = cx.tok("t")
                        kt = off // 128 + u

                        def _v2(ve, kt=kt, t2=t2):
                            cx.wait(ve, t2, g0)
                            cx.inc(ve.tensor_copy(out=vtm_l[:, 65 * kt:65 * kt + 64],
                                                  in_=cx.pst[0:128, 0:64]), "v")

                        cx.V(_v2)

                if dbg and a == 0:
                    def _gd(ge):
                        cx.wait(ge, cx.tok("v"), cx.tok("s"))
                        cx.inc(ge.dma_start(out=dbg_cos.ap(), in_=cosL[:]), "g", 16)
                        cx.inc(ge.dma_start(out=dbg_sin.ap(), in_=sinL[:]), "g", 16)
                        cx.inc(ge.dma_start(out=dbg_k.ap(), in_=kk_l[:]), "g", 16)

                    cx.G(_gd)
                    cx.barrier()
                for j in range(4 * a, 4 * a + 4):
                    qtw = emit_wtiles(cx, lwq_b, 64 * j, 64, 8, wt, after=(cx.tok("t"), cx.tok("v")))

                    def _w(te, qtw=qtw):
                        cx.wait(te, qtw)

                    cx.T(_w)
                    lqpipe = mk_pipe()
                    for tb in range(4):
                        off = HALO + 512 * tb
                        emit_proj_rope(cx, [wt[:, 64 * k:64 * k + 64] for k in range(8)],
                                       [l_in[ci][:, off:off + 512] for ci in range(8)],
                                       8, off, 512, cosL, sinL,
                                       qcur[:, 512 * tb:512 * tb + 512], True, None,
                                       pipe=lqpipe)
                    pipe_flush(cx, lqpipe)

                    if dbg and j == 0:
                        def _gq(ge):
                            cx.wait(ge, cx.tok("v"), cx.tok("s"))
                            cx.inc(ge.dma_start(out=dbg_q.ap(), in_=qcur[:]), "g", 16)

                        cx.G(_gq)
                        cx.barrier()

                    def o_dst(g, j=j):
                        return o_l[j // 2][64 * (j % 2):64 * (j % 2) + 64, 512 * g:512 * g + 512]

                    emit_local_attn(cx, j, qcur, kk_l, vtm_l, o_dst, lmask, lmask_g)
            cx.barrier()
            dump(dbg_ol, o_l, TOK) if dbg else None

        with contextlib.ExitStack() as ph:
            wt = ph.enter_context(nc.sbuf_tensor("wtB3", [128, 8 * 128], BF16))
            xf = [ph.enter_context(nc.sbuf_tensor("xfB3_%d" % i, [128, 512], F32)) for i in range(8)]
            for tb in range(4):
                for m in range(8):
                    qtw = emit_wtiles(cx, lwo_b, 128 * m, 128, 8, wt, after=(cx.tok("t"),))

                    def _t(te, tb=tb, m=m, qtw=qtw):
                        cx.wait(te, qtw, cx.tok("v"))
                        mm = None
                        for k in range(8):
                            mm = te.matmul(out=cx.ps[0][:, 0:512], lhsT=wt[:, 128 * k:128 * k + 128],
                                           rhs=o_l[k][:, 512 * tb:512 * tb + 512],
                                           start=(k == 0), stop=(k == 7))
                        cx.inc(mm, "t")

                    cx.T(_t)
                    t1 = cx.tok("t")

                    def _v(ve, tb=tb, m=m, t1=t1):
                        cx.wait(ve, t1)
                        cx.inc(ve.tensor_tensor(out=h1[m][:, HALO + 512 * tb:HALO + 512 * tb + 512],
                                                in0=h1[m][:, HALO + 512 * tb:HALO + 512 * tb + 512],
                                                in1=cx.ps[0][:, 0:512], op=ALU.add), "v")

                    cx.V(_v)
            cx.barrier()
            for tb in range(4):
                off = HALO + 512 * tb

                def _q(qe, tb=tb):
                    cx.wait(qe, cx.tok("v"))
                    for ci in range(8):
                        cx.inc(qe.dma_start(out=xf[ci][:],
                                            in_=x0own.ap()[128 * ci:128 * ci + 128,
                                                           512 * tb:512 * tb + 512]), "q", 16)

                cx.Q(_q)
                qt = cx.tok("q")
                st = emit_stats(cx, [h1[ci][:, off:off + 512] for ci in range(8)], 512, 1e-5, True,
                                cx.scr["r0"][0:1, 0:512], cx.scr["r1"][0:1, 0:512],
                                cx.scr["r2"][0:1, 0:512], after=(cx.tok("v"),))
                b1 = emit_bcast(cx, cx.scr["r2"][0:1, 0:512], 0, 128, 512, after=(st,))
                b2 = emit_bcast(cx, cx.scr["r1"][0:1, 0:512], 1, 128, 512, after=(st,))

                def _v(ve, off=off, qt=qt, b2=b2):
                    cx.wait(ve, b2, qt)
                    for ci in range(8):
                        ve.tensor_tensor(out=h1[ci][:, off:off + 512], in0=h1[ci][:, off:off + 512],
                                         in1=cx.ps[1][:, 0:512], op=ALU.mult)
                        ve.drain()
                        ve.tensor_tensor(out=h1[ci][:, off:off + 512], in0=h1[ci][:, off:off + 512],
                                         in1=cx.ps[0][:, 0:512], op=ALU.subtract)
                        ve.drain()
                        ve.tensor_tensor(out=h1[ci][:, off:off + 512], in0=h1[ci][:, off:off + 512],
                                         in1=xf[ci][:], op=ALU.add)
                        ve.drain()
                    cx.inc(ve.tensor_copy(out=scr_mark(cx), in_=scr_mark(cx)), "v")

                cx.V(_v)
            cx.barrier()
            if dbg:
                def _gx(ge):
                    cx.wait(ge, cx.tok("v"))
                    for ci in range(8):
                        cx.inc(ge.dma_start(out=dbg_x1.ap()[128 * ci:128 * ci + 128, :],
                                            in_=h1[ci][:, HALO:NKV]), "g", 16)

                cx.G(_gx)
                cx.barrier()
        po.close()

        NT = 256
        with contextlib.ExitStack() as ph:
            xnT = ph.enter_context(nc.sbuf_tensor("xnT", [128, 8 * NT], F32))
            xqT = ph.enter_context(nc.sbuf_tensor("xqT", [128, 8 * NT], BF16))
            fsb = [ph.enter_context(nc.sbuf_tensor("fsb%d" % i, [128, NT], F32)) for i in range(8)]
            hf = ph.enter_context(nc.sbuf_tensor("hf", [128, 32 * NT], F32))
            hq = ph.enter_context(nc.sbuf_tensor("hq", [128, 32 * NT], BF16))
            wg0 = ph.enter_context(nc.sbuf_tensor("wg0", [128, 1024], BF16))
            wg1 = ph.enter_context(nc.sbuf_tensor("wg1", [128, 1024], BF16))
            wu0 = ph.enter_context(nc.sbuf_tensor("wu0", [128, 1024], BF16))
            wu1 = ph.enter_context(nc.sbuf_tensor("wu1", [128, 1024], BF16))
            wd0 = ph.enter_context(nc.sbuf_tensor("wd0", [128, 4096], BF16))
            wd1 = ph.enter_context(nc.sbuf_tensor("wd1", [128, 4096], BF16))
            amx = ph.enter_context(nc.sbuf_tensor("amx", [128, 8], F32))
            x2stg = ph.enter_context(nc.sbuf_tensor("x2stg", [128, NT], BF16))
            rowx = ph.enter_context(nc.sbuf_tensor("rowx", [1, NT], F32))
            rowy = ph.enter_context(nc.sbuf_tensor("rowy", [1, NT], F32))
            rowz = ph.enter_context(nc.sbuf_tensor("rowz", [1, NT], F32))

            def xnc(ci):
                return xnT[:, NT * ci:NT * ci + NT]

            def hs(m):
                return hf[:, NT * m:NT * m + NT]

            def hqs(m):
                return hq[:, NT * m:NT * m + NT]

            for tc in range(2048 // NT):
                off = HALO + NT * tc
                st = emit_stats(cx, [h1[ci][:, off:off + NT] for ci in range(8)], NT, 1e-5, True,
                                cx.scr["r0"][0:1, 0:NT], cx.scr["r1"][0:1, 0:NT],
                                cx.scr["r2"][0:1, 0:NT], after=(cx.tok("v"), cx.tok("g")))
                b1 = emit_bcast(cx, cx.scr["r2"][0:1, 0:NT], 2, 128, NT, after=(st,))
                b2 = emit_bcast(cx, cx.scr["r1"][0:1, 0:NT], 3, 128, NT, after=(st,))

                def _v1(ve, off=off, b2=b2):
                    cx.wait(ve, b2)
                    for ci in range(8):
                        ve.tensor_tensor(out=xnc(ci), in0=h1[ci][:, off:off + NT],
                                         in1=cx.ps[3][:, 0:NT], op=ALU.mult)
                    ve.drain()
                    for ci in range(8):
                        ve.tensor_tensor(out=xnc(ci), in0=xnc(ci),
                                         in1=cx.ps[2][:, 0:NT], op=ALU.subtract)
                    cx.inc(ve.tensor_copy(out=scr_mark(cx), in_=scr_mark(cx)), "v")

                cx.V(_v1)

                emit_absmax(cx, [xnc(ci) for ci in range(8)], amx, rowx, rowy, NT)

                sb = emit_bcast(cx, rowy[0:1, 0:NT], 2, 128, NT, after=(cx.tok("v"),))

                def _vE3(ve, sb=sb):
                    cx.wait(ve, sb)
                    cx.inc(ve.tensor_copy(out=cx.scr["E"][:, 0:NT], in_=cx.ps[2][:, 0:NT]), "v")

                cx.V(_vE3)
                e3 = cx.tok("v")
                t3 = cx.tok("t")

                def _g3(ge, e3=e3, t3=t3):
                    cx.wait(ge, e3, t3, cx.tok("g"))
                    for p in range(4):
                        for j in range(2):
                            dst = (cx.scr["D"][:, 256 * j:256 * j + 256] if p == 0 else
                                   xnT[:, 512 * (p - 1) + 256 * j:512 * (p - 1) + 256 * j + 256])
                            cx.inc(ge.tensor_tensor(
                                out=dst, in0=xnT[:, 512 * p + 256 * j:512 * p + 256 * j + 256],
                                in1=cx.scr["E"][:, 0:NT], op=ALU.mult), "g")

                cx.G(_g3)
                g3b = cx.cnt["g"] - 8
                for p in range(4):
                    s_ap = (cx.scr["D"][:, 0:512] if p == 0 else
                            xnT[:, 512 * (p - 1):512 * p])
                    emit_round_clip(cx, s_ap, xqT[:, 512 * p:512 * p + 512], -128, 127,
                                    n=512, after=(("g", g3b + 2 * (p + 1)),))

                vpre4 = cx.tok("v")

                def _gi4(ge, vpre4=vpre4):
                    cx.wait(ge, vpre4, cx.tok("g"))
                    ge.memset(cx.scr["E"][:, 0:NT], -3.0e38)
                    cx.inc(ge.memset(cx.scr["E"][:, NT:2 * NT], 3.0e38), "g")

                cx.G(_gi4)
                g_init4 = cx.tok("g")
                t_base = cx.cnt["t"]
                s_base = cx.cnt["s"]
                v_base = cx.cnt["v"]
                for m in range(32):
                    ev = m % 2 == 0
                    qg = emit_wtiles(cx, wgq, 128 * m, 128, 8, wg0 if ev else wg1,
                                     after=(("t", t_base + max(0, m - 1)),))
                    qu = emit_wtiles(cx, wuq, 128 * m, 128, 8, wu0 if ev else wu1,
                                     after=(qg,))
                    psG = cx.ps[0] if ev else cx.ps[2]
                    psU = cx.ps[1] if ev else cx.ps[3]
                    scrR = "A" if ev else "C"
                    scrS = "B" if ev else "D"

                    def _t(te, m=m, qu=qu, psG=psG, psU=psU, ev=ev):
                        cx.wait(te, qu, ("s", s_base + max(0, m - 1)),
                                ("v", v_base + max(0, m - 1)))
                        wgt = wg0 if ev else wg1
                        wut = wu0 if ev else wu1
                        for k in range(8):
                            te.matmul(out=psG[:, 0:NT], lhsT=wgt[:, 128 * k:128 * k + 128],
                                      rhs=xqT[:, NT * k:NT * k + NT],
                                      start=(k == 0), stop=(k == 7))
                        mm = None
                        for k in range(8):
                            mm = te.matmul(out=psU[:, 0:NT], lhsT=wut[:, 128 * k:128 * k + 128],
                                           rhs=xqT[:, NT * k:NT * k + NT],
                                           start=(k == 0), stop=(k == 7))
                        cx.inc(mm, "t")

                    cx.T(_t)
                    t1 = cx.tok("t")

                    def _s(se, t1=t1, m=m, psG=psG, scrR=scrR):
                        cx.wait(se, t1, ("v", v_base + max(0, m - 1)))
                        cx.inc(se.activation(out=cx.scr[scrR][:, 0:NT], in_=psG[:, 0:NT],
                                             func=AF.Relu), "s")

                    cx.Sc(_s)
                    s1 = cx.tok("s")

                    def _v3(ve, m=m, s1=s1, psU=psU, scrR=scrR, scrS=scrS):
                        if m == 0:
                            cx.wait(ve, s1, g_init4)
                        else:
                            cx.wait(ve, s1)
                        ve.tensor_tensor(out=cx.scr[scrS][:, 0:NT], in0=cx.scr[scrR][:, 0:NT],
                                         in1=cx.scr[scrR][:, 0:NT], op=ALU.mult)
                        ve.drain()
                        ve.tensor_tensor(out=hs(m), in0=cx.scr[scrS][:, 0:NT],
                                         in1=psU[:, 0:NT], op=ALU.mult)
                        ve.drain()
                        ve.tensor_tensor(out=cx.scr["E"][:, 0:NT], in0=cx.scr["E"][:, 0:NT],
                                         in1=hs(m), op=ALU.max)
                        cx.inc(ve.tensor_tensor(out=cx.scr["E"][:, NT:2 * NT],
                                                in0=cx.scr["E"][:, NT:2 * NT],
                                                in1=hs(m), op=ALU.min), "v")

                    cx.V(_v3)

                vfin4 = cx.tok("v")

                def _vA5(ve, vfin4=vfin4):
                    cx.wait(ve, vfin4)
                    cx.inc(ve.scalar_tensor_tensor(out=cx.scr["D"][:, 0:NT],
                                                   in0=cx.scr["E"][:, NT:2 * NT], scalar=-1.0,
                                                   in1=cx.scr["E"][:, 0:NT],
                                                   op0=ALU.mult, op1=ALU.max), "v")

                cx.V(_vA5)
                emit_absmax_tail(cx, amx, None, None, NT, raw_amax_row=rowz)

                def _v4(ve):
                    cx.wait(ve, cx.tok("v"), cx.tok("s"))
                    ve.tensor_tensor(out=rowy[0:1, 0:NT], in0=rowx[0:1, 0:NT],
                                     in1=rowx[0:1, 0:NT], op=ALU.mult)
                    ve.drain()
                    ve.tensor_tensor(out=rowy[0:1, 0:NT], in0=rowy[0:1, 0:NT],
                                     in1=rowx[0:1, 0:NT], op=ALU.mult)
                    ve.drain()
                    ve.tensor_scalar(out=rowy[0:1, 0:NT], in0=rowy[0:1, 0:NT],
                                     scalar1=wrec_s[0:1, 0:1], scalar2=wrec_s[0:1, 0:1],
                                     op0=ALU.mult, op1=ALU.mult)
                    ve.drain()
                    ve.tensor_scalar(out=rowy[0:1, 0:NT], in0=rowy[0:1, 0:NT],
                                     scalar1=wrec_s[0:1, 1:2], scalar2=None, op0=ALU.mult)
                    ve.drain()
                    ve.tensor_tensor(out=rowz[0:1, 0:NT], in0=rowz[0:1, 0:NT],
                                     in1=rowy[0:1, 0:NT], op=ALU.mult)
                    ve.drain()
                    ve.tensor_scalar(out=rowz[0:1, 0:NT], in0=rowz[0:1, 0:NT],
                                     scalar1=1e-5, scalar2=None, op0=ALU.max)
                    ve.drain()
                    ve.reciprocal(out=rowx[0:1, 0:NT], in_=rowz[0:1, 0:NT])
                    ve.drain()
                    ve.tensor_tensor(out=rowx[0:1, 0:NT], in0=rowx[0:1, 0:NT],
                                     in1=rowy[0:1, 0:NT], op=ALU.mult)
                    ve.drain()
                    ve.tensor_scalar(out=rowx[0:1, 0:NT], in0=rowx[0:1, 0:NT],
                                     scalar1=127.0, scalar2=None, op0=ALU.mult)
                    ve.drain()
                    ve.tensor_scalar(out=rowz[0:1, 0:NT], in0=rowz[0:1, 0:NT],
                                     scalar1=1.0 / 127.0, scalar2=wrec_s[0:1, 2:3],
                                     op0=ALU.mult, op1=ALU.mult)
                    ve.drain()
                    cx.inc(ve.tensor_copy(out=scr_mark(cx), in_=scr_mark(cx)), "v")

                cx.V(_v4)

                fb = emit_bcast(cx, rowx[0:1, 0:NT], 2, 128, NT, after=(cx.tok("v"),))

                def _vE6(ve, fb=fb):
                    cx.wait(ve, fb)
                    cx.inc(ve.tensor_copy(out=cx.scr["E"][:, 0:NT], in_=cx.ps[2][:, 0:NT]), "v")

                cx.V(_vE6)
                e6 = cx.tok("v")
                t6 = cx.tok("t")

                def _g6(ge, e6=e6, t6=t6):
                    cx.wait(ge, e6, t6, cx.tok("g"))
                    for p in range(16):
                        for j in range(2):
                            dst = (cx.scr["D"][:, 256 * j:256 * j + 256] if p == 0 else
                                   hf[:, 512 * (p - 1) + 256 * j:512 * (p - 1) + 256 * j + 256])
                            cx.inc(ge.tensor_tensor(
                                out=dst, in0=hf[:, 512 * p + 256 * j:512 * p + 256 * j + 256],
                                in1=cx.scr["E"][:, 0:NT], op=ALU.mult), "g")

                cx.G(_g6)
                g6b = cx.cnt["g"] - 32
                for p in range(16):
                    s_ap = (cx.scr["D"][:, 0:512] if p == 0 else
                            hf[:, 512 * (p - 1):512 * p])
                    emit_round_clip(cx, s_ap, hq[:, 512 * p:512 * p + 512], -128, 127,
                                    n=512, after=(("g", g6b + 2 * (p + 1)),))

                fdb = emit_bcast(cx, rowz[0:1, 0:NT], 3, 128, NT, after=(cx.tok("v"),))

                def _vf(ve, fdb=fdb, g6e=("g", g6b + 32)):
                    cx.wait(ve, fdb, g6e)
                    cx.inc(ve.tensor_copy(out=cx.scr["E"][:, 0:NT], in_=cx.ps[3][:, 0:NT]), "v")

                cx.V(_vf)
                fdb = cx.tok("v")
                t_base2 = cx.cnt["t"]
                v_base2 = cx.cnt["v"]
                for m2 in range(8):
                    qd = emit_wtiles(cx, wdq, 128 * m2, 128, 32, wd0 if m2 % 2 == 0 else wd1,
                                     after=(("t", t_base2 + max(0, m2 - 1)),))
                    psD = cx.ps[0] if m2 % 2 == 0 else cx.ps[1]

                    def _t2(te, m2=m2, qd=qd, psD=psD):
                        cx.wait(te, qd, ("v", v_base2 + max(0, m2 - 1)))
                        wdt = wd0 if m2 % 2 == 0 else wd1
                        mm = None
                        for k in range(32):
                            mm = te.matmul(out=psD[:, 0:NT], lhsT=wdt[:, 128 * k:128 * k + 128],
                                           rhs=hqs(k), start=(k == 0), stop=(k == 31))
                        cx.inc(mm, "t")

                    cx.T(_t2)
                    t2 = cx.tok("t")

                    def _v6(ve, m2=m2, t2=t2, fdb=fdb, psD=psD):
                        cx.wait(ve, t2, fdb)
                        cx.inc(ve.tensor_tensor(out=fsb[m2][:], in0=psD[:, 0:NT],
                                                in1=cx.scr["E"][:, 0:NT], op=ALU.mult), "v")

                    cx.V(_v6)

                if dbg and tc == 4:
                    def _gd4(ge, tc=tc):
                        cx.wait(ge, cx.tok("v"), cx.tok("s"), cx.tok("t"))
                        for ci in range(8):
                            cx.inc(ge.dma_start(out=dbg_xq.ap()[128 * ci:128 * ci + 128, :],
                                                in_=xq[ci][:]), "g", 16)
                            cx.inc(ge.dma_start(out=dbg_fsb.ap()[128 * ci:128 * ci + 128, :],
                                                in_=fsb[ci][:]), "g", 16)
                        cx.inc(ge.dma_start(out=dbg_hq.ap(), in_=hq[:]), "g", 16)
                        cx.inc(ge.dma_start(out=dbg_hf.ap(), in_=hf[:]), "g", 16)
                        cx.inc(ge.dma_start(out=dbg_rows.ap()[0:1, :], in_=rowx[0:1, 0:NT]), "g", 16)
                        cx.inc(ge.dma_start(out=dbg_rows.ap()[1:2, :], in_=rowy[0:1, 0:NT]), "g", 16)
                        cx.inc(ge.dma_start(out=dbg_rows.ap()[2:3, :], in_=rowz[0:1, 0:NT]), "g", 16)

                    cx.G(_gd4)
                    cx.barrier()

                st2 = emit_stats(cx, [fsb[ci][:] for ci in range(8)], NT, 1e-5, True,
                                 cx.scr["r0"][0:1, 0:NT], cx.scr["r1"][0:1, 0:NT],
                                 cx.scr["r2"][0:1, 0:NT], after=(cx.tok("v"),))
                b1 = emit_bcast(cx, cx.scr["r2"][0:1, 0:NT], 2, 128, NT, after=(st2,))
                b2 = emit_bcast(cx, cx.scr["r1"][0:1, 0:NT], 3, 128, NT, after=(st2,))

                def _v7(ve, off=off, b2=b2):
                    cx.wait(ve, b2)
                    for ci in range(8):
                        ve.tensor_tensor(out=fsb[ci][:], in0=fsb[ci][:],
                                         in1=cx.ps[3][:, 0:NT], op=ALU.mult)
                    ve.drain()
                    for ci in range(8):
                        ve.tensor_tensor(out=fsb[ci][:], in0=fsb[ci][:],
                                         in1=cx.ps[2][:, 0:NT], op=ALU.subtract)
                    ve.drain()
                    for ci in range(8):
                        ve.tensor_tensor(out=h1[ci][:, off:off + NT], in0=h1[ci][:, off:off + NT],
                                         in1=fsb[ci][:], op=ALU.add)
                    cx.inc(ve.tensor_copy(out=scr_mark(cx), in_=scr_mark(cx)), "v")

                cx.V(_v7)
                st3 = emit_stats(cx, [h1[ci][:, off:off + NT] for ci in range(8)], NT, 1e-5, True,
                                 cx.scr["r0"][0:1, 0:NT], cx.scr["r1"][0:1, 0:NT],
                                 cx.scr["r2"][0:1, 0:NT], after=(cx.tok("v"),))
                b1 = emit_bcast(cx, cx.scr["r2"][0:1, 0:NT], 2, 128, NT, after=(st3,))
                b2 = emit_bcast(cx, cx.scr["r1"][0:1, 0:NT], 3, 128, NT, after=(st3,))
                def _v8(ve, off=off, b2=b2):
                    cx.wait(ve, b2, cx.tok("g"))
                    for ci in range(8):
                        ve.tensor_tensor(out=fsb[ci][:], in0=h1[ci][:, off:off + NT],
                                         in1=cx.ps[3][:, 0:NT], op=ALU.mult)
                    ve.drain()
                    for ci in range(8):
                        ve.tensor_tensor(out=xqT[:, NT * ci:NT * ci + NT], in0=fsb[ci][:],
                                         in1=cx.ps[2][:, 0:NT], op=ALU.subtract)
                    cx.inc(ve.tensor_copy(out=scr_mark(cx), in_=scr_mark(cx)), "v")

                cx.V(_v8)
                vt8 = cx.tok("v")

                def _g8(ge, tc=tc, vt8=vt8):
                    cx.wait(ge, vt8)
                    for ci in range(8):
                        cx.inc(ge.dma_start(out=x2n_mine.ap()[128 * ci:128 * ci + 128,
                                                               NT * tc:NT * tc + NT],
                                            in_=xqT[:, NT * ci:NT * ci + NT]), "g", 16)

                cx.G(_g8)
            cx.barrier()


        ccX = emit_cc(cx, "AllGather", ALU.bypass, [x2n_mine.ap().opt()], [x2n_ag.ap().opt()],
                      after=(cx.tok("g"),))
        with contextlib.ExitStack() as rg:
            stgx = rg.enter_context(nc.sbuf_tensor("rg2a", [128, TOK], BF16))
            for ci in range(8):
                for half in range(2):
                    def _gi(ge, ci=ci, half=half, ccX=ccX):
                        cx.wait(ge, ccX, cx.tok("g"))
                        cx.inc(ge.indirect_dma_start(
                            out=stgx[:], out_offset=None, in_=x2n_ag[:, :],
                            in_offset=bass.IndirectOffsetOnAxis(
                                ap=idxX_s[:, 2 * ci + half:2 * ci + half + 1], axis=0)), "g", 16)
                        cx.wait(ge, cx.tok("g"))
                        cx.inc(ge.dma_start(out=x2nf_mine.ap()[128 * ci:128 * ci + 128,
                                                               TOK * half:TOK * half + TOK],
                                            in_=stgx[:]), "g", 16)

                    cx.G(_gi)
        cx.barrier()

        f3ph = contextlib.ExitStack()

        def f3sb(name, shape, dt=BF16):
            return f3ph.enter_context(nc.sbuf_tensor("f3" + name, shape, dt))

        xs = [f3sb("xs%d" % i, [128, S]) for i in range(8)]
        qcur = f3sb("qcurC", [64, S])
        kcur = f3sb("kcurC", [64, S])
        vtm = f3sb("vtmC", [128, 65 * 32])
        ocur = f3sb("ocurC", [64, S])
        gmask_s = f3sb("gmaskC", [128, 4 * 512])
        vf_scr = f3sb("vfC", [64, 512])
        wt = f3sb("wtC", [128, 512])
        emit_load_consts(cx)

        def _q0(qe):
            cx.inc(qe.dma_start(out=gmask_s[:], in_=gmask.ap()), "q", 16)
            for ci in range(8):
                cx.inc(qe.dma_start(out=xs[ci][:], in_=x2nf_mine.ap()[128 * ci:128 * ci + 128, :]), "q", 16)

        cx.Q(_q0)
        cx.barrier()

        def proj_plain(w_dram, col0, dst):
            qtw = emit_wtiles(cx, w_dram, col0, 64, 8, wt, after=(cx.tok("t"), cx.tok("v")))
            v_base = cx.cnt["v"]
            ent_v = cx.tok("v")
            for nb in range(8):
                psP = cx.ps[0] if nb % 2 == 0 else cx.ps[1]

                def _t(te, nb=nb, qtw=qtw, psP=psP):
                    if nb < 2:
                        cx.wait(te, qtw, ent_v)
                    else:
                        cx.wait(te, qtw, ("v", v_base + nb - 1))
                    mm = None
                    for k in range(8):
                        mm = te.matmul(out=psP[0:64, 0:512], lhsT=wt[:, 64 * k:64 * k + 64],
                                       rhs=xs[k][:, 512 * nb:512 * nb + 512],
                                       start=(k == 0), stop=(k == 7))
                    cx.inc(mm, "t")

                cx.T(_t)
                t1 = cx.tok("t")

                def _v(ve, nb=nb, t1=t1, dst=dst, psP=psP):
                    cx.wait(ve, t1)
                    cx.inc(ve.tensor_copy(out=dst[:, 512 * nb:512 * nb + 512],
                                          in_=psP[0:64, 0:512]), "v")

                cx.V(_v)

        for j in range(8):
            proj_plain(wqC_b, 64 * j, qcur)
            proj_plain(wkC_b, 64 * j, kcur)
            qtw = emit_wtiles(cx, wvC_b, 64 * j, 64, 8, wt, after=(cx.tok("t"), cx.tok("v")))

            def _w(te, qtw=qtw):
                cx.wait(te, qtw)

            cx.T(_w)
            emit_v_build(cx, [wt[:, 64 * k:64 * k + 64] for k in range(8)],
                         lambda k, nb: xs[k][:, 512 * nb:512 * nb + 512],
                         8, 32, vtm, vf_scr)
            vend = emit_attn_job(cx, qcur, kcur, vtm, ocur, S, None,
                                 cx.c["biasC1"], gmask_s)

            def _g(ge, j=j, vend=vend):
                cx.wait(ge, vend)
                cx.inc(ge.dma_start(out=ogc.ap()[64 * j:64 * j + 64, :], in_=ocur[:]), "g", 16)

            cx.G(_g)
            gt = cx.tok("g")

            def _v(ve, gt=gt):
                cx.wait(ve, gt)

            cx.V(_v)


        f3ph.close()
        ccC = emit_cc(cx, "AllGather", ALU.bypass, [ogc.ap().opt()], [ogc_ag.ap().opt()],
                      after=(cx.tok("g"),))
        with contextlib.ExitStack() as rg:
            stc = rg.enter_context(nc.sbuf_tensor("rg3a", [128, S], BF16))
            stc2 = rg.enter_context(nc.sbuf_tensor("rg3b", [128, TOK], BF16))
            stc3 = rg.enter_context(nc.sbuf_tensor("rg3c", [128, TOK], BF16))
            for ci in range(8):
                def _gi(ge, ci=ci, ccC=ccC):
                    cx.wait(ge, ccC, cx.tok("v"))
                    cx.inc(ge.indirect_dma_start(
                        out=stc[:], out_offset=None, in_=ogc_ag[:, :],
                        in_offset=bass.IndirectOffsetOnAxis(ap=idxA_s[:, ci:ci + 1], axis=0)), "g", 16)

                cx.G(_gi)
                gi = cx.tok("g")

                def _vs(ve, gi=gi):
                    cx.wait(ve, gi)
                    ve.tensor_scalar(out=stc2[:], in0=stc[:, 0:TOK],
                                     scalar1=sel_s[:, 0:1], scalar2=None, op0=ALU.mult)
                    ve.drain()
                    ve.tensor_scalar(out=stc3[:], in0=stc[:, TOK:2 * TOK],
                                     scalar1=sel_s[:, 1:2], scalar2=None, op0=ALU.mult)
                    ve.drain()
                    cx.inc(ve.tensor_tensor(out=stc2[:], in0=stc2[:], in1=stc3[:], op=ALU.add), "v")

                cx.V(_vs)
                vs = cx.tok("v")

                def _go(ge, ci=ci, vs=vs):
                    cx.wait(ge, vs)
                    cx.inc(ge.dma_start(out=og1_mine.ap()[128 * ci:128 * ci + 128, :],
                                        in_=stc2[:]), "g", 16)

                cx.G(_go)
        cx.barrier()

        xn = [cx.sbuf("xnD%d" % i, [128, 512], F32) for i in range(8)]
        xq = [cx.sbuf("xqD%d" % i, [128, 512], BF16) for i in range(8)]
        fsb = [cx.sbuf("fsbD%d" % i, [128, 512], F32) for i in range(8)]
        ogt = [cx.sbuf("ogtD%d" % i, [128, 512], BF16) for i in range(8)]
        wt = cx.sbuf("wtD", [128, 8 * 128], BF16)
        amx = cx.sbuf("amxD", [128, 8], F32)
        rowx = cx.sbuf("rowxD", [1, 512], F32)
        rowy = cx.sbuf("rowyD", [1, 512], F32)

        for tc in range(4):
            def _q(qe, tc=tc):
                cx.wait(qe, cx.tok("t"), cx.tok("v"))
                for ci in range(8):
                    cx.inc(qe.dma_start(out=ogt[ci][:],
                                        in_=og1_mine.ap()[128 * ci:128 * ci + 128,
                                                          512 * tc:512 * tc + 512]), "q", 16)

            cx.Q(_q)
            qt = cx.tok("q")
            for m in range(8):
                qtw = emit_wtiles(cx, woT_b, 128 * m, 128, 8, wt, after=(cx.tok("t"),))

                def _t(te, m=m, qtw=qtw, qt=qt):
                    cx.wait(te, qtw, qt, cx.tok("v"))
                    mm = None
                    for k in range(8):
                        mm = te.matmul(out=cx.ps[0][:, 0:512], lhsT=wt[:, 128 * k:128 * k + 128],
                                       rhs=ogt[k][:], start=(k == 0), stop=(k == 7))
                    cx.inc(mm, "t")

                cx.T(_t)
                t1 = cx.tok("t")

                def _v(ve, m=m, t1=t1):
                    cx.wait(ve, t1)
                    cx.inc(ve.tensor_copy(out=fsb[m][:], in_=cx.ps[0][:, 0:512]), "v")

                cx.V(_v)
            st = emit_stats(cx, [fsb[ci][:] for ci in range(8)], 512, 1e-5, True,
                            cx.scr["r0"][0:1, 0:512], cx.scr["r1"][0:1, 0:512],
                            cx.scr["r2"][0:1, 0:512], after=(cx.tok("v"),))
            b1 = emit_bcast(cx, cx.scr["r2"][0:1, 0:512], 2, 128, 512, after=(st,))
            b2 = emit_bcast(cx, cx.scr["r1"][0:1, 0:512], 3, 128, 512, after=(st,))

            def _v2(ve, tc=tc, b2=b2, qt=qt):
                cx.wait(ve, b2, qt)
                for ci in range(8):
                    ve.tensor_tensor(out=fsb[ci][:], in0=fsb[ci][:], in1=cx.ps[3][:, 0:512], op=ALU.mult)
                    ve.drain()
                    ve.tensor_tensor(out=fsb[ci][:], in0=fsb[ci][:], in1=cx.ps[2][:, 0:512], op=ALU.subtract)
                    ve.drain()
                    ve.tensor_tensor(out=h1[ci][:, HALO + 512 * tc:HALO + 512 * tc + 512], in0=fsb[ci][:],
                                     in1=h1[ci][:, HALO + 512 * tc:HALO + 512 * tc + 512], op=ALU.add)
                    ve.drain()
                cx.inc(ve.tensor_copy(out=scr_mark(cx), in_=scr_mark(cx)), "v")

            cx.V(_v2)
        cx.barrier()

        for tc in range(4):
            off = 512 * tc
            st = emit_stats(cx, [h1[ci][:, HALO + off:HALO + off + 512] for ci in range(8)], 512, 1e-5, True,
                            cx.scr["r0"][0:1, 0:512], cx.scr["r1"][0:1, 0:512],
                            cx.scr["r2"][0:1, 0:512], after=(cx.tok("v"), cx.tok("g")))
            b1 = emit_bcast(cx, cx.scr["r2"][0:1, 0:512], 2, 128, 512, after=(st,))
            b2 = emit_bcast(cx, cx.scr["r1"][0:1, 0:512], 3, 128, 512, after=(st,))

            def _v1(ve, off=off, b2=b2):
                cx.wait(ve, b2)
                for ci in range(8):
                    ve.tensor_tensor(out=xn[ci][:], in0=h1[ci][:, HALO + off:HALO + off + 512],
                                     in1=cx.ps[3][:, 0:512], op=ALU.mult)
                    ve.drain()
                    ve.tensor_tensor(out=xn[ci][:], in0=xn[ci][:], in1=cx.ps[2][:, 0:512], op=ALU.subtract)
                    ve.drain()
                cx.inc(ve.tensor_copy(out=scr_mark(cx), in_=scr_mark(cx)), "v")

            cx.V(_v1)
            st2 = emit_stats(cx, [xn[ci][:] for ci in range(8)], 512, 1e-6, False,
                             None, cx.scr["r1"][0:1, 0:512], None, after=(cx.tok("v"),))
            b3 = emit_bcast(cx, cx.scr["r1"][0:1, 0:512], 2, 128, 512, after=(st2,))

            def _v2(ve, b3=b3):
                cx.wait(ve, b3)
                for ci in range(8):
                    ve.tensor_tensor(out=xn[ci][:], in0=xn[ci][:], in1=cx.ps[2][:, 0:512], op=ALU.mult)
                    ve.drain()
                cx.inc(ve.tensor_copy(out=scr_mark(cx), in_=scr_mark(cx)), "v")

            cx.V(_v2)
            emit_absmax(cx, [xn[ci][:] for ci in range(8)], amx, rowx, rowy)
            sb = emit_bcast(cx, rowy[0:1, 0:512], 2, 128, 512, after=(cx.tok("v"),))
            for ci in range(8):
                def _v3(ve, ci=ci, sb=sb):
                    cx.wait(ve, sb)
                    cx.inc(ve.tensor_tensor(out=cx.scr["D"][:], in0=xn[ci][:],
                                            in1=cx.ps[2][:, 0:512], op=ALU.mult), "v")

                cx.V(_v3)
                emit_round_clip(cx, cx.scr["D"][:], xq[ci][:], -128, 127, after=(cx.tok("v"),))
            def _v4(ve):
                cx.wait(ve, cx.tok("v"))
                ve.tensor_scalar(out=rowx[0:1, 0:512], in0=rowx[0:1, 0:512],
                                 scalar1=wsc_s[0:1, 0:1], scalar2=None, op0=ALU.mult)
                ve.drain()
                cx.inc(ve.tensor_copy(out=scr_mark(cx), in_=scr_mark(cx)), "v")

            cx.V(_v4)
            scb = emit_bcast(cx, rowx[0:1, 0:512], 3, 128, 512, after=(cx.tok("v"),))

            def _vsc(ve, scb=scb):
                cx.wait(ve, scb)
                cx.inc(ve.tensor_copy(out=cx.scr["E"][:], in_=cx.ps[3][:, 0:512]), "v")

            cx.V(_vsc)
            scb = cx.tok("v")
            for m in range(8):
                qtw = emit_wtiles(cx, w8T_b, 128 * m, 128, 8, wt, after=(cx.tok("t"),))

                def _t(te, m=m, qtw=qtw):
                    cx.wait(te, qtw, cx.tok("v"))
                    mm = None
                    for k in range(8):
                        mm = te.matmul(out=cx.ps[0][:, 0:512], lhsT=wt[:, 128 * k:128 * k + 128],
                                       rhs=xq[k][:], start=(k == 0), stop=(k == 7))
                    cx.inc(mm, "t")

                cx.T(_t)
                t1 = cx.tok("t")

                def _v5(ve, m=m, t1=t1, scb=scb):
                    cx.wait(ve, t1, scb)
                    ve.tensor_tensor(out=fsb[m][:], in0=cx.ps[0][:, 0:512],
                                     in1=cx.scr["E"][:], op=ALU.mult)
                    ve.drain()
                    cx.inc(ve.tensor_scalar(out=fsb[m][:], in0=fsb[m][:],
                                            scalar1=fb_s[:, m:m + 1], scalar2=None, op0=ALU.add), "v")

                cx.V(_v5)
            st3 = emit_stats(cx, [fsb[ci][:] for ci in range(8)], 512, 1e-5, True,
                             cx.scr["r0"][0:1, 0:512], cx.scr["r1"][0:1, 0:512],
                             cx.scr["r2"][0:1, 0:512], after=(cx.tok("v"),))
            b1 = emit_bcast(cx, cx.scr["r2"][0:1, 0:512], 2, 128, 512, after=(st3,))
            b2 = emit_bcast(cx, cx.scr["r1"][0:1, 0:512], 3, 128, 512, after=(st3,))
            for ci in range(8):
                def _v6(ve, ci=ci, off=off, b2=b2):
                    cx.wait(ve, b2, cx.tok("g"))
                    ve.tensor_tensor(out=fsb[ci][:], in0=fsb[ci][:], in1=cx.ps[3][:, 0:512], op=ALU.mult)
                    ve.drain()
                    ve.tensor_tensor(out=fsb[ci][:], in0=fsb[ci][:], in1=cx.ps[2][:, 0:512], op=ALU.subtract)
                    ve.drain()
                    cx.inc(ve.tensor_tensor(out=h1[ci][:, HALO + off:HALO + off + 512], in0=h1[ci][:, HALO + off:HALO + off + 512],
                                            in1=fsb[ci][:], op=ALU.add), "v")

                cx.V(_v6)
                vt = cx.tok("v")

                def _g(ge, ci=ci, tc=tc, vt=vt):
                    cx.wait(ge, vt)
                    cx.inc(ge.dma_start(out=x4out.ap()[128 * ci:128 * ci + 128,
                                                       512 * tc:512 * tc + 512],
                                        in_=h1[ci][:, HALO + 512 * tc:HALO + 512 * tc + 512]), "g", 16)

                cx.G(_g)
        cx.barrier()


        def _gend(ge):
            cx.wait(ge, cx.tok("g"))

        cx.G(_gend)
    return nc


def _prep_fused(inputs):
    consts = _consts_np()
    gm = _gmask_np()
    x0 = np.asarray(inputs["L0_attn_hidden"], np.float32)
    n1 = np.ascontiguousarray(np.asarray(inputs["L0_attn_norm1_w"], np.float32).reshape(8, 128).T)
    n2 = np.ascontiguousarray(np.asarray(inputs["L0_attn_norm2_w"], np.float32).reshape(8, 128).T)
    wq = np.asarray(inputs["L0_attn_g_wq"], np.float32)
    wkvd = np.ascontiguousarray(np.asarray(inputs["L0_attn_g_kv_down"], np.float32))
    wku = np.asarray(inputs["L0_attn_g_k_up"], np.float32)
    wvu = np.asarray(inputs["L0_attn_g_v_up"], np.float32)
    gwo = np.ascontiguousarray(np.asarray(inputs["L0_attn_g_wo"], np.float32))
    lwq = np.ascontiguousarray(_perm_cols(np.asarray(inputs["L0_attn_l_wq"], np.float32)))
    lwk = np.ascontiguousarray(_perm_cols(np.asarray(inputs["L0_attn_l_wk"], np.float32)))
    lwv = np.ascontiguousarray(np.asarray(inputs["L0_attn_l_wv"], np.float32))
    lwo = np.ascontiguousarray(np.asarray(inputs["L0_attn_l_wo"], np.float32))
    wgT = np.ascontiguousarray(np.asarray(inputs["L0_ffn_w_gate"], np.float32).T)
    wuT = np.ascontiguousarray(np.asarray(inputs["L0_ffn_w_up"], np.float32).T)
    wdT = np.ascontiguousarray(np.asarray(inputs["L0_ffn_w_down"], np.float32).T)
    wqC = np.ascontiguousarray(np.asarray(inputs["L1_attn_W_Q"], np.float32).transpose(2, 0, 1).reshape(D, D))
    wkC = np.ascontiguousarray(np.asarray(inputs["L1_attn_W_K"], np.float32).transpose(2, 0, 1).reshape(D, D))
    wvC = np.ascontiguousarray(np.asarray(inputs["L1_attn_W_V"], np.float32).transpose(2, 0, 1).reshape(D, D))
    woT = np.ascontiguousarray(np.asarray(inputs["L1_attn_W_O"], np.float32).T)
    w8T = np.ascontiguousarray(np.asarray(inputs["L1_ffn_w_int8"], np.int32).T)
    ws = np.full((1, 4), float(np.asarray(inputs["L1_ffn_weight_scale"], np.float32)), np.float32)
    fb = np.ascontiguousarray(np.asarray(inputs["L1_ffn_bias"], np.float32).reshape(8, 128).T)
    pos = np.arange(S, dtype=np.float32).reshape(1, S)
    maps = []
    for r in range(NCORES):
        b, hs, own0 = r // 2, (r % 2) * 8, (r % 2) * TOK
        x0fb = np.ascontiguousarray(x0[b].T)
        m = dict(consts)
        m["x0f"] = x0fb
        m["x0own"] = np.ascontiguousarray(x0fb[:, own0:own0 + TOK])
        m["x0halo"] = np.ascontiguousarray(x0fb[:, 1792:2048])
        m["n1w"] = n1
        m["n2w"] = n2
        m["wqA"] = np.ascontiguousarray(_perm_cols(wq[:, 64 * hs:64 * (hs + 8)]))
        m["wkvd"] = wkvd
        m["wkuA"] = np.ascontiguousarray(_perm_cols(wku[:, 128 * (r % 2):128 * (r % 2) + 128]))
        m["wvuA"] = np.ascontiguousarray(wvu[:, 128 * (r % 2):128 * (r % 2) + 128])
        m["gmask"] = gm
        m["posAll"] = pos
        m["posKV"] = np.concatenate([np.arange(1792, 2048), np.arange(own0, own0 + TOK)]
                                    ).astype(np.float32).reshape(1, NKV)
        m["lmask"] = _lmask_np(r)
        m["gwo_f"] = gwo
        m["lwq_f"] = lwq
        m["lwk_f"] = lwk
        m["lwv_f"] = lwv
        m["lwo_f"] = lwo
        m["wgT"] = np.ascontiguousarray(wgT[128 * r:128 * r + 128])
        m["wuT"] = np.ascontiguousarray(wuT[128 * r:128 * r + 128])
        m["wdT"] = np.ascontiguousarray(wdT[512 * r:512 * r + 512])
        m["wqC_f"] = np.ascontiguousarray(wqC[:, 512 * (r % 2):512 * (r % 2) + 512])
        m["wkC_f"] = np.ascontiguousarray(wkC[:, 512 * (r % 2):512 * (r % 2) + 512])
        m["wvC_f"] = np.ascontiguousarray(wvC[:, 512 * (r % 2):512 * (r % 2) + 512])
        m["woT_f"] = woT
        m["w8T_i"] = w8T
        m["wscale"] = ws
        m["fbias"] = fb
        sel = np.zeros((128, 2), np.float32)
        sel[:, r % 2] = 1.0
        m["selOwn"] = sel
        p = np.arange(128, dtype=np.uint32)
        m["idxA"] = np.ascontiguousarray(
            np.stack([1024 * b + 128 * ci + p for ci in range(8)], axis=1)).astype(np.uint32)
        m["idxX"] = np.ascontiguousarray(
            np.stack([1024 * (2 * b + half) + 128 * ci + p
                      for ci in range(8) for half in range(2)], axis=1)).astype(np.uint32)
        maps.append(m)
    return maps


def kernel_multi(**inputs):
    return _kernel_multi_impl(**inputs)


_kernel_multi_impl = kernel


def kernel(**inputs):
    if "F" not in _cache:
        _cache["F"] = build_fused()
    maps = _prep_fused(inputs)
    res = run_bass_kernel_spmd(_cache["F"], maps, core_ids=list(range(NCORES)))
    out = np.empty((B, S, D), np.float32)
    for r in range(NCORES):
        b, own0 = r // 2, (r % 2) * TOK
        out[b, own0:own0 + TOK, :] = np.asarray(res.results[r]["x4out"]).T
    return out
